# version 03
# baseline (speedup 1.0000x reference)
import math
from contextlib import ExitStack

import numpy as np
import concourse.bass as bass
import concourse.mybir as mybir
from concourse.bass_utils import run_bass_kernel_spmd
from concourse.bass_core_rust import IndirectOffsetOnAxis

F32 = mybir.dt.float32
BF16 = mybir.dt.bfloat16
U32 = mybir.dt.uint32
I32 = mybir.dt.int32
AF = mybir.ActivationFunctionType
ALU = mybir.AluOpType
AXX = mybir.AxisListType.X

T = 4096
D = 1024
NT = 32
KC = 8
PI = math.pi
INV2PI = 1.0 / (2.0 * PI)


class Sem:
    _n = [0]

    def __init__(self, ctx, nc, name):
        Sem._n[0] += 1
        self.h = ctx.enter_context(nc.semaphore(f"{name}_{Sem._n[0]}"))
        self.v = 0


class Scope:
    def __init__(self, kb):
        self.kb = kb
        self.ctx = ExitStack()

    def sb(self, shape, dt, name=None):
        return self.ctx.enter_context(
            self.kb.nc.sbuf_tensor(self.kb.name((name or "sb") + "_"),
                                   shape, dt))

    def ps(self, shape, dt=F32, name=None):
        return self.ctx.enter_context(
            self.kb.nc.psum_tensor(self.kb.name((name or "ps") + "_"),
                                   shape, dt))

    def __enter__(self):
        return self

    def __exit__(self, *a):
        self.ctx.close()
        return False


class KB:
    def __init__(self, nc):
        self.nc = nc
        self.ctx = ExitStack()
        self._n = 0

    def name(self, p="t"):
        self._n += 1
        return f"{p}{self._n}"

    def sb(self, shape, dt, name=None):
        tgt = self._redirect[-1].ctx if getattr(self, "_redirect", None) else self.ctx
        return tgt.enter_context(
            self.nc.sbuf_tensor(self.name((name or "sb") + "_"), shape, dt))

    def ps(self, shape, dt=F32, name=None):
        tgt = self._redirect[-1].ctx if getattr(self, "_redirect", None) else self.ctx
        return tgt.enter_context(
            self.nc.psum_tensor(self.name((name or "ps") + "_"), shape, dt))

    def push_scope(self, sc):
        if not hasattr(self, "_redirect"):
            self._redirect = []
        self._redirect.append(sc)

    def pop_scope(self):
        self._redirect.pop()

    def sem(self, name=None):
        return Sem(self.ctx, self.nc, name or self.name("sem"))

    def nsem(self, name):
        if not hasattr(self, "_nsems"):
            self._nsems = {}
        if name not in self._nsems:
            self._nsems[name] = self.sem(name)
        return self._nsems[name]

    def esem(self, ename):
        if not hasattr(self, "_esems"):
            self._esems = {}
        if ename not in self._esems:
            self._esems[ename] = self.sem("edge_" + ename)
        return self._esems[ename]

    def edge(self, ename, inst, e):
        s = self.esem(ename)
        inst.then_inc(s.h, 1)
        s.v += 1
        e.wait_ge(s.h, s.v)

    def scope(self):
        return Scope(self)


def build_consts(kb, want, sc=None):
    nc = kb.nc
    al = sc if sc is not None else kb
    c = {}
    alloc = {
        "ident_bf": ([128, 128], BF16), "ident_f": ([128, 128], F32),
        "ones_row_bf": ([1, 128], BF16), "ones_row_f": ([1, 128], F32),
        "ones_col_f": ([128, 1], F32), "uexc_f": ([128, 128], F32),
        "allones_f": ([128, 128], F32), "pos_tm": ([128, 32], F32),
        "indic_t": ([128, 8], F32), "indic8": ([8, 128], F32),
    }
    for w in want:
        c[w] = al.sb(*alloc[w], name="c_" + w)
    need_io = any(w in want for w in ("ident_bf", "ident_f", "uexc_f"))
    io = al.sb([128, 128], F32, "c_io") if need_io else None
    s1 = (al.sb([128, 136], F32, "c_s1")
          if ("indic_t" in want or "indic8" in want) else None)
    with nc.Block() as blk:
        @blk.gpsimd
        def _(g):
            if io is not None:
                g.iota(io[:], [[1, 128]], channel_multiplier=-1,
                       allow_small_or_imprecise_dtypes=True)
            if "pos_tm" in c:
                g.iota(c["pos_tm"][:], [[1, 32]], channel_multiplier=32,
                       allow_small_or_imprecise_dtypes=True)
            if "indic_t" in c:
                g.iota(s1[:, 0:8], [[-16, 8]], channel_multiplier=1,
                       allow_small_or_imprecise_dtypes=True)
            if "indic8" in c:
                g.iota(s1[:, 8:136], [[1, 128]], channel_multiplier=-16,
                       allow_small_or_imprecise_dtypes=True)
    with nc.Block() as blk:
        @blk.vector
        def _(v):
            if "ident_bf" in c:
                v.tensor_scalar(c["ident_bf"][:], io[:], 0.0, None, ALU.is_equal)
            if "ident_f" in c:
                v.tensor_scalar(c["ident_f"][:], io[:], 0.0, None, ALU.is_equal)
            if "uexc_f" in c:
                v.tensor_scalar(c["uexc_f"][:], io[:], 0.0, None, ALU.is_gt)
            for nm in ("ones_row_bf", "ones_row_f", "ones_col_f", "allones_f"):
                if nm in c:
                    v.memset(c[nm][:], 1.0)
            if "indic_t" in c:
                a = s1[:, 0:8]
                b_ = s1[:, 136:144] if False else None
                i1 = v.tensor_scalar(c["indic_t"][:], a, -0.5, None, ALU.is_gt)
                kb.edge("v", i1, v)
                i1 = v.tensor_scalar(s1[:, 0:8], a, 15.5, None, ALU.is_lt)
                kb.edge("v", i1, v)
                i1 = v.tensor_mul(c["indic_t"][:], c["indic_t"][:], s1[:, 0:8])
                kb.edge("v", i1, v)
            if "indic8" in c:
                a = s1[0:8, 8:136]
                i1 = v.tensor_scalar(c["indic8"][:], a, -0.5, None, ALU.is_gt)
                kb.edge("v", i1, v)
                i1 = v.tensor_scalar(s1[0:8, 8:136], a, 15.5, None, ALU.is_lt)
                kb.edge("v", i1, v)
                i1 = v.tensor_mul(c["indic8"][:], c["indic8"][:], s1[0:8, 8:136])
                kb.edge("v", i1, v)
    return c


def emit_ln(kb, x_dram, out_slice, keep_x=None, nt=NT, group=8, tile_off=0):
    nc = kb.nc
    with kb.scope() as sc:
        xg = sc.sb([128, group, D], F32)
        sq = sc.sb([128, 2, D], F32)
        st = sc.sb([128, group, 8], F32)
        ld = kb.nsem("ln_ld")
        s1 = kb.nsem("ln_s1")
        v1 = kb.nsem("ln_v1")
        for g0 in range(0, nt, group):
            gn = min(group, nt - g0)
            ld0, s10, v10 = ld.v, s1.v, v1.v
            with nc.Block() as blk:
                @blk.sync
                def _(e, g0=g0, gn=gn):
                    for i in range(gn):
                        t = tile_off + g0 + i
                        e.dma_start(out=xg[:, i, :],
                                    in_=x_dram[t * 128:(t + 1) * 128, :]
                                    ).then_inc(ld.h, 16)

                @blk.scalar
                def _(e, g0=g0, gn=gn, ld0=ld0, s10=s10, v10=v10):
                    e.wait_ge(ld.h, ld0 + 16 * gn)
                    for i in range(gn):
                        e.activation(sq[:, i % 2, :], xg[:, i, :], AF.Square,
                                     accum_out=st[:, i, 0:1]).then_inc(s1.h, 1)
                    e.wait_ge(v1.h, v10 + 1)
                    e.activation(st[:, 0:gn, 5:6], st[:, 0:gn, 4:5],
                                 AF.Sqrt).then_inc(s1.h, 1)

                @blk.vector
                def _(e, g0=g0, gn=gn, ld0=ld0, s10=s10, v10=v10):
                    e.wait_ge(ld.h, ld0 + 16 * gn)
                    for i in range(gn):
                        ri = e.reduce_sum(st[:, i, 1:2], xg[:, i, :], axis=AXX)
                    kb.edge("v", ri, e)
                    e.wait_ge(s1.h, s10 + gn)
                    g_ = st[:, 0:gn, :]
                    i1 = e.tensor_scalar(g_[:, :, 2:3], g_[:, :, 1:2], 1.0 / D,
                                         None, ALU.mult)
                    kb.edge("v", i1, e)
                    i1 = e.tensor_tensor(g_[:, :, 3:4], g_[:, :, 2:3],
                                         g_[:, :, 2:3], ALU.mult)
                    kb.edge("v", i1, e)
                    i1 = e.scalar_tensor_tensor(g_[:, :, 4:5], g_[:, :, 0:1],
                                                1.0 / D, g_[:, :, 3:4],
                                                ALU.mult, ALU.subtract)
                    kb.edge("v", i1, e)
                    e.tensor_scalar(g_[:, :, 4:5], g_[:, :, 4:5], 1e-5, None,
                                    ALU.add).then_inc(v1.h, 1)
                    e.wait_ge(s1.h, s10 + gn + 1)
                    i1 = e.reciprocal(g_[:, :, 6:7], g_[:, :, 5:6])
                    kb.edge("v", i1, e)
                    i1 = e.tensor_tensor(g_[:, :, 7:8], g_[:, :, 2:3],
                                         g_[:, :, 6:7], ALU.mult)
                    kb.edge("v", i1, e)
                    i1 = e.tensor_scalar(g_[:, :, 7:8], g_[:, :, 7:8], -1.0, None,
                                         ALU.mult)
                    kb.edge("v", i1, e)
                    for i in range(gn):
                        e.tensor_scalar(out_slice(g0 + i), xg[:, i, :],
                                        st[:, i, 6:7], st[:, i, 7:8],
                                        ALU.mult, ALU.add)
                    if keep_x is not None:
                        for i in range(gn):
                            e.tensor_copy(keep_x(g0 + i), xg[:, i, :])
            ld.v += 16 * gn
            s1.v += gn + 1
            v1.v += 1


def emit_transposes(kb, pairs, ident_bf, grp=2):
    nc = kb.nc
    groups = [pairs[i:i + grp] for i in range(0, len(pairs), grp)]
    with kb.scope() as sc:
        ps = sc.ps([128, 2 * grp, 512], F32)
        pe = kb.nsem("tp_pe")
        ev = kb.nsem("tp_ev")
        pe0, ev0 = pe.v, ev.v
        with nc.Block() as blk:
            @blk.tensor
            def _(e):
                for gi, grp_pairs in enumerate(groups):
                    if gi >= 2:
                        e.wait_ge(ev.h, ev0 + gi - 1)
                    base = (gi % 2) * grp
                    for j, (src, dst) in enumerate(grp_pairs):
                        mi = e.matmul(ps[:, base + j, 0:128], src, ident_bf[:],
                                      start=True, stop=True)
                    mi.then_inc(pe.h, 1)

            @blk.vector
            def _(e):
                for gi, grp_pairs in enumerate(groups):
                    e.wait_ge(pe.h, pe0 + gi + 1)
                    base = (gi % 2) * grp
                    for j, (src, dst) in enumerate(grp_pairs):
                        mi = e.tensor_copy(dst, ps[:, base + j, 0:128])
                    mi.then_inc(ev.h, 1)
        pe.v += len(groups)
        ev.v += len(groups)


def emit_l0_attn(kb, x0, win, x1, stage="wo", dbg=None):
    nc = kb.nc
    c = build_consts(kb, ["ident_bf", "ident_f", "ones_row_f"])
    ctxd = nc.dram_tensor(kb.name("ctxd"), [KC, 128, T], BF16)

    bsb = {}
    for nm in ("bq", "bk", "bv", "bo"):
        bsb[nm] = kb.sb([128, KC], F32, nm + "_sb")
    wl = kb.sem("wl")
    with nc.Block() as blk:
        @blk.gpsimd
        def _(g):
            for nm in ("bq", "bk", "bv", "bo"):
                g.dma_start(out=bsb[nm][:], in_=win[nm]).then_inc(wl.h, 16)
            g.wait_ge(wl.h, wl.v + 64)
    wl.v += 64

    c4 = kb.sb([128, T], BF16, "c4")
    s4 = kb.sb([128, T], BF16, "s4")
    with kb.scope() as sc:
        ang = sc.sb([32, T], F32, "ang")
        carg = sc.sb([32, T], F32, "carg")
        iscr = sc.sb([32, T], I32, "iscr")
        invf = sc.sb([1, 32], F32, "invf")
        posr = sc.sb([1, T], F32, "posr")
        c32 = sc.sb([32, T], F32, "c32")
        s32 = sc.sb([32, T], F32, "s32")
        aps = sc.ps([32, 512], F32, name="angps")
        with nc.Block() as blk:
            @blk.gpsimd
            def _(g):
                g.iota(invf[:], [[1, 32]], channel_multiplier=0,
                       allow_small_or_imprecise_dtypes=True)
                g.iota(posr[:], [[1, T]], channel_multiplier=0,
                       allow_small_or_imprecise_dtypes=True)
        with nc.Block() as blk:
            @blk.scalar
            def _(e):
                e.activation(invf[:], invf[:], AF.Exp,
                             scale=-math.log(10000.0) / 32.0)
        sm = kb.sem("angmm")
        with nc.Block() as blk:
            @blk.tensor
            def _(e):
                for q in range(T // 512):
                    if q >= 1:
                        e.wait_ge(sm.h, 2 * q)
                    e.matmul(aps[:], invf[:], posr[:, q * 512:(q + 1) * 512],
                             start=True, stop=True).then_inc(sm.h, 1)

            @blk.vector
            def _(e):
                for q in range(T // 512):
                    e.wait_ge(sm.h, 2 * q + 1)
                    e.tensor_copy(ang[:, q * 512:(q + 1) * 512],
                                  aps[:]).then_inc(sm.h, 1)
        sm.v += 2 * (T // 512)
        w32 = sc.sb([32, T], F32, "w32")
        with nc.Block() as blk:
            @blk.vector
            def _(e):
                i1 = e.tensor_scalar(carg[:], ang[:], INV2PI, None, ALU.mult)
                kb.edge("v", i1, e)
                i1 = e.tensor_copy(iscr[:], carg[:])
                kb.edge("v", i1, e)
                i1 = e.tensor_copy(carg[:], iscr[:])
                kb.edge("v", i1, e)
                i1 = e.scalar_tensor_tensor(w32[:], carg[:], -2.0 * PI, ang[:],
                                            ALU.mult, ALU.add)
                kb.edge("v", i1, e)
                i1 = e.tensor_scalar(c32[:], w32[:], PI, None, ALU.is_gt)
                kb.edge("v", i1, e)
                i1 = e.scalar_tensor_tensor(w32[:], c32[:], -2.0 * PI, w32[:],
                                            ALU.mult, ALU.add)
                kb.edge("v", i1, e)
                i1 = e.tensor_scalar(c32[:], w32[:], -PI, None, ALU.is_lt)
                kb.edge("v", i1, e)
                i1 = e.scalar_tensor_tensor(w32[:], c32[:], 2.0 * PI, w32[:],
                                            ALU.mult, ALU.add)
                kb.edge("v", i1, e)
                i1 = e.tensor_scalar(carg[:], w32[:], PI / 2, None, ALU.add)
                kb.edge("v", i1, e)
                i1 = e.tensor_scalar(c32[:], carg[:], PI, None, ALU.is_gt)
                kb.edge("v", i1, e)
                e.scalar_tensor_tensor(carg[:], c32[:], -2.0 * PI, carg[:],
                                       ALU.mult, ALU.add)
        with nc.Block() as blk:
            @blk.scalar
            def _(e):
                i1 = e.activation(s32[:], w32[:], AF.Sin)
                kb.edge("s", i1, e)
                e.activation(c32[:], carg[:], AF.Sin)
        rp = kb.sem("rep")
        with nc.Block() as blk:
            @blk.gpsimd
            def _(e):
                n = rp.v
                for r in range(4):
                    e.dma_start(out=c4[r * 32:(r + 1) * 32, :],
                                in_=c32[:]).then_inc(rp.h, 16)
                    e.dma_start(out=s4[r * 32:(r + 1) * 32, :],
                                in_=s32[:]).then_inc(rp.h, 16)
                e.wait_ge(rp.h, n + 128)
        rp.v += 128

    if stage == "trig":
        dsem = kb.sem()
        with nc.Block() as blk:
            @blk.gpsimd
            def _(e):
                e.dma_start(out=dbg[:, 0, :], in_=c4[:]).then_inc(dsem.h, 16)
                e.dma_start(out=dbg[:, 1, :], in_=s4[:]).then_inc(dsem.h, 16)
                e.wait_ge(dsem.h, 32)
        return

    lnT = kb.sb([128, KC, T], BF16, "lnT")
    with kb.scope() as sc:
        lnw = sc.sb([128, NT, D], BF16, "lnw")
        emit_ln(kb, x0, lambda t: lnw[:, t, :])
        if stage == "ln0":
            dsem = kb.sem()
            with nc.Block() as blk:
                @blk.gpsimd
                def _(e):
                    for t in range(8):
                        e.dma_start(out=dbg[:, t, 0:D],
                                    in_=lnw[:, t, :]).then_inc(dsem.h, 16)
                    e.wait_ge(dsem.h, 128)
            return
        pairs = [(lnw[:, t, k * 128:(k + 1) * 128],
                  lnT[:, k, t * 128:(t + 1) * 128])
                 for t in range(NT) for k in range(KC)]
        import os
        ntp = int(os.environ.get("NTP", "0"))
        if ntp:
            pairs = pairs[:ntp]
        emit_transposes(kb, pairs, c["ident_bf"])

    def dump(srcs):
        dsem = kb.sem()
        with nc.Block() as blk:
            @blk.gpsimd
            def _(e):
                n = 0
                for sap, dap in srcs:
                    e.dma_start(out=dap, in_=sap).then_inc(dsem.h, 16)
                    n += 16
                e.wait_ge(dsem.h, dsem.v + n)
        dsem.v += 16 * len(srcs)

    if stage == "ln":
        import os
        nd = int(os.environ.get("DUMPN", "8"))
        if nd:
            dump([(lnT[:, k, :], dbg[:, k, :]) for k in range(nd)])
        return

    s_wg = kb.sem("s_wg")
    s_mm = kb.sem("s_mm")
    s_ev = kb.sem("s_ev")
    s_tp = kb.sem("s_tp")
    s_cp = kb.sem("s_cp")
    f_mm = kb.sem("f_mm")
    f_ex = kb.sem("f_ex")
    f_nr = kb.sem("f_nr")
    f_st = kb.sem("f_st")
    NKT = 32
    for g in range(4):
        with kb.scope() as sc:
            wqg = sc.sb([128, KC, 2, 128], BF16, f"wqg{g}")
            wkg = sc.sb([128, KC, 2, 128], BF16, f"wkg{g}")
            wvg = sc.sb([128, KC, 256], BF16, f"wvg{g}")
            qg = sc.sb([128, 2, T], BF16, f"qg{g}")
            kg = sc.sb([128, 2, T], BF16, f"kg{g}")
            v2 = sc.sb([128, NKT, 260], BF16, f"v2g{g}")
            ctxs = sc.sb([128, 2, T], BF16, f"ctxs{g}")
            stg = sc.sb([128, 2, 512], F32, f"stg{g}")
            rs1 = sc.sb([128, 512], F32, f"rs1{g}")
            rs2 = sc.sb([128, 512], F32, f"rs2{g}")
            vstg = sc.sb([128, 2, 512], BF16, f"vstg{g}")
            pT = sc.sb([128, 2, 4, 512], BF16, f"pT{g}")
            rec = sc.sb([1, 4, 512], F32, f"rec{g}")
            ctm = sc.sb([64, 4, 512], F32, f"ctm{g}")
            scp = kb.scope()
            scp.__enter__()
            mmps = scp.ps([128, 2, 512], F32, name=f"mmps{g}")
            tps = scp.ps([128, 2, 512], F32, name=f"tps{g}")

            wg0 = s_wg.v
            with nc.Block() as blk:
                @blk.gpsimd
                def _(e, g=g, wg0=wg0):
                    for dst, src in ((wqg, win["wq"]), (wkg, win["wk"])):
                        ap = bass.AP(src.tensor, src.offset + g * 128,
                                     [src.ap[0], src.ap[1], [512, 2], [1, 128]])
                        e.dma_start(out=dst[:], in_=ap).then_inc(s_wg.h, 16)
                    ap = bass.AP(win["wv"].tensor,
                                 win["wv"].offset + 2 * g * 128,
                                 [win["wv"].ap[0], win["wv"].ap[1], [1, 256]])
                    e.dma_start(out=wvg[:], in_=ap).then_inc(s_wg.h, 16)
                    e.wait_ge(s_wg.h, wg0 + 48)

                @blk.vector
                def _(e):
                    for pr in range(2):
                        for off in (64, 129):
                            e.memset(v2[:, :, pr * 130 + off], 1.0)
            s_wg.v += 48

            import os
            _pn = int(os.environ.get("PROJN", "8"))
            _pk = os.environ.get("PROJK", "qkv")
            for qc in range(_pn):
                qs = slice(qc * 512, (qc + 1) * 512)
                for (kind, wsl, bname, outT) in (("q", wqg, "bq", qg),
                                                 ("k", wkg, "bk", kg),
                                                 ("v", wvg, "bv", None)):
                    if kind not in _pk:
                        continue
                    mm0, ev0, tp0, cp0 = s_mm.v, s_ev.v, s_tp.v, s_cp.v
                    with nc.Block() as blk:
                        @blk.tensor
                        def _(e, g=g, qs=qs, kind=kind, wsl=wsl,
                              mm0=mm0, ev0=ev0, cp0=cp0):
                            for cl in range(2):
                                for k in range(KC):
                                    src = (wsl[:, k, cl, :] if kind != "v"
                                           else wsl[:, k, cl * 128:(cl + 1) * 128])
                                    mi = e.matmul(mmps[:, cl, :], src,
                                                  lnT[:, k, qs],
                                                  start=(k == 0),
                                                  stop=(k == KC - 1))
                                mi.then_inc(s_mm.h, 1)
                            if kind == "v" and os.environ.get("PROJV", "full") != "mm":
                                i = 0
                                for cl in range(2):
                                    e.wait_ge(s_ev.h, ev0 + cl + 1)
                                    for t4 in range(4):
                                        if i >= 2:
                                            e.wait_ge(s_cp.h, cp0 + i - 1)
                                        e.matmul(
                                            tps[:, i % 2, 0:128],
                                            vstg[:, cl, t4 * 128:(t4 + 1) * 128],
                                            c["ident_bf"][:], start=True,
                                            stop=True).then_inc(s_tp.h, 1)
                                        i += 1

                        @blk.vector
                        def _(e, g=g, qc=qc, qs=qs, kind=kind, bname=bname,
                              outT=outT, mm0=mm0, tp0=tp0):
                            for cl in range(2):
                                e.wait_ge(s_mm.h, mm0 + cl + 1)
                                if kind == "v":
                                    bcol = 2 * g + cl
                                    e.tensor_scalar(vstg[:, cl, :],
                                                    mmps[:, cl, :],
                                                    bsb["bv"][:, bcol:bcol + 1],
                                                    None, ALU.add
                                                    ).then_inc(s_ev.h, 1)
                                else:
                                    bcol = g + 4 * cl
                                    e.tensor_scalar(stg[:, cl, :],
                                                    mmps[:, cl, :],
                                                    bsb[bname][:, bcol:bcol + 1],
                                                    None, ALU.add
                                                    ).then_inc(s_ev.h, 1)
                            if kind != "v":
                                x1a, x2a = stg[:, 0, :], stg[:, 1, :]
                                cb, sb_ = c4[:, qs], s4[:, qs]
                                kb.edge("v", e.tensor_scalar(
                                    rs1[:], x1a, 1.0, None, ALU.mult), e)
                                e.tensor_tensor(rs1[:], x1a, cb, ALU.mult)
                                i1 = e.tensor_tensor(rs2[:], x2a, sb_, ALU.mult)
                                kb.edge("v", i1, e)
                                i1 = e.tensor_tensor(outT[:, 0, qs], rs1[:],
                                                     rs2[:], ALU.subtract)
                                kb.edge("v", i1, e)
                                e.tensor_tensor(rs1[:], x1a, sb_, ALU.mult)
                                i1 = e.tensor_tensor(rs2[:], x2a, cb, ALU.mult)
                                kb.edge("v", i1, e)
                                e.tensor_tensor(outT[:, 1, qs], rs1[:], rs2[:],
                                                ALU.add)
                            elif os.environ.get("PROJV", "full") == "full":
                                kt0 = qc * 4
                                i = 0
                                for cl in range(2):
                                    for t4 in range(4):
                                        e.wait_ge(s_tp.h, tp0 + i + 1)
                                        kt = kt0 + t4
                                        e.tensor_copy(
                                            v2[:, kt, cl * 130:cl * 130 + 64],
                                            tps[:, i % 2, 0:64])
                                        e.tensor_copy(
                                            v2[:, kt, cl * 130 + 65:cl * 130 + 129],
                                            tps[:, i % 2, 64:128]
                                            ).then_inc(s_cp.h, 1)
                                        i += 1
                    s_mm.v += 2
                    s_ev.v += 2
                    if kind == "v":
                        pv_ = os.environ.get("PROJV", "full")
                        if pv_ != "mm":
                            s_tp.v += 8
                        if pv_ == "full":
                            s_cp.v += 8

            if stage == "proj" and g == 0:
                dump([(qg[:, cl, :], dbg[:, cl, :]) for cl in range(2)]
                     + [(kg[:, cl, :], dbg[:, 2 + cl, :]) for cl in range(2)]
                     + [(v2[:, kt, :], dbg[:, 4 + kt, 0:260]) for kt in range(4)])
                return

            scp.__exit__(None, None, None)
            sc_ps = sc.ps([128, 4, 512], F32, name=f"scps{g}")
            pv_ps = sc.ps([65, 4, 512], F32, name=f"pvps{g}")
            for qc in range(8):
                qs = slice(qc * 512, (qc + 1) * 512)
                mm0, ex0, nr0 = f_mm.v, f_ex.v, f_nr.v
                with nc.Block() as blk:
                    @blk.tensor
                    def _(e, g=g, qs=qs, mm0=mm0, ex0=ex0, nr0=nr0):
                        def scores(kt):
                            ks = slice(kt * 128, (kt + 1) * 128)
                            for h4 in range(4):
                                r = slice(32 * h4, 32 * h4 + 32)
                                e.matmul(sc_ps[:, h4, :], kg[r, 0, ks],
                                         qg[r, 0, qs], start=True, stop=False,
                                         tile_position=(32 * h4, 0))
                                mi = e.matmul(sc_ps[:, h4, :], kg[r, 1, ks],
                                              qg[r, 1, qs], start=False,
                                              stop=True,
                                              tile_position=(32 * h4, 0))
                            mi.then_inc(f_mm.h, 1)

                        def pv(kt):
                            for h4 in range(4):
                                blkc = (h4 // 2) * 130 + (h4 % 2) * 65
                                mi = e.matmul(pv_ps[:, h4, :],
                                              v2[:, kt, blkc:blkc + 65],
                                              pT[:, kt % 2, h4, :],
                                              start=(kt == 0),
                                              stop=(kt == NKT - 1))
                            mi.then_inc(f_mm.h, 1)

                        scores(0)
                        for kt in range(1, NKT):
                            e.wait_ge(f_ex.h, ex0 + kt)
                            pv(kt - 1)
                            scores(kt)
                        e.wait_ge(f_ex.h, ex0 + NKT)
                        pv(NKT - 1)
                        e.wait_ge(f_nr.h, nr0 + 1)
                        for h4 in range(4):
                            mi = e.matmul(sc_ps[0:64, h4, :],
                                          c["ones_row_f"][:, 0:64],
                                          rec[:, h4, :], start=True, stop=True)
                        mi.then_inc(f_mm.h, 1)

                    @blk.scalar
                    def _(e, mm0=mm0):
                        for kt in range(NKT):
                            e.wait_ge(f_mm.h, mm0 + 2 * kt + 1)
                            e.activation(pT[:, kt % 2, :, :], sc_ps[:],
                                         AF.Exp, scale=0.125).then_inc(f_ex.h, 1)

                    @blk.vector
                    def _(e, g=g, qs=qs, mm0=mm0):
                        e.wait_ge(f_mm.h, mm0 + 2 * NKT)
                        e.reciprocal(rec[:], pv_ps[64:65, :, :])
                        e.tensor_copy(ctm[:], pv_ps[0:64, :, :]).then_inc(f_nr.h, 1)
                        e.wait_ge(f_nr.h, nr0 + 1)
                        e.wait_ge(f_mm.h, mm0 + 2 * NKT + 1)
                        for h4 in range(4):
                            cl = h4 // 2
                            rowb = 64 * (h4 % 2)
                            e.tensor_tensor(ctxs[rowb:rowb + 64, cl, qs],
                                            ctm[:, h4, :], sc_ps[0:64, h4, :],
                                            ALU.mult)
                f_mm.v += 2 * NKT + 1
                f_ex.v += NKT
                f_nr.v += 1

            if stage == "flash" and g == 0:
                dump([(ctxs[:, cl, :], dbg[:, cl, :]) for cl in range(2)])
                return

            st0 = f_st.v
            with nc.Block() as blk:
                @blk.sync
                def _(e, g=g, st0=st0):
                    for cl in range(2):
                        e.dma_start(out=ctxd.ap()[2 * g + cl, :, :],
                                    in_=ctxs[:, cl, :]).then_inc(f_st.h, 16)
                    e.wait_ge(f_st.h, st0 + 32)
            f_st.v += 32

    with kb.scope() as sc:
        wosb = sc.sb([128, KC, D], BF16, "wo_sb")
        cload = sc.sb([128, 2, KC, 512], BF16, "cload")
        x0t = sc.sb([128, 4, D], F32, "x0t")
        x1t = sc.sb([128, 4, D], F32, "x1t")
        ostg = sc.sb([128, 512], F32, "ostg")
        mmps = sc.ps([128, 2, 512], F32, name="mmps_o")
        tpf = sc.ps([128, 2, 512], F32, name="tpf_o")
        wl0 = wl.v
        with nc.Block() as blk:
            @blk.gpsimd
            def _(e):
                e.dma_start(out=wosb[:], in_=win["wo"]).then_inc(wl.h, 16)
                e.wait_ge(wl.h, wl0 + 16)
        wl.v += 16
        o_ld = kb.sem("o_ld")
        o_cl = kb.sem("o_cl")
        o_mm = kb.sem("o_mm")
        o_ev = kb.sem("o_ev")
        o_tp = kb.sem("o_tp")
        o_ad = kb.sem("o_ad")
        o_st = kb.sem("o_st")
        for qc in range(8):
            qs = slice(qc * 512, (qc + 1) * 512)
            ld0, cl0, mm0, ev0, tp0, ad0, st0 = (o_ld.v, o_cl.v, o_mm.v,
                                                 o_ev.v, o_tp.v, o_ad.v, o_st.v)
            with nc.Block() as blk:
                @blk.sync
                def _(e, qc=qc, ld0=ld0, cl0=cl0, ad0=ad0, st0=st0):
                    ap = bass.AP(ctxd.ap().tensor, qc * 512,
                                 [[T, 128], [128 * T, KC], [1, 512]])
                    e.dma_start(out=cload[:, qc % 2, :, :],
                                in_=ap).then_inc(o_cl.h, 16)
                    for t4 in range(4):
                        t = qc * 4 + t4
                        e.dma_start(out=x0t[:, t4, :],
                                    in_=x0[t * 128:(t + 1) * 128, :]
                                    ).then_inc(o_ld.h, 16)
                    e.wait_ge(o_ad.h, ad0 + 32)
                    for t4 in range(4):
                        t = qc * 4 + t4
                        e.dma_start(out=x1[t * 128:(t + 1) * 128, :],
                                    in_=x1t[:, t4, :]).then_inc(o_st.h, 16)
                    e.wait_ge(o_st.h, st0 + 64)

                @blk.tensor
                def _(e, qc=qc, cl0=cl0, mm0=mm0, ev0=ev0, ad0=ad0):
                    e.wait_ge(o_cl.h, cl0 + 16)
                    for m in range(KC):
                        if m >= 2:
                            e.wait_ge(o_ev.h, ev0 + m - 1)
                        for k in range(KC):
                            mi = e.matmul(mmps[:, m % 2, :],
                                          wosb[:, k, m * 128:(m + 1) * 128],
                                          cload[:, qc % 2, k, :],
                                          start=(k == 0), stop=(k == KC - 1))
                        mi.then_inc(o_mm.h, 1)
                        e.wait_ge(o_ev.h, ev0 + m + 1)
                        for t4 in range(4):
                            i = 4 * m + t4
                            if i >= 2:
                                e.wait_ge(o_ad.h, ad0 + i - 1)
                            e.matmul(tpf[:, i % 2, 0:128],
                                     ostg[:, t4 * 128:(t4 + 1) * 128],
                                     c["ident_f"][:], start=True,
                                     stop=True).then_inc(o_tp.h, 1)

                @blk.vector
                def _(e, qc=qc, mm0=mm0, tp0=tp0, ld0=ld0):
                    for m in range(KC):
                        e.wait_ge(o_mm.h, mm0 + m + 1)
                        e.tensor_scalar(ostg[:], mmps[:, m % 2, :],
                                        bsb["bo"][:, m:m + 1], None,
                                        ALU.add).then_inc(o_ev.h, 1)
                        for t4 in range(4):
                            i = 4 * m + t4
                            e.wait_ge(o_tp.h, tp0 + i + 1)
                            if m == 0 and t4 == 0:
                                e.wait_ge(o_ld.h, ld0 + 64)
                            e.tensor_tensor(x1t[:, t4, m * 128:(m + 1) * 128],
                                            tpf[:, i % 2, 0:128],
                                            x0t[:, t4, m * 128:(m + 1) * 128],
                                            ALU.add).then_inc(o_ad.h, 1)
            o_ld.v += 64
            o_cl.v += 16
            o_mm.v += KC
            o_ev.v += KC
            o_tp.v += 32
            o_ad.v += 32
            o_st.v += 64


def build_l0_attn(nc):
    kb = KB(nc)
    x0 = nc.dram_tensor("x0", [T, D], F32, kind="ExternalInput").ap()
    win = {}
    for nm in ("wq", "wk", "wv", "wo"):
        win[nm] = nc.dram_tensor(nm, [128, KC, D], F32, kind="ExternalInput").ap()
    for nm in ("bq", "bk", "bv", "bo"):
        win[nm] = nc.dram_tensor(nm, [128, KC], F32, kind="ExternalInput").ap()
    x1 = nc.dram_tensor("x1", [T, D], F32, kind="ExternalOutput").ap()
    emit_l0_attn(kb, x0, win, x1)
    kb.ctx.close()
    return nc


def _dm_w(w):
    dout = w.shape[1]
    return np.ascontiguousarray(w.reshape(KC, 128, dout).transpose(1, 0, 2))


def _dm_b(b):
    n = b.shape[0] // 128
    return np.ascontiguousarray(b.reshape(n, 128).T)


def _qk_perm():
    perm = np.zeros(D, np.int64)
    for h in range(16):
        for half in range(2):
            for f in range(32):
                nat = h * 64 + half * 32 + f
                pos = (h // 4 + 4 * half) * 128 + (h % 4) * 32 + f
                perm[pos] = nat
    return perm


_QKPERM = _qk_perm()


def prep_l0a(ins, core):
    return {
        "x0": np.ascontiguousarray(ins["L0_attn_x"][core]),
        "wq": _dm_w(ins["L0_attn_Wq"][:, _QKPERM]),
        "wk": _dm_w(ins["L0_attn_Wk"][:, _QKPERM]),
        "wv": _dm_w(ins["L0_attn_Wv"]),
        "wo": _dm_w(ins["L0_attn_Wo"]),
        "bq": _dm_b(ins["L0_attn_bq"][_QKPERM]),
        "bk": _dm_b(ins["L0_attn_bk"][_QKPERM]),
        "bv": _dm_b(ins["L0_attn_bv"]),
        "bo": _dm_b(ins["L0_attn_bo"]),
    }




def emit_l0_ffn(kb, x1, win, x2, stage="full", dbg=None):
    nc = kb.nc
    E = 8
    KCAP = 1024
    c = build_consts(kb, ["ident_bf", "ident_f", "ones_row_f", "uexc_f",
                          "allones_f", "pos_tm", "indic_t", "indic8"])
    compactx = [nc.dram_tensor(kb.name(f"cpx{e}"), [T, 1032], BF16)
                for e in range(E)]
    contrib = [nc.dram_tensor(kb.name(f"ctb{e}"), [KCAP + 1, D], F32)
               for e in range(E)]
    th_dram = nc.dram_tensor(kb.name("thd"), [128, 1], F32)

    gate_sb = kb.sb([128, KC, 8], BF16, "gate_sb")
    zrow = kb.sb([1, D], F32, "zrow")
    wl = kb.sem("fwl")
    with nc.Block() as blk:
        @blk.vector
        def _(e):
            e.memset(zrow[:], 0.0)

        @blk.gpsimd
        def _(g):
            g.dma_start(out=gate_sb[:], in_=win["gate"]).then_inc(wl.h, 16)
            for e_ in range(E):
                g.dma_start(out=contrib[e_].ap()[KCAP:KCAP + 1, :],
                            in_=zrow[:]).then_inc(wl.h, 16)
            g.wait_ge(wl.h, wl.v + 16 * 9)
    wl.v += 16 * 9

    dest_u = kb.sb([128, 8, NT], U32, "dest_u")
    gidx_u = kb.sb([128, 8, NT], U32, "gidx_u")
    scw = kb.scope()
    scw.__enter__()
    wide = scw.sb([128, NT, 1040], BF16, "wide")
    aff = scw.sb([128, NT, 8], F32, "aff")
    with kb.scope() as sc:
        emit_ln(kb, x1, lambda t: wide[:, t, 0:D])
        ln1T = sc.sb([128, KC, T], BF16, "ln1T")
        pairs = [(wide[:, t, k * 128:(k + 1) * 128],
                  ln1T[:, k, t * 128:(t + 1) * 128])
                 for t in range(NT) for k in range(KC)]
        emit_transposes(kb, pairs, c["ident_bf"])

        gps = sc.ps([128, 2, 512], F32, name="gps")
        gstat = sc.sb([128, 2, 8], F32, "gstat")
        gexp = sc.sb([128, 2, 8], F32, "gexp")
        g_mm = kb.sem("g_mm")
        g_ev = kb.sem("g_ev")
        g_ac = kb.sem("g_ac")
        for t0 in range(0, NT, 8):
            mm0, ev0, ac0 = g_mm.v, g_ev.v, g_ac.v
            with nc.Block() as blk:
                @blk.tensor
                def _(e, t0=t0, mm0=mm0, ac0=ac0):
                    for i in range(8):
                        t = t0 + i
                        if i >= 2:
                            e.wait_ge(g_ac.h, ac0 + i - 1)
                        for k in range(KC):
                            mi = e.matmul(gps[:, i % 2, 0:8],
                                          ln1T[:, k, t * 128:(t + 1) * 128],
                                          gate_sb[:, k, :],
                                          start=(k == 0), stop=(k == KC - 1))
                        mi.then_inc(g_mm.h, 1)

                @blk.vector
                def _(e, t0=t0, mm0=mm0, ac0=ac0):
                    for i in range(8):
                        t = t0 + i
                        sl = i % 2
                        e.wait_ge(g_mm.h, mm0 + i + 1)
                        i1 = e.reduce_max(gstat[:, sl, 0:1], gps[:, sl, 0:8],
                                          axis=AXX)
                        kb.edge("v", i1, e)
                        i1 = e.tensor_scalar(gstat[:, sl, 1:2], gstat[:, sl, 0:1],
                                             -1.0, None, ALU.mult)
                        i1.then_inc(g_ev.h, 1)
                        e.wait_ge(g_ac.h, ac0 + i + 1)
                        i1 = e.reciprocal(gstat[:, sl, 3:4], gstat[:, sl, 2:3])
                        kb.edge("v", i1, e)
                        i1 = e.tensor_scalar(aff[:, t, :], gexp[:, sl, :],
                                             gstat[:, sl, 3:4], None, ALU.mult)
                        kb.edge("v", i1, e)
                        e.tensor_copy(wide[:, t, 1024:1032],
                                      aff[:, t, :]).then_inc(g_ev.h, 1)

                @blk.scalar
                def _(e, t0=t0, ev0=ev0):
                    for i in range(8):
                        sl = i % 2
                        e.wait_ge(g_ev.h, ev0 + 2 * i + 1)
                        e.activation(gexp[:, sl, :], gps[:, sl, 0:8], AF.Exp,
                                     bias=gstat[:, sl, 1:2],
                                     accum_out=gstat[:, sl, 2:3]
                                     ).then_inc(g_ac.h, 1)
            g_mm.v += 8
            g_ev.v += 16
            g_ac.v += 8

    if stage == "gate":
        dsem = kb.sem()
        with nc.Block() as blk:
            @blk.gpsimd
            def _(e):
                for t in range(8):
                    e.dma_start(out=dbg[:, t, 0:8],
                                in_=aff[:, t, :]).then_inc(dsem.h, 16)
                e.wait_ge(dsem.h, dsem.v + 128)
        dsem.v += 128
        return

    with kb.scope() as sc:
        affT = sc.sb([8, T], F32, "affT")
        affrep = sc.sb([128, 256], F32, "affrep")
        mskb = sc.sb([128, 256], F32, "mskb")
        bis = sc.sb([128, 8], F32, "bis")
        bcond = sc.sb([128, 1], U32, "bcond")
        cnt8 = sc.sb([8, 1], F32, "cnt8")
        tps = sc.ps([128, 2, 512], F32, name="tps_f3")
        pairs2 = []
        for t in range(NT):
            pairs2.append((aff[:, t, :], None))
        b_tp = kb.sem("b_tp")
        b_cp = kb.sem("b_cp")
        tp0, cp0 = b_tp.v, b_cp.v
        with nc.Block() as blk:
            @blk.tensor
            def _(e, tp0=tp0, cp0=cp0):
                for t in range(NT):
                    if t >= 2:
                        e.wait_ge(b_cp.h, cp0 + t - 1)
                    e.matmul(tps[0:8, t % 2, 0:128], aff[:, t, :],
                             c["ident_f"][:], start=True,
                             stop=True).then_inc(b_tp.h, 1)

            @blk.vector
            def _(e, tp0=tp0, cp0=cp0):
                for t in range(NT):
                    e.wait_ge(b_tp.h, tp0 + t + 1)
                    e.tensor_copy(affT[:, t * 128:(t + 1) * 128],
                                  tps[0:8, t % 2, 0:128]).then_inc(b_cp.h, 1)
        b_tp.v += NT
        b_cp.v += NT
        rp = kb.sem("b_rp")
        with nc.Block() as blk:
            @blk.sync
            def _(e):
                n0 = rp.v
                for s in range(16):
                    dst = affrep[s:s + 113:16, :]
                    e.dma_start(out=dst,
                                in_=affT[:, s * 256:(s + 1) * 256]
                                ).then_inc(rp.h, 16)
                e.wait_ge(rp.h, n0 + 256)
        rp.v += 256
        with nc.Block() as blk:
            @blk.vector
            def _(e):
                e.memset(bis[:, 0:1], 0.0)
                e.memset(bis[:, 1:2], 1.0)
                e.memset(bis[:, 2:3], 0.5)
        b_dv = kb.sem("b_dv")
        b_mm = kb.sem("b_mm")
        b_ac = kb.sem("b_ac")
        with nc.Block() as blk:
            @blk.vector
            def _(e):
                dv0, mm0 = b_dv.v, b_mm.v
                for it in range(30):
                    i1 = e.tensor_scalar(mskb[:], affrep[:], bis[:, 2:3], None,
                                         ALU.is_ge)
                    kb.edge("v", i1, e)
                    i1 = e.reduce_sum(bis[:, 4:5], mskb[:], axis=AXX)
                    kb.edge("v", i1, e)
                    e.tensor_copy(bis[:, 5:6], bis[:, 4:5]).then_inc(b_dv.h, 1)
                    e.wait_ge(b_mm.h, mm0 + 2 * (it + 1))
                    i1 = e.tensor_scalar(bcond[:], tps[:, 1, 0:1], 1023.5,
                                         None, ALU.is_gt)
                    kb.edge("v", i1, e)
                    i1 = e.select(bis[:, 0:1], bcond[:], bis[:, 2:3],
                                  bis[:, 0:1])
                    kb.edge("v", i1, e)
                    i1 = e.select(bis[:, 1:2], bcond[:], bis[:, 1:2],
                                  bis[:, 2:3])
                    kb.edge("v", i1, e)
                    i1 = e.tensor_tensor(bis[:, 2:3], bis[:, 0:1], bis[:, 1:2],
                                         ALU.add)
                    kb.edge("v", i1, e)
                    i1 = e.tensor_scalar(bis[:, 2:3], bis[:, 2:3], 0.5, None,
                                         ALU.mult)
                    kb.edge("v", i1, e)
                b_dv.v += 30

            @blk.tensor
            def _(e):
                dv0, mm0, ac0 = b_dv.v - 30, b_mm.v, b_ac.v
                for it in range(30):
                    e.wait_ge(b_dv.h, dv0 + it + 1)
                    e.matmul(tps[0:8, 0, 0:1], c["indic_t"][:], bis[:, 5:6],
                             start=True, stop=True).then_inc(b_mm.h, 1)
                    e.wait_ge(b_ac.h, ac0 + it + 1)
                    e.matmul(tps[:, 1, 0:1], c["indic8"][:], cnt8[:],
                             start=True, stop=True).then_inc(b_mm.h, 1)
                b_mm.v += 60

            @blk.scalar
            def _(e):
                mm0, ac0 = b_mm.v - 60, b_ac.v
                for it in range(30):
                    e.wait_ge(b_mm.h, mm0 + 2 * it + 1)
                    e.copy(cnt8[:], tps[0:8, 0, 0:1]).then_inc(b_ac.h, 1)
                b_ac.v += 30
        th_tm = sc.sb([128, 8], F32, "th_tm")
        tl = kb.sem("b_tl")
        with nc.Block() as blk:
            @blk.sync
            def _(e):
                n0 = tl.v
                e.dma_start(out=th_dram.ap(), in_=bis[:, 0:1]).then_inc(tl.h, 16)
                e.wait_ge(tl.h, n0 + 16)
                src = bass.AP(th_dram.ap().tensor, 0, [[0, 128], [16, 8]])
                e.dma_start(out=th_tm[:], in_=src,
                            allow_slow_non_contiguous=True).then_inc(tl.h, 16)
                e.wait_ge(tl.h, n0 + 32)
        tl.v += 32

        dscr = sc.sb([128, 4, NT], F32, "dscr")
        msku = sc.sb([128, NT], U32, "msku")
        dcol = sc.sb([128, 8], F32, "dcol")
        d_dv = kb.sem("d_dv")
        d_mm = kb.sem("d_mm")
        with nc.Block() as blk:
            @blk.vector
            def _(e):
                dv0, mm0 = d_dv.v, d_mm.v
                for ei in range(E):
                    aslice = aff[:, :, ei]
                    i1 = e.tensor_scalar(dscr[:, 0, :], aslice,
                                         th_tm[:, ei:ei + 1], None, ALU.is_ge)
                    kb.edge("v", i1, e)
                    i1 = e.tensor_copy(msku[:], dscr[:, 0, :])
                    kb.edge("v", i1, e)
                    i1 = e.tensor_tensor_scan(dscr[:, 1, :], dscr[:, 0, :],
                                              dscr[:, 0, :], 0.0,
                                              ALU.add, ALU.bypass)
                    kb.edge("v", i1, e)
                    e.tensor_copy(dcol[:, 3:4],
                                  dscr[:, 1, NT - 1:NT]).then_inc(d_dv.h, 1)
                    e.wait_ge(d_mm.h, mm0 + 2 * (ei + 1))
                    i1 = e.tensor_copy(dcol[:, 0:1], tps[:, 0, 0:1])
                    kb.edge("v", i1, e)
                    i1 = e.tensor_copy(dcol[:, 2:3], tps[:, 1, 0:1])
                    kb.edge("v", i1, e)
                    i1 = e.tensor_scalar(dcol[:, 1:2], dcol[:, 0:1], -1.0,
                                         None, ALU.mult)
                    kb.edge("v", i1, e)
                    i1 = e.tensor_scalar(dscr[:, 2, :], dscr[:, 1, :],
                                         dcol[:, 0:1], -1.0, ALU.add, ALU.add)
                    kb.edge("v", i1, e)
                    i1 = e.tensor_tensor(dscr[:, 3, :], c["pos_tm"][:],
                                         dscr[:, 1, :], ALU.subtract)
                    kb.edge("v", i1, e)
                    i1 = e.tensor_scalar(dscr[:, 3, :], dscr[:, 3, :],
                                         dcol[:, 1:2], dcol[:, 2:3],
                                         ALU.add, ALU.add)
                    kb.edge("v", i1, e)
                    i1 = e.select(dscr[:, 1, :], msku[:], dscr[:, 2, :],
                                  dscr[:, 3, :])
                    kb.edge("v", i1, e)
                    i1 = e.tensor_copy(dest_u[:, ei, :], dscr[:, 1, :])
                    kb.edge("v", i1, e)
                    i1 = e.tensor_scalar(dscr[:, 2, :], dscr[:, 1, :], 1024.0,
                                         None, ALU.min)
                    kb.edge("v", i1, e)
                    e.tensor_copy(gidx_u[:, ei, :], dscr[:, 2, :])
                d_dv.v += E

            @blk.tensor
            def _(e):
                dv0, mm0 = d_dv.v - E, d_mm.v
                for ei in range(E):
                    e.wait_ge(d_dv.h, dv0 + ei + 1)
                    e.matmul(tps[:, 0, 0:1], c["uexc_f"][:], dcol[:, 3:4],
                             start=True, stop=True).then_inc(d_mm.h, 1)
                    e.matmul(tps[:, 1, 0:1], c["allones_f"][:], dcol[:, 3:4],
                             start=True, stop=True).then_inc(d_mm.h, 1)
                d_mm.v += 2 * E

    if stage == "topk":
        dsem = kb.sem()
        with nc.Block() as blk:
            @blk.gpsimd
            def _(e):
                n0 = dsem.v
                e.dma_start(out=dbg[:, 0, 0:8], in_=th_tm[:]).then_inc(dsem.h, 16)
                for ei in range(E):
                    e.dma_start(out=dbg[:, 1 + ei, 0:NT],
                                in_=dest_u[:, ei, :]).then_inc(dsem.h, 16)
                e.wait_ge(dsem.h, n0 + 16 * 9)
        dsem.v += 16 * 9
        return

    sc_sem = kb.sem("sc_sem")
    with nc.Block() as blk:
        @blk.gpsimd
        def _(e):
            n0 = sc_sem.v
            n = 0
            for mt in range(NT):
                for ei in range(E):
                    e.indirect_dma_start(
                        out=compactx[ei].ap(),
                        out_offset=IndirectOffsetOnAxis(dest_u[:, ei, mt:mt + 1], 0),
                        in_=wide[:, mt, 0:1032], in_offset=None,
                    ).then_inc(sc_sem.h, 16)
                    n += 16
            e.wait_ge(sc_sem.h, n0 + n)
    sc_sem.v += 16 * NT * E
    scw.__exit__(None, None, None)

    if stage == "scatter":
        dsem = kb.sem()
        with nc.Block() as blk:
            @blk.gpsimd
            def _(e):
                n0 = dsem.v
                for t in range(4):
                    e.dma_start(out=dbg[:, t, 0:1032],
                                in_=compactx[0].ap()[t * 128:(t + 1) * 128, :]
                                ).then_inc(dsem.h, 16)
                e.wait_ge(dsem.h, n0 + 64)
        dsem.v += 64
        return

    with kb.scope() as sc:
        xsel = sc.sb([128, 8, 1032], BF16, "xsel")
        xselT = sc.sb([128, KC, 1024], BF16, "xselT")
        u_sb = sc.sb([128, 16, 1024], BF16, "u_sb")
        h_sb = sc.sb([128, 16, 1024], BF16, "h_sb")
        wd_sb = sc.sb([128, 2, 16, 1024], BF16, "wd_sb")
        wchk = sc.sb([128, 2, KC, 128], BF16, "wchk")
        gat = sc.sb([128, 8], F32, "gat")
        sg = sc.sb([128, 2, 512], F32, "sg")
        cstage = sc.sb([128, 2, 512], F32, "cstage")
        eps_ = sc.ps([128, 2, 512], F32, name="eps_f6")
        tpp = sc.ps([128, 2, 512], F32, name="tpp_f6")
        x_ld = kb.sem("x_ld")
        x_tp = kb.sem("x_tp")
        x_cp = kb.sem("x_cp")
        w_l0 = (kb.sem("w_l0a"), kb.sem("w_l0b"))
        w_l1 = (kb.sem("w_l1a"), kb.sem("w_l1b"))
        wd_l = kb.sem("wd_l")
        m_mm = kb.sem("m_mm")
        m_ev = kb.sem("m_ev")
        m_ac = kb.sem("m_ac")
        c_st = kb.sem("c_st")
        for ei in range(E):
            ld0, wd0 = x_ld.v, wd_l.v
            with nc.Block() as blk:
                @blk.sync
                def _(e, ei=ei, ld0=ld0):
                    ap = compactx[ei].ap()[0:1024, :].rearrange(
                        "(s p) w -> p s w", p=128)
                    e.dma_start(out=xsel[:], in_=ap).then_inc(x_ld.h, 16)

                @blk.gpsimd
                def _(e, ei=ei, wd0=wd0):
                    e.dma_start(out=wd_sb[:, ei % 2, :, :],
                                in_=win["wd"].ap()[ei]).then_inc(wd_l.h, 16)

                @blk.vector
                def _(e, ei=ei, ld0=ld0):
                    e.wait_ge(x_ld.h, ld0 + 16)
                    i1 = e.tensor_copy(gat[:, :], xsel[:, :, 1024 + ei])
                    kb.edge("v", i1, e)
            x_ld.v += 16
            wd_l.v += 16
            tp0, cp0 = x_tp.v, x_cp.v
            with nc.Block() as blk:
                @blk.tensor
                def _(e, tp0=tp0, cp0=cp0):
                    i = 0
                    for st8 in range(8):
                        for k in range(KC):
                            if i >= 2:
                                e.wait_ge(x_cp.h, cp0 + i - 1)
                            e.matmul(tpp[:, i % 2, 0:128],
                                     xsel[:, st8, k * 128:(k + 1) * 128],
                                     c["ident_bf"][:], start=True,
                                     stop=True).then_inc(x_tp.h, 1)
                            i += 1

                @blk.vector
                def _(e, tp0=tp0, cp0=cp0):
                    i = 0
                    for st8 in range(8):
                        for k in range(KC):
                            e.wait_ge(x_tp.h, tp0 + i + 1)
                            e.tensor_copy(xselT[:, k, st8 * 128:(st8 + 1) * 128],
                                          tpp[:, i % 2, 0:128]
                                          ).then_inc(x_cp.h, 1)
                            i += 1
            x_tp.v += 64
            x_cp.v += 64
            for (wname, wl_sem, out_mode) in (("wu", w_l0, "u"), ("wg", w_l1, "g")):
                mm0, ev0, ac0 = m_mm.v, m_ev.v, m_ac.v
                wl0 = (wl_sem[0].v, wl_sem[1].v)
                with nc.Block() as blk:
                    @blk.gpsimd
                    def _(e, ei=ei, wname=wname, wl_sem=wl_sem, wl0=wl0,
                          mm0=mm0):
                        for ic in range(16):
                            if ic >= 2:
                                e.wait_ge(m_mm.h, mm0 + 2 * (ic - 1))
                            ap = win[wname].ap()[ei, :, :, ic * 128:(ic + 1) * 128]
                            e.dma_start(out=wchk[:, ic % 2, :, :],
                                        in_=ap).then_inc(wl_sem[ic % 2].h, 16)

                    @blk.tensor
                    def _(e, wl_sem=wl_sem, wl0=wl0, mm0=mm0, ev0=ev0):
                        for ic in range(16):
                            e.wait_ge(wl_sem[ic % 2].h,
                                      wl0[ic % 2] + 16 * (ic // 2 + 1))
                            for half in range(2):
                                if 2 * ic + half >= 2:
                                    e.wait_ge(m_ev.h, ev0 + 2 * ic + half - 1)
                                for k in range(KC):
                                    mi = e.matmul(
                                        eps_[:, half, :],
                                        wchk[:, ic % 2, k, :],
                                        xselT[:, k, half * 512:(half + 1) * 512],
                                        start=(k == 0), stop=(k == KC - 1))
                                mi.then_inc(m_mm.h, 1)

                    if out_mode == "u":
                        @blk.vector
                        def _(e, mm0=mm0, ev0=ev0):
                            for ic in range(16):
                                for half in range(2):
                                    e.wait_ge(m_mm.h, mm0 + 2 * ic + half + 1)
                                    e.tensor_copy(
                                        u_sb[:, ic, half * 512:(half + 1) * 512],
                                        eps_[:, half, :]).then_inc(m_ev.h, 1)
                    else:
                        @blk.scalar
                        def _(e, mm0=mm0, ev0=ev0, ac0=ac0):
                            for ic in range(16):
                                for half in range(2):
                                    e.wait_ge(m_mm.h, mm0 + 2 * ic + half + 1)
                                    if 2 * ic + half >= 2:
                                        e.wait_ge(m_ev.h,
                                                  ev0 + 2 * ic + half - 1)
                                    e.activation(sg[:, half, :], eps_[:, half, :],
                                                 AF.Silu).then_inc(m_ac.h, 1)

                        @blk.vector
                        def _(e, mm0=mm0, ev0=ev0, ac0=ac0):
                            for ic in range(16):
                                for half in range(2):
                                    e.wait_ge(m_ac.h, ac0 + 2 * ic + half + 1)
                                    e.tensor_tensor(
                                        h_sb[:, ic, half * 512:(half + 1) * 512],
                                        sg[:, half, :],
                                        u_sb[:, ic, half * 512:(half + 1) * 512],
                                        ALU.mult).then_inc(m_ev.h, 1)
                m_mm.v += 32
                m_ev.v += 32
                wl_sem[0].v += 16 * 8
                wl_sem[1].v += 16 * 8
                if out_mode == "g":
                    m_ac.v += 32
            mm0, ev0, st0, wd0 = m_mm.v, m_ev.v, c_st.v, wd_l.v - 16
            with nc.Block() as blk:
                @blk.tensor
                def _(e, ei=ei, mm0=mm0, ev0=ev0, wd0=wd0):
                    e.wait_ge(wd_l.h, wd0 + 16)
                    for s8 in range(8):
                        for half in range(2):
                            if 2 * s8 + half >= 2:
                                e.wait_ge(m_ev.h, ev0 + 2 * s8 + half - 1)
                            for ic in range(16):
                                mi = e.matmul(
                                    eps_[:, half, :],
                                    h_sb[:, ic, s8 * 128:(s8 + 1) * 128],
                                    wd_sb[:, ei % 2, ic,
                                          half * 512:(half + 1) * 512],
                                    start=(ic == 0), stop=(ic == 15))
                            mi.then_inc(m_mm.h, 1)

                @blk.vector
                def _(e, ei=ei, mm0=mm0, ev0=ev0, st0=st0):
                    for s8 in range(8):
                        for half in range(2):
                            e.wait_ge(m_mm.h, mm0 + 2 * s8 + half + 1)
                            if s8 >= 1 and half == 0:
                                e.wait_ge(c_st.h, st0 + 16 * s8)
                            e.tensor_scalar(cstage[:, half, :], eps_[:, half, :],
                                            gat[:, s8:s8 + 1], None,
                                            ALU.mult).then_inc(m_ev.h, 1)

                @blk.sync
                def _(e, ei=ei, ev0=ev0, st0=st0):
                    for s8 in range(8):
                        e.wait_ge(m_ev.h, ev0 + 2 * s8 + 2)
                        e.dma_start(
                            out=contrib[ei].ap()[s8 * 128:(s8 + 1) * 128, :],
                            in_=cstage[:].rearrange("p a b -> p (a b)")
                            ).then_inc(c_st.h, 16)
                    e.wait_ge(c_st.h, st0 + 16 * 8)
            m_mm.v += 16
            m_ev.v += 16
            c_st.v += 16 * 8

    if stage == "expert":
        dsem = kb.sem()
        with nc.Block() as blk:
            @blk.gpsimd
            def _(e):
                n0 = dsem.v
                for t in range(8):
                    e.dma_start(out=dbg[:, t, 0:D],
                                in_=contrib[0].ap()[t * 128:(t + 1) * 128, :]
                                ).then_inc(dsem.h, 16)
                e.wait_ge(dsem.h, n0 + 128)
        dsem.v += 128
        return

    with kb.scope() as sc:
        acc = sc.sb([128, 8, D], F32, "acc")
        slot_sems = [kb.sem(f"cmb{s}") for s in range(8)]
        rounds = []
        for r in range(4):
            for s in range(8):
                rounds.append((s, r * 8 + s))
        with nc.Block() as blk:
            @blk.gpsimd
            def _(e):
                for r in range(4):
                    for step in range(10):
                        for s in range(8):
                            t = r * 8 + s
                            sem = slot_sems[s]
                            base = 10 * 16 * r
                            if step == 0:
                                if r > 0:
                                    e.wait_ge(sem.h, base)
                                e.dma_start(out=acc[:, s, :],
                                            in_=x1[t * 128:(t + 1) * 128, :]
                                            ).then_inc(sem.h, 16)
                            elif step <= 8:
                                ei = step - 1
                                e.wait_ge(sem.h, base + 16 * step)
                                e.indirect_dma_start(
                                    out=acc[:, s, :], out_offset=None,
                                    in_=contrib[ei].ap(),
                                    in_offset=IndirectOffsetOnAxis(
                                        gidx_u[:, ei, t:t + 1], 0),
                                    compute_op=ALU.add,
                                ).then_inc(sem.h, 16)
                            else:
                                e.wait_ge(sem.h, base + 16 * 9)
                                e.dma_start(out=x2[t * 128:(t + 1) * 128, :],
                                            in_=acc[:, s, :]).then_inc(sem.h, 16)
                for s in range(8):
                    e.wait_ge(slot_sems[s].h, 10 * 16 * 4)
                    slot_sems[s].v = 10 * 16 * 4


def build_l0_ffn(nc, stage="full"):
    kb = KB(nc)
    x1 = nc.dram_tensor("x1", [T, D], F32, kind="ExternalInput").ap()
    win = {
        "gate": nc.dram_tensor("gate", [128, KC, 8], F32, kind="ExternalInput").ap(),
        "wg": nc.dram_tensor("wg", [8, 128, KC, 2048], F32, kind="ExternalInput"),
        "wu": nc.dram_tensor("wu", [8, 128, KC, 2048], F32, kind="ExternalInput"),
        "wd": nc.dram_tensor("wd", [8, 128, 16, 1024], F32, kind="ExternalInput"),
    }
    x2 = nc.dram_tensor("x2", [T, D], F32, kind="ExternalOutput").ap()
    dbg = (nc.dram_tensor("dbg", [128, 16, 1040], F32,
                          kind="ExternalOutput").ap()
           if stage != "full" else None)
    emit_l0_ffn(kb, x1, win, x2, stage=stage, dbg=dbg)
    kb.ctx.close()
    return nc


def prep_l0f(ins, core, x1=None):
    return {
        "x1": np.ascontiguousarray(x1) if x1 is not None else None,
        "gate": np.ascontiguousarray(
            ins["L0_ffn_gate_w"].reshape(KC, 128, 8).transpose(1, 0, 2)),
        "wg": np.ascontiguousarray(
            ins["L0_ffn_Wg"].reshape(8, KC, 128, 2048).transpose(0, 2, 1, 3)),
        "wu": np.ascontiguousarray(
            ins["L0_ffn_Wu"].reshape(8, KC, 128, 2048).transpose(0, 2, 1, 3)),
        "wd": np.ascontiguousarray(
            ins["L0_ffn_Wd"].reshape(8, 16, 128, 1024).transpose(0, 2, 1, 3)),
    }




def emit_l1_attn(kb, x2, win, x3, stage="full", dbg=None):
    nc = kb.nc
    CH = 128
    NCH = T // CH
    c = build_consts(kb, ["ident_bf", "ident_f", "ones_row_f", "ones_col_f"])

    wnames = ("wpv", "wk1", "wkv", "wA", "wB", "wo1")
    wsb = {nm: kb.sb([128, KC, D], BF16, nm + "_sb") for nm in wnames}
    wsb["wb1"] = kb.sb([128, KC, 512], BF16, "wb1_sb")
    wsb["wg2"] = kb.sb([128, KC, 1], BF16, "wg2_sb")
    wsb["wb2"] = kb.sb([128, 4, 2], BF16, "wb2_sb")
    bsb = {nm: kb.sb([128, KC], F32, nm + "_sb")
           for nm in ("bpv", "bk1", "bkv", "bg1", "lng", "lnb", "bo1")}
    bsb["bb1"] = kb.sb([128, 4], F32, "bb1_sb")
    bsb["bb2"] = kb.sb([1, 2], F32, "bb2_sb")
    bsb["bg2"] = kb.sb([1, 1], F32, "bg2_sb")
    wl = kb.sem("awl")
    with nc.Block() as blk:
        @blk.gpsimd
        def _(g):
            n = 0
            for nm in wnames + ("wb1", "wg2", "wb2"):
                g.dma_start(out=wsb[nm][:], in_=win[nm]).then_inc(wl.h, 16)
                n += 16
            for nm in ("bpv", "bk1", "bkv", "bg1", "lng", "lnb", "bo1",
                       "bb1", "bb2", "bg2"):
                g.dma_start(out=bsb[nm][:], in_=win[nm]).then_inc(wl.h, 16)
                n += 16
            g.wait_ge(wl.h, wl.v + n)
    wl.v += 16 * 19

    amc = kb.sb([128, KC, CH], F32, "amc")
    ams = kb.sb([128, KC, CH], F32, "ams")
    bmc = kb.sb([128, KC, CH], F32, "bmc")
    bms = kb.sb([128, KC, CH], F32, "bms")
    carr1 = kb.sb([1, 4], F32, "carr1")
    rows = kb.sb([1, 16, CH], F32, "rows")
    x2c = kb.sb([128, D], F32, "x2c")
    ln2tm = kb.sb([128, D], BF16, "ln2tm")
    ln2T = kb.sb([128, KC, CH], BF16, "ln2T")
    pvb = kb.sb([128, KC, CH], BF16, "pvb")
    kprb = kb.sb([128, KC, CH], F32, "kprb")
    kvb = kb.sb([128, KC, CH], BF16, "kvb")
    kcb = kb.sb([128, KC, CH + 1], BF16, "kcb")
    ksb_ = kb.sb([128, KC, CH + 1], BF16, "ksb")
    p1b = kb.sb([128, KC, CH], F32, "p1b")
    p2b = kb.sb([128, KC, CH + 1], F32, "p2b")
    ghb = kb.sb([128, KC, CH], BF16, "ghb")
    pcb = kb.sb([128, KC, CH], BF16, "pcb")
    psb2 = kb.sb([128, KC, CH], BF16, "psb2")
    pptm = kb.sb([128, D], F32, "pptm")
    ppT = kb.sb([128, KC, CH], F32, "ppT")
    f32s = kb.sb([128, KC, CH], F32, "f32s")
    f32t = kb.sb([128, KC, CH], F32, "f32t")
    iscr = kb.sb([128, KC, CH], I32, "iscr_a")
    scrA = kb.sb([128, KC, CH], BF16, "scrA")
    scrB = kb.sb([128, KC, CH], BF16, "scrB")
    blend = kb.sb([128, KC, CH], F32, "blend")
    bhb = kb.sb([128, 4, CH], F32, "bhb")
    bhbf = kb.sb([128, 4, CH], BF16, "bhbf")
    lnbf = kb.sb([128, KC, CH], BF16, "lnbf")
    vgb = kb.sb([128, CH], F32, "vgb")
    w0b = kb.sb([128, CH], F32, "w0b")
    w1b = kb.sb([128, CH], F32, "w1b")
    meanb = kb.sb([128, CH], F32, "meanb")
    rstdb = kb.sb([128, CH], F32, "rstdb")
    ostg = kb.sb([128, CH], F32, "ostg_a")
    x3t = kb.sb([128, D], F32, "x3t")
    mmps = kb.ps([128, 2, 512], F32, name="mmps_a")
    rwps = kb.ps([128, 4, 512], F32, name="rwps_a")

    with nc.Block() as blk:
        @blk.vector
        def _(e):
            for b in (amc, ams, bmc, bms):
                e.memset(b[:, :, CH - 1:CH], 0.0)
            e.memset(rows[:, 1, CH - 1:CH], 0.0)
            e.memset(kcb[:, :, 0:1], 0.0)
            e.memset(ksb_[:, :, 0:1], 0.0)
            e.memset(p2b[:, :, 0:1], 0.0)
            i1 = e.tensor_tensor(carr1[:, 1:2], bsb["bb2"][:, 0:1],
                                 bsb["bb2"][:, 1:2], ALU.subtract)
            kb.edge("v", i1, e)

    a_ld = kb.sem("a_ld")
    a_mm = kb.sem("a_mm")
    a_ev = kb.sem("a_ev")
    a_ac = kb.sem("a_ac")
    a_dv = kb.sem("a_dv")
    a_st = kb.sem("a_st")

    def bc8(buf):
        a = buf[:]
        return bass.AP(a.tensor, a.offset, [a.ap[0], [0, KC], a.ap[1]])

    lnst = kb.sb([128, 8], F32, "lnst_a")
    for ci in range(NCH):
        par = ci % 2
        t = ci
        ld0, ac0, dv0 = a_ld.v, a_ac.v, a_dv.v
        with nc.Block() as blk:
            @blk.sync
            def _(e, t=t, ld0=ld0):
                e.dma_start(out=x2c[:],
                            in_=x2[t * 128:(t + 1) * 128, :]).then_inc(a_ld.h, 16)
                e.dma_start(out=pptm[:],
                            in_=win["pp"][t * 128:(t + 1) * 128, :]
                            ).then_inc(a_ld.h, 16)

            @blk.scalar
            def _(e, ld0=ld0, ac0=ac0, dv0=dv0):
                e.wait_ge(a_ld.h, ld0 + 32)
                x2f = x2c[:].rearrange("p (a b) -> p a b", a=KC)
                e.activation(f32t[:], x2f, AF.Square,
                             accum_out=lnst[:, 0:1]).then_inc(a_ac.h, 1)
                e.wait_ge(a_dv.h, dv0 + 1)
                e.activation(lnst[:, 5:6], lnst[:, 4:5],
                             AF.Sqrt).then_inc(a_ac.h, 1)

            @blk.vector
            def _(e, ld0=ld0, ac0=ac0, dv0=dv0):
                e.wait_ge(a_ld.h, ld0 + 32)
                i1 = e.reduce_sum(lnst[:, 1:2], x2c[:], axis=AXX)
                kb.edge("v", i1, e)
                e.wait_ge(a_ac.h, ac0 + 1)
                i1 = e.tensor_scalar(lnst[:, 2:3], lnst[:, 1:2], 1.0 / D,
                                     None, ALU.mult)
                kb.edge("v", i1, e)
                i1 = e.tensor_tensor(lnst[:, 3:4], lnst[:, 2:3], lnst[:, 2:3],
                                     ALU.mult)
                kb.edge("v", i1, e)
                i1 = e.scalar_tensor_tensor(lnst[:, 4:5], lnst[:, 0:1], 1.0 / D,
                                            lnst[:, 3:4], ALU.mult,
                                            ALU.subtract)
                kb.edge("v", i1, e)
                i1 = e.tensor_scalar(lnst[:, 4:5], lnst[:, 4:5], 1e-5, None,
                                     ALU.add)
                i1.then_inc(a_dv.h, 1)
                e.wait_ge(a_ac.h, ac0 + 2)
                i1 = e.reciprocal(lnst[:, 6:7], lnst[:, 5:6])
                kb.edge("v", i1, e)
                i1 = e.tensor_tensor(lnst[:, 7:8], lnst[:, 2:3], lnst[:, 6:7],
                                     ALU.mult)
                kb.edge("v", i1, e)
                i1 = e.tensor_scalar(lnst[:, 7:8], lnst[:, 7:8], -1.0, None,
                                     ALU.mult)
                kb.edge("v", i1, e)
                e.tensor_scalar(ln2tm[:], x2c[:], lnst[:, 6:7], lnst[:, 7:8],
                                ALU.mult, ALU.add)
        a_ld.v += 32
        a_ac.v += 2
        a_dv.v += 1
        pairs = [(ln2tm[:, k * 128:(k + 1) * 128], ln2T[:, k, :])
                 for k in range(KC)]
        emit_transposes(kb, pairs, c["ident_bf"], grp=1)
        pairsf = [(pptm[:, k * 128:(k + 1) * 128], ppT[:, k, :])
                  for k in range(KC)]
        emit_transposes(kb, pairsf, c["ident_f"], grp=1)

        mm0, ev0 = a_mm.v, a_ev.v
        specs = [("wpv", "bpv", pvb, F32, 0), ("wk1", "bk1", kprb, F32, 0),
                 ("wkv", "bkv", kvb, BF16, 0), ("wA", None, p1b, F32, 0),
                 ("wB", None, p2b, F32, 1), ("wb1", None, bhb, F32, 0)]
        with nc.Block() as blk:
            @blk.tensor
            def _(e, mm0=mm0, ev0=ev0):
                gi = 0
                for (wn, bn, outb, dt_, coff) in specs:
                    nm_ = 4 if wn == "wb1" else KC
                    for m in range(nm_):
                        if gi >= 2:
                            e.wait_ge(a_ev.h, ev0 + gi - 1)
                        for k in range(KC):
                            mi = e.matmul(mmps[:, gi % 2, 0:CH],
                                          wsb[wn][:, k, m * 128:(m + 1) * 128],
                                          ln2T[:, k, :], start=(k == 0),
                                          stop=(k == KC - 1))
                        mi.then_inc(a_mm.h, 1)
                        gi += 1

            @blk.vector
            def _(e, ci=ci, mm0=mm0, ev0=ev0):
                if ci > 0:
                    i1 = e.tensor_copy(p2b[:, :, 0:1], p2b[:, :, CH:CH + 1])
                    kb.edge("v", i1, e)
                gi = 0
                for (wn, bn, outb, dt_, coff) in specs:
                    nm_ = 4 if wn == "wb1" else KC
                    for m in range(nm_):
                        e.wait_ge(a_mm.h, mm0 + gi + 1)
                        dst = (outb[:, m, coff:coff + CH])
                        if bn is None:
                            e.tensor_copy(dst, mmps[:, gi % 2, 0:CH]
                                          ).then_inc(a_ev.h, 1)
                        else:
                            e.tensor_scalar(dst, mmps[:, gi % 2, 0:CH],
                                            bsb[bn][:, m:m + 1], None,
                                            ALU.add).then_inc(a_ev.h, 1)
                        gi += 1
        a_mm.v += 44
        a_ev.v += 44

        ac0, dv0, mm0 = a_ac.v, a_dv.v, a_mm.v
        with nc.Block() as blk:
            @blk.vector
            def _(e, ci=ci, par=par, ac0=ac0, dv0=dv0, mm0=mm0):
                i1 = e.tensor_scalar(f32s[:], ppT[:], INV2PI, None, ALU.mult)
                kb.edge("v", i1, e)
                i1 = e.tensor_copy(iscr[:], f32s[:])
                kb.edge("v", i1, e)
                i1 = e.tensor_copy(f32s[:], iscr[:])
                kb.edge("v", i1, e)
                i1 = e.scalar_tensor_tensor(f32t[:], f32s[:], -2.0 * PI,
                                            ppT[:], ALU.mult, ALU.add)
                kb.edge("v", i1, e)
                i1 = e.tensor_scalar(f32s[:], f32t[:], PI, None, ALU.is_gt)
                kb.edge("v", i1, e)
                i1 = e.scalar_tensor_tensor(f32t[:], f32s[:], -2.0 * PI,
                                            f32t[:], ALU.mult, ALU.add)
                kb.edge("v", i1, e)
                i1 = e.tensor_scalar(f32s[:], f32t[:], -PI, None, ALU.is_lt)
                kb.edge("v", i1, e)
                i1 = e.scalar_tensor_tensor(f32t[:], f32s[:], 2.0 * PI,
                                            f32t[:], ALU.mult, ALU.add)
                kb.edge("v", i1, e)
                i1 = e.tensor_scalar(f32s[:], f32t[:], PI / 2, None, ALU.add)
                kb.edge("v", i1, e)
                i1 = e.tensor_scalar(blend[:], f32s[:], PI, None, ALU.is_gt)
                kb.edge("v", i1, e)
                i1 = e.scalar_tensor_tensor(f32s[:], blend[:], -2.0 * PI,
                                            f32s[:], ALU.mult, ALU.add)
                i1.then_inc(a_dv.h, 1)
                e.wait_ge(a_ac.h, ac0 + 3)
                if ci > 0:
                    i1 = e.tensor_copy(kcb[:, :, 0:1], kcb[:, :, CH:CH + 1])
                    kb.edge("v", i1, e)
                    i1 = e.tensor_copy(ksb_[:, :, 0:1], ksb_[:, :, CH:CH + 1])
                    kb.edge("v", i1, e)
                i1 = e.tensor_scalar(f32t[:], scrA[:], PI, None, ALU.mult)
                kb.edge("v", i1, e)
                i1 = e.tensor_scalar(f32s[:], f32t[:], PI / 2, None, ALU.add)
                kb.edge("v", i1, e)
                i1 = e.tensor_scalar(blend[:], f32s[:], PI, None, ALU.is_gt)
                kb.edge("v", i1, e)
                i1 = e.scalar_tensor_tensor(f32s[:], blend[:], -2.0 * PI,
                                            f32s[:], ALU.mult, ALU.add)
                i1.then_inc(a_dv.h, 1)
                e.wait_ge(a_ac.h, ac0 + 4)
                i1 = e.tensor_tensor(f32t[:], p1b[:], p2b[:, :, 0:CH], ALU.add)
                i1.then_inc(a_dv.h, 1)
                e.wait_ge(a_ac.h, ac0 + 7)
                e.wait_ge(a_mm.h, mm0 + 11)
                i1 = e.tensor_tensor(rows[:, 7, :], rwps[0:1, 2, 0:CH],
                                     rows[:, 6, :], ALU.subtract)
                i1.then_inc(a_dv.h, 1)
                e.wait_ge(a_ac.h, ac0 + 9)
                i1 = e.tensor_tensor_scan(rows[:, 1, :], rows[:, 0, :],
                                          rows[:, 0, :],
                                          rows[:, 1, CH - 1:CH],
                                          ALU.add, ALU.bypass)
                kb.edge("v", i1, e)
                i1 = e.tensor_scalar(rows[:, 5, :], rows[:, 1, :], 1.0,
                                     None, ALU.max)
                i1.then_inc(a_dv.h, 1)
                e.wait_ge(a_ac.h, ac0 + 10)
                i1 = e.reciprocal(rows[:, 14, :], rows[:, 15, :])
                kb.edge("v", i1, e)
                i1 = e.tensor_scalar(rows[:, 5, :], rows[:, 14, :], 1.0 / 32.0,
                                     None, ALU.mult)
                kb.edge("v", i1, e)
                i1 = e.tensor_scalar(rows[:, 4, :], rows[:, 3, :], -1.0, 1.0,
                                     ALU.mult, ALU.add)
                kb.edge("v", i1, e)
                i1 = e.tensor_scalar(rows[:, 8, :], rows[:, 3, :], 1.0 / 32.0,
                                     None, ALU.mult)
                kb.edge("v", i1, e)
                i1 = e.tensor_tensor(rows[:, 9, :], rows[:, 4, :],
                                     rows[:, 5, :], ALU.mult)
                i1.then_inc(a_dv.h, 1)
                e.wait_ge(a_mm.h, mm0 + 14)
                i1 = e.tensor_copy(vgb[:], rwps[:, 0, 0:CH])
                kb.edge("v", i1, e)
                i1 = e.tensor_copy(w0b[:], rwps[:, 1, 0:CH])
                kb.edge("v", i1, e)
                i1 = e.tensor_copy(w1b[:], rwps[:, 2, 0:CH])
                i1.then_inc(a_dv.h, 1)

            @blk.scalar
            def _(e, par=par, ac0=ac0, dv0=dv0, mm0=mm0):
                i1 = e.activation(scrA[:], kprb[:], AF.Tanh)
                i1.then_inc(a_ac.h, 1)
                e.wait_ge(a_dv.h, dv0 + 1)
                i1 = e.activation(psb2[:], f32t[:], AF.Sin)
                kb.edge("s", i1, e)
                i1 = e.activation(pcb[:], f32s[:], AF.Sin)
                i1.then_inc(a_ac.h, 2)
                e.wait_ge(a_dv.h, dv0 + 2)
                i1 = e.activation(ksb_[:, :, 1:CH + 1], f32t[:], AF.Sin)
                kb.edge("s", i1, e)
                i1 = e.activation(kcb[:, :, 1:CH + 1], f32s[:], AF.Sin)
                i1.then_inc(a_ac.h, 1)
                e.wait_ge(a_dv.h, dv0 + 3)
                for kc_ in range(KC):
                    i1 = e.activation(ghb[:, kc_, :], f32t[:, kc_, :], AF.Gelu,
                                      bias=bsb["bg1"][:, kc_:kc_ + 1])
                i1.then_inc(a_ac.h, 1)
                for kc_ in range(4):
                    i1 = e.activation(bhbf[:, kc_, :], bhb[:, kc_, :], AF.Gelu,
                                      bias=bsb["bb1"][:, kc_:kc_ + 1])
                i1.then_inc(a_ac.h, 1)
                e.wait_ge(a_mm.h, mm0 + 10)
                i1 = e.copy(rows[:, 6, :], rwps[0:1, 3, 0:CH])
                i1.then_inc(a_ac.h, 1)
                e.wait_ge(a_dv.h, dv0 + 4)
                i1 = e.activation(rows[:, 3, :], rows[:, 7, :], AF.Sigmoid,
                                  bias=carr1[:, 1:2])
                i1.then_inc(a_ac.h, 1)
                e.wait_ge(a_mm.h, mm0 + 9)
                i1 = e.activation(rows[:, 0, :], rwps[0:1, 1, 0:CH], AF.Sigmoid,
                                  bias=bsb["bg2"][:, 0:1])
                i1.then_inc(a_ac.h, 1)
                e.wait_ge(a_dv.h, dv0 + 5)
                i1 = e.activation(rows[:, 15, :], rows[:, 5, :], AF.Sqrt)
                i1.then_inc(a_ac.h, 1)

            @blk.tensor
            def _(e, ac0=ac0, dv0=dv0, mm0=mm0, ev0=a_ev.v):
                e.wait_ge(a_ac.h, ac0 + 5)
                for k in range(KC):
                    mi = e.matmul(rwps[0:1, 1, 0:CH], wsb["wg2"][:, k, :],
                                  ghb[:, k, :], start=(k == 0),
                                  stop=(k == KC - 1))
                mi.then_inc(a_mm.h, 9)
                e.wait_ge(a_ac.h, ac0 + 6)
                for k in range(4):
                    mi = e.matmul(rwps[0:1, 3, 0:CH], wsb["wb2"][:, k, 1:2],
                                  bhbf[:, k, :], start=(k == 0), stop=(k == 3))
                mi.then_inc(a_mm.h, 1)
                for k in range(4):
                    mi = e.matmul(rwps[0:1, 2, 0:CH], wsb["wb2"][:, k, 0:1],
                                  bhbf[:, k, :], start=(k == 0), stop=(k == 3))
                mi.then_inc(a_mm.h, 1)
                e.wait_ge(a_ac.h, ac0 + 8)
                mi = e.matmul(rwps[:, 0, 0:CH], c["ones_row_f"][:],
                              rows[:, 0, :], start=True, stop=True)
                mi.then_inc(a_mm.h, 1)
                e.wait_ge(a_dv.h, dv0 + 6)
                mi = e.matmul(rwps[:, 1, 0:CH], c["ones_row_f"][:],
                              rows[:, 8, :], start=True, stop=True)
                mi.then_inc(a_mm.h, 1)
                mi = e.matmul(rwps[:, 2, 0:CH], c["ones_row_f"][:],
                              rows[:, 9, :], start=True, stop=True)
                mi.then_inc(a_mm.h, 1)
        a_ac.v = ac0 + 10
        a_dv.v = dv0 + 7
        a_mm.v = mm0 + 14

        dv0 = a_dv.v
        with nc.Block() as blk:
            @blk.vector
            def _(e, par=par, dv0=dv0):
                def scans(dstbuf, srcbuf):
                    for k in range(KC):
                        i1 = e.tensor_tensor_scan(
                            dstbuf[:, k, :], srcbuf[:, k, :],
                            srcbuf[:, k, :],
                            dstbuf[:, k, CH - 1:CH],
                            ALU.add, ALU.bypass)
                    kb.edge("v", i1, e)
                i1 = e.tensor_tensor(scrA[:], pcb[:], pvb[:], ALU.mult)
                kb.edge("v", i1, e)
                scans(amc, scrA)
                i1 = e.tensor_tensor(scrA[:], psb2[:], pvb[:], ALU.mult)
                kb.edge("v", i1, e)
                scans(ams, scrA)
                i1 = e.tensor_tensor(scrA[:], kcb[:, :, 0:CH], kvb[:], ALU.mult)
                kb.edge("v", i1, e)
                i1 = e.tensor_tensor(scrB[:], scrA[:], bc8(vgb), ALU.mult)
                kb.edge("v", i1, e)
                scans(bmc, scrB)
                i1 = e.tensor_tensor(scrA[:], ksb_[:, :, 0:CH], kvb[:], ALU.mult)
                kb.edge("v", i1, e)
                i1 = e.tensor_tensor(scrB[:], scrA[:], bc8(vgb), ALU.mult)
                kb.edge("v", i1, e)
                scans(bms, scrB)
                i1 = e.tensor_tensor(f32s[:], pcb[:], amc[:], ALU.mult)
                kb.edge("v", i1, e)
                i1 = e.tensor_tensor(f32t[:], psb2[:], ams[:], ALU.mult)
                kb.edge("v", i1, e)
                i1 = e.tensor_tensor(f32s[:], f32s[:], f32t[:], ALU.add)
                kb.edge("v", i1, e)
                i1 = e.tensor_tensor(blend[:], f32s[:], bc8(w0b), ALU.mult)
                kb.edge("v", i1, e)
                i1 = e.tensor_tensor(f32s[:], kcb[:, :, 1:CH + 1], bmc[:],
                                     ALU.mult)
                kb.edge("v", i1, e)
                i1 = e.tensor_tensor(f32t[:], ksb_[:, :, 1:CH + 1],
                                     bms[:], ALU.mult)
                kb.edge("v", i1, e)
                i1 = e.tensor_tensor(f32s[:], f32s[:], f32t[:], ALU.add)
                kb.edge("v", i1, e)
                i1 = e.tensor_tensor(f32t[:], f32s[:], bc8(w1b), ALU.mult)
                kb.edge("v", i1, e)
                i1 = e.tensor_tensor(blend[:], blend[:], f32t[:], ALU.add)
                i1.then_inc(a_dv.h, 1)
        a_dv.v += 1

        ac0, dv0, mm0 = a_ac.v, a_dv.v, a_mm.v
        with nc.Block() as blk:
            @blk.scalar
            def _(e, ac0=ac0, dv0=dv0):
                i1 = e.activation(f32t[:], blend[:], AF.Square)
                i1.then_inc(a_ac.h, 1)
                e.wait_ge(a_dv.h, dv0 + 1)
                i1 = e.activation(rows[:, 12, :], rows[:, 11, :], AF.Sqrt)
                i1.then_inc(a_ac.h, 1)

            @blk.tensor
            def _(e, ac0=ac0, dv0=dv0, mm0=mm0):
                for k in range(KC):
                    mi = e.matmul(rwps[0:1, 0, 0:CH], c["ones_col_f"][:],
                                  blend[:, k, :], start=(k == 0),
                                  stop=(k == KC - 1))
                mi.then_inc(a_mm.h, 1)
                e.wait_ge(a_ac.h, ac0 + 1)
                for k in range(KC):
                    mi = e.matmul(rwps[0:1, 1, 0:CH], c["ones_col_f"][:],
                                  f32t[:, k, :], start=(k == 0),
                                  stop=(k == KC - 1))
                mi.then_inc(a_mm.h, 1)
                e.wait_ge(a_dv.h, dv0 + 2)
                mi = e.matmul(rwps[:, 2, 0:CH], c["ones_row_f"][:],
                              rows[:, 10, :], start=True, stop=True)
                mi.then_inc(a_mm.h, 1)
                mi = e.matmul(rwps[:, 3, 0:CH], c["ones_row_f"][:],
                              rows[:, 13, :], start=True, stop=True)
                mi.then_inc(a_mm.h, 1)

            @blk.vector
            def _(e, ac0=ac0, dv0=dv0, mm0=mm0):
                e.wait_ge(a_mm.h, mm0 + 2)
                i1 = e.tensor_scalar(rows[:, 10, :], rwps[0:1, 0, 0:CH],
                                     1.0 / D, None, ALU.mult)
                kb.edge("v", i1, e)
                i1 = e.tensor_tensor(rows[:, 14, :], rows[:, 10, :],
                                     rows[:, 10, :], ALU.mult)
                kb.edge("v", i1, e)
                i1 = e.scalar_tensor_tensor(rows[:, 11, :], rwps[0:1, 1, 0:CH],
                                            1.0 / D, rows[:, 14, :],
                                            ALU.mult, ALU.subtract)
                kb.edge("v", i1, e)
                i1 = e.tensor_scalar(rows[:, 11, :], rows[:, 11, :], 1e-5,
                                     None, ALU.add)
                i1.then_inc(a_dv.h, 1)
                e.wait_ge(a_ac.h, ac0 + 2)
                i1 = e.reciprocal(rows[:, 13, :], rows[:, 12, :])
                kb.edge("v", i1, e)
                nc.vector.engine_nop().then_inc(a_dv.h, 1)
                e.wait_ge(a_mm.h, mm0 + 4)
                i1 = e.tensor_copy(meanb[:], rwps[:, 2, 0:CH])
                kb.edge("v", i1, e)
                i1 = e.tensor_copy(rstdb[:], rwps[:, 3, 0:CH])
                kb.edge("v", i1, e)
                i1 = e.tensor_tensor(f32s[:], blend[:], bc8(meanb), ALU.subtract)
                kb.edge("v", i1, e)
                i1 = e.tensor_tensor(f32s[:], f32s[:], bc8(rstdb), ALU.mult)
                kb.edge("v", i1, e)
                for k in range(KC):
                    i1 = e.tensor_scalar(lnbf[:, k, :], f32s[:, k, :],
                                         bsb["lng"][:, k:k + 1],
                                         bsb["lnb"][:, k:k + 1],
                                         ALU.mult, ALU.add)
                i1.then_inc(a_dv.h, 1)
        a_ac.v = ac0 + 2
        a_dv.v = dv0 + 3
        a_mm.v = mm0 + 4

        mm0, ev0, dv0, st0 = a_mm.v, a_ev.v, a_dv.v, a_st.v
        with nc.Block() as blk:
            @blk.tensor
            def _(e, mm0=mm0, ev0=ev0, dv0=dv0):
                for m in range(KC):
                    if m >= 1:
                        e.wait_ge(a_ev.h, ev0 + m)
                    for k in range(KC):
                        mi = e.matmul(mmps[:, m % 2, 0:CH],
                                      wsb["wo1"][:, k, m * 128:(m + 1) * 128],
                                      lnbf[:, k, :], start=(k == 0),
                                      stop=(k == KC - 1))
                    mi.then_inc(a_mm.h, 1)
                    e.wait_ge(a_ev.h, ev0 + m + 1)
                    e.matmul(rwps[:, m % 2, 0:CH], ostg[:],
                             c["ident_f"][:], start=True,
                             stop=True).then_inc(a_mm.h, 1)

            @blk.vector
            def _(e, mm0=mm0, ev0=ev0, dv0=dv0):
                for m in range(KC):
                    e.wait_ge(a_mm.h, mm0 + 2 * m + 1)
                    e.tensor_scalar(ostg[:], mmps[:, m % 2, 0:CH],
                                    bsb["bo1"][:, m:m + 1], None,
                                    ALU.add).then_inc(a_ev.h, 1)
                    e.wait_ge(a_mm.h, mm0 + 2 * m + 2)
                    i1 = e.tensor_tensor(f32s[:, 0, :], rwps[:, m % 2, 0:CH],
                                         ln2tm[:, m * 128:(m + 1) * 128],
                                         ALU.add)
                    kb.edge("v", i1, e)
                    i1 = e.tensor_tensor(x3t[:, m * 128:(m + 1) * 128],
                                         f32s[:, 0, :],
                                         x2c[:, m * 128:(m + 1) * 128],
                                         ALU.add)
                    kb.edge("v", i1, e)
                nc.vector.engine_nop().then_inc(a_dv.h, 1)

            @blk.sync
            def _(e, t=t, dv0=dv0, st0=st0):
                e.wait_ge(a_dv.h, dv0 + 1)
                e.dma_start(out=x3[t * 128:(t + 1) * 128, :],
                            in_=x3t[:]).then_inc(a_st.h, 16)
                e.wait_ge(a_st.h, st0 + 16)
        a_mm.v += 16
        a_ev.v += KC
        a_dv.v += 1
        a_st.v += 16

        if stage == "chunk0" and ci == 0:
            return


def build_l1_attn(nc, stage="full"):
    kb = KB(nc)
    x2 = nc.dram_tensor("x2", [T, D], F32, kind="ExternalInput").ap()
    win = {"pp": nc.dram_tensor("pp", [T, D], F32, kind="ExternalInput").ap()}
    for nm, shp in (("wpv", [128, KC, D]), ("wk1", [128, KC, D]),
                    ("wkv", [128, KC, D]), ("wA", [128, KC, D]),
                    ("wB", [128, KC, D]), ("wo1", [128, KC, D]),
                    ("wb1", [128, KC, 512]), ("wg2", [128, KC, 1]),
                    ("wb2", [128, 4, 2]), ("bpv", [128, KC]),
                    ("bk1", [128, KC]), ("bkv", [128, KC]),
                    ("bg1", [128, KC]), ("lng", [128, KC]),
                    ("lnb", [128, KC]), ("bo1", [128, KC]),
                    ("bb1", [128, 4]), ("bb2", [1, 2]), ("bg2", [1, 1])):
        win[nm] = nc.dram_tensor(nm, shp, F32, kind="ExternalInput").ap()
    x3 = nc.dram_tensor("x3", [T, D], F32, kind="ExternalOutput").ap()
    emit_l1_attn(kb, x2, win, x3, stage=stage)
    kb.ctx.close()
    return nc


def prep_l1a(ins, core, x2=None):
    Wg1 = ins["L1_attn_Wg1"]
    return {
        "x2": np.ascontiguousarray(x2) if x2 is not None else None,
        "pp": np.ascontiguousarray(ins["L1_attn_pos_phases"][:T]),
        "wpv": _dm_w(ins["L1_attn_Wpv"]), "wk1": _dm_w(ins["L1_attn_Wk"]),
        "wkv": _dm_w(ins["L1_attn_Wkv"]), "wA": _dm_w(Wg1[:D]),
        "wB": _dm_w(Wg1[D:]), "wo1": _dm_w(ins["L1_attn_Wo"]),
        "wb1": _dm_w(ins["L1_attn_Wb1"]),
        "wg2": _dm_w(ins["L1_attn_Wg2"]),
        "wb2": np.ascontiguousarray(
            ins["L1_attn_Wb2"].reshape(4, 128, 2).transpose(1, 0, 2)),
        "bpv": _dm_b(ins["L1_attn_bpv"]), "bk1": _dm_b(ins["L1_attn_bk"]),
        "bkv": _dm_b(ins["L1_attn_bkv"]), "bg1": _dm_b(ins["L1_attn_bg1"]),
        "lng": _dm_b(ins["L1_attn_ln_g"]), "lnb": _dm_b(ins["L1_attn_ln_b"]),
        "bo1": _dm_b(ins["L1_attn_bo"]), "bb1": _dm_b(ins["L1_attn_bb1"]),
        "bb2": ins["L1_attn_bb2"].reshape(1, 2),
        "bg2": ins["L1_attn_bg2"].reshape(1, 1),
    }




def emit_l1_ffn(kb, x3, win, xout, n_cores=8, stage="full", dbg=None):
    nc = kb.nc
    NP2 = 2048
    MC = 512
    NMC = T // MC
    c = build_consts(kb, ["ident_bf", "ident_f", "ones_row_bf", "ones_row_f",
                          "ones_col_f"])
    e_dram = nc.dram_tensor(kb.name("edram"), [NP2, T], F32)
    s_dram = nc.dram_tensor(kb.name("sdram"), [NP2, T], F32)

    rows_rc = kb.sb([1, NMC, MC], BF16, "rows_rc")
    slots_bf = kb.sb([128, KC, NP2], BF16, "slots_bf")

    f_ld = kb.sem("f_ld")
    f_mm = kb.sem("f_mm2")
    f_ac = kb.sem("f_ac")
    f_dv = kb.sem("f_dv")
    f_st = kb.sem("f_st2")
    f_cc = kb.sem("f_cc")
    wl = kb.sem("lfwl")

    ln3_dram = nc.dram_tensor(kb.name("ln3d"), [T, D], BF16)
    scw = kb.scope()
    scw.__enter__()
    lnch = scw.sb([128, 4, D], BF16, "lnch")
    sacc = scw.sb([128, KC, NP2], F32, "sacc")
    sc1 = kb.scope()
    sc1.__enter__()
    phi_sb = sc1.sb([128, KC, NP2], BF16, "phi_sb")
    lnstage = sc1.sb([128, 8, D], BF16, "lnstage")
    ln3T = sc1.sb([128, KC, MC], BF16, "ln3T")
    e_sb = sc1.sb([128, 2, 512], F32, "e_sb")
    mmps = sc1.ps([128, 2, 512], F32, name="mmps_f")
    rsps = sc1.ps([128, 2, 512], F32, name="rsps_f")
    with nc.Block() as blk:
        @blk.gpsimd
        def _(g):
            g.dma_start(out=phi_sb[:], in_=win["phi"]).then_inc(wl.h, 16)
            g.wait_ge(wl.h, wl.v + 16)
    wl.v += 16

    for g0 in range(0, NT, 8):
        emit_ln(kb, x3, lambda t: lnstage[:, t % 8, :], nt=8, group=4,
                tile_off=g0)
        st0 = f_st.v
        with nc.Block() as blk:
            @blk.sync
            def _(e, g0=g0, st0=st0):
                for i in range(8):
                    t = g0 + i
                    e.dma_start(out=ln3_dram.ap()[t * 128:(t + 1) * 128, :],
                                in_=lnstage[:, i, :]).then_inc(f_st.h, 16)
                e.wait_ge(f_st.h, st0 + 128)
        f_st.v += 128
    for mc in range(NMC):
        ms = slice(mc * MC, (mc + 1) * MC)
        ld0 = f_ld.v
        with nc.Block() as blk:
            @blk.sync
            def _(e, mc=mc, ld0=ld0):
                e.dma_start(out=lnch[:],
                            in_=ln3_dram.ap()[mc * MC:(mc + 1) * MC, :]
                            .rearrange("(a p) d -> p a d", p=128)
                            ).then_inc(f_ld.h, 16)
                e.wait_ge(f_ld.h, ld0 + 16)
        f_ld.v += 16
        pairs = [(lnch[:, mt, k * 128:(k + 1) * 128],
                  ln3T[:, k, mt * 128:(mt + 1) * 128])
                 for mt in range(4) for k in range(KC)]
        emit_transposes(kb, pairs, c["ident_bf"])
        mm0, ac0, st0, dv0 = f_mm.v, f_ac.v, f_st.v, f_dv.v
        with nc.Block() as blk:
            @blk.tensor
            def _(e, mc=mc, mm0=mm0, ac0=ac0):
                for npc in range(16):
                    if npc >= 2:
                        e.wait_ge(f_ac.h, ac0 + npc - 1)
                    for k in range(KC):
                        mi = e.matmul(mmps[:, npc % 2, :],
                                      phi_sb[:, k, npc * 128:(npc + 1) * 128],
                                      ln3T[:, k, :], start=(k == 0),
                                      stop=(k == KC - 1))
                    mi.then_inc(f_mm.h, 1)
                    e.wait_ge(f_ac.h, ac0 + npc + 1)
                    mi = e.matmul(rsps[0:1, 0, :], c["ones_col_f"][:],
                                  e_sb[:, npc % 2, :], start=(npc == 0),
                                  stop=(npc == 15))
                    mi.then_inc(f_mm.h, 1)

            @blk.scalar
            def _(e, mc=mc, mm0=mm0, st0=st0):
                for npc in range(16):
                    e.wait_ge(f_mm.h, mm0 + 2 * npc + 1)
                    if npc >= 2:
                        e.wait_ge(f_st.h, st0 + 16 * (npc - 1))
                    e.activation(e_sb[:, npc % 2, :], mmps[:, npc % 2, :],
                                 AF.Exp).then_inc(f_ac.h, 1)

            @blk.sync
            def _(e, mc=mc, mm0=mm0, st0=st0):
                for npc in range(16):
                    e.wait_ge(f_mm.h, mm0 + 2 * npc + 2)
                    e.dma_start(
                        out=e_dram.ap()[npc * 128:(npc + 1) * 128, ms],
                        in_=e_sb[:, npc % 2, :]).then_inc(f_st.h, 16)

            @blk.vector
            def _(e, mc=mc, mm0=mm0, dv0=dv0):
                e.wait_ge(f_mm.h, mm0 + 32)
                with nc.allow_low_precision(reason="bf16 combine denominators"):
                    i1 = e.reciprocal(rows_rc[:, mc, :], rsps[0:1, 0, :])
                kb.edge("v", i1, e)
                nc.vector.engine_nop().then_inc(f_dv.h, 1)
        f_mm.v += 32
        f_ac.v += 16
        f_st.v += 16 * 16
        f_dv.v += 1

    cc0, st0 = f_cc.v, f_st.v
    with nc.Block() as blk:
        @blk.gpsimd
        def _(e, cc0=cc0, st0=st0):
            e.wait_ge(f_st.h, st0)
            e.collective_compute(
                "AllReduce", ALU.add,
                replica_groups=[list(range(n_cores))],
                ins=[e_dram.ap().opt()],
                outs=[s_dram.ap().opt()],
            ).then_inc(f_cc.h)
            e.wait_ge(f_cc.h, cc0 + 1)
    f_cc.v += 1

    sc1.__exit__(None, None, None)
    if stage == "g2":
        scw.__exit__(None, None, None)
        return

    sc2 = kb.scope()
    sc2.__enter__()
    en_sb = sc2.sb([128, 2, 512], F32, "en_sb")
    sn_sb = sc2.sb([128, 2, 512], F32, "sn_sb")
    ds1 = sc2.sb([128, 2, 512], F32, "ds1")
    ds2 = sc2.sb([128, 2, 512], F32, "ds2")
    dn_sb = sc2.sb([128, 2, 512], BF16, "dn_sb")
    disp = sc2.sb([128, 4, NP2], BF16, "disp")
    mmps = sc2.ps([128, 2, 512], F32, name="mmps_f2")
    rsps = sc2.ps([128, 2, 512], F32, name="rsps_f2")
    with nc.Block() as blk:
        @blk.vector
        def _(e):
            e.memset(sacc[:], 0.0)
    for mc in range(NMC):
        ms = slice(mc * MC, (mc + 1) * MC)
        ld0, mm0, dv0 = f_ld.v, f_mm.v, f_dv.v
        with nc.Block() as blk:
            @blk.sync
            def _(e, mc=mc, ld0=ld0, dv0=dv0, ac0=f_ac.v):
                e.dma_start(out=lnch[:],
                            in_=ln3_dram.ap()[mc * MC:(mc + 1) * MC, :]
                            .rearrange("(a p) d -> p a d", p=128)
                            ).then_inc(f_ld.h, 16)
                for npc in range(16):
                    if npc >= 2:
                        e.wait_ge(f_ac.h, ac0 + 2 * (npc - 2) + 1)
                    e.dma_start(out=en_sb[:, npc % 2, :],
                                in_=e_dram.ap()[npc * 128:(npc + 1) * 128, ms]
                                ).then_inc(f_ld.h, 16)
                    e.dma_start(out=sn_sb[:, npc % 2, :],
                                in_=s_dram.ap()[npc * 128:(npc + 1) * 128, ms]
                                ).then_inc(f_ld.h, 16)

            @blk.scalar
            def _(e, mc=mc, ld0=ld0, mm0=mm0, dv0=dv0, ac0=f_ac.v):
                for npc in range(16):
                    sl = npc % 2
                    e.wait_ge(f_ld.h, ld0 + 16 + 32 * (npc + 1))
                    i1 = e.activation(ds1[:, sl, :], en_sb[:, sl, :], AF.Ln)
                    kb.edge("s", i1, e)
                    i1 = e.activation(ds2[:, sl, :], sn_sb[:, sl, :], AF.Ln)
                    i1.then_inc(f_ac.h, 1)
                    e.wait_ge(f_dv.h, dv0 + 5 * npc + 1)
                    if npc >= 2:
                        e.wait_ge(f_mm.h, mm0 + 4 * (npc - 1))
                    e.activation(dn_sb[:, sl, :], ds1[:, sl, :],
                                 AF.Exp).then_inc(f_ac.h, 1)

            @blk.vector
            def _(e, mc=mc, mm0=mm0, dv0=dv0, ac0=f_ac.v):
                for it in range(17):
                    if it < 16:
                        sl = it % 2
                        e.wait_ge(f_ac.h, ac0 + 2 * it + 1)
                        i1 = e.tensor_tensor(ds1[:, sl, :], ds1[:, sl, :],
                                             ds2[:, sl, :], ALU.subtract)
                        i1.then_inc(f_dv.h, 1)
                    if it >= 1:
                        g_ = it - 1
                        for mt in range(4):
                            i = 4 * g_ + mt
                            e.wait_ge(f_mm.h, mm0 + i + 1)
                            mi = e.tensor_copy(
                                disp[:, mt, g_ * 128:(g_ + 1) * 128],
                                rsps[:, i % 2, 0:128])
                            mi.then_inc(f_dv.h, 1)

            @blk.tensor
            def _(e, mc=mc, mm0=mm0, dv0=dv0, ac0=f_ac.v):
                def evac_cnt(i):
                    q, r = i // 4, i % 4
                    return dv0 + 5 * q + r + 2
                i = 0
                for npc in range(16):
                    e.wait_ge(f_ac.h, ac0 + 2 * npc + 2)
                    for mt in range(4):
                        if i >= 2:
                            e.wait_ge(f_dv.h, evac_cnt(i - 2))
                        mi = e.matmul(rsps[:, i % 2, 0:128],
                                      dn_sb[:, npc % 2, mt * 128:(mt + 1) * 128],
                                      c["ident_bf"][:], start=True, stop=True)
                        mi.then_inc(f_mm.h, 1)
                        i += 1
        f_ld.v += 16 + 32 * 16
        f_dv.v += 16 + 64
        f_mm.v += 64
        f_ac.v += 32
        mm0, dv0 = f_mm.v, f_dv.v
        with nc.Block() as blk:
            @blk.tensor
            def _(e, mc=mc, mm0=mm0, dv0=dv0):
                for dc in range(KC):
                    for npq in range(4):
                        gi = dc * 4 + npq
                        if gi >= 2:
                            e.wait_ge(f_dv.h, dv0 + gi - 1)
                        for mt in range(4):
                            mi = e.matmul(
                                mmps[:, gi % 2, :],
                                lnch[:, mt, dc * 128:(dc + 1) * 128],
                                disp[:, mt, npq * 512:(npq + 1) * 512],
                                start=(mt == 0), stop=(mt == 3))
                        mi.then_inc(f_mm.h, 1)

            @blk.vector
            def _(e, mc=mc, mm0=mm0, dv0=dv0):
                for dc in range(KC):
                    for npq in range(4):
                        gi = dc * 4 + npq
                        e.wait_ge(f_mm.h, mm0 + gi + 1)
                        i1 = e.tensor_tensor(
                            sacc[:, dc, npq * 512:(npq + 1) * 512],
                            sacc[:, dc, npq * 512:(npq + 1) * 512],
                            mmps[:, gi % 2, :], ALU.add)
                        i1.then_inc(f_dv.h, 1)
        f_mm.v += 32
        f_dv.v += 32

    dv0 = f_dv.v
    with nc.Block() as blk:
        @blk.vector
        def _(e, dv0=dv0):
            for dc in range(KC):
                i1 = e.tensor_copy(slots_bf[:, dc, :], sacc[:, dc, :])
            i1.then_inc(f_dv.h, 1)
    f_dv.v += 1
    sc2.__exit__(None, None, None)
    scw.__exit__(None, None, None)

    if stage == "g3":
        dsem = kb.sem()
        with nc.Block() as blk:
            @blk.gpsimd
            def _(e):
                n0 = dsem.v
                for dc in range(KC):
                    e.dma_start(out=dbg[:, dc, :],
                                in_=slots_bf[:, dc, :]).then_inc(dsem.h, 16)
                e.wait_ge(dsem.h, n0 + 16 * KC)
        dsem.v += 16 * KC
        return

    y_sb = kb.sb([128, 16, D], BF16, "y_sb")
    with kb.scope() as sc:
        w4 = sc.sb([128, 2, 4, D], BF16, "w4")
        brow = sc.sb([1, 2, 4, D], BF16, "brow")
        yps = sc.ps([128, 2, 512], F32, name="yps")
        wsem = (kb.sem("w4a"), kb.sem("w4b"))
        bsem = (kb.sem("b4a"), kb.sem("b4b"))
        y_mm = kb.sem("y_mm")
        y_ev = kb.sem("y_ev")
        for ng in range(16):
            b0, mm0, ev0 = bsem[ng % 2].v, y_mm.v, y_ev.v
            w00 = (wsem[0].v, wsem[1].v)
            with nc.Block() as blk:
                @blk.gpsimd
                def _(e, ng=ng, mm0=mm0, w00=w00, ev0=ev0):
                    bap = bass.AP(win["b"].ap().tensor, ng * 4 * D,
                                  [[4 * D, 1], [D, 4], [1, D]])
                    e.dma_start(out=brow[:, ng % 2, :, :],
                                in_=bap).then_inc(bsem[ng % 2].h, 16)
                    for dc in range(KC):
                        if dc >= 2:
                            e.wait_ge(y_mm.h, mm0 + dc - 1)
                        e.dma_start(
                            out=w4[:, dc % 2, :, :],
                            in_=win["w"].ap()[ng * 4:(ng + 1) * 4, :, dc, :]
                            .rearrange("n p d -> p n d")
                            ).then_inc(wsem[dc % 2].h, 16)

                @blk.tensor
                def _(e, ng=ng, w00=w00, b0=b0, mm0=mm0, ev0=ev0):
                    if ng >= 1:
                        e.wait_ge(y_ev.h, ev0)
                    for dc in range(KC):
                        e.wait_ge(wsem[dc % 2].h,
                                  w00[dc % 2] + 16 * (dc // 2 + 1))
                        for half in range(2):
                            hs = slice(half * 512, (half + 1) * 512)
                            for j in range(4):
                                n = ng * 4 + j
                                mi = e.matmul(
                                    yps[32 * j:32 * j + 32, half, :],
                                    slots_bf[:, dc, n * 32:(n + 1) * 32],
                                    w4[:, dc % 2, j, hs],
                                    start=(dc == 0), stop=False,
                                    tile_position=(0, 32 * j))
                        mi.then_inc(y_mm.h, 1)
                    e.wait_ge(bsem[ng % 2].h, b0 + 16)
                    for half in range(2):
                        hs = slice(half * 512, (half + 1) * 512)
                        for j in range(4):
                            mi = e.matmul(
                                yps[32 * j:32 * j + 32, half, :],
                                c["ones_row_bf"][:, 0:32],
                                brow[0:1, ng % 2, j, hs],
                                start=False, stop=True,
                                tile_position=(0, 32 * j))
                        mi.then_inc(y_mm.h, 1)
                y_mm.v += KC + 2
                wsem[0].v += 16 * 4
                wsem[1].v += 16 * 4
                bsem[ng % 2].v += 16

                @blk.vector
                def _(e, ng=ng, mm0=mm0):
                    for half in range(2):
                        e.wait_ge(y_mm.h, mm0 + KC + half + 1)
                        mi = e.tensor_copy(
                            y_sb[:, ng, half * 512:(half + 1) * 512],
                            yps[:, half, :])
                    mi.then_inc(y_ev.h, 1)
                y_ev.v += 1

    if stage == "g4":
        dsem = kb.sem()
        with nc.Block() as blk:
            @blk.gpsimd
            def _(e):
                n0 = dsem.v
                for q in range(16):
                    e.dma_start(out=dbg[:, q, 0:1024],
                                in_=y_sb[:, q, :]).then_inc(dsem.h, 16)
                e.wait_ge(dsem.h, n0 + 256)
        dsem.v += 256
        return

    with kb.scope() as sc:
        comb = sc.sb([128, 16, 512], BF16, "comb")
        en2 = sc.sb([128, 2, 512], BF16, "en2")
        x3c = sc.sb([128, 4, D], F32, "x3c")
        outt = sc.sb([128, 4, D], F32, "outt")
        ops_ = sc.ps([128, 2, 512], F32, name="ops_g5")
        bcp = sc.ps([128, 2, 512], F32, name="bcp_g5")
        g_ld = kb.sem("g5ld")
        g_ld2 = kb.sem("g5ld2")
        g_mm = kb.sem("g5mm")
        g_dv = kb.sem("g5dv")
        g_st = kb.sem("g5st")
        for mc in range(NMC):
            ms = slice(mc * MC, (mc + 1) * MC)
            ld0, ld20, mm0, dv0, st0 = (g_ld.v, g_ld2.v, g_mm.v, g_dv.v,
                                        g_st.v)
            with nc.Block() as blk:
                @blk.sync
                def _(e, mc=mc, ld20=ld20, dv0=dv0, st0=st0):
                    for mt in range(4):
                        t = mc * 4 + mt
                        e.dma_start(out=x3c[:, mt, :],
                                    in_=x3[t * 128:(t + 1) * 128, :]
                                    ).then_inc(g_ld2.h, 16)
                    e.wait_ge(g_dv.h, dv0 + 16 + 8)
                    for mt in range(4):
                        t = mc * 4 + mt
                        e.dma_start(out=xout[t * 128:(t + 1) * 128, :],
                                    in_=outt[:, mt, :]).then_inc(g_st.h, 16)
                    e.wait_ge(g_st.h, st0 + 64)

                @blk.gpsimd
                def _(e, mc=mc, ld0=ld0, dv0=dv0):
                    for npc in range(16):
                        if npc >= 2:
                            e.wait_ge(g_dv.h, dv0 + npc - 1)
                        e.dma_start(
                            out=en2[:, npc % 2, :],
                            in_=e_dram.ap()[npc * 128:(npc + 1) * 128, ms]
                            ).then_inc(g_ld.h, 16)

                @blk.tensor
                def _(e, mc=mc, ld0=ld0, mm0=mm0, dv0=dv0):
                    for npc in range(16):
                        mi = e.matmul(bcp[:, npc % 2, :], c["ones_row_bf"][:],
                                      rows_rc[:, mc, :], start=True, stop=True)
                        mi.then_inc(g_mm.h, 1)
                        e.wait_ge(g_dv.h, dv0 + npc + 1)
                    for mt in range(4):
                        for half in range(2):
                            hs = slice(half * 512, (half + 1) * 512)
                            gi = 2 * mt + half
                            for npc in range(16):
                                mi = e.matmul(
                                    ops_[:, gi % 2, :],
                                    comb[:, npc, mt * 128:(mt + 1) * 128],
                                    y_sb[:, npc, hs],
                                    start=(npc == 0), stop=(npc == 15))
                            mi.then_inc(g_mm.h, 1)
                            if gi < 7:
                                e.wait_ge(g_dv.h, dv0 + 16 + gi)

                @blk.vector
                def _(e, mc=mc, ld0=ld0, ld20=ld20, mm0=mm0, dv0=dv0):
                    e.wait_ge(g_ld2.h, ld20 + 64)
                    for npc in range(16):
                        e.wait_ge(g_ld.h, ld0 + 16 * (npc + 1))
                        e.wait_ge(g_mm.h, mm0 + npc + 1)
                        i1 = e.tensor_tensor(comb[:, npc, :],
                                             en2[:, npc % 2, :],
                                             bcp[:, npc % 2, :], ALU.mult)
                        i1.then_inc(g_dv.h, 1)
                    for mt in range(4):
                        for half in range(2):
                            hs = slice(half * 512, (half + 1) * 512)
                            gi = 2 * mt + half
                            e.wait_ge(g_mm.h, mm0 + 16 + gi + 1)
                            i1 = e.tensor_tensor(outt[:, mt, hs],
                                                 ops_[:, gi % 2, :],
                                                 x3c[:, mt, hs], ALU.add)
                            i1.then_inc(g_dv.h, 1)
            g_ld.v += 16 * 16
            g_ld2.v += 64
            g_mm.v += 16 + 8
            g_dv.v += 16 + 8
            g_st.v += 64


def build_l1_ffn(nc, n_cores=8, stage="full"):
    kb = KB(nc)
    x3 = nc.dram_tensor("x3", [T, D], F32, kind="ExternalInput").ap()
    win = {
        "phi": nc.dram_tensor("phi", [128, KC, 2048], F32,
                              kind="ExternalInput").ap(),
        "w": nc.dram_tensor("w", [64, 128, KC, D], F32, kind="ExternalInput"),
        "b": nc.dram_tensor("b", [64, D], F32, kind="ExternalInput"),
    }
    xout = nc.dram_tensor("xout", [T, D], F32, kind="ExternalOutput").ap()
    dbg = (nc.dram_tensor("dbg", [128, 16, 2048], F32,
                          kind="ExternalOutput").ap()
           if stage != "full" else None)
    emit_l1_ffn(kb, x3, win, xout, n_cores=n_cores, stage=stage, dbg=dbg)
    kb.ctx.close()
    return nc


def prep_l1f(ins, core, x3=None):
    return {
        "x3": np.ascontiguousarray(x3) if x3 is not None else None,
        "phi": np.ascontiguousarray(
            ins["L1_ffn_phi"].reshape(KC, 128, 2048).transpose(1, 0, 2)),
        "w": np.ascontiguousarray(
            ins["L1_ffn_W"].reshape(64, KC, 128, D).transpose(0, 2, 1, 3)),
        "b": np.ascontiguousarray(ins["L1_ffn_b"]),
    }


def build_fused(nc):
    kb = KB(nc)
    x0 = nc.dram_tensor("x0", [T, D], F32, kind="ExternalInput").ap()
    out = nc.dram_tensor("xout", [T, D], F32, kind="ExternalOutput").ap()
    x1 = nc.dram_tensor("x1i", [T, D], F32).ap()
    x2 = nc.dram_tensor("x2i", [T, D], F32).ap()
    x3 = nc.dram_tensor("x3i", [T, D], F32).ap()

    win1 = {}
    for nm in ("wq", "wk", "wv", "wo"):
        win1[nm] = nc.dram_tensor(nm, [128, KC, D], F32,
                                  kind="ExternalInput").ap()
    for nm in ("bq", "bk", "bv", "bo"):
        win1[nm] = nc.dram_tensor(nm, [128, KC], F32,
                                  kind="ExternalInput").ap()
    win2 = {
        "gate": nc.dram_tensor("gate", [128, KC, 8], F32,
                               kind="ExternalInput").ap(),
        "wg": nc.dram_tensor("wg", [8, 128, KC, 2048], F32,
                             kind="ExternalInput"),
        "wu": nc.dram_tensor("wu", [8, 128, KC, 2048], F32,
                             kind="ExternalInput"),
        "wd": nc.dram_tensor("wd", [8, 128, 16, 1024], F32,
                             kind="ExternalInput"),
    }
    win3 = {"pp": nc.dram_tensor("pp", [T, D], F32, kind="ExternalInput").ap()}
    for nm, shp in (("wpv", [128, KC, D]), ("wk1", [128, KC, D]),
                    ("wkv", [128, KC, D]), ("wA", [128, KC, D]),
                    ("wB", [128, KC, D]), ("wo1", [128, KC, D]),
                    ("wb1", [128, KC, 512]), ("wg2", [128, KC, 1]),
                    ("wb2", [128, 4, 2]), ("bpv", [128, KC]),
                    ("bk1", [128, KC]), ("bkv", [128, KC]),
                    ("bg1", [128, KC]), ("lng", [128, KC]),
                    ("lnb", [128, KC]), ("bo1", [128, KC]),
                    ("bb1", [128, 4]), ("bb2", [1, 2]), ("bg2", [1, 1])):
        win3[nm] = nc.dram_tensor(nm, shp, F32, kind="ExternalInput").ap()
    win4 = {
        "phi": nc.dram_tensor("phi", [128, KC, 2048], F32,
                              kind="ExternalInput").ap(),
        "w": nc.dram_tensor("w", [64, 128, KC, D], F32, kind="ExternalInput"),
        "b": nc.dram_tensor("b", [64, D], F32, kind="ExternalInput"),
    }

    for emitfn, args in ((emit_l0_attn, (kb, x0, win1, x1)),
                         (emit_l0_ffn, (kb, x1, win2, x2)),
                         (emit_l1_attn, (kb, x2, win3, x3)),
                         (emit_l1_ffn, (kb, x3, win4, out))):
        sc = kb.scope()
        sc.__enter__()
        kb.push_scope(sc)
        emitfn(*args)
        kb.pop_scope()
        sc.__exit__(None, None, None)
    kb.ctx.close()
    return nc


def prep_fused(ins, core):
    m = prep_l0a(ins, core)
    m.update(prep_l0f(ins, core))
    m.update(prep_l1a(ins, core))
    m.update(prep_l1f(ins, core))
    for k in ("x1", "x2", "x3"):
        m.pop(k, None)
    return m


_NC_CACHE = {}


def _get_nc(name, builder):
    if name not in _NC_CACHE:
        nc = bass.Bass("TRN2", target_bir_lowering=False)
        builder(nc)
        _NC_CACHE[name] = nc
    return _NC_CACHE[name]


def kernel(**inputs):
    ins = inputs
    cores = list(range(8))
    nc = _get_nc("fused", build_fused)
    maps = [prep_fused(ins, c_) for c_ in cores]
    res = run_bass_kernel_spmd(nc, maps, core_ids=cores)
    return np.stack([res.results[c_]["xout"] for c_ in cores])


def kernel_multi(**inputs):
    ins = inputs
    ncores = 8
    cores = list(range(ncores))

    nc1 = _get_nc("l0a", build_l0_attn)
    m0 = prep_l0a(ins, 0)
    maps = []
    for c_ in cores:
        m = dict(m0)
        m["x0"] = np.ascontiguousarray(ins["L0_attn_x"][c_])
        maps.append(m)
    r1 = run_bass_kernel_spmd(nc1, maps, core_ids=cores)
    x1s = [r1.results[c_]["x1"] for c_ in cores]

    nc2 = _get_nc("l0f", build_l0_ffn)
    m0 = prep_l0f(ins, 0)
    maps = []
    for c_ in cores:
        m = dict(m0)
        m["x1"] = np.ascontiguousarray(x1s[c_])
        maps.append(m)
    r2 = run_bass_kernel_spmd(nc2, maps, core_ids=cores)
    x2s = [r2.results[c_]["x2"] for c_ in cores]

    nc3 = _get_nc("l1a", build_l1_attn)
    m0 = prep_l1a(ins, 0)
    maps = []
    for c_ in cores:
        m = dict(m0)
        m["x2"] = np.ascontiguousarray(x2s[c_])
        maps.append(m)
    r3 = run_bass_kernel_spmd(nc3, maps, core_ids=cores)
    x3s = [r3.results[c_]["x3"] for c_ in cores]

    nc4 = _get_nc("l1f", build_l1_ffn)
    m0 = prep_l1f(ins, 0)
    maps = []
    for c_ in cores:
        m = dict(m0)
        m["x3"] = np.ascontiguousarray(x3s[c_])
        maps.append(m)
    r4 = run_bass_kernel_spmd(nc4, maps, core_ids=cores)
    out = np.stack([r4.results[c_]["xout"] for c_ in cores])
    return out





# version 04
# speedup vs baseline: 1.0768x; 1.0768x over previous
import math
from contextlib import ExitStack

import numpy as np
import concourse.bass as bass
import concourse.mybir as mybir
from concourse.bass_utils import run_bass_kernel_spmd
from concourse.bass_core_rust import IndirectOffsetOnAxis

F32 = mybir.dt.float32
BF16 = mybir.dt.bfloat16
U32 = mybir.dt.uint32
I32 = mybir.dt.int32
AF = mybir.ActivationFunctionType
ALU = mybir.AluOpType
AXX = mybir.AxisListType.X

T = 4096
D = 1024
NT = 32
KC = 8
PI = math.pi
INV2PI = 1.0 / (2.0 * PI)


class Sem:
    _n = [0]

    def __init__(self, ctx, nc, name):
        Sem._n[0] += 1
        self.h = ctx.enter_context(nc.semaphore(f"{name}_{Sem._n[0]}"))
        self.v = 0


class Scope:
    def __init__(self, kb):
        self.kb = kb
        self.ctx = ExitStack()

    def sb(self, shape, dt, name=None):
        return self.ctx.enter_context(
            self.kb.nc.sbuf_tensor(self.kb.name((name or "sb") + "_"),
                                   shape, dt))

    def ps(self, shape, dt=F32, name=None):
        return self.ctx.enter_context(
            self.kb.nc.psum_tensor(self.kb.name((name or "ps") + "_"),
                                   shape, dt))

    def __enter__(self):
        return self

    def __exit__(self, *a):
        self.ctx.close()
        return False


class KB:
    def __init__(self, nc):
        self.nc = nc
        self.ctx = ExitStack()
        self._n = 0

    def name(self, p="t"):
        self._n += 1
        return f"{p}{self._n}"

    def sb(self, shape, dt, name=None):
        tgt = self._redirect[-1].ctx if getattr(self, "_redirect", None) else self.ctx
        return tgt.enter_context(
            self.nc.sbuf_tensor(self.name((name or "sb") + "_"), shape, dt))

    def ps(self, shape, dt=F32, name=None):
        tgt = self._redirect[-1].ctx if getattr(self, "_redirect", None) else self.ctx
        return tgt.enter_context(
            self.nc.psum_tensor(self.name((name or "ps") + "_"), shape, dt))

    def push_scope(self, sc):
        if not hasattr(self, "_redirect"):
            self._redirect = []
        self._redirect.append(sc)

    def pop_scope(self):
        self._redirect.pop()

    def sem(self, name=None):
        return Sem(self.ctx, self.nc, name or self.name("sem"))

    def nsem(self, name):
        if not hasattr(self, "_nsems"):
            self._nsems = {}
        if name not in self._nsems:
            self._nsems[name] = self.sem(name)
        return self._nsems[name]

    def esem(self, ename):
        if not hasattr(self, "_esems"):
            self._esems = {}
        if ename not in self._esems:
            self._esems[ename] = self.sem("edge_" + ename)
        return self._esems[ename]

    def edge(self, ename, inst, e):
        s = self.esem(ename)
        inst.then_inc(s.h, 1)
        s.v += 1
        e.wait_ge(s.h, s.v)

    def scope(self):
        return Scope(self)


def build_consts(kb, want, sc=None):
    nc = kb.nc
    al = sc if sc is not None else kb
    c = {}
    alloc = {
        "ident_bf": ([128, 128], BF16), "ident_f": ([128, 128], F32),
        "ones_row_bf": ([1, 128], BF16), "ones_row_f": ([1, 128], F32),
        "ones_col_f": ([128, 1], F32), "uexc_f": ([128, 128], F32),
        "allones_f": ([128, 128], F32), "pos_tm": ([128, 32], F32),
        "indic_t": ([128, 8], F32), "indic8": ([8, 128], F32),
        "ones_col_bf": ([128, 1], BF16),
    }
    for w in want:
        c[w] = al.sb(*alloc[w], name="c_" + w)
    need_io = any(w in want for w in ("ident_bf", "ident_f", "uexc_f"))
    io = al.sb([128, 128], F32, "c_io") if need_io else None
    s1 = (al.sb([128, 136], F32, "c_s1")
          if ("indic_t" in want or "indic8" in want) else None)
    with nc.Block() as blk:
        @blk.gpsimd
        def _(g):
            if io is not None:
                g.iota(io[:], [[1, 128]], channel_multiplier=-1,
                       allow_small_or_imprecise_dtypes=True)
            if "pos_tm" in c:
                g.iota(c["pos_tm"][:], [[1, 32]], channel_multiplier=32,
                       allow_small_or_imprecise_dtypes=True)
            if "indic_t" in c:
                g.iota(s1[:, 0:8], [[-16, 8]], channel_multiplier=1,
                       allow_small_or_imprecise_dtypes=True)
            if "indic8" in c:
                g.iota(s1[:, 8:136], [[1, 128]], channel_multiplier=-16,
                       allow_small_or_imprecise_dtypes=True)
    with nc.Block() as blk:
        @blk.vector
        def _(v):
            if "ident_bf" in c:
                v.tensor_scalar(c["ident_bf"][:], io[:], 0.0, None, ALU.is_equal)
            if "ident_f" in c:
                v.tensor_scalar(c["ident_f"][:], io[:], 0.0, None, ALU.is_equal)
            if "uexc_f" in c:
                v.tensor_scalar(c["uexc_f"][:], io[:], 0.0, None, ALU.is_gt)
            for nm in ("ones_row_bf", "ones_row_f", "ones_col_f", "allones_f",
                       "ones_col_bf"):
                if nm in c:
                    v.memset(c[nm][:], 1.0)
            if "indic_t" in c:
                a = s1[:, 0:8]
                b_ = s1[:, 136:144] if False else None
                i1 = v.tensor_scalar(c["indic_t"][:], a, -0.5, None, ALU.is_gt)
                kb.edge("v", i1, v)
                i1 = v.tensor_scalar(s1[:, 0:8], a, 15.5, None, ALU.is_lt)
                kb.edge("v", i1, v)
                i1 = v.tensor_mul(c["indic_t"][:], c["indic_t"][:], s1[:, 0:8])
                kb.edge("v", i1, v)
            if "indic8" in c:
                a = s1[0:8, 8:136]
                i1 = v.tensor_scalar(c["indic8"][:], a, -0.5, None, ALU.is_gt)
                kb.edge("v", i1, v)
                i1 = v.tensor_scalar(s1[0:8, 8:136], a, 15.5, None, ALU.is_lt)
                kb.edge("v", i1, v)
                i1 = v.tensor_mul(c["indic8"][:], c["indic8"][:], s1[0:8, 8:136])
                kb.edge("v", i1, v)
    return c


def emit_ln(kb, x_dram, out_slice, keep_x=None, nt=NT, group=8, tile_off=0):
    nc = kb.nc
    with kb.scope() as sc:
        xg = sc.sb([128, group, D], F32)
        sq = sc.sb([128, 2, D], F32)
        st = sc.sb([128, group, 8], F32)
        ld = kb.nsem("ln_ld")
        s1 = kb.nsem("ln_s1")
        v1 = kb.nsem("ln_v1")
        for g0 in range(0, nt, group):
            gn = min(group, nt - g0)
            ld0, s10, v10 = ld.v, s1.v, v1.v
            with nc.Block() as blk:
                @blk.sync
                def _(e, g0=g0, gn=gn):
                    for i in range(gn):
                        t = tile_off + g0 + i
                        e.dma_start(out=xg[:, i, :],
                                    in_=x_dram[t * 128:(t + 1) * 128, :]
                                    ).then_inc(ld.h, 16)

                @blk.scalar
                def _(e, g0=g0, gn=gn, ld0=ld0, s10=s10, v10=v10):
                    e.wait_ge(ld.h, ld0 + 16 * gn)
                    for i in range(gn):
                        e.activation(sq[:, i % 2, :], xg[:, i, :], AF.Square,
                                     accum_out=st[:, i, 0:1]).then_inc(s1.h, 1)
                    e.wait_ge(v1.h, v10 + 1)
                    e.activation(st[:, 0:gn, 5:6], st[:, 0:gn, 4:5],
                                 AF.Sqrt).then_inc(s1.h, 1)

                @blk.vector
                def _(e, g0=g0, gn=gn, ld0=ld0, s10=s10, v10=v10):
                    e.wait_ge(ld.h, ld0 + 16 * gn)
                    for i in range(gn):
                        ri = e.reduce_sum(st[:, i, 1:2], xg[:, i, :], axis=AXX)
                    kb.edge("v", ri, e)
                    e.wait_ge(s1.h, s10 + gn)
                    g_ = st[:, 0:gn, :]
                    i1 = e.tensor_scalar(g_[:, :, 2:3], g_[:, :, 1:2], 1.0 / D,
                                         None, ALU.mult)
                    kb.edge("v", i1, e)
                    i1 = e.tensor_tensor(g_[:, :, 3:4], g_[:, :, 2:3],
                                         g_[:, :, 2:3], ALU.mult)
                    kb.edge("v", i1, e)
                    i1 = e.scalar_tensor_tensor(g_[:, :, 4:5], g_[:, :, 0:1],
                                                1.0 / D, g_[:, :, 3:4],
                                                ALU.mult, ALU.subtract)
                    kb.edge("v", i1, e)
                    e.tensor_scalar(g_[:, :, 4:5], g_[:, :, 4:5], 1e-5, None,
                                    ALU.add).then_inc(v1.h, 1)
                    e.wait_ge(s1.h, s10 + gn + 1)
                    i1 = e.reciprocal(g_[:, :, 6:7], g_[:, :, 5:6])
                    kb.edge("v", i1, e)
                    i1 = e.tensor_tensor(g_[:, :, 7:8], g_[:, :, 2:3],
                                         g_[:, :, 6:7], ALU.mult)
                    kb.edge("v", i1, e)
                    i1 = e.tensor_scalar(g_[:, :, 7:8], g_[:, :, 7:8], -1.0, None,
                                         ALU.mult)
                    kb.edge("v", i1, e)
                    for i in range(gn):
                        e.tensor_scalar(out_slice(g0 + i), xg[:, i, :],
                                        st[:, i, 6:7], st[:, i, 7:8],
                                        ALU.mult, ALU.add)
                    if keep_x is not None:
                        for i in range(gn):
                            e.tensor_copy(keep_x(g0 + i), xg[:, i, :])
            ld.v += 16 * gn
            s1.v += gn + 1
            v1.v += 1


def emit_transposes(kb, pairs, ident_bf, grp=2):
    nc = kb.nc
    groups = [pairs[i:i + grp] for i in range(0, len(pairs), grp)]
    with kb.scope() as sc:
        ps = sc.ps([128, 2 * grp, 512], F32)
        pe = kb.nsem("tp_pe")
        ev = kb.nsem("tp_ev")
        pe0, ev0 = pe.v, ev.v
        with nc.Block() as blk:
            @blk.tensor
            def _(e):
                for gi, grp_pairs in enumerate(groups):
                    if gi >= 2:
                        e.wait_ge(ev.h, ev0 + gi - 1)
                    base = (gi % 2) * grp
                    for j, (src, dst) in enumerate(grp_pairs):
                        mi = e.matmul(ps[:, base + j, 0:128], src, ident_bf[:],
                                      start=True, stop=True)
                    mi.then_inc(pe.h, 1)

            @blk.vector
            def _(e):
                for gi, grp_pairs in enumerate(groups):
                    e.wait_ge(pe.h, pe0 + gi + 1)
                    base = (gi % 2) * grp
                    for j, (src, dst) in enumerate(grp_pairs):
                        mi = e.tensor_copy(dst, ps[:, base + j, 0:128])
                    mi.then_inc(ev.h, 1)
        pe.v += len(groups)
        ev.v += len(groups)


def emit_l0_attn(kb, x0, win, x1, stage="wo", dbg=None):
    nc = kb.nc
    c = build_consts(kb, ["ident_bf", "ident_f", "ones_row_f"])
    ctxd = nc.dram_tensor(kb.name("ctxd"), [KC, 128, T], BF16)

    bsb = {}
    for nm in ("bq", "bk", "bv", "bo"):
        bsb[nm] = kb.sb([128, KC], F32, nm + "_sb")
    wl = kb.sem("wl")
    with nc.Block() as blk:
        @blk.gpsimd
        def _(g):
            for nm in ("bq", "bk", "bv", "bo"):
                g.dma_start(out=bsb[nm][:], in_=win[nm]).then_inc(wl.h, 16)
            g.wait_ge(wl.h, wl.v + 64)
    wl.v += 64

    c4 = kb.sb([128, T], BF16, "c4")
    s4 = kb.sb([128, T], BF16, "s4")
    with kb.scope() as sc:
        ang = sc.sb([32, T], F32, "ang")
        carg = sc.sb([32, T], F32, "carg")
        iscr = sc.sb([32, T], I32, "iscr")
        invf = sc.sb([1, 32], F32, "invf")
        posr = sc.sb([1, T], F32, "posr")
        c32 = sc.sb([32, T], F32, "c32")
        s32 = sc.sb([32, T], F32, "s32")
        aps = sc.ps([32, 512], F32, name="angps")
        with nc.Block() as blk:
            @blk.gpsimd
            def _(g):
                g.iota(invf[:], [[1, 32]], channel_multiplier=0,
                       allow_small_or_imprecise_dtypes=True)
                g.iota(posr[:], [[1, T]], channel_multiplier=0,
                       allow_small_or_imprecise_dtypes=True)
        with nc.Block() as blk:
            @blk.scalar
            def _(e):
                e.activation(invf[:], invf[:], AF.Exp,
                             scale=-math.log(10000.0) / 32.0)
        sm = kb.sem("angmm")
        with nc.Block() as blk:
            @blk.tensor
            def _(e):
                for q in range(T // 512):
                    if q >= 1:
                        e.wait_ge(sm.h, 2 * q)
                    e.matmul(aps[:], invf[:], posr[:, q * 512:(q + 1) * 512],
                             start=True, stop=True).then_inc(sm.h, 1)

            @blk.vector
            def _(e):
                for q in range(T // 512):
                    e.wait_ge(sm.h, 2 * q + 1)
                    e.tensor_copy(ang[:, q * 512:(q + 1) * 512],
                                  aps[:]).then_inc(sm.h, 1)
        sm.v += 2 * (T // 512)
        w32 = sc.sb([32, T], F32, "w32")
        with nc.Block() as blk:
            @blk.vector
            def _(e):
                i1 = e.tensor_scalar(carg[:], ang[:], INV2PI, None, ALU.mult)
                kb.edge("v", i1, e)
                i1 = e.tensor_copy(iscr[:], carg[:])
                kb.edge("v", i1, e)
                i1 = e.tensor_copy(carg[:], iscr[:])
                kb.edge("v", i1, e)
                i1 = e.scalar_tensor_tensor(w32[:], carg[:], -2.0 * PI, ang[:],
                                            ALU.mult, ALU.add)
                kb.edge("v", i1, e)
                i1 = e.tensor_scalar(c32[:], w32[:], PI, None, ALU.is_gt)
                kb.edge("v", i1, e)
                i1 = e.scalar_tensor_tensor(w32[:], c32[:], -2.0 * PI, w32[:],
                                            ALU.mult, ALU.add)
                kb.edge("v", i1, e)
                i1 = e.tensor_scalar(c32[:], w32[:], -PI, None, ALU.is_lt)
                kb.edge("v", i1, e)
                i1 = e.scalar_tensor_tensor(w32[:], c32[:], 2.0 * PI, w32[:],
                                            ALU.mult, ALU.add)
                kb.edge("v", i1, e)
                i1 = e.tensor_scalar(carg[:], w32[:], PI / 2, None, ALU.add)
                kb.edge("v", i1, e)
                i1 = e.tensor_scalar(c32[:], carg[:], PI, None, ALU.is_gt)
                kb.edge("v", i1, e)
                e.scalar_tensor_tensor(carg[:], c32[:], -2.0 * PI, carg[:],
                                       ALU.mult, ALU.add)
        with nc.Block() as blk:
            @blk.scalar
            def _(e):
                i1 = e.activation(s32[:], w32[:], AF.Sin)
                kb.edge("s", i1, e)
                e.activation(c32[:], carg[:], AF.Sin)
        rp = kb.sem("rep")
        with nc.Block() as blk:
            @blk.gpsimd
            def _(e):
                n = rp.v
                for r in range(4):
                    e.dma_start(out=c4[r * 32:(r + 1) * 32, :],
                                in_=c32[:]).then_inc(rp.h, 16)
                    e.dma_start(out=s4[r * 32:(r + 1) * 32, :],
                                in_=s32[:]).then_inc(rp.h, 16)
                e.wait_ge(rp.h, n + 128)
        rp.v += 128

    if stage == "trig":
        dsem = kb.sem()
        with nc.Block() as blk:
            @blk.gpsimd
            def _(e):
                e.dma_start(out=dbg[:, 0, :], in_=c4[:]).then_inc(dsem.h, 16)
                e.dma_start(out=dbg[:, 1, :], in_=s4[:]).then_inc(dsem.h, 16)
                e.wait_ge(dsem.h, 32)
        return

    lnT = kb.sb([128, KC, T], BF16, "lnT")
    with kb.scope() as sc:
        lnw = sc.sb([128, NT, D], BF16, "lnw")
        emit_ln(kb, x0, lambda t: lnw[:, t, :])
        if stage == "ln0":
            dsem = kb.sem()
            with nc.Block() as blk:
                @blk.gpsimd
                def _(e):
                    for t in range(8):
                        e.dma_start(out=dbg[:, t, 0:D],
                                    in_=lnw[:, t, :]).then_inc(dsem.h, 16)
                    e.wait_ge(dsem.h, 128)
            return
        pairs = [(lnw[:, t, k * 128:(k + 1) * 128],
                  lnT[:, k, t * 128:(t + 1) * 128])
                 for t in range(NT) for k in range(KC)]
        import os
        ntp = int(os.environ.get("NTP", "0"))
        if ntp:
            pairs = pairs[:ntp]
        emit_transposes(kb, pairs, c["ident_bf"])

    def dump(srcs):
        dsem = kb.sem()
        with nc.Block() as blk:
            @blk.gpsimd
            def _(e):
                n = 0
                for sap, dap in srcs:
                    e.dma_start(out=dap, in_=sap).then_inc(dsem.h, 16)
                    n += 16
                e.wait_ge(dsem.h, dsem.v + n)
        dsem.v += 16 * len(srcs)

    if stage == "ln":
        import os
        nd = int(os.environ.get("DUMPN", "8"))
        if nd:
            dump([(lnT[:, k, :], dbg[:, k, :]) for k in range(nd)])
        return

    s_wg = kb.sem("s_wg")
    s_mm = kb.sem("s_mm")
    s_ev = kb.sem("s_ev")
    s_tp = kb.sem("s_tp")
    s_cp = kb.sem("s_cp")
    f_mm = kb.sem("f_mm")
    f_ex = kb.sem("f_ex")
    f_nr = kb.sem("f_nr")
    f_st = kb.sem("f_st")
    NKT = 32
    for g in range(4):
        with kb.scope() as sc:
            wqg = sc.sb([128, KC, 2, 128], BF16, f"wqg{g}")
            wkg = sc.sb([128, KC, 2, 128], BF16, f"wkg{g}")
            wvg = sc.sb([128, KC, 256], BF16, f"wvg{g}")
            qg = sc.sb([128, 2, T], BF16, f"qg{g}")
            kg = sc.sb([128, 2, T], BF16, f"kg{g}")
            v2 = sc.sb([128, NKT, 260], BF16, f"v2g{g}")
            ctxs = sc.sb([128, 2, T], BF16, f"ctxs{g}")
            stg = sc.sb([128, 2, 512], F32, f"stg{g}")
            rs1 = sc.sb([128, 512], F32, f"rs1{g}")
            rs2 = sc.sb([128, 512], F32, f"rs2{g}")
            vstg = sc.sb([128, 2, 512], BF16, f"vstg{g}")
            pT = sc.sb([128, 2, 4, 512], BF16, f"pT{g}")
            rec = sc.sb([1, 4, 512], F32, f"rec{g}")
            ctm = sc.sb([64, 4, 512], F32, f"ctm{g}")
            scp = kb.scope()
            scp.__enter__()
            mmps = scp.ps([128, 2, 512], F32, name=f"mmps{g}")
            tps = scp.ps([128, 2, 512], F32, name=f"tps{g}")

            wg0 = s_wg.v
            with nc.Block() as blk:
                @blk.gpsimd
                def _(e, g=g, wg0=wg0):
                    for dst, src in ((wqg, win["wq"]), (wkg, win["wk"])):
                        ap = bass.AP(src.tensor, src.offset + g * 128,
                                     [src.ap[0], src.ap[1], [512, 2], [1, 128]])
                        e.dma_start(out=dst[:], in_=ap).then_inc(s_wg.h, 16)
                    ap = bass.AP(win["wv"].tensor,
                                 win["wv"].offset + 2 * g * 128,
                                 [win["wv"].ap[0], win["wv"].ap[1], [1, 256]])
                    e.dma_start(out=wvg[:], in_=ap).then_inc(s_wg.h, 16)
                    e.wait_ge(s_wg.h, wg0 + 48)

                @blk.vector
                def _(e):
                    for pr in range(2):
                        for off in (64, 129):
                            e.memset(v2[:, :, pr * 130 + off], 1.0)
            s_wg.v += 48

            import os
            _pn = int(os.environ.get("PROJN", "8"))
            _pk = os.environ.get("PROJK", "qkv")
            for qc in range(_pn):
                qs = slice(qc * 512, (qc + 1) * 512)
                for (kind, wsl, bname, outT) in (("q", wqg, "bq", qg),
                                                 ("k", wkg, "bk", kg),
                                                 ("v", wvg, "bv", None)):
                    if kind not in _pk:
                        continue
                    mm0, ev0, tp0, cp0 = s_mm.v, s_ev.v, s_tp.v, s_cp.v
                    with nc.Block() as blk:
                        @blk.tensor
                        def _(e, g=g, qs=qs, kind=kind, wsl=wsl,
                              mm0=mm0, ev0=ev0, cp0=cp0):
                            for cl in range(2):
                                for k in range(KC):
                                    src = (wsl[:, k, cl, :] if kind != "v"
                                           else wsl[:, k, cl * 128:(cl + 1) * 128])
                                    mi = e.matmul(mmps[:, cl, :], src,
                                                  lnT[:, k, qs],
                                                  start=(k == 0),
                                                  stop=(k == KC - 1))
                                mi.then_inc(s_mm.h, 1)
                            if kind == "v" and os.environ.get("PROJV", "full") != "mm":
                                i = 0
                                for cl in range(2):
                                    e.wait_ge(s_ev.h, ev0 + cl + 1)
                                    for t4 in range(4):
                                        if i >= 2:
                                            e.wait_ge(s_cp.h, cp0 + i - 1)
                                        e.matmul(
                                            tps[:, i % 2, 0:128],
                                            vstg[:, cl, t4 * 128:(t4 + 1) * 128],
                                            c["ident_bf"][:], start=True,
                                            stop=True).then_inc(s_tp.h, 1)
                                        i += 1

                        @blk.vector
                        def _(e, g=g, qc=qc, qs=qs, kind=kind, bname=bname,
                              outT=outT, mm0=mm0, tp0=tp0):
                            for cl in range(2):
                                e.wait_ge(s_mm.h, mm0 + cl + 1)
                                if kind == "v":
                                    bcol = 2 * g + cl
                                    e.tensor_scalar(vstg[:, cl, :],
                                                    mmps[:, cl, :],
                                                    bsb["bv"][:, bcol:bcol + 1],
                                                    None, ALU.add
                                                    ).then_inc(s_ev.h, 1)
                                else:
                                    bcol = g + 4 * cl
                                    e.tensor_scalar(stg[:, cl, :],
                                                    mmps[:, cl, :],
                                                    bsb[bname][:, bcol:bcol + 1],
                                                    None, ALU.add
                                                    ).then_inc(s_ev.h, 1)
                            if kind != "v":
                                x1a, x2a = stg[:, 0, :], stg[:, 1, :]
                                cb, sb_ = c4[:, qs], s4[:, qs]
                                kb.edge("v", e.tensor_scalar(
                                    rs1[:], x1a, 1.0, None, ALU.mult), e)
                                e.tensor_tensor(rs1[:], x1a, cb, ALU.mult)
                                i1 = e.tensor_tensor(rs2[:], x2a, sb_, ALU.mult)
                                kb.edge("v", i1, e)
                                i1 = e.tensor_tensor(outT[:, 0, qs], rs1[:],
                                                     rs2[:], ALU.subtract)
                                kb.edge("v", i1, e)
                                e.tensor_tensor(rs1[:], x1a, sb_, ALU.mult)
                                i1 = e.tensor_tensor(rs2[:], x2a, cb, ALU.mult)
                                kb.edge("v", i1, e)
                                e.tensor_tensor(outT[:, 1, qs], rs1[:], rs2[:],
                                                ALU.add)
                            elif os.environ.get("PROJV", "full") == "full":
                                kt0 = qc * 4
                                i = 0
                                for cl in range(2):
                                    for t4 in range(4):
                                        e.wait_ge(s_tp.h, tp0 + i + 1)
                                        kt = kt0 + t4
                                        e.tensor_copy(
                                            v2[:, kt, cl * 130:cl * 130 + 64],
                                            tps[:, i % 2, 0:64])
                                        e.tensor_copy(
                                            v2[:, kt, cl * 130 + 65:cl * 130 + 129],
                                            tps[:, i % 2, 64:128]
                                            ).then_inc(s_cp.h, 1)
                                        i += 1
                    s_mm.v += 2
                    s_ev.v += 2
                    if kind == "v":
                        pv_ = os.environ.get("PROJV", "full")
                        if pv_ != "mm":
                            s_tp.v += 8
                        if pv_ == "full":
                            s_cp.v += 8

            if stage == "proj" and g == 0:
                dump([(qg[:, cl, :], dbg[:, cl, :]) for cl in range(2)]
                     + [(kg[:, cl, :], dbg[:, 2 + cl, :]) for cl in range(2)]
                     + [(v2[:, kt, :], dbg[:, 4 + kt, 0:260]) for kt in range(4)])
                return

            scp.__exit__(None, None, None)
            sc_ps = sc.ps([128, 4, 512], F32, name=f"scps{g}")
            pv_ps = sc.ps([65, 4, 512], F32, name=f"pvps{g}")
            for qc in range(8):
                qs = slice(qc * 512, (qc + 1) * 512)
                mm0, ex0, nr0 = f_mm.v, f_ex.v, f_nr.v
                with nc.Block() as blk:
                    @blk.tensor
                    def _(e, g=g, qs=qs, mm0=mm0, ex0=ex0, nr0=nr0):
                        def scores(st_, kt):
                            ks = slice(kt * 128, (kt + 1) * 128)
                            for hh in range(2):
                                h4 = 2 * st_ + hh
                                r = slice(32 * h4, 32 * h4 + 32)
                                e.matmul(sc_ps[:, h4, :], kg[r, 0, ks],
                                         qg[r, 0, qs], start=True, stop=False,
                                         tile_position=(32 * h4, 0))
                                mi = e.matmul(sc_ps[:, h4, :], kg[r, 1, ks],
                                              qg[r, 1, qs], start=False,
                                              stop=True,
                                              tile_position=(32 * h4, 0))
                            mi.then_inc(f_mm.h, 1)

                        def pv(st_, kt):
                            for hh in range(2):
                                h4 = 2 * st_ + hh
                                blkc = (h4 // 2) * 130 + (h4 % 2) * 65
                                mi = e.matmul(pv_ps[:, h4, :],
                                              v2[:, kt, blkc:blkc + 65],
                                              pT[:, kt % 2, h4, :],
                                              start=(kt == 0),
                                              stop=(kt == NKT - 1))
                            mi.then_inc(f_mm.h, 1)

                        scores(0, 0)
                        scores(1, 0)
                        for kt in range(1, NKT):
                            e.wait_ge(f_ex.h, ex0 + 2 * kt - 1)
                            pv(0, kt - 1)
                            scores(0, kt)
                            e.wait_ge(f_ex.h, ex0 + 2 * kt)
                            pv(1, kt - 1)
                            scores(1, kt)
                        e.wait_ge(f_ex.h, ex0 + 2 * NKT - 1)
                        pv(0, NKT - 1)
                        e.wait_ge(f_ex.h, ex0 + 2 * NKT)
                        pv(1, NKT - 1)
                        e.wait_ge(f_nr.h, nr0 + 1)
                        for h4 in range(4):
                            mi = e.matmul(sc_ps[0:64, h4, :],
                                          c["ones_row_f"][:, 0:64],
                                          rec[:, h4, :], start=True, stop=True)
                        mi.then_inc(f_mm.h, 1)

                    @blk.scalar
                    def _(e, mm0=mm0):
                        def mmcnt_scores(st_, kt):
                            if kt == 0:
                                return mm0 + 1 + st_
                            return mm0 + 2 + 4 * (kt - 1) + 2 + 2 * st_
                        for kt in range(NKT):
                            for st_ in range(2):
                                e.wait_ge(f_mm.h, mmcnt_scores(st_, kt))
                                e.activation(pT[:, kt % 2, 2 * st_:2 * st_ + 2, :],
                                             sc_ps[:, 2 * st_:2 * st_ + 2, :],
                                             AF.Exp, scale=0.125
                                             ).then_inc(f_ex.h, 1)

                    @blk.vector
                    def _(e, g=g, qs=qs, mm0=mm0):
                        end = mm0 + 2 + 4 * (NKT - 1) + 2
                        e.wait_ge(f_mm.h, end)
                        e.reciprocal(rec[:], pv_ps[64:65, :, :])
                        e.tensor_copy(ctm[:], pv_ps[0:64, :, :]).then_inc(f_nr.h, 1)
                        e.wait_ge(f_nr.h, nr0 + 1)
                        e.wait_ge(f_mm.h, end + 1)
                        for h4 in range(4):
                            cl = h4 // 2
                            rowb = 64 * (h4 % 2)
                            e.tensor_tensor(ctxs[rowb:rowb + 64, cl, qs],
                                            ctm[:, h4, :], sc_ps[0:64, h4, :],
                                            ALU.mult)
                f_mm.v += 2 + 4 * (NKT - 1) + 2 + 1
                f_ex.v += 2 * NKT
                f_nr.v += 1

            if stage == "flash" and g == 0:
                dump([(ctxs[:, cl, :], dbg[:, cl, :]) for cl in range(2)])
                return

            st0 = f_st.v
            with nc.Block() as blk:
                @blk.sync
                def _(e, g=g, st0=st0):
                    for cl in range(2):
                        e.dma_start(out=ctxd.ap()[2 * g + cl, :, :],
                                    in_=ctxs[:, cl, :]).then_inc(f_st.h, 16)
                    e.wait_ge(f_st.h, st0 + 32)
            f_st.v += 32

    with kb.scope() as sc:
        wosb = sc.sb([128, KC, D], BF16, "wo_sb")
        cload = sc.sb([128, 2, KC, 512], BF16, "cload")
        x0t = sc.sb([128, 4, D], F32, "x0t")
        x1t = sc.sb([128, 4, D], F32, "x1t")
        ostg = sc.sb([128, 512], F32, "ostg")
        mmps = sc.ps([128, 2, 512], F32, name="mmps_o")
        tpf = sc.ps([128, 2, 512], F32, name="tpf_o")
        wl0 = wl.v
        with nc.Block() as blk:
            @blk.gpsimd
            def _(e):
                e.dma_start(out=wosb[:], in_=win["wo"]).then_inc(wl.h, 16)
                e.wait_ge(wl.h, wl0 + 16)
        wl.v += 16
        o_ld = kb.sem("o_ld")
        o_cl = kb.sem("o_cl")
        o_mm = kb.sem("o_mm")
        o_ev = kb.sem("o_ev")
        o_tp = kb.sem("o_tp")
        o_ad = kb.sem("o_ad")
        o_st = kb.sem("o_st")
        for qc in range(8):
            qs = slice(qc * 512, (qc + 1) * 512)
            ld0, cl0, mm0, ev0, tp0, ad0, st0 = (o_ld.v, o_cl.v, o_mm.v,
                                                 o_ev.v, o_tp.v, o_ad.v, o_st.v)
            with nc.Block() as blk:
                @blk.sync
                def _(e, qc=qc, ld0=ld0, cl0=cl0, ad0=ad0, st0=st0):
                    ap = bass.AP(ctxd.ap().tensor, qc * 512,
                                 [[T, 128], [128 * T, KC], [1, 512]])
                    e.dma_start(out=cload[:, qc % 2, :, :],
                                in_=ap).then_inc(o_cl.h, 16)
                    for t4 in range(4):
                        t = qc * 4 + t4
                        e.dma_start(out=x0t[:, t4, :],
                                    in_=x0[t * 128:(t + 1) * 128, :]
                                    ).then_inc(o_ld.h, 16)
                    e.wait_ge(o_ad.h, ad0 + 32)
                    for t4 in range(4):
                        t = qc * 4 + t4
                        e.dma_start(out=x1[t * 128:(t + 1) * 128, :],
                                    in_=x1t[:, t4, :]).then_inc(o_st.h, 16)
                    e.wait_ge(o_st.h, st0 + 64)

                @blk.tensor
                def _(e, qc=qc, cl0=cl0, mm0=mm0, ev0=ev0, ad0=ad0):
                    e.wait_ge(o_cl.h, cl0 + 16)
                    for m in range(KC):
                        if m >= 2:
                            e.wait_ge(o_ev.h, ev0 + m - 1)
                        for k in range(KC):
                            mi = e.matmul(mmps[:, m % 2, :],
                                          wosb[:, k, m * 128:(m + 1) * 128],
                                          cload[:, qc % 2, k, :],
                                          start=(k == 0), stop=(k == KC - 1))
                        mi.then_inc(o_mm.h, 1)
                        e.wait_ge(o_ev.h, ev0 + m + 1)
                        for t4 in range(4):
                            i = 4 * m + t4
                            if i >= 2:
                                e.wait_ge(o_ad.h, ad0 + i - 1)
                            e.matmul(tpf[:, i % 2, 0:128],
                                     ostg[:, t4 * 128:(t4 + 1) * 128],
                                     c["ident_f"][:], start=True,
                                     stop=True).then_inc(o_tp.h, 1)

                @blk.vector
                def _(e, qc=qc, mm0=mm0, tp0=tp0, ld0=ld0):
                    for m in range(KC):
                        e.wait_ge(o_mm.h, mm0 + m + 1)
                        e.tensor_scalar(ostg[:], mmps[:, m % 2, :],
                                        bsb["bo"][:, m:m + 1], None,
                                        ALU.add).then_inc(o_ev.h, 1)
                        for t4 in range(4):
                            i = 4 * m + t4
                            e.wait_ge(o_tp.h, tp0 + i + 1)
                            if m == 0 and t4 == 0:
                                e.wait_ge(o_ld.h, ld0 + 64)
                            e.tensor_tensor(x1t[:, t4, m * 128:(m + 1) * 128],
                                            tpf[:, i % 2, 0:128],
                                            x0t[:, t4, m * 128:(m + 1) * 128],
                                            ALU.add).then_inc(o_ad.h, 1)
            o_ld.v += 64
            o_cl.v += 16
            o_mm.v += KC
            o_ev.v += KC
            o_tp.v += 32
            o_ad.v += 32
            o_st.v += 64


def build_l0_attn(nc):
    kb = KB(nc)
    x0 = nc.dram_tensor("x0", [T, D], F32, kind="ExternalInput").ap()
    win = {}
    for nm in ("wq", "wk", "wv", "wo"):
        win[nm] = nc.dram_tensor(nm, [128, KC, D], F32, kind="ExternalInput").ap()
    for nm in ("bq", "bk", "bv", "bo"):
        win[nm] = nc.dram_tensor(nm, [128, KC], F32, kind="ExternalInput").ap()
    x1 = nc.dram_tensor("x1", [T, D], F32, kind="ExternalOutput").ap()
    emit_l0_attn(kb, x0, win, x1)
    kb.ctx.close()
    return nc


def _dm_w(w):
    dout = w.shape[1]
    return np.ascontiguousarray(w.reshape(KC, 128, dout).transpose(1, 0, 2))


def _dm_b(b):
    n = b.shape[0] // 128
    return np.ascontiguousarray(b.reshape(n, 128).T)


def _qk_perm():
    perm = np.zeros(D, np.int64)
    for h in range(16):
        for half in range(2):
            for f in range(32):
                nat = h * 64 + half * 32 + f
                pos = (h // 4 + 4 * half) * 128 + (h % 4) * 32 + f
                perm[pos] = nat
    return perm


_QKPERM = _qk_perm()


def prep_l0a(ins, core):
    return {
        "x0": np.ascontiguousarray(ins["L0_attn_x"][core]),
        "wq": _dm_w(ins["L0_attn_Wq"][:, _QKPERM]),
        "wk": _dm_w(ins["L0_attn_Wk"][:, _QKPERM]),
        "wv": _dm_w(ins["L0_attn_Wv"]),
        "wo": _dm_w(ins["L0_attn_Wo"]),
        "bq": _dm_b(ins["L0_attn_bq"][_QKPERM]),
        "bk": _dm_b(ins["L0_attn_bk"][_QKPERM]),
        "bv": _dm_b(ins["L0_attn_bv"]),
        "bo": _dm_b(ins["L0_attn_bo"]),
    }




def emit_l0_ffn(kb, x1, win, x2, stage="full", dbg=None):
    nc = kb.nc
    E = 8
    KCAP = 1024
    c = build_consts(kb, ["ident_bf", "ident_f", "ones_row_f", "uexc_f",
                          "allones_f", "pos_tm", "indic_t", "indic8"])
    compactx = [nc.dram_tensor(kb.name(f"cpx{e}"), [T, 1032], BF16)
                for e in range(E)]
    contrib = [nc.dram_tensor(kb.name(f"ctb{e}"), [KCAP + 1, D], F32)
               for e in range(E)]
    th_dram = nc.dram_tensor(kb.name("thd"), [128, 1], F32)

    gate_sb = kb.sb([128, KC, 8], BF16, "gate_sb")
    zrow = kb.sb([1, D], F32, "zrow")
    wl = kb.sem("fwl")
    with nc.Block() as blk:
        @blk.vector
        def _(e):
            e.memset(zrow[:], 0.0)

        @blk.gpsimd
        def _(g):
            g.dma_start(out=gate_sb[:], in_=win["gate"]).then_inc(wl.h, 16)
            for e_ in range(E):
                g.dma_start(out=contrib[e_].ap()[KCAP:KCAP + 1, :],
                            in_=zrow[:]).then_inc(wl.h, 16)
            g.wait_ge(wl.h, wl.v + 16 * 9)
    wl.v += 16 * 9

    dest_u = kb.sb([128, 8, NT], U32, "dest_u")
    gidx_u = kb.sb([128, 8, NT], U32, "gidx_u")
    scw = kb.scope()
    scw.__enter__()
    wide = scw.sb([128, NT, 1040], BF16, "wide")
    aff = scw.sb([128, NT, 8], F32, "aff")
    with kb.scope() as sc:
        emit_ln(kb, x1, lambda t: wide[:, t, 0:D])
        ln1T = sc.sb([128, KC, T], BF16, "ln1T")
        pairs = [(wide[:, t, k * 128:(k + 1) * 128],
                  ln1T[:, k, t * 128:(t + 1) * 128])
                 for t in range(NT) for k in range(KC)]
        emit_transposes(kb, pairs, c["ident_bf"])

        gps = sc.ps([128, 2, 512], F32, name="gps")
        gstat = sc.sb([128, 2, 8], F32, "gstat")
        gexp = sc.sb([128, 2, 8], F32, "gexp")
        g_mm = kb.sem("g_mm")
        g_ev = kb.sem("g_ev")
        g_ac = kb.sem("g_ac")
        for t0 in range(0, NT, 8):
            mm0, ev0, ac0 = g_mm.v, g_ev.v, g_ac.v
            with nc.Block() as blk:
                @blk.tensor
                def _(e, t0=t0, mm0=mm0, ac0=ac0):
                    for i in range(8):
                        t = t0 + i
                        if i >= 2:
                            e.wait_ge(g_ac.h, ac0 + i - 1)
                        for k in range(KC):
                            mi = e.matmul(gps[:, i % 2, 0:8],
                                          ln1T[:, k, t * 128:(t + 1) * 128],
                                          gate_sb[:, k, :],
                                          start=(k == 0), stop=(k == KC - 1))
                        mi.then_inc(g_mm.h, 1)

                @blk.vector
                def _(e, t0=t0, mm0=mm0, ac0=ac0):
                    for i in range(8):
                        t = t0 + i
                        sl = i % 2
                        e.wait_ge(g_mm.h, mm0 + i + 1)
                        i1 = e.reduce_max(gstat[:, sl, 0:1], gps[:, sl, 0:8],
                                          axis=AXX)
                        kb.edge("v", i1, e)
                        i1 = e.tensor_scalar(gstat[:, sl, 1:2], gstat[:, sl, 0:1],
                                             -1.0, None, ALU.mult)
                        i1.then_inc(g_ev.h, 1)
                        e.wait_ge(g_ac.h, ac0 + i + 1)
                        i1 = e.reciprocal(gstat[:, sl, 3:4], gstat[:, sl, 2:3])
                        kb.edge("v", i1, e)
                        i1 = e.tensor_scalar(aff[:, t, :], gexp[:, sl, :],
                                             gstat[:, sl, 3:4], None, ALU.mult)
                        kb.edge("v", i1, e)
                        e.tensor_copy(wide[:, t, 1024:1032],
                                      aff[:, t, :]).then_inc(g_ev.h, 1)

                @blk.scalar
                def _(e, t0=t0, ev0=ev0):
                    for i in range(8):
                        sl = i % 2
                        e.wait_ge(g_ev.h, ev0 + 2 * i + 1)
                        e.activation(gexp[:, sl, :], gps[:, sl, 0:8], AF.Exp,
                                     bias=gstat[:, sl, 1:2],
                                     accum_out=gstat[:, sl, 2:3]
                                     ).then_inc(g_ac.h, 1)
            g_mm.v += 8
            g_ev.v += 16
            g_ac.v += 8

    if stage == "gate":
        dsem = kb.sem()
        with nc.Block() as blk:
            @blk.gpsimd
            def _(e):
                for t in range(8):
                    e.dma_start(out=dbg[:, t, 0:8],
                                in_=aff[:, t, :]).then_inc(dsem.h, 16)
                e.wait_ge(dsem.h, dsem.v + 128)
        dsem.v += 128
        return

    with kb.scope() as sc:
        affT = sc.sb([8, T], F32, "affT")
        affrep = sc.sb([128, 256], F32, "affrep")
        mskb = sc.sb([128, 256], F32, "mskb")
        bis = sc.sb([128, 8], F32, "bis")
        bcond = sc.sb([128, 1], U32, "bcond")
        cnt8 = sc.sb([8, 1], F32, "cnt8")
        tps = sc.ps([128, 2, 512], F32, name="tps_f3")
        pairs2 = []
        for t in range(NT):
            pairs2.append((aff[:, t, :], None))
        b_tp = kb.sem("b_tp")
        b_cp = kb.sem("b_cp")
        tp0, cp0 = b_tp.v, b_cp.v
        with nc.Block() as blk:
            @blk.tensor
            def _(e, tp0=tp0, cp0=cp0):
                for t in range(NT):
                    if t >= 2:
                        e.wait_ge(b_cp.h, cp0 + t - 1)
                    e.matmul(tps[0:8, t % 2, 0:128], aff[:, t, :],
                             c["ident_f"][:], start=True,
                             stop=True).then_inc(b_tp.h, 1)

            @blk.vector
            def _(e, tp0=tp0, cp0=cp0):
                for t in range(NT):
                    e.wait_ge(b_tp.h, tp0 + t + 1)
                    e.tensor_copy(affT[:, t * 128:(t + 1) * 128],
                                  tps[0:8, t % 2, 0:128]).then_inc(b_cp.h, 1)
        b_tp.v += NT
        b_cp.v += NT
        rp = kb.sem("b_rp")
        with nc.Block() as blk:
            @blk.sync
            def _(e):
                n0 = rp.v
                for s in range(16):
                    dst = affrep[s:s + 113:16, :]
                    e.dma_start(out=dst,
                                in_=affT[:, s * 256:(s + 1) * 256]
                                ).then_inc(rp.h, 16)
                e.wait_ge(rp.h, n0 + 256)
        rp.v += 256
        with nc.Block() as blk:
            @blk.vector
            def _(e):
                e.memset(bis[:, 0:1], 0.0)
                e.memset(bis[:, 1:2], 1.0)
                e.memset(bis[:, 2:3], 0.5)
        b_dv = kb.sem("b_dv")
        b_mm = kb.sem("b_mm")
        b_ac = kb.sem("b_ac")
        with nc.Block() as blk:
            @blk.vector
            def _(e):
                dv0, mm0 = b_dv.v, b_mm.v
                for it in range(30):
                    i1 = e.tensor_scalar(mskb[:], affrep[:], bis[:, 2:3], None,
                                         ALU.is_ge)
                    kb.edge("v", i1, e)
                    i1 = e.reduce_sum(bis[:, 4:5], mskb[:], axis=AXX)
                    kb.edge("v", i1, e)
                    e.tensor_copy(bis[:, 5:6], bis[:, 4:5]).then_inc(b_dv.h, 1)
                    e.wait_ge(b_mm.h, mm0 + 2 * (it + 1))
                    i1 = e.tensor_scalar(bcond[:], tps[:, 1, 0:1], 1023.5,
                                         None, ALU.is_gt)
                    kb.edge("v", i1, e)
                    i1 = e.select(bis[:, 0:1], bcond[:], bis[:, 2:3],
                                  bis[:, 0:1])
                    kb.edge("v", i1, e)
                    i1 = e.select(bis[:, 1:2], bcond[:], bis[:, 1:2],
                                  bis[:, 2:3])
                    kb.edge("v", i1, e)
                    i1 = e.tensor_tensor(bis[:, 2:3], bis[:, 0:1], bis[:, 1:2],
                                         ALU.add)
                    kb.edge("v", i1, e)
                    i1 = e.tensor_scalar(bis[:, 2:3], bis[:, 2:3], 0.5, None,
                                         ALU.mult)
                    kb.edge("v", i1, e)
                b_dv.v += 30

            @blk.tensor
            def _(e):
                dv0, mm0, ac0 = b_dv.v - 30, b_mm.v, b_ac.v
                for it in range(30):
                    e.wait_ge(b_dv.h, dv0 + it + 1)
                    e.matmul(tps[0:8, 0, 0:1], c["indic_t"][:], bis[:, 5:6],
                             start=True, stop=True).then_inc(b_mm.h, 1)
                    e.wait_ge(b_ac.h, ac0 + it + 1)
                    e.matmul(tps[:, 1, 0:1], c["indic8"][:], cnt8[:],
                             start=True, stop=True).then_inc(b_mm.h, 1)
                b_mm.v += 60

            @blk.scalar
            def _(e):
                mm0, ac0 = b_mm.v - 60, b_ac.v
                for it in range(30):
                    e.wait_ge(b_mm.h, mm0 + 2 * it + 1)
                    e.copy(cnt8[:], tps[0:8, 0, 0:1]).then_inc(b_ac.h, 1)
                b_ac.v += 30
        th_tm = sc.sb([128, 8], F32, "th_tm")
        tl = kb.sem("b_tl")
        with nc.Block() as blk:
            @blk.sync
            def _(e):
                n0 = tl.v
                e.dma_start(out=th_dram.ap(), in_=bis[:, 0:1]).then_inc(tl.h, 16)
                e.wait_ge(tl.h, n0 + 16)
                src = bass.AP(th_dram.ap().tensor, 0, [[0, 128], [16, 8]])
                e.dma_start(out=th_tm[:], in_=src,
                            allow_slow_non_contiguous=True).then_inc(tl.h, 16)
                e.wait_ge(tl.h, n0 + 32)
        tl.v += 32

        dscr = sc.sb([128, 4, NT], F32, "dscr")
        msku = sc.sb([128, NT], U32, "msku")
        dcol = sc.sb([128, 8], F32, "dcol")
        d_dv = kb.sem("d_dv")
        d_mm = kb.sem("d_mm")
        with nc.Block() as blk:
            @blk.vector
            def _(e):
                dv0, mm0 = d_dv.v, d_mm.v
                for ei in range(E):
                    aslice = aff[:, :, ei]
                    i1 = e.tensor_scalar(dscr[:, 0, :], aslice,
                                         th_tm[:, ei:ei + 1], None, ALU.is_ge)
                    kb.edge("v", i1, e)
                    i1 = e.tensor_copy(msku[:], dscr[:, 0, :])
                    kb.edge("v", i1, e)
                    i1 = e.tensor_tensor_scan(dscr[:, 1, :], dscr[:, 0, :],
                                              dscr[:, 0, :], 0.0,
                                              ALU.add, ALU.bypass)
                    kb.edge("v", i1, e)
                    e.tensor_copy(dcol[:, 3:4],
                                  dscr[:, 1, NT - 1:NT]).then_inc(d_dv.h, 1)
                    e.wait_ge(d_mm.h, mm0 + 2 * (ei + 1))
                    i1 = e.tensor_copy(dcol[:, 0:1], tps[:, 0, 0:1])
                    kb.edge("v", i1, e)
                    i1 = e.tensor_copy(dcol[:, 2:3], tps[:, 1, 0:1])
                    kb.edge("v", i1, e)
                    i1 = e.tensor_scalar(dcol[:, 1:2], dcol[:, 0:1], -1.0,
                                         None, ALU.mult)
                    kb.edge("v", i1, e)
                    i1 = e.tensor_scalar(dscr[:, 2, :], dscr[:, 1, :],
                                         dcol[:, 0:1], -1.0, ALU.add, ALU.add)
                    kb.edge("v", i1, e)
                    i1 = e.tensor_tensor(dscr[:, 3, :], c["pos_tm"][:],
                                         dscr[:, 1, :], ALU.subtract)
                    kb.edge("v", i1, e)
                    i1 = e.tensor_scalar(dscr[:, 3, :], dscr[:, 3, :],
                                         dcol[:, 1:2], dcol[:, 2:3],
                                         ALU.add, ALU.add)
                    kb.edge("v", i1, e)
                    i1 = e.select(dscr[:, 1, :], msku[:], dscr[:, 2, :],
                                  dscr[:, 3, :])
                    kb.edge("v", i1, e)
                    i1 = e.tensor_copy(dest_u[:, ei, :], dscr[:, 1, :])
                    kb.edge("v", i1, e)
                    i1 = e.tensor_scalar(dscr[:, 2, :], dscr[:, 1, :], 1024.0,
                                         None, ALU.min)
                    kb.edge("v", i1, e)
                    e.tensor_copy(gidx_u[:, ei, :], dscr[:, 2, :])
                d_dv.v += E

            @blk.tensor
            def _(e):
                dv0, mm0 = d_dv.v - E, d_mm.v
                for ei in range(E):
                    e.wait_ge(d_dv.h, dv0 + ei + 1)
                    e.matmul(tps[:, 0, 0:1], c["uexc_f"][:], dcol[:, 3:4],
                             start=True, stop=True).then_inc(d_mm.h, 1)
                    e.matmul(tps[:, 1, 0:1], c["allones_f"][:], dcol[:, 3:4],
                             start=True, stop=True).then_inc(d_mm.h, 1)
                d_mm.v += 2 * E

    if stage == "topk":
        dsem = kb.sem()
        with nc.Block() as blk:
            @blk.gpsimd
            def _(e):
                n0 = dsem.v
                e.dma_start(out=dbg[:, 0, 0:8], in_=th_tm[:]).then_inc(dsem.h, 16)
                for ei in range(E):
                    e.dma_start(out=dbg[:, 1 + ei, 0:NT],
                                in_=dest_u[:, ei, :]).then_inc(dsem.h, 16)
                e.wait_ge(dsem.h, n0 + 16 * 9)
        dsem.v += 16 * 9
        return

    sc_sem = kb.sem("sc_sem")
    with nc.Block() as blk:
        @blk.gpsimd
        def _(e):
            n0 = sc_sem.v
            n = 0
            for mt in range(NT):
                for ei in range(E):
                    e.indirect_dma_start(
                        out=compactx[ei].ap(),
                        out_offset=IndirectOffsetOnAxis(dest_u[:, ei, mt:mt + 1], 0),
                        in_=wide[:, mt, 0:1032], in_offset=None,
                    ).then_inc(sc_sem.h, 16)
                    n += 16
            e.wait_ge(sc_sem.h, n0 + n)
    sc_sem.v += 16 * NT * E
    scw.__exit__(None, None, None)

    if stage == "scatter":
        dsem = kb.sem()
        with nc.Block() as blk:
            @blk.gpsimd
            def _(e):
                n0 = dsem.v
                for t in range(4):
                    e.dma_start(out=dbg[:, t, 0:1032],
                                in_=compactx[0].ap()[t * 128:(t + 1) * 128, :]
                                ).then_inc(dsem.h, 16)
                e.wait_ge(dsem.h, n0 + 64)
        dsem.v += 64
        return

    with kb.scope() as sc:
        xsel = sc.sb([128, 8, 1032], BF16, "xsel")
        xselT = sc.sb([128, KC, 1024], BF16, "xselT")
        u_sb = sc.sb([128, 16, 1024], BF16, "u_sb")
        h_sb = sc.sb([128, 16, 1024], BF16, "h_sb")
        wd_sb = sc.sb([128, 2, 16, 1024], BF16, "wd_sb")
        wchk = sc.sb([128, 2, KC, 128], BF16, "wchk")
        gat = sc.sb([128, 8], F32, "gat")
        sg = sc.sb([128, 2, 512], F32, "sg")
        cstage = sc.sb([128, 2, 512], F32, "cstage")
        eps_ = sc.ps([128, 2, 512], F32, name="eps_f6")
        tpp = sc.ps([128, 2, 512], F32, name="tpp_f6")
        x_ld = kb.sem("x_ld")
        x_tp = kb.sem("x_tp")
        x_cp = kb.sem("x_cp")
        w_l0 = (kb.sem("w_l0a"), kb.sem("w_l0b"))
        w_l1 = (kb.sem("w_l1a"), kb.sem("w_l1b"))
        wd_l = kb.sem("wd_l")
        m_mm = kb.sem("m_mm")
        m_ev = kb.sem("m_ev")
        m_ac = kb.sem("m_ac")
        c_st = kb.sem("c_st")
        for ei in range(E):
            ld0, wd0 = x_ld.v, wd_l.v
            with nc.Block() as blk:
                @blk.sync
                def _(e, ei=ei, ld0=ld0):
                    ap = compactx[ei].ap()[0:1024, :].rearrange(
                        "(s p) w -> p s w", p=128)
                    e.dma_start(out=xsel[:], in_=ap).then_inc(x_ld.h, 16)

                @blk.gpsimd
                def _(e, ei=ei, wd0=wd0):
                    e.dma_start(out=wd_sb[:, ei % 2, :, :],
                                in_=win["wd"].ap()[ei]).then_inc(wd_l.h, 16)

                @blk.vector
                def _(e, ei=ei, ld0=ld0):
                    e.wait_ge(x_ld.h, ld0 + 16)
                    i1 = e.tensor_copy(gat[:, :], xsel[:, :, 1024 + ei])
                    kb.edge("v", i1, e)
            x_ld.v += 16
            wd_l.v += 16
            tp0, cp0 = x_tp.v, x_cp.v
            with nc.Block() as blk:
                @blk.tensor
                def _(e, tp0=tp0, cp0=cp0):
                    i = 0
                    for st8 in range(8):
                        for k in range(KC):
                            if i >= 2:
                                e.wait_ge(x_cp.h, cp0 + i - 1)
                            e.matmul(tpp[:, i % 2, 0:128],
                                     xsel[:, st8, k * 128:(k + 1) * 128],
                                     c["ident_bf"][:], start=True,
                                     stop=True).then_inc(x_tp.h, 1)
                            i += 1

                @blk.vector
                def _(e, tp0=tp0, cp0=cp0):
                    i = 0
                    for st8 in range(8):
                        for k in range(KC):
                            e.wait_ge(x_tp.h, tp0 + i + 1)
                            e.tensor_copy(xselT[:, k, st8 * 128:(st8 + 1) * 128],
                                          tpp[:, i % 2, 0:128]
                                          ).then_inc(x_cp.h, 1)
                            i += 1
            x_tp.v += 64
            x_cp.v += 64
            for (wname, wl_sem, out_mode) in (("wu", w_l0, "u"), ("wg", w_l1, "g")):
                mm0, ev0, ac0 = m_mm.v, m_ev.v, m_ac.v
                wl0 = (wl_sem[0].v, wl_sem[1].v)
                with nc.Block() as blk:
                    @blk.gpsimd
                    def _(e, ei=ei, wname=wname, wl_sem=wl_sem, wl0=wl0,
                          mm0=mm0):
                        for ic in range(16):
                            if ic >= 2:
                                e.wait_ge(m_mm.h, mm0 + 2 * (ic - 1))
                            ap = win[wname].ap()[ei, :, :, ic * 128:(ic + 1) * 128]
                            e.dma_start(out=wchk[:, ic % 2, :, :],
                                        in_=ap).then_inc(wl_sem[ic % 2].h, 16)

                    @blk.tensor
                    def _(e, wl_sem=wl_sem, wl0=wl0, mm0=mm0, ev0=ev0):
                        for ic in range(16):
                            e.wait_ge(wl_sem[ic % 2].h,
                                      wl0[ic % 2] + 16 * (ic // 2 + 1))
                            for half in range(2):
                                if 2 * ic + half >= 2:
                                    e.wait_ge(m_ev.h, ev0 + 2 * ic + half - 1)
                                for k in range(KC):
                                    mi = e.matmul(
                                        eps_[:, half, :],
                                        wchk[:, ic % 2, k, :],
                                        xselT[:, k, half * 512:(half + 1) * 512],
                                        start=(k == 0), stop=(k == KC - 1))
                                mi.then_inc(m_mm.h, 1)

                    if out_mode == "u":
                        @blk.vector
                        def _(e, mm0=mm0, ev0=ev0):
                            for ic in range(16):
                                for half in range(2):
                                    e.wait_ge(m_mm.h, mm0 + 2 * ic + half + 1)
                                    e.tensor_copy(
                                        u_sb[:, ic, half * 512:(half + 1) * 512],
                                        eps_[:, half, :]).then_inc(m_ev.h, 1)
                    else:
                        @blk.scalar
                        def _(e, mm0=mm0, ev0=ev0, ac0=ac0):
                            for ic in range(16):
                                for half in range(2):
                                    e.wait_ge(m_mm.h, mm0 + 2 * ic + half + 1)
                                    if 2 * ic + half >= 2:
                                        e.wait_ge(m_ev.h,
                                                  ev0 + 2 * ic + half - 1)
                                    e.activation(sg[:, half, :], eps_[:, half, :],
                                                 AF.Silu).then_inc(m_ac.h, 1)

                        @blk.vector
                        def _(e, mm0=mm0, ev0=ev0, ac0=ac0):
                            for ic in range(16):
                                for half in range(2):
                                    e.wait_ge(m_ac.h, ac0 + 2 * ic + half + 1)
                                    e.tensor_tensor(
                                        h_sb[:, ic, half * 512:(half + 1) * 512],
                                        sg[:, half, :],
                                        u_sb[:, ic, half * 512:(half + 1) * 512],
                                        ALU.mult).then_inc(m_ev.h, 1)
                m_mm.v += 32
                m_ev.v += 32
                wl_sem[0].v += 16 * 8
                wl_sem[1].v += 16 * 8
                if out_mode == "g":
                    m_ac.v += 32
            mm0, ev0, st0, wd0 = m_mm.v, m_ev.v, c_st.v, wd_l.v - 16
            with nc.Block() as blk:
                @blk.tensor
                def _(e, ei=ei, mm0=mm0, ev0=ev0, wd0=wd0):
                    e.wait_ge(wd_l.h, wd0 + 16)
                    for s8 in range(8):
                        for half in range(2):
                            if 2 * s8 + half >= 2:
                                e.wait_ge(m_ev.h, ev0 + 2 * s8 + half - 1)
                            for ic in range(16):
                                mi = e.matmul(
                                    eps_[:, half, :],
                                    h_sb[:, ic, s8 * 128:(s8 + 1) * 128],
                                    wd_sb[:, ei % 2, ic,
                                          half * 512:(half + 1) * 512],
                                    start=(ic == 0), stop=(ic == 15))
                            mi.then_inc(m_mm.h, 1)

                @blk.vector
                def _(e, ei=ei, mm0=mm0, ev0=ev0, st0=st0):
                    for s8 in range(8):
                        for half in range(2):
                            e.wait_ge(m_mm.h, mm0 + 2 * s8 + half + 1)
                            if s8 >= 1 and half == 0:
                                e.wait_ge(c_st.h, st0 + 16 * s8)
                            e.tensor_scalar(cstage[:, half, :], eps_[:, half, :],
                                            gat[:, s8:s8 + 1], None,
                                            ALU.mult).then_inc(m_ev.h, 1)

                @blk.sync
                def _(e, ei=ei, ev0=ev0, st0=st0):
                    for s8 in range(8):
                        e.wait_ge(m_ev.h, ev0 + 2 * s8 + 2)
                        e.dma_start(
                            out=contrib[ei].ap()[s8 * 128:(s8 + 1) * 128, :],
                            in_=cstage[:].rearrange("p a b -> p (a b)")
                            ).then_inc(c_st.h, 16)
                    e.wait_ge(c_st.h, st0 + 16 * 8)
            m_mm.v += 16
            m_ev.v += 16
            c_st.v += 16 * 8

    if stage == "expert":
        dsem = kb.sem()
        with nc.Block() as blk:
            @blk.gpsimd
            def _(e):
                n0 = dsem.v
                for t in range(8):
                    e.dma_start(out=dbg[:, t, 0:D],
                                in_=contrib[0].ap()[t * 128:(t + 1) * 128, :]
                                ).then_inc(dsem.h, 16)
                e.wait_ge(dsem.h, n0 + 128)
        dsem.v += 128
        return

    with kb.scope() as sc:
        acc = sc.sb([128, 8, D], F32, "acc")
        slot_sems = [kb.sem(f"cmb{s}") for s in range(8)]
        rounds = []
        for r in range(4):
            for s in range(8):
                rounds.append((s, r * 8 + s))
        with nc.Block() as blk:
            @blk.gpsimd
            def _(e):
                for r in range(4):
                    for step in range(10):
                        for s in range(8):
                            t = r * 8 + s
                            sem = slot_sems[s]
                            base = 10 * 16 * r
                            if step == 0:
                                if r > 0:
                                    e.wait_ge(sem.h, base)
                                e.dma_start(out=acc[:, s, :],
                                            in_=x1[t * 128:(t + 1) * 128, :]
                                            ).then_inc(sem.h, 16)
                            elif step <= 8:
                                ei = step - 1
                                e.wait_ge(sem.h, base + 16 * step)
                                e.indirect_dma_start(
                                    out=acc[:, s, :], out_offset=None,
                                    in_=contrib[ei].ap(),
                                    in_offset=IndirectOffsetOnAxis(
                                        gidx_u[:, ei, t:t + 1], 0),
                                    compute_op=ALU.add,
                                ).then_inc(sem.h, 16)
                            else:
                                e.wait_ge(sem.h, base + 16 * 9)
                                e.dma_start(out=x2[t * 128:(t + 1) * 128, :],
                                            in_=acc[:, s, :]).then_inc(sem.h, 16)
                for s in range(8):
                    e.wait_ge(slot_sems[s].h, 10 * 16 * 4)
                    slot_sems[s].v = 10 * 16 * 4


def build_l0_ffn(nc, stage="full"):
    kb = KB(nc)
    x1 = nc.dram_tensor("x1", [T, D], F32, kind="ExternalInput").ap()
    win = {
        "gate": nc.dram_tensor("gate", [128, KC, 8], F32, kind="ExternalInput").ap(),
        "wg": nc.dram_tensor("wg", [8, 128, KC, 2048], F32, kind="ExternalInput"),
        "wu": nc.dram_tensor("wu", [8, 128, KC, 2048], F32, kind="ExternalInput"),
        "wd": nc.dram_tensor("wd", [8, 128, 16, 1024], F32, kind="ExternalInput"),
    }
    x2 = nc.dram_tensor("x2", [T, D], F32, kind="ExternalOutput").ap()
    dbg = (nc.dram_tensor("dbg", [128, 16, 1040], F32,
                          kind="ExternalOutput").ap()
           if stage != "full" else None)
    emit_l0_ffn(kb, x1, win, x2, stage=stage, dbg=dbg)
    kb.ctx.close()
    return nc


def prep_l0f(ins, core, x1=None):
    return {
        "x1": np.ascontiguousarray(x1) if x1 is not None else None,
        "gate": np.ascontiguousarray(
            ins["L0_ffn_gate_w"].reshape(KC, 128, 8).transpose(1, 0, 2)),
        "wg": np.ascontiguousarray(
            ins["L0_ffn_Wg"].reshape(8, KC, 128, 2048).transpose(0, 2, 1, 3)),
        "wu": np.ascontiguousarray(
            ins["L0_ffn_Wu"].reshape(8, KC, 128, 2048).transpose(0, 2, 1, 3)),
        "wd": np.ascontiguousarray(
            ins["L0_ffn_Wd"].reshape(8, 16, 128, 1024).transpose(0, 2, 1, 3)),
    }




def emit_l1_attn(kb, x2, win, x3, stage="full", dbg=None):
    nc = kb.nc
    CH = 128
    NCH = T // CH
    c = build_consts(kb, ["ident_bf", "ident_f", "ones_row_f", "ones_col_f"])

    wnames = ("wpv", "wk1", "wkv", "wA", "wB", "wo1")
    wsb = {nm: kb.sb([128, KC, D], BF16, nm + "_sb") for nm in wnames}
    wsb["wb1"] = kb.sb([128, KC, 512], BF16, "wb1_sb")
    wsb["wg2"] = kb.sb([128, KC, 1], BF16, "wg2_sb")
    wsb["wb2"] = kb.sb([128, 4, 2], BF16, "wb2_sb")
    bsb = {nm: kb.sb([128, KC], F32, nm + "_sb")
           for nm in ("bpv", "bk1", "bkv", "bg1", "lng", "lnb", "bo1")}
    bsb["bb1"] = kb.sb([128, 4], F32, "bb1_sb")
    bsb["bb2"] = kb.sb([1, 2], F32, "bb2_sb")
    bsb["bg2"] = kb.sb([1, 1], F32, "bg2_sb")
    wl = kb.sem("awl")
    with nc.Block() as blk:
        @blk.gpsimd
        def _(g):
            n = 0
            for nm in wnames + ("wb1", "wg2", "wb2"):
                g.dma_start(out=wsb[nm][:], in_=win[nm]).then_inc(wl.h, 16)
                n += 16
            for nm in ("bpv", "bk1", "bkv", "bg1", "lng", "lnb", "bo1",
                       "bb1", "bb2", "bg2"):
                g.dma_start(out=bsb[nm][:], in_=win[nm]).then_inc(wl.h, 16)
                n += 16
            g.wait_ge(wl.h, wl.v + n)
    wl.v += 16 * 19

    amc = kb.sb([128, KC, CH], F32, "amc")
    ams = kb.sb([128, KC, CH], F32, "ams")
    bmc = kb.sb([128, KC, CH], F32, "bmc")
    bms = kb.sb([128, KC, CH], F32, "bms")
    carr1 = kb.sb([1, 4], F32, "carr1")
    rows = kb.sb([1, 16, CH], F32, "rows")
    x2c = kb.sb([128, D], F32, "x2c")
    ln2tm = kb.sb([128, D], BF16, "ln2tm")
    ln2T = kb.sb([128, KC, CH], BF16, "ln2T")
    pvb = kb.sb([128, KC, CH], BF16, "pvb")
    kprb = kb.sb([128, KC, CH], F32, "kprb")
    kvb = kb.sb([128, KC, CH], BF16, "kvb")
    kcb = kb.sb([128, KC, CH + 1], BF16, "kcb")
    ksb_ = kb.sb([128, KC, CH + 1], BF16, "ksb")
    p1b = kb.sb([128, KC, CH], F32, "p1b")
    p2b = kb.sb([128, KC, CH + 1], F32, "p2b")
    ghb = kb.sb([128, KC, CH], BF16, "ghb")
    pcb = kb.sb([128, KC, CH], BF16, "pcb")
    psb2 = kb.sb([128, KC, CH], BF16, "psb2")
    pptm = kb.sb([128, D], F32, "pptm")
    ppT = kb.sb([128, KC, CH], F32, "ppT")
    f32s = kb.sb([128, KC, CH], F32, "f32s")
    f32t = kb.sb([128, KC, CH], F32, "f32t")
    iscr = kb.sb([128, KC, CH], I32, "iscr_a")
    scrA = kb.sb([128, KC, CH], BF16, "scrA")
    scrB = kb.sb([128, KC, CH], BF16, "scrB")
    blend = kb.sb([128, KC, CH], F32, "blend")
    bhb = kb.sb([128, 4, CH], F32, "bhb")
    bhbf = kb.sb([128, 4, CH], BF16, "bhbf")
    lnbf = kb.sb([128, KC, CH], BF16, "lnbf")
    vgb = kb.sb([128, CH], F32, "vgb")
    w0b = kb.sb([128, CH], F32, "w0b")
    w1b = kb.sb([128, CH], F32, "w1b")
    meanb = kb.sb([128, CH], F32, "meanb")
    rstdb = kb.sb([128, CH], F32, "rstdb")
    ostg = kb.sb([128, CH], F32, "ostg_a")
    x3t = kb.sb([128, D], F32, "x3t")
    mmps = kb.ps([128, 2, 512], F32, name="mmps_a")
    rwps = kb.ps([128, 4, 512], F32, name="rwps_a")

    with nc.Block() as blk:
        @blk.vector
        def _(e):
            for b in (amc, ams, bmc, bms):
                e.memset(b[:, :, CH - 1:CH], 0.0)
            e.memset(rows[:, 1, CH - 1:CH], 0.0)
            e.memset(kcb[:, :, 0:1], 0.0)
            e.memset(ksb_[:, :, 0:1], 0.0)
            e.memset(p2b[:, :, 0:1], 0.0)
            i1 = e.tensor_tensor(carr1[:, 1:2], bsb["bb2"][:, 0:1],
                                 bsb["bb2"][:, 1:2], ALU.subtract)
            kb.edge("v", i1, e)

    a_ld = kb.sem("a_ld")
    a_mm = kb.sem("a_mm")
    a_ev = kb.sem("a_ev")
    a_ac = kb.sem("a_ac")
    a_dv = kb.sem("a_dv")
    a_st = kb.sem("a_st")

    def bc8(buf):
        a = buf[:]
        return bass.AP(a.tensor, a.offset, [a.ap[0], [0, KC], a.ap[1]])

    lnst = kb.sb([128, 8], F32, "lnst_a")
    for ci in range(NCH):
        par = ci % 2
        t = ci
        ld0, ac0, dv0 = a_ld.v, a_ac.v, a_dv.v
        with nc.Block() as blk:
            @blk.sync
            def _(e, t=t, ld0=ld0):
                e.dma_start(out=x2c[:],
                            in_=x2[t * 128:(t + 1) * 128, :]).then_inc(a_ld.h, 16)
                e.dma_start(out=pptm[:],
                            in_=win["pp"][t * 128:(t + 1) * 128, :]
                            ).then_inc(a_ld.h, 16)

            @blk.scalar
            def _(e, ld0=ld0, ac0=ac0, dv0=dv0):
                e.wait_ge(a_ld.h, ld0 + 32)
                x2f = x2c[:].rearrange("p (a b) -> p a b", a=KC)
                e.activation(f32t[:], x2f, AF.Square,
                             accum_out=lnst[:, 0:1]).then_inc(a_ac.h, 1)
                e.wait_ge(a_dv.h, dv0 + 1)
                e.activation(lnst[:, 5:6], lnst[:, 4:5],
                             AF.Sqrt).then_inc(a_ac.h, 1)

            @blk.vector
            def _(e, ld0=ld0, ac0=ac0, dv0=dv0):
                e.wait_ge(a_ld.h, ld0 + 32)
                i1 = e.reduce_sum(lnst[:, 1:2], x2c[:], axis=AXX)
                kb.edge("v", i1, e)
                e.wait_ge(a_ac.h, ac0 + 1)
                i1 = e.tensor_scalar(lnst[:, 2:3], lnst[:, 1:2], 1.0 / D,
                                     None, ALU.mult)
                kb.edge("v", i1, e)
                i1 = e.tensor_tensor(lnst[:, 3:4], lnst[:, 2:3], lnst[:, 2:3],
                                     ALU.mult)
                kb.edge("v", i1, e)
                i1 = e.scalar_tensor_tensor(lnst[:, 4:5], lnst[:, 0:1], 1.0 / D,
                                            lnst[:, 3:4], ALU.mult,
                                            ALU.subtract)
                kb.edge("v", i1, e)
                i1 = e.tensor_scalar(lnst[:, 4:5], lnst[:, 4:5], 1e-5, None,
                                     ALU.add)
                i1.then_inc(a_dv.h, 1)
                e.wait_ge(a_ac.h, ac0 + 2)
                i1 = e.reciprocal(lnst[:, 6:7], lnst[:, 5:6])
                kb.edge("v", i1, e)
                i1 = e.tensor_tensor(lnst[:, 7:8], lnst[:, 2:3], lnst[:, 6:7],
                                     ALU.mult)
                kb.edge("v", i1, e)
                i1 = e.tensor_scalar(lnst[:, 7:8], lnst[:, 7:8], -1.0, None,
                                     ALU.mult)
                kb.edge("v", i1, e)
                e.tensor_scalar(ln2tm[:], x2c[:], lnst[:, 6:7], lnst[:, 7:8],
                                ALU.mult, ALU.add)
        a_ld.v += 32
        a_ac.v += 2
        a_dv.v += 1
        pairs = [(ln2tm[:, k * 128:(k + 1) * 128], ln2T[:, k, :])
                 for k in range(KC)]
        emit_transposes(kb, pairs, c["ident_bf"], grp=1)
        pairsf = [(pptm[:, k * 128:(k + 1) * 128], ppT[:, k, :])
                  for k in range(KC)]
        emit_transposes(kb, pairsf, c["ident_f"], grp=1)

        mm0, ev0 = a_mm.v, a_ev.v
        specs = [("wpv", "bpv", pvb, F32, 0), ("wk1", "bk1", kprb, F32, 0),
                 ("wkv", "bkv", kvb, BF16, 0), ("wA", None, p1b, F32, 0),
                 ("wB", None, p2b, F32, 1), ("wb1", None, bhb, F32, 0)]
        with nc.Block() as blk:
            @blk.tensor
            def _(e, mm0=mm0, ev0=ev0):
                gi = 0
                for (wn, bn, outb, dt_, coff) in specs:
                    nm_ = 4 if wn == "wb1" else KC
                    for m in range(nm_):
                        if gi >= 2:
                            e.wait_ge(a_ev.h, ev0 + gi - 1)
                        for k in range(KC):
                            mi = e.matmul(mmps[:, gi % 2, 0:CH],
                                          wsb[wn][:, k, m * 128:(m + 1) * 128],
                                          ln2T[:, k, :], start=(k == 0),
                                          stop=(k == KC - 1))
                        mi.then_inc(a_mm.h, 1)
                        gi += 1

            @blk.vector
            def _(e, ci=ci, mm0=mm0, ev0=ev0):
                if ci > 0:
                    i1 = e.tensor_copy(p2b[:, :, 0:1], p2b[:, :, CH:CH + 1])
                    kb.edge("v", i1, e)
                gi = 0
                for (wn, bn, outb, dt_, coff) in specs:
                    nm_ = 4 if wn == "wb1" else KC
                    for m in range(nm_):
                        e.wait_ge(a_mm.h, mm0 + gi + 1)
                        dst = (outb[:, m, coff:coff + CH])
                        if bn is None:
                            e.tensor_copy(dst, mmps[:, gi % 2, 0:CH]
                                          ).then_inc(a_ev.h, 1)
                        else:
                            e.tensor_scalar(dst, mmps[:, gi % 2, 0:CH],
                                            bsb[bn][:, m:m + 1], None,
                                            ALU.add).then_inc(a_ev.h, 1)
                        gi += 1
        a_mm.v += 44
        a_ev.v += 44

        ac0, dv0, mm0 = a_ac.v, a_dv.v, a_mm.v
        with nc.Block() as blk:
            @blk.vector
            def _(e, ci=ci, par=par, ac0=ac0, dv0=dv0, mm0=mm0):
                i1 = e.tensor_scalar(f32s[:], ppT[:], INV2PI, None, ALU.mult)
                kb.edge("v", i1, e)
                i1 = e.tensor_copy(iscr[:], f32s[:])
                kb.edge("v", i1, e)
                i1 = e.tensor_copy(f32s[:], iscr[:])
                kb.edge("v", i1, e)
                i1 = e.scalar_tensor_tensor(f32t[:], f32s[:], -2.0 * PI,
                                            ppT[:], ALU.mult, ALU.add)
                kb.edge("v", i1, e)
                i1 = e.tensor_scalar(f32s[:], f32t[:], PI, None, ALU.is_gt)
                kb.edge("v", i1, e)
                i1 = e.scalar_tensor_tensor(f32t[:], f32s[:], -2.0 * PI,
                                            f32t[:], ALU.mult, ALU.add)
                kb.edge("v", i1, e)
                i1 = e.tensor_scalar(f32s[:], f32t[:], -PI, None, ALU.is_lt)
                kb.edge("v", i1, e)
                i1 = e.scalar_tensor_tensor(f32t[:], f32s[:], 2.0 * PI,
                                            f32t[:], ALU.mult, ALU.add)
                kb.edge("v", i1, e)
                i1 = e.tensor_scalar(f32s[:], f32t[:], PI / 2, None, ALU.add)
                kb.edge("v", i1, e)
                i1 = e.tensor_scalar(blend[:], f32s[:], PI, None, ALU.is_gt)
                kb.edge("v", i1, e)
                i1 = e.scalar_tensor_tensor(f32s[:], blend[:], -2.0 * PI,
                                            f32s[:], ALU.mult, ALU.add)
                i1.then_inc(a_dv.h, 1)
                e.wait_ge(a_ac.h, ac0 + 3)
                if ci > 0:
                    i1 = e.tensor_copy(kcb[:, :, 0:1], kcb[:, :, CH:CH + 1])
                    kb.edge("v", i1, e)
                    i1 = e.tensor_copy(ksb_[:, :, 0:1], ksb_[:, :, CH:CH + 1])
                    kb.edge("v", i1, e)
                i1 = e.tensor_scalar(f32t[:], scrA[:], PI, None, ALU.mult)
                kb.edge("v", i1, e)
                i1 = e.tensor_scalar(f32s[:], f32t[:], PI / 2, None, ALU.add)
                kb.edge("v", i1, e)
                i1 = e.tensor_scalar(blend[:], f32s[:], PI, None, ALU.is_gt)
                kb.edge("v", i1, e)
                i1 = e.scalar_tensor_tensor(f32s[:], blend[:], -2.0 * PI,
                                            f32s[:], ALU.mult, ALU.add)
                i1.then_inc(a_dv.h, 1)
                e.wait_ge(a_ac.h, ac0 + 4)
                i1 = e.tensor_tensor(f32t[:], p1b[:], p2b[:, :, 0:CH], ALU.add)
                i1.then_inc(a_dv.h, 1)
                e.wait_ge(a_ac.h, ac0 + 7)
                e.wait_ge(a_mm.h, mm0 + 11)
                i1 = e.tensor_tensor(rows[:, 7, :], rwps[0:1, 2, 0:CH],
                                     rows[:, 6, :], ALU.subtract)
                i1.then_inc(a_dv.h, 1)
                e.wait_ge(a_ac.h, ac0 + 9)
                i1 = e.tensor_tensor_scan(rows[:, 1, :], rows[:, 0, :],
                                          rows[:, 0, :],
                                          rows[:, 1, CH - 1:CH],
                                          ALU.add, ALU.bypass)
                kb.edge("v", i1, e)
                i1 = e.tensor_scalar(rows[:, 5, :], rows[:, 1, :], 1.0,
                                     None, ALU.max)
                i1.then_inc(a_dv.h, 1)
                e.wait_ge(a_ac.h, ac0 + 10)
                i1 = e.reciprocal(rows[:, 14, :], rows[:, 15, :])
                kb.edge("v", i1, e)
                i1 = e.tensor_scalar(rows[:, 5, :], rows[:, 14, :], 1.0 / 32.0,
                                     None, ALU.mult)
                kb.edge("v", i1, e)
                i1 = e.tensor_scalar(rows[:, 4, :], rows[:, 3, :], -1.0, 1.0,
                                     ALU.mult, ALU.add)
                kb.edge("v", i1, e)
                i1 = e.tensor_scalar(rows[:, 8, :], rows[:, 3, :], 1.0 / 32.0,
                                     None, ALU.mult)
                kb.edge("v", i1, e)
                i1 = e.tensor_tensor(rows[:, 9, :], rows[:, 4, :],
                                     rows[:, 5, :], ALU.mult)
                i1.then_inc(a_dv.h, 1)
                e.wait_ge(a_mm.h, mm0 + 14)
                i1 = e.tensor_copy(vgb[:], rwps[:, 0, 0:CH])
                kb.edge("v", i1, e)
                i1 = e.tensor_copy(w0b[:], rwps[:, 1, 0:CH])
                kb.edge("v", i1, e)
                i1 = e.tensor_copy(w1b[:], rwps[:, 2, 0:CH])
                i1.then_inc(a_dv.h, 1)

            @blk.scalar
            def _(e, par=par, ac0=ac0, dv0=dv0, mm0=mm0):
                i1 = e.activation(scrA[:], kprb[:], AF.Tanh)
                i1.then_inc(a_ac.h, 1)
                e.wait_ge(a_dv.h, dv0 + 1)
                i1 = e.activation(psb2[:], f32t[:], AF.Sin)
                kb.edge("s", i1, e)
                i1 = e.activation(pcb[:], f32s[:], AF.Sin)
                i1.then_inc(a_ac.h, 2)
                e.wait_ge(a_dv.h, dv0 + 2)
                i1 = e.activation(ksb_[:, :, 1:CH + 1], f32t[:], AF.Sin)
                kb.edge("s", i1, e)
                i1 = e.activation(kcb[:, :, 1:CH + 1], f32s[:], AF.Sin)
                i1.then_inc(a_ac.h, 1)
                e.wait_ge(a_dv.h, dv0 + 3)
                for kc_ in range(KC):
                    i1 = e.activation(ghb[:, kc_, :], f32t[:, kc_, :], AF.Gelu,
                                      bias=bsb["bg1"][:, kc_:kc_ + 1])
                i1.then_inc(a_ac.h, 1)
                for kc_ in range(4):
                    i1 = e.activation(bhbf[:, kc_, :], bhb[:, kc_, :], AF.Gelu,
                                      bias=bsb["bb1"][:, kc_:kc_ + 1])
                i1.then_inc(a_ac.h, 1)
                e.wait_ge(a_mm.h, mm0 + 10)
                i1 = e.copy(rows[:, 6, :], rwps[0:1, 3, 0:CH])
                i1.then_inc(a_ac.h, 1)
                e.wait_ge(a_dv.h, dv0 + 4)
                i1 = e.activation(rows[:, 3, :], rows[:, 7, :], AF.Sigmoid,
                                  bias=carr1[:, 1:2])
                i1.then_inc(a_ac.h, 1)
                e.wait_ge(a_mm.h, mm0 + 9)
                i1 = e.activation(rows[:, 0, :], rwps[0:1, 1, 0:CH], AF.Sigmoid,
                                  bias=bsb["bg2"][:, 0:1])
                i1.then_inc(a_ac.h, 1)
                e.wait_ge(a_dv.h, dv0 + 5)
                i1 = e.activation(rows[:, 15, :], rows[:, 5, :], AF.Sqrt)
                i1.then_inc(a_ac.h, 1)

            @blk.tensor
            def _(e, ac0=ac0, dv0=dv0, mm0=mm0, ev0=a_ev.v):
                e.wait_ge(a_ac.h, ac0 + 5)
                for k in range(KC):
                    mi = e.matmul(rwps[0:1, 1, 0:CH], wsb["wg2"][:, k, :],
                                  ghb[:, k, :], start=(k == 0),
                                  stop=(k == KC - 1))
                mi.then_inc(a_mm.h, 9)
                e.wait_ge(a_ac.h, ac0 + 6)
                for k in range(4):
                    mi = e.matmul(rwps[0:1, 3, 0:CH], wsb["wb2"][:, k, 1:2],
                                  bhbf[:, k, :], start=(k == 0), stop=(k == 3))
                mi.then_inc(a_mm.h, 1)
                for k in range(4):
                    mi = e.matmul(rwps[0:1, 2, 0:CH], wsb["wb2"][:, k, 0:1],
                                  bhbf[:, k, :], start=(k == 0), stop=(k == 3))
                mi.then_inc(a_mm.h, 1)
                e.wait_ge(a_ac.h, ac0 + 8)
                mi = e.matmul(rwps[:, 0, 0:CH], c["ones_row_f"][:],
                              rows[:, 0, :], start=True, stop=True)
                mi.then_inc(a_mm.h, 1)
                e.wait_ge(a_dv.h, dv0 + 6)
                mi = e.matmul(rwps[:, 1, 0:CH], c["ones_row_f"][:],
                              rows[:, 8, :], start=True, stop=True)
                mi.then_inc(a_mm.h, 1)
                mi = e.matmul(rwps[:, 2, 0:CH], c["ones_row_f"][:],
                              rows[:, 9, :], start=True, stop=True)
                mi.then_inc(a_mm.h, 1)
        a_ac.v = ac0 + 10
        a_dv.v = dv0 + 7
        a_mm.v = mm0 + 14

        dv0 = a_dv.v
        with nc.Block() as blk:
            @blk.vector
            def _(e, par=par, dv0=dv0):
                def scans(dstbuf, srcbuf):
                    for k in range(KC):
                        i1 = e.tensor_tensor_scan(
                            dstbuf[:, k, :], srcbuf[:, k, :],
                            srcbuf[:, k, :],
                            dstbuf[:, k, CH - 1:CH],
                            ALU.add, ALU.bypass)
                    kb.edge("v", i1, e)
                i1 = e.tensor_tensor(scrA[:], pcb[:], pvb[:], ALU.mult)
                kb.edge("v", i1, e)
                scans(amc, scrA)
                i1 = e.tensor_tensor(scrA[:], psb2[:], pvb[:], ALU.mult)
                kb.edge("v", i1, e)
                scans(ams, scrA)
                i1 = e.tensor_tensor(scrA[:], kcb[:, :, 0:CH], kvb[:], ALU.mult)
                kb.edge("v", i1, e)
                i1 = e.tensor_tensor(scrB[:], scrA[:], bc8(vgb), ALU.mult)
                kb.edge("v", i1, e)
                scans(bmc, scrB)
                i1 = e.tensor_tensor(scrA[:], ksb_[:, :, 0:CH], kvb[:], ALU.mult)
                kb.edge("v", i1, e)
                i1 = e.tensor_tensor(scrB[:], scrA[:], bc8(vgb), ALU.mult)
                kb.edge("v", i1, e)
                scans(bms, scrB)
                i1 = e.tensor_tensor(f32s[:], pcb[:], amc[:], ALU.mult)
                kb.edge("v", i1, e)
                i1 = e.tensor_tensor(f32t[:], psb2[:], ams[:], ALU.mult)
                kb.edge("v", i1, e)
                i1 = e.tensor_tensor(f32s[:], f32s[:], f32t[:], ALU.add)
                kb.edge("v", i1, e)
                i1 = e.tensor_tensor(blend[:], f32s[:], bc8(w0b), ALU.mult)
                kb.edge("v", i1, e)
                i1 = e.tensor_tensor(f32s[:], kcb[:, :, 1:CH + 1], bmc[:],
                                     ALU.mult)
                kb.edge("v", i1, e)
                i1 = e.tensor_tensor(f32t[:], ksb_[:, :, 1:CH + 1],
                                     bms[:], ALU.mult)
                kb.edge("v", i1, e)
                i1 = e.tensor_tensor(f32s[:], f32s[:], f32t[:], ALU.add)
                kb.edge("v", i1, e)
                i1 = e.tensor_tensor(f32t[:], f32s[:], bc8(w1b), ALU.mult)
                kb.edge("v", i1, e)
                i1 = e.tensor_tensor(blend[:], blend[:], f32t[:], ALU.add)
                i1.then_inc(a_dv.h, 1)
        a_dv.v += 1

        ac0, dv0, mm0 = a_ac.v, a_dv.v, a_mm.v
        with nc.Block() as blk:
            @blk.scalar
            def _(e, ac0=ac0, dv0=dv0):
                i1 = e.activation(f32t[:], blend[:], AF.Square)
                i1.then_inc(a_ac.h, 1)
                e.wait_ge(a_dv.h, dv0 + 1)
                i1 = e.activation(rows[:, 12, :], rows[:, 11, :], AF.Sqrt)
                i1.then_inc(a_ac.h, 1)

            @blk.tensor
            def _(e, ac0=ac0, dv0=dv0, mm0=mm0):
                for k in range(KC):
                    mi = e.matmul(rwps[0:1, 0, 0:CH], c["ones_col_f"][:],
                                  blend[:, k, :], start=(k == 0),
                                  stop=(k == KC - 1))
                mi.then_inc(a_mm.h, 1)
                e.wait_ge(a_ac.h, ac0 + 1)
                for k in range(KC):
                    mi = e.matmul(rwps[0:1, 1, 0:CH], c["ones_col_f"][:],
                                  f32t[:, k, :], start=(k == 0),
                                  stop=(k == KC - 1))
                mi.then_inc(a_mm.h, 1)
                e.wait_ge(a_dv.h, dv0 + 2)
                mi = e.matmul(rwps[:, 2, 0:CH], c["ones_row_f"][:],
                              rows[:, 10, :], start=True, stop=True)
                mi.then_inc(a_mm.h, 1)
                mi = e.matmul(rwps[:, 3, 0:CH], c["ones_row_f"][:],
                              rows[:, 13, :], start=True, stop=True)
                mi.then_inc(a_mm.h, 1)

            @blk.vector
            def _(e, ac0=ac0, dv0=dv0, mm0=mm0):
                e.wait_ge(a_mm.h, mm0 + 2)
                i1 = e.tensor_scalar(rows[:, 10, :], rwps[0:1, 0, 0:CH],
                                     1.0 / D, None, ALU.mult)
                kb.edge("v", i1, e)
                i1 = e.tensor_tensor(rows[:, 14, :], rows[:, 10, :],
                                     rows[:, 10, :], ALU.mult)
                kb.edge("v", i1, e)
                i1 = e.scalar_tensor_tensor(rows[:, 11, :], rwps[0:1, 1, 0:CH],
                                            1.0 / D, rows[:, 14, :],
                                            ALU.mult, ALU.subtract)
                kb.edge("v", i1, e)
                i1 = e.tensor_scalar(rows[:, 11, :], rows[:, 11, :], 1e-5,
                                     None, ALU.add)
                i1.then_inc(a_dv.h, 1)
                e.wait_ge(a_ac.h, ac0 + 2)
                i1 = e.reciprocal(rows[:, 13, :], rows[:, 12, :])
                kb.edge("v", i1, e)
                nc.vector.engine_nop().then_inc(a_dv.h, 1)
                e.wait_ge(a_mm.h, mm0 + 4)
                i1 = e.tensor_copy(meanb[:], rwps[:, 2, 0:CH])
                kb.edge("v", i1, e)
                i1 = e.tensor_copy(rstdb[:], rwps[:, 3, 0:CH])
                kb.edge("v", i1, e)
                i1 = e.tensor_tensor(f32s[:], blend[:], bc8(meanb), ALU.subtract)
                kb.edge("v", i1, e)
                i1 = e.tensor_tensor(f32s[:], f32s[:], bc8(rstdb), ALU.mult)
                kb.edge("v", i1, e)
                for k in range(KC):
                    i1 = e.tensor_scalar(lnbf[:, k, :], f32s[:, k, :],
                                         bsb["lng"][:, k:k + 1],
                                         bsb["lnb"][:, k:k + 1],
                                         ALU.mult, ALU.add)
                i1.then_inc(a_dv.h, 1)
        a_ac.v = ac0 + 2
        a_dv.v = dv0 + 3
        a_mm.v = mm0 + 4

        mm0, ev0, dv0, st0 = a_mm.v, a_ev.v, a_dv.v, a_st.v
        with nc.Block() as blk:
            @blk.tensor
            def _(e, mm0=mm0, ev0=ev0, dv0=dv0):
                for m in range(KC):
                    if m >= 1:
                        e.wait_ge(a_ev.h, ev0 + m)
                    for k in range(KC):
                        mi = e.matmul(mmps[:, m % 2, 0:CH],
                                      wsb["wo1"][:, k, m * 128:(m + 1) * 128],
                                      lnbf[:, k, :], start=(k == 0),
                                      stop=(k == KC - 1))
                    mi.then_inc(a_mm.h, 1)
                    e.wait_ge(a_ev.h, ev0 + m + 1)
                    e.matmul(rwps[:, m % 2, 0:CH], ostg[:],
                             c["ident_f"][:], start=True,
                             stop=True).then_inc(a_mm.h, 1)

            @blk.vector
            def _(e, mm0=mm0, ev0=ev0, dv0=dv0):
                for m in range(KC):
                    e.wait_ge(a_mm.h, mm0 + 2 * m + 1)
                    e.tensor_scalar(ostg[:], mmps[:, m % 2, 0:CH],
                                    bsb["bo1"][:, m:m + 1], None,
                                    ALU.add).then_inc(a_ev.h, 1)
                    e.wait_ge(a_mm.h, mm0 + 2 * m + 2)
                    i1 = e.tensor_tensor(f32s[:, 0, :], rwps[:, m % 2, 0:CH],
                                         ln2tm[:, m * 128:(m + 1) * 128],
                                         ALU.add)
                    kb.edge("v", i1, e)
                    i1 = e.tensor_tensor(x3t[:, m * 128:(m + 1) * 128],
                                         f32s[:, 0, :],
                                         x2c[:, m * 128:(m + 1) * 128],
                                         ALU.add)
                    kb.edge("v", i1, e)
                nc.vector.engine_nop().then_inc(a_dv.h, 1)

            @blk.sync
            def _(e, t=t, dv0=dv0, st0=st0):
                e.wait_ge(a_dv.h, dv0 + 1)
                e.dma_start(out=x3[t * 128:(t + 1) * 128, :],
                            in_=x3t[:]).then_inc(a_st.h, 16)
                e.wait_ge(a_st.h, st0 + 16)
        a_mm.v += 16
        a_ev.v += KC
        a_dv.v += 1
        a_st.v += 16

        if stage == "chunk0" and ci == 0:
            return


def build_l1_attn(nc, stage="full"):
    kb = KB(nc)
    x2 = nc.dram_tensor("x2", [T, D], F32, kind="ExternalInput").ap()
    win = {"pp": nc.dram_tensor("pp", [T, D], F32, kind="ExternalInput").ap()}
    for nm, shp in (("wpv", [128, KC, D]), ("wk1", [128, KC, D]),
                    ("wkv", [128, KC, D]), ("wA", [128, KC, D]),
                    ("wB", [128, KC, D]), ("wo1", [128, KC, D]),
                    ("wb1", [128, KC, 512]), ("wg2", [128, KC, 1]),
                    ("wb2", [128, 4, 2]), ("bpv", [128, KC]),
                    ("bk1", [128, KC]), ("bkv", [128, KC]),
                    ("bg1", [128, KC]), ("lng", [128, KC]),
                    ("lnb", [128, KC]), ("bo1", [128, KC]),
                    ("bb1", [128, 4]), ("bb2", [1, 2]), ("bg2", [1, 1])):
        win[nm] = nc.dram_tensor(nm, shp, F32, kind="ExternalInput").ap()
    x3 = nc.dram_tensor("x3", [T, D], F32, kind="ExternalOutput").ap()
    emit_l1_attn(kb, x2, win, x3, stage=stage)
    kb.ctx.close()
    return nc


def prep_l1a(ins, core, x2=None):
    Wg1 = ins["L1_attn_Wg1"]
    return {
        "x2": np.ascontiguousarray(x2) if x2 is not None else None,
        "pp": np.ascontiguousarray(ins["L1_attn_pos_phases"][:T]),
        "wpv": _dm_w(ins["L1_attn_Wpv"]), "wk1": _dm_w(ins["L1_attn_Wk"]),
        "wkv": _dm_w(ins["L1_attn_Wkv"]), "wA": _dm_w(Wg1[:D]),
        "wB": _dm_w(Wg1[D:]), "wo1": _dm_w(ins["L1_attn_Wo"]),
        "wb1": _dm_w(ins["L1_attn_Wb1"]),
        "wg2": _dm_w(ins["L1_attn_Wg2"]),
        "wb2": np.ascontiguousarray(
            ins["L1_attn_Wb2"].reshape(4, 128, 2).transpose(1, 0, 2)),
        "bpv": _dm_b(ins["L1_attn_bpv"]), "bk1": _dm_b(ins["L1_attn_bk"]),
        "bkv": _dm_b(ins["L1_attn_bkv"]), "bg1": _dm_b(ins["L1_attn_bg1"]),
        "lng": _dm_b(ins["L1_attn_ln_g"]), "lnb": _dm_b(ins["L1_attn_ln_b"]),
        "bo1": _dm_b(ins["L1_attn_bo"]), "bb1": _dm_b(ins["L1_attn_bb1"]),
        "bb2": ins["L1_attn_bb2"].reshape(1, 2),
        "bg2": ins["L1_attn_bg2"].reshape(1, 1),
    }




def emit_l1_ffn(kb, x3, win, xout, n_cores=8, stage="full", dbg=None):
    nc = kb.nc
    NP2 = 2048
    MC = 512
    NMC = T // MC
    c = build_consts(kb, ["ident_bf", "ident_f", "ones_row_bf", "ones_row_f",
                          "ones_col_f", "ones_row_bf", "ones_col_bf"])
    e_dram = nc.dram_tensor(kb.name("edram"), [NP2, T], BF16)
    s_dram = nc.dram_tensor(kb.name("sdram"), [NP2, T], BF16)

    rows_rc = kb.sb([1, NMC, MC], BF16, "rows_rc")
    slots_bf = kb.sb([128, KC, NP2], BF16, "slots_bf")

    f_ld = kb.sem("f_ld")
    f_mm = kb.sem("f_mm2")
    f_ac = kb.sem("f_ac")
    f_dv = kb.sem("f_dv")
    f_st = kb.sem("f_st2")
    f_cc = kb.sem("f_cc")
    wl = kb.sem("lfwl")

    ln3_dram = nc.dram_tensor(kb.name("ln3d"), [T, D], BF16)
    scw = kb.scope()
    scw.__enter__()
    lnch = scw.sb([128, 4, D], BF16, "lnch")
    sacc = scw.sb([128, KC, NP2], F32, "sacc")
    sc1 = kb.scope()
    sc1.__enter__()
    phi_sb = sc1.sb([128, KC, NP2], BF16, "phi_sb")
    lnstage = sc1.sb([128, 8, D], BF16, "lnstage")
    ln3T = sc1.sb([128, KC, MC], BF16, "ln3T")
    e_sb = sc1.sb([128, 2, 512], BF16, "e_sb")
    mmps = sc1.ps([128, 2, 512], F32, name="mmps_f")
    rsps = sc1.ps([128, 2, 512], F32, name="rsps_f")
    with nc.Block() as blk:
        @blk.gpsimd
        def _(g):
            g.dma_start(out=phi_sb[:], in_=win["phi"]).then_inc(wl.h, 16)
            g.wait_ge(wl.h, wl.v + 16)
    wl.v += 16

    for g0 in range(0, NT, 8):
        emit_ln(kb, x3, lambda t: lnstage[:, t % 8, :], nt=8, group=4,
                tile_off=g0)
        st0 = f_st.v
        with nc.Block() as blk:
            @blk.sync
            def _(e, g0=g0, st0=st0):
                for i in range(8):
                    t = g0 + i
                    e.dma_start(out=ln3_dram.ap()[t * 128:(t + 1) * 128, :],
                                in_=lnstage[:, i, :]).then_inc(f_st.h, 16)
                e.wait_ge(f_st.h, st0 + 128)
        f_st.v += 128
    for mc in range(NMC):
        ms = slice(mc * MC, (mc + 1) * MC)
        ld0 = f_ld.v
        with nc.Block() as blk:
            @blk.sync
            def _(e, mc=mc, ld0=ld0):
                e.dma_start(out=lnch[:],
                            in_=ln3_dram.ap()[mc * MC:(mc + 1) * MC, :]
                            .rearrange("(a p) d -> p a d", p=128)
                            ).then_inc(f_ld.h, 16)
                e.wait_ge(f_ld.h, ld0 + 16)
        f_ld.v += 16
        pairs = [(lnch[:, mt, k * 128:(k + 1) * 128],
                  ln3T[:, k, mt * 128:(mt + 1) * 128])
                 for mt in range(4) for k in range(KC)]
        emit_transposes(kb, pairs, c["ident_bf"])
        mm0, ac0, st0, dv0 = f_mm.v, f_ac.v, f_st.v, f_dv.v
        with nc.Block() as blk:
            @blk.tensor
            def _(e, mc=mc, mm0=mm0, ac0=ac0):
                for npc in range(16):
                    if npc >= 2:
                        e.wait_ge(f_ac.h, ac0 + npc - 1)
                    for k in range(KC):
                        mi = e.matmul(mmps[:, npc % 2, :],
                                      phi_sb[:, k, npc * 128:(npc + 1) * 128],
                                      ln3T[:, k, :], start=(k == 0),
                                      stop=(k == KC - 1))
                    mi.then_inc(f_mm.h, 1)
                    e.wait_ge(f_ac.h, ac0 + npc + 1)
                    mi = e.matmul(rsps[0:1, 0, :], c["ones_col_bf"][:],
                                  e_sb[:, npc % 2, :], start=(npc == 0),
                                  stop=(npc == 15))
                    mi.then_inc(f_mm.h, 1)

            @blk.scalar
            def _(e, mc=mc, mm0=mm0, st0=st0):
                for npc in range(16):
                    e.wait_ge(f_mm.h, mm0 + 2 * npc + 1)
                    if npc >= 2:
                        e.wait_ge(f_st.h, st0 + 16 * (npc - 1))
                    e.activation(e_sb[:, npc % 2, :], mmps[:, npc % 2, :],
                                 AF.Exp).then_inc(f_ac.h, 1)

            @blk.sync
            def _(e, mc=mc, mm0=mm0, st0=st0):
                for npc in range(16):
                    e.wait_ge(f_mm.h, mm0 + 2 * npc + 2)
                    e.dma_start(
                        out=e_dram.ap()[npc * 128:(npc + 1) * 128, ms],
                        in_=e_sb[:, npc % 2, :]).then_inc(f_st.h, 16)

            @blk.vector
            def _(e, mc=mc, mm0=mm0, dv0=dv0):
                e.wait_ge(f_mm.h, mm0 + 32)
                with nc.allow_low_precision(reason="bf16 combine denominators"):
                    i1 = e.reciprocal(rows_rc[:, mc, :], rsps[0:1, 0, :])
                kb.edge("v", i1, e)
                nc.vector.engine_nop().then_inc(f_dv.h, 1)
        f_mm.v += 32
        f_ac.v += 16
        f_st.v += 16 * 16
        f_dv.v += 1

    cc0, st0 = f_cc.v, f_st.v
    with nc.Block() as blk:
        @blk.gpsimd
        def _(e, cc0=cc0, st0=st0):
            e.wait_ge(f_st.h, st0)
            e.collective_compute(
                "AllReduce", ALU.add,
                replica_groups=[list(range(n_cores))],
                ins=[e_dram.ap().opt()],
                outs=[s_dram.ap().opt()],
            ).then_inc(f_cc.h)
            e.wait_ge(f_cc.h, cc0 + 1)
    f_cc.v += 1

    sc1.__exit__(None, None, None)
    if stage == "g2":
        scw.__exit__(None, None, None)
        return

    sc2 = kb.scope()
    sc2.__enter__()
    en_sb = sc2.sb([128, 2, 512], BF16, "en_sb")
    sn_sb = sc2.sb([128, 2, 512], BF16, "sn_sb")
    ds1 = sc2.sb([128, 2, 512], F32, "ds1")
    ds2 = sc2.sb([128, 2, 512], F32, "ds2")
    dn_sb = sc2.sb([128, 2, 512], BF16, "dn_sb")
    disp = sc2.sb([128, 4, NP2], BF16, "disp")
    mmps = sc2.ps([128, 2, 512], F32, name="mmps_f2")
    rsps = sc2.ps([128, 2, 512], F32, name="rsps_f2")
    with nc.Block() as blk:
        @blk.vector
        def _(e):
            e.memset(sacc[:], 0.0)
    for mc in range(NMC):
        ms = slice(mc * MC, (mc + 1) * MC)
        ld0, mm0, dv0 = f_ld.v, f_mm.v, f_dv.v
        with nc.Block() as blk:
            @blk.sync
            def _(e, mc=mc, ld0=ld0, dv0=dv0, ac0=f_ac.v):
                e.dma_start(out=lnch[:],
                            in_=ln3_dram.ap()[mc * MC:(mc + 1) * MC, :]
                            .rearrange("(a p) d -> p a d", p=128)
                            ).then_inc(f_ld.h, 16)
                for npc in range(16):
                    if npc >= 2:
                        e.wait_ge(f_ac.h, ac0 + 2 * (npc - 2) + 1)
                    e.dma_start(out=en_sb[:, npc % 2, :],
                                in_=e_dram.ap()[npc * 128:(npc + 1) * 128, ms]
                                ).then_inc(f_ld.h, 16)
                    e.dma_start(out=sn_sb[:, npc % 2, :],
                                in_=s_dram.ap()[npc * 128:(npc + 1) * 128, ms]
                                ).then_inc(f_ld.h, 16)

            @blk.scalar
            def _(e, mc=mc, ld0=ld0, mm0=mm0, dv0=dv0, ac0=f_ac.v):
                for npc in range(16):
                    sl = npc % 2
                    e.wait_ge(f_ld.h, ld0 + 16 + 32 * (npc + 1))
                    i1 = e.activation(ds1[:, sl, :], en_sb[:, sl, :], AF.Ln)
                    kb.edge("s", i1, e)
                    i1 = e.activation(ds2[:, sl, :], sn_sb[:, sl, :], AF.Ln)
                    i1.then_inc(f_ac.h, 1)
                    e.wait_ge(f_dv.h, dv0 + 5 * npc + 1)
                    if npc >= 2:
                        e.wait_ge(f_mm.h, mm0 + 4 * (npc - 1))
                    e.activation(dn_sb[:, sl, :], ds1[:, sl, :],
                                 AF.Exp).then_inc(f_ac.h, 1)

            @blk.vector
            def _(e, mc=mc, mm0=mm0, dv0=dv0, ac0=f_ac.v):
                for it in range(17):
                    if it < 16:
                        sl = it % 2
                        e.wait_ge(f_ac.h, ac0 + 2 * it + 1)
                        i1 = e.tensor_tensor(ds1[:, sl, :], ds1[:, sl, :],
                                             ds2[:, sl, :], ALU.subtract)
                        i1.then_inc(f_dv.h, 1)
                    if it >= 1:
                        g_ = it - 1
                        for mt in range(4):
                            i = 4 * g_ + mt
                            e.wait_ge(f_mm.h, mm0 + i + 1)
                            mi = e.tensor_copy(
                                disp[:, mt, g_ * 128:(g_ + 1) * 128],
                                rsps[:, i % 2, 0:128])
                            mi.then_inc(f_dv.h, 1)

            @blk.tensor
            def _(e, mc=mc, mm0=mm0, dv0=dv0, ac0=f_ac.v):
                def evac_cnt(i):
                    q, r = i // 4, i % 4
                    return dv0 + 5 * q + r + 2
                i = 0
                for npc in range(16):
                    e.wait_ge(f_ac.h, ac0 + 2 * npc + 2)
                    for mt in range(4):
                        if i >= 2:
                            e.wait_ge(f_dv.h, evac_cnt(i - 2))
                        mi = e.matmul(rsps[:, i % 2, 0:128],
                                      dn_sb[:, npc % 2, mt * 128:(mt + 1) * 128],
                                      c["ident_bf"][:], start=True, stop=True)
                        mi.then_inc(f_mm.h, 1)
                        i += 1
        f_ld.v += 16 + 32 * 16
        f_dv.v += 16 + 64
        f_mm.v += 64
        f_ac.v += 32
        mm0, dv0 = f_mm.v, f_dv.v
        with nc.Block() as blk:
            @blk.tensor
            def _(e, mc=mc, mm0=mm0, dv0=dv0):
                for dc in range(KC):
                    for npq in range(4):
                        gi = dc * 4 + npq
                        if gi >= 2:
                            e.wait_ge(f_dv.h, dv0 + gi - 1)
                        for mt in range(4):
                            mi = e.matmul(
                                mmps[:, gi % 2, :],
                                lnch[:, mt, dc * 128:(dc + 1) * 128],
                                disp[:, mt, npq * 512:(npq + 1) * 512],
                                start=(mt == 0), stop=(mt == 3))
                        mi.then_inc(f_mm.h, 1)

            @blk.vector
            def _(e, mc=mc, mm0=mm0, dv0=dv0):
                for dc in range(KC):
                    for npq in range(4):
                        gi = dc * 4 + npq
                        e.wait_ge(f_mm.h, mm0 + gi + 1)
                        i1 = e.tensor_tensor(
                            sacc[:, dc, npq * 512:(npq + 1) * 512],
                            sacc[:, dc, npq * 512:(npq + 1) * 512],
                            mmps[:, gi % 2, :], ALU.add)
                        i1.then_inc(f_dv.h, 1)
        f_mm.v += 32
        f_dv.v += 32

    dv0 = f_dv.v
    with nc.Block() as blk:
        @blk.vector
        def _(e, dv0=dv0):
            for dc in range(KC):
                i1 = e.tensor_copy(slots_bf[:, dc, :], sacc[:, dc, :])
            i1.then_inc(f_dv.h, 1)
    f_dv.v += 1
    sc2.__exit__(None, None, None)
    scw.__exit__(None, None, None)

    if stage == "g3":
        dsem = kb.sem()
        with nc.Block() as blk:
            @blk.gpsimd
            def _(e):
                n0 = dsem.v
                for dc in range(KC):
                    e.dma_start(out=dbg[:, dc, :],
                                in_=slots_bf[:, dc, :]).then_inc(dsem.h, 16)
                e.wait_ge(dsem.h, n0 + 16 * KC)
        dsem.v += 16 * KC
        return

    y_sb = kb.sb([128, 16, D], BF16, "y_sb")
    with kb.scope() as sc:
        w4 = sc.sb([128, 2, 4, D], BF16, "w4")
        brow = sc.sb([1, 2, 4, D], BF16, "brow")
        yps = sc.ps([128, 2, 512], F32, name="yps")
        wsem = (kb.sem("w4a"), kb.sem("w4b"))
        bsem = (kb.sem("b4a"), kb.sem("b4b"))
        y_mm = kb.sem("y_mm")
        y_ev = kb.sem("y_ev")
        for ng in range(16):
            b0, mm0, ev0 = bsem[ng % 2].v, y_mm.v, y_ev.v
            w00 = (wsem[0].v, wsem[1].v)
            with nc.Block() as blk:
                @blk.gpsimd
                def _(e, ng=ng, mm0=mm0, w00=w00, ev0=ev0):
                    bap = bass.AP(win["b"].ap().tensor, ng * 4 * D,
                                  [[4 * D, 1], [D, 4], [1, D]])
                    e.dma_start(out=brow[:, ng % 2, :, :],
                                in_=bap).then_inc(bsem[ng % 2].h, 16)
                    for dc in range(KC):
                        if dc >= 2:
                            e.wait_ge(y_mm.h, mm0 + dc - 1)
                        e.dma_start(
                            out=w4[:, dc % 2, :, :],
                            in_=win["w"].ap()[ng * 4:(ng + 1) * 4, :, dc, :]
                            .rearrange("n p d -> p n d")
                            ).then_inc(wsem[dc % 2].h, 16)

                @blk.tensor
                def _(e, ng=ng, w00=w00, b0=b0, mm0=mm0, ev0=ev0):
                    if ng >= 1:
                        e.wait_ge(y_ev.h, ev0)
                    for dc in range(KC):
                        e.wait_ge(wsem[dc % 2].h,
                                  w00[dc % 2] + 16 * (dc // 2 + 1))
                        for half in range(2):
                            hs = slice(half * 512, (half + 1) * 512)
                            for j in range(4):
                                n = ng * 4 + j
                                mi = e.matmul(
                                    yps[32 * j:32 * j + 32, half, :],
                                    slots_bf[:, dc, n * 32:(n + 1) * 32],
                                    w4[:, dc % 2, j, hs],
                                    start=(dc == 0), stop=False,
                                    tile_position=(0, 32 * j))
                        mi.then_inc(y_mm.h, 1)
                    e.wait_ge(bsem[ng % 2].h, b0 + 16)
                    for half in range(2):
                        hs = slice(half * 512, (half + 1) * 512)
                        for j in range(4):
                            mi = e.matmul(
                                yps[32 * j:32 * j + 32, half, :],
                                c["ones_row_bf"][:, 0:32],
                                brow[0:1, ng % 2, j, hs],
                                start=False, stop=True,
                                tile_position=(0, 32 * j))
                        mi.then_inc(y_mm.h, 1)
                y_mm.v += KC + 2
                wsem[0].v += 16 * 4
                wsem[1].v += 16 * 4
                bsem[ng % 2].v += 16

                @blk.vector
                def _(e, ng=ng, mm0=mm0):
                    for half in range(2):
                        e.wait_ge(y_mm.h, mm0 + KC + half + 1)
                        mi = e.tensor_copy(
                            y_sb[:, ng, half * 512:(half + 1) * 512],
                            yps[:, half, :])
                    mi.then_inc(y_ev.h, 1)
                y_ev.v += 1

    if stage == "g4":
        dsem = kb.sem()
        with nc.Block() as blk:
            @blk.gpsimd
            def _(e):
                n0 = dsem.v
                for q in range(16):
                    e.dma_start(out=dbg[:, q, 0:1024],
                                in_=y_sb[:, q, :]).then_inc(dsem.h, 16)
                e.wait_ge(dsem.h, n0 + 256)
        dsem.v += 256
        return

    with kb.scope() as sc:
        comb = sc.sb([128, 16, 512], BF16, "comb")
        en2 = sc.sb([128, 2, 512], BF16, "en2")
        x3c = sc.sb([128, 4, D], F32, "x3c")
        outt = sc.sb([128, 4, D], F32, "outt")
        ops_ = sc.ps([128, 2, 512], F32, name="ops_g5")
        bcp = sc.ps([128, 2, 512], F32, name="bcp_g5")
        g_ld = kb.sem("g5ld")
        g_ld2 = kb.sem("g5ld2")
        g_mm = kb.sem("g5mm")
        g_dv = kb.sem("g5dv")
        g_st = kb.sem("g5st")
        for mc in range(NMC):
            ms = slice(mc * MC, (mc + 1) * MC)
            ld0, ld20, mm0, dv0, st0 = (g_ld.v, g_ld2.v, g_mm.v, g_dv.v,
                                        g_st.v)
            with nc.Block() as blk:
                @blk.sync
                def _(e, mc=mc, ld20=ld20, dv0=dv0, st0=st0):
                    for mt in range(4):
                        t = mc * 4 + mt
                        e.dma_start(out=x3c[:, mt, :],
                                    in_=x3[t * 128:(t + 1) * 128, :]
                                    ).then_inc(g_ld2.h, 16)
                    e.wait_ge(g_dv.h, dv0 + 16 + 8)
                    for mt in range(4):
                        t = mc * 4 + mt
                        e.dma_start(out=xout[t * 128:(t + 1) * 128, :],
                                    in_=outt[:, mt, :]).then_inc(g_st.h, 16)
                    e.wait_ge(g_st.h, st0 + 64)

                @blk.gpsimd
                def _(e, mc=mc, ld0=ld0, dv0=dv0):
                    for npc in range(16):
                        if npc >= 2:
                            e.wait_ge(g_dv.h, dv0 + npc - 1)
                        e.dma_start(
                            out=en2[:, npc % 2, :],
                            in_=e_dram.ap()[npc * 128:(npc + 1) * 128, ms]
                            ).then_inc(g_ld.h, 16)

                @blk.tensor
                def _(e, mc=mc, ld0=ld0, mm0=mm0, dv0=dv0):
                    for npc in range(16):
                        mi = e.matmul(bcp[:, npc % 2, :], c["ones_row_bf"][:],
                                      rows_rc[:, mc, :], start=True, stop=True)
                        mi.then_inc(g_mm.h, 1)
                        e.wait_ge(g_dv.h, dv0 + npc + 1)
                    for mt in range(4):
                        for half in range(2):
                            hs = slice(half * 512, (half + 1) * 512)
                            gi = 2 * mt + half
                            for npc in range(16):
                                mi = e.matmul(
                                    ops_[:, gi % 2, :],
                                    comb[:, npc, mt * 128:(mt + 1) * 128],
                                    y_sb[:, npc, hs],
                                    start=(npc == 0), stop=(npc == 15))
                            mi.then_inc(g_mm.h, 1)
                            if gi < 7:
                                e.wait_ge(g_dv.h, dv0 + 16 + gi)

                @blk.vector
                def _(e, mc=mc, ld0=ld0, ld20=ld20, mm0=mm0, dv0=dv0):
                    e.wait_ge(g_ld2.h, ld20 + 64)
                    for npc in range(16):
                        e.wait_ge(g_ld.h, ld0 + 16 * (npc + 1))
                        e.wait_ge(g_mm.h, mm0 + npc + 1)
                        i1 = e.tensor_tensor(comb[:, npc, :],
                                             en2[:, npc % 2, :],
                                             bcp[:, npc % 2, :], ALU.mult)
                        i1.then_inc(g_dv.h, 1)
                    for mt in range(4):
                        for half in range(2):
                            hs = slice(half * 512, (half + 1) * 512)
                            gi = 2 * mt + half
                            e.wait_ge(g_mm.h, mm0 + 16 + gi + 1)
                            i1 = e.tensor_tensor(outt[:, mt, hs],
                                                 ops_[:, gi % 2, :],
                                                 x3c[:, mt, hs], ALU.add)
                            i1.then_inc(g_dv.h, 1)
            g_ld.v += 16 * 16
            g_ld2.v += 64
            g_mm.v += 16 + 8
            g_dv.v += 16 + 8
            g_st.v += 64


def build_l1_ffn(nc, n_cores=8, stage="full"):
    kb = KB(nc)
    x3 = nc.dram_tensor("x3", [T, D], F32, kind="ExternalInput").ap()
    win = {
        "phi": nc.dram_tensor("phi", [128, KC, 2048], F32,
                              kind="ExternalInput").ap(),
        "w": nc.dram_tensor("w", [64, 128, KC, D], F32, kind="ExternalInput"),
        "b": nc.dram_tensor("b", [64, D], F32, kind="ExternalInput"),
    }
    xout = nc.dram_tensor("xout", [T, D], F32, kind="ExternalOutput").ap()
    dbg = (nc.dram_tensor("dbg", [128, 16, 2048], F32,
                          kind="ExternalOutput").ap()
           if stage != "full" else None)
    emit_l1_ffn(kb, x3, win, xout, n_cores=n_cores, stage=stage, dbg=dbg)
    kb.ctx.close()
    return nc


def prep_l1f(ins, core, x3=None):
    return {
        "x3": np.ascontiguousarray(x3) if x3 is not None else None,
        "phi": np.ascontiguousarray(
            ins["L1_ffn_phi"].reshape(KC, 128, 2048).transpose(1, 0, 2)),
        "w": np.ascontiguousarray(
            ins["L1_ffn_W"].reshape(64, KC, 128, D).transpose(0, 2, 1, 3)),
        "b": np.ascontiguousarray(ins["L1_ffn_b"]),
    }


def build_fused(nc):
    kb = KB(nc)
    x0 = nc.dram_tensor("x0", [T, D], F32, kind="ExternalInput").ap()
    out = nc.dram_tensor("xout", [T, D], F32, kind="ExternalOutput").ap()
    x1 = nc.dram_tensor("x1i", [T, D], F32).ap()
    x2 = nc.dram_tensor("x2i", [T, D], F32).ap()
    x3 = nc.dram_tensor("x3i", [T, D], F32).ap()

    win1 = {}
    for nm in ("wq", "wk", "wv", "wo"):
        win1[nm] = nc.dram_tensor(nm, [128, KC, D], F32,
                                  kind="ExternalInput").ap()
    for nm in ("bq", "bk", "bv", "bo"):
        win1[nm] = nc.dram_tensor(nm, [128, KC], F32,
                                  kind="ExternalInput").ap()
    win2 = {
        "gate": nc.dram_tensor("gate", [128, KC, 8], F32,
                               kind="ExternalInput").ap(),
        "wg": nc.dram_tensor("wg", [8, 128, KC, 2048], F32,
                             kind="ExternalInput"),
        "wu": nc.dram_tensor("wu", [8, 128, KC, 2048], F32,
                             kind="ExternalInput"),
        "wd": nc.dram_tensor("wd", [8, 128, 16, 1024], F32,
                             kind="ExternalInput"),
    }
    win3 = {"pp": nc.dram_tensor("pp", [T, D], F32, kind="ExternalInput").ap()}
    for nm, shp in (("wpv", [128, KC, D]), ("wk1", [128, KC, D]),
                    ("wkv", [128, KC, D]), ("wA", [128, KC, D]),
                    ("wB", [128, KC, D]), ("wo1", [128, KC, D]),
                    ("wb1", [128, KC, 512]), ("wg2", [128, KC, 1]),
                    ("wb2", [128, 4, 2]), ("bpv", [128, KC]),
                    ("bk1", [128, KC]), ("bkv", [128, KC]),
                    ("bg1", [128, KC]), ("lng", [128, KC]),
                    ("lnb", [128, KC]), ("bo1", [128, KC]),
                    ("bb1", [128, 4]), ("bb2", [1, 2]), ("bg2", [1, 1])):
        win3[nm] = nc.dram_tensor(nm, shp, F32, kind="ExternalInput").ap()
    win4 = {
        "phi": nc.dram_tensor("phi", [128, KC, 2048], F32,
                              kind="ExternalInput").ap(),
        "w": nc.dram_tensor("w", [64, 128, KC, D], F32, kind="ExternalInput"),
        "b": nc.dram_tensor("b", [64, D], F32, kind="ExternalInput"),
    }

    for emitfn, args in ((emit_l0_attn, (kb, x0, win1, x1)),
                         (emit_l0_ffn, (kb, x1, win2, x2)),
                         (emit_l1_attn, (kb, x2, win3, x3)),
                         (emit_l1_ffn, (kb, x3, win4, out))):
        sc = kb.scope()
        sc.__enter__()
        kb.push_scope(sc)
        emitfn(*args)
        kb.pop_scope()
        sc.__exit__(None, None, None)
    kb.ctx.close()
    return nc


def prep_fused(ins, core):
    m = prep_l0a(ins, core)
    m.update(prep_l0f(ins, core))
    m.update(prep_l1a(ins, core))
    m.update(prep_l1f(ins, core))
    for k in ("x1", "x2", "x3"):
        m.pop(k, None)
    return m


_NC_CACHE = {}


def _get_nc(name, builder):
    if name not in _NC_CACHE:
        nc = bass.Bass("TRN2", target_bir_lowering=False)
        builder(nc)
        _NC_CACHE[name] = nc
    return _NC_CACHE[name]


def kernel(**inputs):
    ins = inputs
    cores = list(range(8))
    nc = _get_nc("fused", build_fused)
    maps = [prep_fused(ins, c_) for c_ in cores]
    res = run_bass_kernel_spmd(nc, maps, core_ids=cores)
    return np.stack([res.results[c_]["xout"] for c_ in cores])


def kernel_multi(**inputs):
    ins = inputs
    ncores = 8
    cores = list(range(ncores))

    nc1 = _get_nc("l0a", build_l0_attn)
    m0 = prep_l0a(ins, 0)
    maps = []
    for c_ in cores:
        m = dict(m0)
        m["x0"] = np.ascontiguousarray(ins["L0_attn_x"][c_])
        maps.append(m)
    r1 = run_bass_kernel_spmd(nc1, maps, core_ids=cores)
    x1s = [r1.results[c_]["x1"] for c_ in cores]

    nc2 = _get_nc("l0f", build_l0_ffn)
    m0 = prep_l0f(ins, 0)
    maps = []
    for c_ in cores:
        m = dict(m0)
        m["x1"] = np.ascontiguousarray(x1s[c_])
        maps.append(m)
    r2 = run_bass_kernel_spmd(nc2, maps, core_ids=cores)
    x2s = [r2.results[c_]["x2"] for c_ in cores]

    nc3 = _get_nc("l1a", build_l1_attn)
    m0 = prep_l1a(ins, 0)
    maps = []
    for c_ in cores:
        m = dict(m0)
        m["x2"] = np.ascontiguousarray(x2s[c_])
        maps.append(m)
    r3 = run_bass_kernel_spmd(nc3, maps, core_ids=cores)
    x3s = [r3.results[c_]["x3"] for c_ in cores]

    nc4 = _get_nc("l1f", build_l1_ffn)
    m0 = prep_l1f(ins, 0)
    maps = []
    for c_ in cores:
        m = dict(m0)
        m["x3"] = np.ascontiguousarray(x3s[c_])
        maps.append(m)
    r4 = run_bass_kernel_spmd(nc4, maps, core_ids=cores)
    out = np.stack([r4.results[c_]["xout"] for c_ in cores])
    return out





# version 05
# speedup vs baseline: 1.1785x; 1.0945x over previous
import math
from contextlib import ExitStack

import numpy as np
import concourse.bass as bass
import concourse.mybir as mybir
from concourse.bass_utils import run_bass_kernel_spmd
from concourse.bass_core_rust import IndirectOffsetOnAxis

F32 = mybir.dt.float32
BF16 = mybir.dt.bfloat16
U32 = mybir.dt.uint32
I32 = mybir.dt.int32
AF = mybir.ActivationFunctionType
ALU = mybir.AluOpType
AXX = mybir.AxisListType.X

T = 4096
D = 1024
NT = 32
KC = 8
PI = math.pi
INV2PI = 1.0 / (2.0 * PI)


class Sem:
    _n = [0]

    def __init__(self, ctx, nc, name):
        Sem._n[0] += 1
        self.h = ctx.enter_context(nc.semaphore(f"{name}_{Sem._n[0]}"))
        self.v = 0


class Scope:
    def __init__(self, kb):
        self.kb = kb
        self.ctx = ExitStack()

    def sb(self, shape, dt, name=None):
        return self.ctx.enter_context(
            self.kb.nc.sbuf_tensor(self.kb.name((name or "sb") + "_"),
                                   shape, dt))

    def ps(self, shape, dt=F32, name=None):
        return self.ctx.enter_context(
            self.kb.nc.psum_tensor(self.kb.name((name or "ps") + "_"),
                                   shape, dt))

    def __enter__(self):
        return self

    def __exit__(self, *a):
        self.ctx.close()
        return False


class KB:
    def __init__(self, nc):
        self.nc = nc
        self.ctx = ExitStack()
        self._n = 0

    def name(self, p="t"):
        self._n += 1
        return f"{p}{self._n}"

    def sb(self, shape, dt, name=None):
        tgt = self._redirect[-1].ctx if getattr(self, "_redirect", None) else self.ctx
        return tgt.enter_context(
            self.nc.sbuf_tensor(self.name((name or "sb") + "_"), shape, dt))

    def ps(self, shape, dt=F32, name=None):
        tgt = self._redirect[-1].ctx if getattr(self, "_redirect", None) else self.ctx
        return tgt.enter_context(
            self.nc.psum_tensor(self.name((name or "ps") + "_"), shape, dt))

    def push_scope(self, sc):
        if not hasattr(self, "_redirect"):
            self._redirect = []
        self._redirect.append(sc)

    def pop_scope(self):
        self._redirect.pop()

    def sem(self, name=None):
        return Sem(self.ctx, self.nc, name or self.name("sem"))

    def nsem(self, name):
        if not hasattr(self, "_nsems"):
            self._nsems = {}
        if name not in self._nsems:
            self._nsems[name] = self.sem(name)
        return self._nsems[name]

    def esem(self, ename):
        if not hasattr(self, "_esems"):
            self._esems = {}
        if ename not in self._esems:
            self._esems[ename] = self.sem("edge_" + ename)
        return self._esems[ename]

    def edge(self, ename, inst, e):
        s = self.esem(ename)
        inst.then_inc(s.h, 1)
        s.v += 1
        e.wait_ge(s.h, s.v)

    def scope(self):
        return Scope(self)


def build_consts(kb, want, sc=None):
    nc = kb.nc
    al = sc if sc is not None else kb
    c = {}
    alloc = {
        "ident_bf": ([128, 128], BF16), "ident_f": ([128, 128], F32),
        "ones_row_bf": ([1, 128], BF16), "ones_row_f": ([1, 128], F32),
        "ones_col_f": ([128, 1], F32), "uexc_f": ([128, 128], F32),
        "allones_f": ([128, 128], F32), "pos_tm": ([128, 32], F32),
        "indic_t": ([128, 8], F32), "indic8": ([8, 128], F32),
        "ones_col_bf": ([128, 1], BF16),
    }
    for w in want:
        c[w] = al.sb(*alloc[w], name="c_" + w)
    need_io = any(w in want for w in ("ident_bf", "ident_f", "uexc_f"))
    io = al.sb([128, 128], F32, "c_io") if need_io else None
    s1 = (al.sb([128, 136], F32, "c_s1")
          if ("indic_t" in want or "indic8" in want) else None)
    with nc.Block() as blk:
        @blk.gpsimd
        def _(g):
            if io is not None:
                g.iota(io[:], [[1, 128]], channel_multiplier=-1,
                       allow_small_or_imprecise_dtypes=True)
            if "pos_tm" in c:
                g.iota(c["pos_tm"][:], [[1, 32]], channel_multiplier=32,
                       allow_small_or_imprecise_dtypes=True)
            if "indic_t" in c:
                g.iota(s1[:, 0:8], [[-16, 8]], channel_multiplier=1,
                       allow_small_or_imprecise_dtypes=True)
            if "indic8" in c:
                g.iota(s1[:, 8:136], [[1, 128]], channel_multiplier=-16,
                       allow_small_or_imprecise_dtypes=True)
    with nc.Block() as blk:
        @blk.vector
        def _(v):
            if "ident_bf" in c:
                v.tensor_scalar(c["ident_bf"][:], io[:], 0.0, None, ALU.is_equal)
            if "ident_f" in c:
                v.tensor_scalar(c["ident_f"][:], io[:], 0.0, None, ALU.is_equal)
            if "uexc_f" in c:
                v.tensor_scalar(c["uexc_f"][:], io[:], 0.0, None, ALU.is_gt)
            for nm in ("ones_row_bf", "ones_row_f", "ones_col_f", "allones_f",
                       "ones_col_bf"):
                if nm in c:
                    v.memset(c[nm][:], 1.0)
            if "indic_t" in c:
                a = s1[:, 0:8]
                b_ = s1[:, 136:144] if False else None
                i1 = v.tensor_scalar(c["indic_t"][:], a, -0.5, None, ALU.is_gt)
                kb.edge("v", i1, v)
                i1 = v.tensor_scalar(s1[:, 0:8], a, 15.5, None, ALU.is_lt)
                kb.edge("v", i1, v)
                i1 = v.tensor_mul(c["indic_t"][:], c["indic_t"][:], s1[:, 0:8])
                kb.edge("v", i1, v)
            if "indic8" in c:
                a = s1[0:8, 8:136]
                i1 = v.tensor_scalar(c["indic8"][:], a, -0.5, None, ALU.is_gt)
                kb.edge("v", i1, v)
                i1 = v.tensor_scalar(s1[0:8, 8:136], a, 15.5, None, ALU.is_lt)
                kb.edge("v", i1, v)
                i1 = v.tensor_mul(c["indic8"][:], c["indic8"][:], s1[0:8, 8:136])
                kb.edge("v", i1, v)
    return c


def emit_ln(kb, x_dram, out_slice, keep_x=None, nt=NT, group=8, tile_off=0):
    nc = kb.nc
    with kb.scope() as sc:
        xg = sc.sb([128, group, D], F32)
        sq = sc.sb([128, 2, D], F32)
        st = sc.sb([128, group, 8], F32)
        ld = kb.nsem("ln_ld")
        s1 = kb.nsem("ln_s1")
        v1 = kb.nsem("ln_v1")
        for g0 in range(0, nt, group):
            gn = min(group, nt - g0)
            ld0, s10, v10 = ld.v, s1.v, v1.v
            with nc.Block() as blk:
                @blk.sync
                def _(e, g0=g0, gn=gn):
                    for i in range(gn):
                        t = tile_off + g0 + i
                        e.dma_start(out=xg[:, i, :],
                                    in_=x_dram[t * 128:(t + 1) * 128, :]
                                    ).then_inc(ld.h, 16)

                @blk.scalar
                def _(e, g0=g0, gn=gn, ld0=ld0, s10=s10, v10=v10):
                    e.wait_ge(ld.h, ld0 + 16 * gn)
                    for i in range(gn):
                        e.activation(sq[:, i % 2, :], xg[:, i, :], AF.Square,
                                     accum_out=st[:, i, 0:1]).then_inc(s1.h, 1)
                    e.wait_ge(v1.h, v10 + 1)
                    e.activation(st[:, 0:gn, 5:6], st[:, 0:gn, 4:5],
                                 AF.Sqrt).then_inc(s1.h, 1)

                @blk.vector
                def _(e, g0=g0, gn=gn, ld0=ld0, s10=s10, v10=v10):
                    e.wait_ge(ld.h, ld0 + 16 * gn)
                    for i in range(gn):
                        ri = e.reduce_sum(st[:, i, 1:2], xg[:, i, :], axis=AXX)
                    kb.edge("v", ri, e)
                    e.wait_ge(s1.h, s10 + gn)
                    g_ = st[:, 0:gn, :]
                    i1 = e.tensor_scalar(g_[:, :, 2:3], g_[:, :, 1:2], 1.0 / D,
                                         None, ALU.mult)
                    kb.edge("v", i1, e)
                    i1 = e.tensor_tensor(g_[:, :, 3:4], g_[:, :, 2:3],
                                         g_[:, :, 2:3], ALU.mult)
                    kb.edge("v", i1, e)
                    i1 = e.scalar_tensor_tensor(g_[:, :, 4:5], g_[:, :, 0:1],
                                                1.0 / D, g_[:, :, 3:4],
                                                ALU.mult, ALU.subtract)
                    kb.edge("v", i1, e)
                    e.tensor_scalar(g_[:, :, 4:5], g_[:, :, 4:5], 1e-5, None,
                                    ALU.add).then_inc(v1.h, 1)
                    e.wait_ge(s1.h, s10 + gn + 1)
                    i1 = e.reciprocal(g_[:, :, 6:7], g_[:, :, 5:6])
                    kb.edge("v", i1, e)
                    i1 = e.tensor_tensor(g_[:, :, 7:8], g_[:, :, 2:3],
                                         g_[:, :, 6:7], ALU.mult)
                    kb.edge("v", i1, e)
                    i1 = e.tensor_scalar(g_[:, :, 7:8], g_[:, :, 7:8], -1.0, None,
                                         ALU.mult)
                    kb.edge("v", i1, e)
                    for i in range(gn):
                        e.tensor_scalar(out_slice(g0 + i), xg[:, i, :],
                                        st[:, i, 6:7], st[:, i, 7:8],
                                        ALU.mult, ALU.add)
                    if keep_x is not None:
                        for i in range(gn):
                            e.tensor_copy(keep_x(g0 + i), xg[:, i, :])
            ld.v += 16 * gn
            s1.v += gn + 1
            v1.v += 1


def emit_transposes(kb, pairs, ident_bf, grp=2):
    nc = kb.nc
    groups = [pairs[i:i + grp] for i in range(0, len(pairs), grp)]
    with kb.scope() as sc:
        ps = sc.ps([128, 2 * grp, 512], F32)
        pe = kb.nsem("tp_pe")
        ev = kb.nsem("tp_ev")
        pe0, ev0 = pe.v, ev.v
        with nc.Block() as blk:
            @blk.tensor
            def _(e):
                for gi, grp_pairs in enumerate(groups):
                    if gi >= 2:
                        e.wait_ge(ev.h, ev0 + gi - 1)
                    base = (gi % 2) * grp
                    for j, (src, dst) in enumerate(grp_pairs):
                        mi = e.matmul(ps[:, base + j, 0:128], src, ident_bf[:],
                                      start=True, stop=True)
                    mi.then_inc(pe.h, 1)

            @blk.vector
            def _(e):
                for gi, grp_pairs in enumerate(groups):
                    e.wait_ge(pe.h, pe0 + gi + 1)
                    base = (gi % 2) * grp
                    for j, (src, dst) in enumerate(grp_pairs):
                        mi = e.tensor_copy(dst, ps[:, base + j, 0:128])
                    mi.then_inc(ev.h, 1)
        pe.v += len(groups)
        ev.v += len(groups)


def emit_l0_attn(kb, x0, win, x1, stage="wo", dbg=None):
    nc = kb.nc
    c = build_consts(kb, ["ident_bf", "ident_f", "ones_row_f"])
    ctxd = nc.dram_tensor(kb.name("ctxd"), [KC, 128, T], BF16)

    bsb = {}
    for nm in ("bq", "bk", "bv", "bo"):
        bsb[nm] = kb.sb([128, KC], F32, nm + "_sb")
    wl = kb.sem("wl")
    with nc.Block() as blk:
        @blk.gpsimd
        def _(g):
            for nm in ("bq", "bk", "bv", "bo"):
                g.dma_start(out=bsb[nm][:], in_=win[nm]).then_inc(wl.h, 16)
            g.wait_ge(wl.h, wl.v + 64)
    wl.v += 64

    c4 = kb.sb([128, T], BF16, "c4")
    s4 = kb.sb([128, T], BF16, "s4")
    with kb.scope() as sc:
        ang = sc.sb([32, T], F32, "ang")
        carg = sc.sb([32, T], F32, "carg")
        iscr = sc.sb([32, T], I32, "iscr")
        invf = sc.sb([1, 32], F32, "invf")
        posr = sc.sb([1, T], F32, "posr")
        c32 = sc.sb([32, T], F32, "c32")
        s32 = sc.sb([32, T], F32, "s32")
        aps = sc.ps([32, 512], F32, name="angps")
        with nc.Block() as blk:
            @blk.gpsimd
            def _(g):
                g.iota(invf[:], [[1, 32]], channel_multiplier=0,
                       allow_small_or_imprecise_dtypes=True)
                g.iota(posr[:], [[1, T]], channel_multiplier=0,
                       allow_small_or_imprecise_dtypes=True)
        with nc.Block() as blk:
            @blk.scalar
            def _(e):
                e.activation(invf[:], invf[:], AF.Exp,
                             scale=-math.log(10000.0) / 32.0)
        sm = kb.sem("angmm")
        with nc.Block() as blk:
            @blk.tensor
            def _(e):
                for q in range(T // 512):
                    if q >= 1:
                        e.wait_ge(sm.h, 2 * q)
                    e.matmul(aps[:], invf[:], posr[:, q * 512:(q + 1) * 512],
                             start=True, stop=True).then_inc(sm.h, 1)

            @blk.vector
            def _(e):
                for q in range(T // 512):
                    e.wait_ge(sm.h, 2 * q + 1)
                    e.tensor_copy(ang[:, q * 512:(q + 1) * 512],
                                  aps[:]).then_inc(sm.h, 1)
        sm.v += 2 * (T // 512)
        w32 = sc.sb([32, T], F32, "w32")
        with nc.Block() as blk:
            @blk.vector
            def _(e):
                i1 = e.tensor_scalar(carg[:], ang[:], INV2PI, None, ALU.mult)
                kb.edge("v", i1, e)
                i1 = e.tensor_copy(iscr[:], carg[:])
                kb.edge("v", i1, e)
                i1 = e.tensor_copy(carg[:], iscr[:])
                kb.edge("v", i1, e)
                i1 = e.scalar_tensor_tensor(w32[:], carg[:], -2.0 * PI, ang[:],
                                            ALU.mult, ALU.add)
                kb.edge("v", i1, e)
                i1 = e.tensor_scalar(c32[:], w32[:], PI, None, ALU.is_gt)
                kb.edge("v", i1, e)
                i1 = e.scalar_tensor_tensor(w32[:], c32[:], -2.0 * PI, w32[:],
                                            ALU.mult, ALU.add)
                kb.edge("v", i1, e)
                i1 = e.tensor_scalar(c32[:], w32[:], -PI, None, ALU.is_lt)
                kb.edge("v", i1, e)
                i1 = e.scalar_tensor_tensor(w32[:], c32[:], 2.0 * PI, w32[:],
                                            ALU.mult, ALU.add)
                kb.edge("v", i1, e)
                i1 = e.tensor_scalar(carg[:], w32[:], PI / 2, None, ALU.add)
                kb.edge("v", i1, e)
                i1 = e.tensor_scalar(c32[:], carg[:], PI, None, ALU.is_gt)
                kb.edge("v", i1, e)
                e.scalar_tensor_tensor(carg[:], c32[:], -2.0 * PI, carg[:],
                                       ALU.mult, ALU.add)
        with nc.Block() as blk:
            @blk.scalar
            def _(e):
                i1 = e.activation(s32[:], w32[:], AF.Sin)
                kb.edge("s", i1, e)
                e.activation(c32[:], carg[:], AF.Sin)
        rp = kb.sem("rep")
        with nc.Block() as blk:
            @blk.gpsimd
            def _(e):
                n = rp.v
                for r in range(4):
                    e.dma_start(out=c4[r * 32:(r + 1) * 32, :],
                                in_=c32[:]).then_inc(rp.h, 16)
                    e.dma_start(out=s4[r * 32:(r + 1) * 32, :],
                                in_=s32[:]).then_inc(rp.h, 16)
                e.wait_ge(rp.h, n + 128)
        rp.v += 128

    if stage == "trig":
        dsem = kb.sem()
        with nc.Block() as blk:
            @blk.gpsimd
            def _(e):
                e.dma_start(out=dbg[:, 0, :], in_=c4[:]).then_inc(dsem.h, 16)
                e.dma_start(out=dbg[:, 1, :], in_=s4[:]).then_inc(dsem.h, 16)
                e.wait_ge(dsem.h, 32)
        return

    lnT = kb.sb([128, KC, T], BF16, "lnT")
    with kb.scope() as sc:
        lnw = sc.sb([128, NT, D], BF16, "lnw")
        emit_ln(kb, x0, lambda t: lnw[:, t, :])
        if stage == "ln0":
            dsem = kb.sem()
            with nc.Block() as blk:
                @blk.gpsimd
                def _(e):
                    for t in range(8):
                        e.dma_start(out=dbg[:, t, 0:D],
                                    in_=lnw[:, t, :]).then_inc(dsem.h, 16)
                    e.wait_ge(dsem.h, 128)
            return
        pairs = [(lnw[:, t, k * 128:(k + 1) * 128],
                  lnT[:, k, t * 128:(t + 1) * 128])
                 for t in range(NT) for k in range(KC)]
        import os
        ntp = int(os.environ.get("NTP", "0"))
        if ntp:
            pairs = pairs[:ntp]
        emit_transposes(kb, pairs, c["ident_bf"])

    def dump(srcs):
        dsem = kb.sem()
        with nc.Block() as blk:
            @blk.gpsimd
            def _(e):
                n = 0
                for sap, dap in srcs:
                    e.dma_start(out=dap, in_=sap).then_inc(dsem.h, 16)
                    n += 16
                e.wait_ge(dsem.h, dsem.v + n)
        dsem.v += 16 * len(srcs)

    if stage == "ln":
        import os
        nd = int(os.environ.get("DUMPN", "8"))
        if nd:
            dump([(lnT[:, k, :], dbg[:, k, :]) for k in range(nd)])
        return

    s_wg = kb.sem("s_wg")
    s_mm = kb.sem("s_mm")
    s_ev = kb.sem("s_ev")
    s_tp = kb.sem("s_tp")
    s_cp = kb.sem("s_cp")
    f_mm = kb.sem("f_mm")
    f_ex = kb.sem("f_ex")
    f_nr = kb.sem("f_nr")
    f_st = kb.sem("f_st")
    NKT = 32
    for g in range(4):
        with kb.scope() as sc:
            wqg = sc.sb([128, KC, 2, 128], BF16, f"wqg{g}")
            wkg = sc.sb([128, KC, 2, 128], BF16, f"wkg{g}")
            wvg = sc.sb([128, KC, 256], BF16, f"wvg{g}")
            qg = sc.sb([128, 2, T], BF16, f"qg{g}")
            kg = sc.sb([128, 2, T], BF16, f"kg{g}")
            v2 = sc.sb([128, NKT, 260], BF16, f"v2g{g}")
            ctxs = sc.sb([128, 2, T], BF16, f"ctxs{g}")
            stg = sc.sb([128, 2, 512], F32, f"stg{g}")
            rs1 = sc.sb([128, 512], F32, f"rs1{g}")
            rs2 = sc.sb([128, 512], F32, f"rs2{g}")
            vstg = sc.sb([128, 2, 512], BF16, f"vstg{g}")
            pT = sc.sb([128, 2, 4, 512], BF16, f"pT{g}")
            rec = sc.sb([1, 4, 512], F32, f"rec{g}")
            ctm = sc.sb([64, 4, 512], F32, f"ctm{g}")
            scp = kb.scope()
            scp.__enter__()
            mmps = scp.ps([128, 2, 512], F32, name=f"mmps{g}")
            tps = scp.ps([128, 2, 512], F32, name=f"tps{g}")

            wg0 = s_wg.v
            with nc.Block() as blk:
                @blk.gpsimd
                def _(e, g=g, wg0=wg0):
                    for dst, src in ((wqg, win["wq"]), (wkg, win["wk"])):
                        ap = bass.AP(src.tensor, src.offset + g * 128,
                                     [src.ap[0], src.ap[1], [512, 2], [1, 128]])
                        e.dma_start(out=dst[:], in_=ap).then_inc(s_wg.h, 16)
                    ap = bass.AP(win["wv"].tensor,
                                 win["wv"].offset + 2 * g * 128,
                                 [win["wv"].ap[0], win["wv"].ap[1], [1, 256]])
                    e.dma_start(out=wvg[:], in_=ap).then_inc(s_wg.h, 16)
                    e.wait_ge(s_wg.h, wg0 + 48)

                @blk.vector
                def _(e):
                    for pr in range(2):
                        for off in (64, 129):
                            e.memset(v2[:, :, pr * 130 + off], 1.0)
            s_wg.v += 48

            import os
            _pn = int(os.environ.get("PROJN", "8"))
            _pk = os.environ.get("PROJK", "qkv")
            for qc in range(_pn):
                qs = slice(qc * 512, (qc + 1) * 512)
                for (kind, wsl, bname, outT) in (("q", wqg, "bq", qg),
                                                 ("k", wkg, "bk", kg),
                                                 ("v", wvg, "bv", None)):
                    if kind not in _pk:
                        continue
                    mm0, ev0, tp0, cp0 = s_mm.v, s_ev.v, s_tp.v, s_cp.v
                    with nc.Block() as blk:
                        @blk.tensor
                        def _(e, g=g, qs=qs, kind=kind, wsl=wsl,
                              mm0=mm0, ev0=ev0, cp0=cp0):
                            for cl in range(2):
                                for k in range(KC):
                                    src = (wsl[:, k, cl, :] if kind != "v"
                                           else wsl[:, k, cl * 128:(cl + 1) * 128])
                                    mi = e.matmul(mmps[:, cl, :], src,
                                                  lnT[:, k, qs],
                                                  start=(k == 0),
                                                  stop=(k == KC - 1))
                                mi.then_inc(s_mm.h, 1)
                            if kind == "v" and os.environ.get("PROJV", "full") != "mm":
                                i = 0
                                for cl in range(2):
                                    e.wait_ge(s_ev.h, ev0 + cl + 1)
                                    for t4 in range(4):
                                        if i >= 2:
                                            e.wait_ge(s_cp.h, cp0 + i - 1)
                                        e.matmul(
                                            tps[:, i % 2, 0:128],
                                            vstg[:, cl, t4 * 128:(t4 + 1) * 128],
                                            c["ident_bf"][:], start=True,
                                            stop=True).then_inc(s_tp.h, 1)
                                        i += 1

                        @blk.vector
                        def _(e, g=g, qc=qc, qs=qs, kind=kind, bname=bname,
                              outT=outT, mm0=mm0, tp0=tp0):
                            for cl in range(2):
                                e.wait_ge(s_mm.h, mm0 + cl + 1)
                                if kind == "v":
                                    bcol = 2 * g + cl
                                    e.tensor_scalar(vstg[:, cl, :],
                                                    mmps[:, cl, :],
                                                    bsb["bv"][:, bcol:bcol + 1],
                                                    None, ALU.add
                                                    ).then_inc(s_ev.h, 1)
                                else:
                                    bcol = g + 4 * cl
                                    e.tensor_scalar(stg[:, cl, :],
                                                    mmps[:, cl, :],
                                                    bsb[bname][:, bcol:bcol + 1],
                                                    None, ALU.add
                                                    ).then_inc(s_ev.h, 1)
                            if kind != "v":
                                x1a, x2a = stg[:, 0, :], stg[:, 1, :]
                                cb, sb_ = c4[:, qs], s4[:, qs]
                                kb.edge("v", e.tensor_scalar(
                                    rs1[:], x1a, 1.0, None, ALU.mult), e)
                                e.tensor_tensor(rs1[:], x1a, cb, ALU.mult)
                                i1 = e.tensor_tensor(rs2[:], x2a, sb_, ALU.mult)
                                kb.edge("v", i1, e)
                                i1 = e.tensor_tensor(outT[:, 0, qs], rs1[:],
                                                     rs2[:], ALU.subtract)
                                kb.edge("v", i1, e)
                                e.tensor_tensor(rs1[:], x1a, sb_, ALU.mult)
                                i1 = e.tensor_tensor(rs2[:], x2a, cb, ALU.mult)
                                kb.edge("v", i1, e)
                                e.tensor_tensor(outT[:, 1, qs], rs1[:], rs2[:],
                                                ALU.add)
                            elif os.environ.get("PROJV", "full") == "full":
                                kt0 = qc * 4
                                i = 0
                                for cl in range(2):
                                    for t4 in range(4):
                                        e.wait_ge(s_tp.h, tp0 + i + 1)
                                        kt = kt0 + t4
                                        e.tensor_copy(
                                            v2[:, kt, cl * 130:cl * 130 + 64],
                                            tps[:, i % 2, 0:64])
                                        e.tensor_copy(
                                            v2[:, kt, cl * 130 + 65:cl * 130 + 129],
                                            tps[:, i % 2, 64:128]
                                            ).then_inc(s_cp.h, 1)
                                        i += 1
                    s_mm.v += 2
                    s_ev.v += 2
                    if kind == "v":
                        pv_ = os.environ.get("PROJV", "full")
                        if pv_ != "mm":
                            s_tp.v += 8
                        if pv_ == "full":
                            s_cp.v += 8

            if stage == "proj" and g == 0:
                dump([(qg[:, cl, :], dbg[:, cl, :]) for cl in range(2)]
                     + [(kg[:, cl, :], dbg[:, 2 + cl, :]) for cl in range(2)]
                     + [(v2[:, kt, :], dbg[:, 4 + kt, 0:260]) for kt in range(4)])
                return

            scp.__exit__(None, None, None)
            sc_ps = sc.ps([128, 4, 512], F32, name=f"scps{g}")
            pv_ps = sc.ps([65, 4, 512], F32, name=f"pvps{g}")
            for qc in range(8):
                qs = slice(qc * 512, (qc + 1) * 512)
                mm0, ex0, nr0 = f_mm.v, f_ex.v, f_nr.v
                with nc.Block() as blk:
                    @blk.tensor
                    def _(e, g=g, qs=qs, mm0=mm0, ex0=ex0, nr0=nr0):
                        def scores(st_, kt):
                            ks = slice(kt * 128, (kt + 1) * 128)
                            for hh in range(2):
                                h4 = 2 * st_ + hh
                                r = slice(32 * h4, 32 * h4 + 32)
                                e.matmul(sc_ps[:, h4, :], kg[r, 0, ks],
                                         qg[r, 0, qs], start=True, stop=False,
                                         tile_position=(32 * h4, 0))
                                mi = e.matmul(sc_ps[:, h4, :], kg[r, 1, ks],
                                              qg[r, 1, qs], start=False,
                                              stop=True,
                                              tile_position=(32 * h4, 0))
                            mi.then_inc(f_mm.h, 1)

                        def pv(st_, kt):
                            for hh in range(2):
                                h4 = 2 * st_ + hh
                                blkc = (h4 // 2) * 130 + (h4 % 2) * 65
                                mi = e.matmul(pv_ps[:, h4, :],
                                              v2[:, kt, blkc:blkc + 65],
                                              pT[:, kt % 2, h4, :],
                                              start=(kt == 0),
                                              stop=(kt == NKT - 1))
                            mi.then_inc(f_mm.h, 1)

                        scores(0, 0)
                        scores(1, 0)
                        for kt in range(1, NKT):
                            e.wait_ge(f_ex.h, ex0 + 2 * kt - 1)
                            pv(0, kt - 1)
                            scores(0, kt)
                            e.wait_ge(f_ex.h, ex0 + 2 * kt)
                            pv(1, kt - 1)
                            scores(1, kt)
                        e.wait_ge(f_ex.h, ex0 + 2 * NKT - 1)
                        pv(0, NKT - 1)
                        e.wait_ge(f_ex.h, ex0 + 2 * NKT)
                        pv(1, NKT - 1)
                        e.wait_ge(f_nr.h, nr0 + 1)
                        for h4 in range(4):
                            mi = e.matmul(sc_ps[0:64, h4, :],
                                          c["ones_row_f"][:, 0:64],
                                          rec[:, h4, :], start=True, stop=True)
                        mi.then_inc(f_mm.h, 1)

                    @blk.scalar
                    def _(e, mm0=mm0):
                        def mmcnt_scores(st_, kt):
                            if kt == 0:
                                return mm0 + 1 + st_
                            return mm0 + 2 + 4 * (kt - 1) + 2 + 2 * st_
                        for kt in range(NKT):
                            for st_ in range(2):
                                e.wait_ge(f_mm.h, mmcnt_scores(st_, kt))
                                e.activation(pT[:, kt % 2, 2 * st_:2 * st_ + 2, :],
                                             sc_ps[:, 2 * st_:2 * st_ + 2, :],
                                             AF.Exp, scale=0.125
                                             ).then_inc(f_ex.h, 1)

                    @blk.vector
                    def _(e, g=g, qs=qs, mm0=mm0):
                        end = mm0 + 2 + 4 * (NKT - 1) + 2
                        e.wait_ge(f_mm.h, end)
                        e.reciprocal(rec[:], pv_ps[64:65, :, :])
                        e.tensor_copy(ctm[:], pv_ps[0:64, :, :]).then_inc(f_nr.h, 1)
                        e.wait_ge(f_nr.h, nr0 + 1)
                        e.wait_ge(f_mm.h, end + 1)
                        for h4 in range(4):
                            cl = h4 // 2
                            rowb = 64 * (h4 % 2)
                            e.tensor_tensor(ctxs[rowb:rowb + 64, cl, qs],
                                            ctm[:, h4, :], sc_ps[0:64, h4, :],
                                            ALU.mult)
                f_mm.v += 2 + 4 * (NKT - 1) + 2 + 1
                f_ex.v += 2 * NKT
                f_nr.v += 1

            if stage == "flash" and g == 0:
                dump([(ctxs[:, cl, :], dbg[:, cl, :]) for cl in range(2)])
                return

            st0 = f_st.v
            with nc.Block() as blk:
                @blk.sync
                def _(e, g=g, st0=st0):
                    for cl in range(2):
                        e.dma_start(out=ctxd.ap()[2 * g + cl, :, :],
                                    in_=ctxs[:, cl, :]).then_inc(f_st.h, 16)
                    e.wait_ge(f_st.h, st0 + 32)
            f_st.v += 32

    with kb.scope() as sc:
        wosb = sc.sb([128, KC, D], BF16, "wo_sb")
        cload = sc.sb([128, 2, KC, 512], BF16, "cload")
        x0t = sc.sb([128, 4, D], F32, "x0t")
        x1t = sc.sb([128, 4, D], F32, "x1t")
        ostg = sc.sb([128, 512], F32, "ostg")
        mmps = sc.ps([128, 2, 512], F32, name="mmps_o")
        tpf = sc.ps([128, 2, 512], F32, name="tpf_o")
        wl0 = wl.v
        with nc.Block() as blk:
            @blk.gpsimd
            def _(e):
                e.dma_start(out=wosb[:], in_=win["wo"]).then_inc(wl.h, 16)
                e.wait_ge(wl.h, wl0 + 16)
        wl.v += 16
        o_ld = kb.sem("o_ld")
        o_cl = kb.sem("o_cl")
        o_mm = kb.sem("o_mm")
        o_ev = kb.sem("o_ev")
        o_tp = kb.sem("o_tp")
        o_ad = kb.sem("o_ad")
        o_st = kb.sem("o_st")
        for qc in range(8):
            qs = slice(qc * 512, (qc + 1) * 512)
            ld0, cl0, mm0, ev0, tp0, ad0, st0 = (o_ld.v, o_cl.v, o_mm.v,
                                                 o_ev.v, o_tp.v, o_ad.v, o_st.v)
            with nc.Block() as blk:
                @blk.sync
                def _(e, qc=qc, ld0=ld0, cl0=cl0, ad0=ad0, st0=st0):
                    ap = bass.AP(ctxd.ap().tensor, qc * 512,
                                 [[T, 128], [128 * T, KC], [1, 512]])
                    e.dma_start(out=cload[:, qc % 2, :, :],
                                in_=ap).then_inc(o_cl.h, 16)
                    for t4 in range(4):
                        t = qc * 4 + t4
                        e.dma_start(out=x0t[:, t4, :],
                                    in_=x0[t * 128:(t + 1) * 128, :]
                                    ).then_inc(o_ld.h, 16)
                    e.wait_ge(o_ad.h, ad0 + 32)
                    for t4 in range(4):
                        t = qc * 4 + t4
                        e.dma_start(out=x1[t * 128:(t + 1) * 128, :],
                                    in_=x1t[:, t4, :]).then_inc(o_st.h, 16)
                    e.wait_ge(o_st.h, st0 + 64)

                @blk.tensor
                def _(e, qc=qc, cl0=cl0, mm0=mm0, ev0=ev0, ad0=ad0):
                    e.wait_ge(o_cl.h, cl0 + 16)
                    for m in range(KC):
                        if m >= 2:
                            e.wait_ge(o_ev.h, ev0 + m - 1)
                        for k in range(KC):
                            mi = e.matmul(mmps[:, m % 2, :],
                                          wosb[:, k, m * 128:(m + 1) * 128],
                                          cload[:, qc % 2, k, :],
                                          start=(k == 0), stop=(k == KC - 1))
                        mi.then_inc(o_mm.h, 1)
                        e.wait_ge(o_ev.h, ev0 + m + 1)
                        for t4 in range(4):
                            i = 4 * m + t4
                            if i >= 2:
                                e.wait_ge(o_ad.h, ad0 + i - 1)
                            e.matmul(tpf[:, i % 2, 0:128],
                                     ostg[:, t4 * 128:(t4 + 1) * 128],
                                     c["ident_f"][:], start=True,
                                     stop=True).then_inc(o_tp.h, 1)

                @blk.vector
                def _(e, qc=qc, mm0=mm0, tp0=tp0, ld0=ld0):
                    for m in range(KC):
                        e.wait_ge(o_mm.h, mm0 + m + 1)
                        e.tensor_scalar(ostg[:], mmps[:, m % 2, :],
                                        bsb["bo"][:, m:m + 1], None,
                                        ALU.add).then_inc(o_ev.h, 1)
                        for t4 in range(4):
                            i = 4 * m + t4
                            e.wait_ge(o_tp.h, tp0 + i + 1)
                            if m == 0 and t4 == 0:
                                e.wait_ge(o_ld.h, ld0 + 64)
                            e.tensor_tensor(x1t[:, t4, m * 128:(m + 1) * 128],
                                            tpf[:, i % 2, 0:128],
                                            x0t[:, t4, m * 128:(m + 1) * 128],
                                            ALU.add).then_inc(o_ad.h, 1)
            o_ld.v += 64
            o_cl.v += 16
            o_mm.v += KC
            o_ev.v += KC
            o_tp.v += 32
            o_ad.v += 32
            o_st.v += 64


def build_l0_attn(nc):
    kb = KB(nc)
    x0 = nc.dram_tensor("x0", [T, D], F32, kind="ExternalInput").ap()
    win = {}
    for nm in ("wq", "wk", "wv", "wo"):
        win[nm] = nc.dram_tensor(nm, [128, KC, D], F32, kind="ExternalInput").ap()
    for nm in ("bq", "bk", "bv", "bo"):
        win[nm] = nc.dram_tensor(nm, [128, KC], F32, kind="ExternalInput").ap()
    x1 = nc.dram_tensor("x1", [T, D], F32, kind="ExternalOutput").ap()
    emit_l0_attn(kb, x0, win, x1)
    kb.ctx.close()
    return nc


def _dm_w(w):
    dout = w.shape[1]
    return np.ascontiguousarray(w.reshape(KC, 128, dout).transpose(1, 0, 2))


def _dm_b(b):
    n = b.shape[0] // 128
    return np.ascontiguousarray(b.reshape(n, 128).T)


def _qk_perm():
    perm = np.zeros(D, np.int64)
    for h in range(16):
        for half in range(2):
            for f in range(32):
                nat = h * 64 + half * 32 + f
                pos = (h // 4 + 4 * half) * 128 + (h % 4) * 32 + f
                perm[pos] = nat
    return perm


_QKPERM = _qk_perm()


def prep_l0a(ins, core):
    return {
        "x0": np.ascontiguousarray(ins["L0_attn_x"][core]),
        "wq": _dm_w(ins["L0_attn_Wq"][:, _QKPERM]),
        "wk": _dm_w(ins["L0_attn_Wk"][:, _QKPERM]),
        "wv": _dm_w(ins["L0_attn_Wv"]),
        "wo": _dm_w(ins["L0_attn_Wo"]),
        "bq": _dm_b(ins["L0_attn_bq"][_QKPERM]),
        "bk": _dm_b(ins["L0_attn_bk"][_QKPERM]),
        "bv": _dm_b(ins["L0_attn_bv"]),
        "bo": _dm_b(ins["L0_attn_bo"]),
    }




def emit_l0_ffn(kb, x1, win, x2, stage="full", dbg=None):
    nc = kb.nc
    E = 8
    KCAP = 1024
    c = build_consts(kb, ["ident_bf", "ident_f", "ones_row_f", "uexc_f",
                          "allones_f", "pos_tm", "indic_t", "indic8"])
    compactx = [nc.dram_tensor(kb.name(f"cpx{e}"), [T, 1032], BF16)
                for e in range(E)]
    contrib = [nc.dram_tensor(kb.name(f"ctb{e}"), [KCAP + 1, D], F32)
               for e in range(E)]
    th_dram = nc.dram_tensor(kb.name("thd"), [128, 1], F32)

    gate_sb = kb.sb([128, KC, 8], BF16, "gate_sb")
    zrow = kb.sb([1, D], F32, "zrow")
    wl = kb.sem("fwl")
    with nc.Block() as blk:
        @blk.vector
        def _(e):
            e.memset(zrow[:], 0.0)

        @blk.gpsimd
        def _(g):
            g.dma_start(out=gate_sb[:], in_=win["gate"]).then_inc(wl.h, 16)
            for e_ in range(E):
                g.dma_start(out=contrib[e_].ap()[KCAP:KCAP + 1, :],
                            in_=zrow[:]).then_inc(wl.h, 16)
            g.wait_ge(wl.h, wl.v + 16 * 9)
    wl.v += 16 * 9

    dest_u = kb.sb([128, 8, NT], U32, "dest_u")
    gidx_u = kb.sb([128, 8, NT], U32, "gidx_u")
    scw = kb.scope()
    scw.__enter__()
    wide = scw.sb([128, NT, 1040], BF16, "wide")
    aff = scw.sb([128, NT, 8], F32, "aff")
    with kb.scope() as sc:
        emit_ln(kb, x1, lambda t: wide[:, t, 0:D])
        ln1T = sc.sb([128, KC, T], BF16, "ln1T")
        pairs = [(wide[:, t, k * 128:(k + 1) * 128],
                  ln1T[:, k, t * 128:(t + 1) * 128])
                 for t in range(NT) for k in range(KC)]
        emit_transposes(kb, pairs, c["ident_bf"])

        gps = sc.ps([128, 2, 512], F32, name="gps")
        gstat = sc.sb([128, 2, 8], F32, "gstat")
        gexp = sc.sb([128, 2, 8], F32, "gexp")
        g_mm = kb.sem("g_mm")
        g_ev = kb.sem("g_ev")
        g_ac = kb.sem("g_ac")
        for t0 in range(0, NT, 8):
            mm0, ev0, ac0 = g_mm.v, g_ev.v, g_ac.v
            with nc.Block() as blk:
                @blk.tensor
                def _(e, t0=t0, mm0=mm0, ac0=ac0):
                    for i in range(8):
                        t = t0 + i
                        if i >= 2:
                            e.wait_ge(g_ac.h, ac0 + i - 1)
                        for k in range(KC):
                            mi = e.matmul(gps[:, i % 2, 0:8],
                                          ln1T[:, k, t * 128:(t + 1) * 128],
                                          gate_sb[:, k, :],
                                          start=(k == 0), stop=(k == KC - 1))
                        mi.then_inc(g_mm.h, 1)

                @blk.vector
                def _(e, t0=t0, mm0=mm0, ac0=ac0):
                    for i in range(8):
                        t = t0 + i
                        sl = i % 2
                        e.wait_ge(g_mm.h, mm0 + i + 1)
                        i1 = e.reduce_max(gstat[:, sl, 0:1], gps[:, sl, 0:8],
                                          axis=AXX)
                        kb.edge("v", i1, e)
                        i1 = e.tensor_scalar(gstat[:, sl, 1:2], gstat[:, sl, 0:1],
                                             -1.0, None, ALU.mult)
                        i1.then_inc(g_ev.h, 1)
                        e.wait_ge(g_ac.h, ac0 + i + 1)
                        i1 = e.reciprocal(gstat[:, sl, 3:4], gstat[:, sl, 2:3])
                        kb.edge("v", i1, e)
                        i1 = e.tensor_scalar(aff[:, t, :], gexp[:, sl, :],
                                             gstat[:, sl, 3:4], None, ALU.mult)
                        kb.edge("v", i1, e)
                        e.tensor_copy(wide[:, t, 1024:1032],
                                      aff[:, t, :]).then_inc(g_ev.h, 1)

                @blk.scalar
                def _(e, t0=t0, ev0=ev0):
                    for i in range(8):
                        sl = i % 2
                        e.wait_ge(g_ev.h, ev0 + 2 * i + 1)
                        e.activation(gexp[:, sl, :], gps[:, sl, 0:8], AF.Exp,
                                     bias=gstat[:, sl, 1:2],
                                     accum_out=gstat[:, sl, 2:3]
                                     ).then_inc(g_ac.h, 1)
            g_mm.v += 8
            g_ev.v += 16
            g_ac.v += 8

    if stage == "gate":
        dsem = kb.sem()
        with nc.Block() as blk:
            @blk.gpsimd
            def _(e):
                for t in range(8):
                    e.dma_start(out=dbg[:, t, 0:8],
                                in_=aff[:, t, :]).then_inc(dsem.h, 16)
                e.wait_ge(dsem.h, dsem.v + 128)
        dsem.v += 128
        return

    with kb.scope() as sc:
        affT = sc.sb([8, T], F32, "affT")
        affrep = sc.sb([128, 256], F32, "affrep")
        mskb = sc.sb([128, 256], F32, "mskb")
        bis = sc.sb([128, 8], F32, "bis")
        bcond = sc.sb([128, 1], U32, "bcond")
        cnt8 = sc.sb([8, 1], F32, "cnt8")
        tps = sc.ps([128, 2, 512], F32, name="tps_f3")
        pairs2 = []
        for t in range(NT):
            pairs2.append((aff[:, t, :], None))
        b_tp = kb.sem("b_tp")
        b_cp = kb.sem("b_cp")
        tp0, cp0 = b_tp.v, b_cp.v
        with nc.Block() as blk:
            @blk.tensor
            def _(e, tp0=tp0, cp0=cp0):
                for t in range(NT):
                    if t >= 2:
                        e.wait_ge(b_cp.h, cp0 + t - 1)
                    e.matmul(tps[0:8, t % 2, 0:128], aff[:, t, :],
                             c["ident_f"][:], start=True,
                             stop=True).then_inc(b_tp.h, 1)

            @blk.vector
            def _(e, tp0=tp0, cp0=cp0):
                for t in range(NT):
                    e.wait_ge(b_tp.h, tp0 + t + 1)
                    e.tensor_copy(affT[:, t * 128:(t + 1) * 128],
                                  tps[0:8, t % 2, 0:128]).then_inc(b_cp.h, 1)
        b_tp.v += NT
        b_cp.v += NT
        rp = kb.sem("b_rp")
        with nc.Block() as blk:
            @blk.sync
            def _(e):
                n0 = rp.v
                for s in range(16):
                    dst = affrep[s:s + 113:16, :]
                    e.dma_start(out=dst,
                                in_=affT[:, s * 256:(s + 1) * 256]
                                ).then_inc(rp.h, 16)
                e.wait_ge(rp.h, n0 + 256)
        rp.v += 256
        with nc.Block() as blk:
            @blk.vector
            def _(e):
                e.memset(bis[:, 0:1], 0.0)
                e.memset(bis[:, 1:2], 1.0)
                e.memset(bis[:, 2:3], 0.5)
        b_dv = kb.sem("b_dv")
        b_mm = kb.sem("b_mm")
        b_ac = kb.sem("b_ac")
        with nc.Block() as blk:
            @blk.vector
            def _(e):
                dv0, mm0 = b_dv.v, b_mm.v
                for it in range(30):
                    i1 = e.tensor_scalar(mskb[:], affrep[:], bis[:, 2:3], None,
                                         ALU.is_ge)
                    kb.edge("v", i1, e)
                    i1 = e.reduce_sum(bis[:, 4:5], mskb[:], axis=AXX)
                    kb.edge("v", i1, e)
                    e.tensor_copy(bis[:, 5:6], bis[:, 4:5]).then_inc(b_dv.h, 1)
                    e.wait_ge(b_mm.h, mm0 + 2 * (it + 1))
                    i1 = e.tensor_scalar(bcond[:], tps[:, 1, 0:1], 1023.5,
                                         None, ALU.is_gt)
                    kb.edge("v", i1, e)
                    i1 = e.select(bis[:, 0:1], bcond[:], bis[:, 2:3],
                                  bis[:, 0:1])
                    kb.edge("v", i1, e)
                    i1 = e.select(bis[:, 1:2], bcond[:], bis[:, 1:2],
                                  bis[:, 2:3])
                    kb.edge("v", i1, e)
                    i1 = e.tensor_tensor(bis[:, 2:3], bis[:, 0:1], bis[:, 1:2],
                                         ALU.add)
                    kb.edge("v", i1, e)
                    i1 = e.tensor_scalar(bis[:, 2:3], bis[:, 2:3], 0.5, None,
                                         ALU.mult)
                    kb.edge("v", i1, e)
                b_dv.v += 30

            @blk.tensor
            def _(e):
                dv0, mm0, ac0 = b_dv.v - 30, b_mm.v, b_ac.v
                for it in range(30):
                    e.wait_ge(b_dv.h, dv0 + it + 1)
                    e.matmul(tps[0:8, 0, 0:1], c["indic_t"][:], bis[:, 5:6],
                             start=True, stop=True).then_inc(b_mm.h, 1)
                    e.wait_ge(b_ac.h, ac0 + it + 1)
                    e.matmul(tps[:, 1, 0:1], c["indic8"][:], cnt8[:],
                             start=True, stop=True).then_inc(b_mm.h, 1)
                b_mm.v += 60

            @blk.scalar
            def _(e):
                mm0, ac0 = b_mm.v - 60, b_ac.v
                for it in range(30):
                    e.wait_ge(b_mm.h, mm0 + 2 * it + 1)
                    e.copy(cnt8[:], tps[0:8, 0, 0:1]).then_inc(b_ac.h, 1)
                b_ac.v += 30
        th_tm = sc.sb([128, 8], F32, "th_tm")
        tl = kb.sem("b_tl")
        with nc.Block() as blk:
            @blk.sync
            def _(e):
                n0 = tl.v
                e.dma_start(out=th_dram.ap(), in_=bis[:, 0:1]).then_inc(tl.h, 16)
                e.wait_ge(tl.h, n0 + 16)
                src = bass.AP(th_dram.ap().tensor, 0, [[0, 128], [16, 8]])
                e.dma_start(out=th_tm[:], in_=src,
                            allow_slow_non_contiguous=True).then_inc(tl.h, 16)
                e.wait_ge(tl.h, n0 + 32)
        tl.v += 32

        dscr = sc.sb([128, 4, NT], F32, "dscr")
        msku = sc.sb([128, NT], U32, "msku")
        dcol = sc.sb([128, 8], F32, "dcol")
        d_dv = kb.sem("d_dv")
        d_mm = kb.sem("d_mm")
        with nc.Block() as blk:
            @blk.vector
            def _(e):
                dv0, mm0 = d_dv.v, d_mm.v
                for ei in range(E):
                    aslice = aff[:, :, ei]
                    i1 = e.tensor_scalar(dscr[:, 0, :], aslice,
                                         th_tm[:, ei:ei + 1], None, ALU.is_ge)
                    kb.edge("v", i1, e)
                    i1 = e.tensor_copy(msku[:], dscr[:, 0, :])
                    kb.edge("v", i1, e)
                    i1 = e.tensor_tensor_scan(dscr[:, 1, :], dscr[:, 0, :],
                                              dscr[:, 0, :], 0.0,
                                              ALU.add, ALU.bypass)
                    kb.edge("v", i1, e)
                    e.tensor_copy(dcol[:, 3:4],
                                  dscr[:, 1, NT - 1:NT]).then_inc(d_dv.h, 1)
                    e.wait_ge(d_mm.h, mm0 + 2 * (ei + 1))
                    i1 = e.tensor_copy(dcol[:, 0:1], tps[:, 0, 0:1])
                    kb.edge("v", i1, e)
                    i1 = e.tensor_copy(dcol[:, 2:3], tps[:, 1, 0:1])
                    kb.edge("v", i1, e)
                    i1 = e.tensor_scalar(dcol[:, 1:2], dcol[:, 0:1], -1.0,
                                         None, ALU.mult)
                    kb.edge("v", i1, e)
                    i1 = e.tensor_scalar(dscr[:, 2, :], dscr[:, 1, :],
                                         dcol[:, 0:1], -1.0, ALU.add, ALU.add)
                    kb.edge("v", i1, e)
                    i1 = e.tensor_tensor(dscr[:, 3, :], c["pos_tm"][:],
                                         dscr[:, 1, :], ALU.subtract)
                    kb.edge("v", i1, e)
                    i1 = e.tensor_scalar(dscr[:, 3, :], dscr[:, 3, :],
                                         dcol[:, 1:2], dcol[:, 2:3],
                                         ALU.add, ALU.add)
                    kb.edge("v", i1, e)
                    i1 = e.select(dscr[:, 1, :], msku[:], dscr[:, 2, :],
                                  dscr[:, 3, :])
                    kb.edge("v", i1, e)
                    i1 = e.tensor_copy(dest_u[:, ei, :], dscr[:, 1, :])
                    kb.edge("v", i1, e)
                    i1 = e.tensor_scalar(dscr[:, 2, :], dscr[:, 1, :], 1024.0,
                                         None, ALU.min)
                    kb.edge("v", i1, e)
                    e.tensor_copy(gidx_u[:, ei, :], dscr[:, 2, :])
                d_dv.v += E

            @blk.tensor
            def _(e):
                dv0, mm0 = d_dv.v - E, d_mm.v
                for ei in range(E):
                    e.wait_ge(d_dv.h, dv0 + ei + 1)
                    e.matmul(tps[:, 0, 0:1], c["uexc_f"][:], dcol[:, 3:4],
                             start=True, stop=True).then_inc(d_mm.h, 1)
                    e.matmul(tps[:, 1, 0:1], c["allones_f"][:], dcol[:, 3:4],
                             start=True, stop=True).then_inc(d_mm.h, 1)
                d_mm.v += 2 * E

    if stage == "topk":
        dsem = kb.sem()
        with nc.Block() as blk:
            @blk.gpsimd
            def _(e):
                n0 = dsem.v
                e.dma_start(out=dbg[:, 0, 0:8], in_=th_tm[:]).then_inc(dsem.h, 16)
                for ei in range(E):
                    e.dma_start(out=dbg[:, 1 + ei, 0:NT],
                                in_=dest_u[:, ei, :]).then_inc(dsem.h, 16)
                e.wait_ge(dsem.h, n0 + 16 * 9)
        dsem.v += 16 * 9
        return

    sc_sem = kb.sem("sc_sem")
    with nc.Block() as blk:
        @blk.gpsimd
        def _(e):
            n0 = sc_sem.v
            n = 0
            for mt in range(NT):
                for ei in range(E):
                    e.indirect_dma_start(
                        out=compactx[ei].ap(),
                        out_offset=IndirectOffsetOnAxis(dest_u[:, ei, mt:mt + 1], 0),
                        in_=wide[:, mt, 0:1032], in_offset=None,
                    ).then_inc(sc_sem.h, 16)
                    n += 16
            e.wait_ge(sc_sem.h, n0 + n)
    sc_sem.v += 16 * NT * E
    scw.__exit__(None, None, None)

    if stage == "scatter":
        dsem = kb.sem()
        with nc.Block() as blk:
            @blk.gpsimd
            def _(e):
                n0 = dsem.v
                for t in range(4):
                    e.dma_start(out=dbg[:, t, 0:1032],
                                in_=compactx[0].ap()[t * 128:(t + 1) * 128, :]
                                ).then_inc(dsem.h, 16)
                e.wait_ge(dsem.h, n0 + 64)
        dsem.v += 64
        return

    with kb.scope() as sc:
        xsel = sc.sb([128, 8, 1032], BF16, "xsel")
        xselT = sc.sb([128, KC, 1024], BF16, "xselT")
        u_sb = sc.sb([128, 16, 1024], BF16, "u_sb")
        h_sb = sc.sb([128, 16, 1024], BF16, "h_sb")
        wd_sb = sc.sb([128, 2, 16, 1024], BF16, "wd_sb")
        wchk = sc.sb([128, 2, KC, 128], BF16, "wchk")
        gat = sc.sb([128, 8], F32, "gat")
        sg = sc.sb([128, 2, 512], F32, "sg")
        cstage = sc.sb([128, 2, 512], F32, "cstage")
        eps_ = sc.ps([128, 2, 512], F32, name="eps_f6")
        tpp = sc.ps([128, 2, 512], F32, name="tpp_f6")
        x_ld = kb.sem("x_ld")
        x_tp = kb.sem("x_tp")
        x_cp = kb.sem("x_cp")
        w_l0 = (kb.sem("w_l0a"), kb.sem("w_l0b"))
        w_l1 = (kb.sem("w_l1a"), kb.sem("w_l1b"))
        wd_l = kb.sem("wd_l")
        m_mm = kb.sem("m_mm")
        m_ev = kb.sem("m_ev")
        m_ac = kb.sem("m_ac")
        c_st = kb.sem("c_st")
        for ei in range(E):
            ld0, wd0 = x_ld.v, wd_l.v
            with nc.Block() as blk:
                @blk.sync
                def _(e, ei=ei, ld0=ld0):
                    ap = compactx[ei].ap()[0:1024, :].rearrange(
                        "(s p) w -> p s w", p=128)
                    e.dma_start(out=xsel[:], in_=ap).then_inc(x_ld.h, 16)

                @blk.gpsimd
                def _(e, ei=ei, wd0=wd0):
                    e.dma_start(out=wd_sb[:, ei % 2, :, :],
                                in_=win["wd"].ap()[ei]).then_inc(wd_l.h, 16)

                @blk.vector
                def _(e, ei=ei, ld0=ld0):
                    e.wait_ge(x_ld.h, ld0 + 16)
                    i1 = e.tensor_copy(gat[:, :], xsel[:, :, 1024 + ei])
                    kb.edge("v", i1, e)
            x_ld.v += 16
            wd_l.v += 16
            tp0, cp0 = x_tp.v, x_cp.v
            with nc.Block() as blk:
                @blk.tensor
                def _(e, tp0=tp0, cp0=cp0):
                    i = 0
                    for st8 in range(8):
                        for k in range(KC):
                            if i >= 2:
                                e.wait_ge(x_cp.h, cp0 + i - 1)
                            e.matmul(tpp[:, i % 2, 0:128],
                                     xsel[:, st8, k * 128:(k + 1) * 128],
                                     c["ident_bf"][:], start=True,
                                     stop=True).then_inc(x_tp.h, 1)
                            i += 1

                @blk.vector
                def _(e, tp0=tp0, cp0=cp0):
                    i = 0
                    for st8 in range(8):
                        for k in range(KC):
                            e.wait_ge(x_tp.h, tp0 + i + 1)
                            e.tensor_copy(xselT[:, k, st8 * 128:(st8 + 1) * 128],
                                          tpp[:, i % 2, 0:128]
                                          ).then_inc(x_cp.h, 1)
                            i += 1
            x_tp.v += 64
            x_cp.v += 64
            for (wname, wl_sem, out_mode) in (("wu", w_l0, "u"), ("wg", w_l1, "g")):
                mm0, ev0, ac0 = m_mm.v, m_ev.v, m_ac.v
                wl0 = (wl_sem[0].v, wl_sem[1].v)
                with nc.Block() as blk:
                    @blk.gpsimd
                    def _(e, ei=ei, wname=wname, wl_sem=wl_sem, wl0=wl0,
                          mm0=mm0):
                        for ic in range(16):
                            if ic >= 2:
                                e.wait_ge(m_mm.h, mm0 + 2 * (ic - 1))
                            ap = win[wname].ap()[ei, :, :, ic * 128:(ic + 1) * 128]
                            e.dma_start(out=wchk[:, ic % 2, :, :],
                                        in_=ap).then_inc(wl_sem[ic % 2].h, 16)

                    @blk.tensor
                    def _(e, wl_sem=wl_sem, wl0=wl0, mm0=mm0, ev0=ev0):
                        for ic in range(16):
                            e.wait_ge(wl_sem[ic % 2].h,
                                      wl0[ic % 2] + 16 * (ic // 2 + 1))
                            for half in range(2):
                                if 2 * ic + half >= 2:
                                    e.wait_ge(m_ev.h, ev0 + 2 * ic + half - 1)
                                for k in range(KC):
                                    mi = e.matmul(
                                        eps_[:, half, :],
                                        wchk[:, ic % 2, k, :],
                                        xselT[:, k, half * 512:(half + 1) * 512],
                                        start=(k == 0), stop=(k == KC - 1))
                                mi.then_inc(m_mm.h, 1)

                    if out_mode == "u":
                        @blk.vector
                        def _(e, mm0=mm0, ev0=ev0):
                            for ic in range(16):
                                for half in range(2):
                                    e.wait_ge(m_mm.h, mm0 + 2 * ic + half + 1)
                                    e.tensor_copy(
                                        u_sb[:, ic, half * 512:(half + 1) * 512],
                                        eps_[:, half, :]).then_inc(m_ev.h, 1)
                    else:
                        @blk.scalar
                        def _(e, mm0=mm0, ev0=ev0, ac0=ac0):
                            for ic in range(16):
                                for half in range(2):
                                    e.wait_ge(m_mm.h, mm0 + 2 * ic + half + 1)
                                    if 2 * ic + half >= 2:
                                        e.wait_ge(m_ev.h,
                                                  ev0 + 2 * ic + half - 1)
                                    e.activation(sg[:, half, :], eps_[:, half, :],
                                                 AF.Silu).then_inc(m_ac.h, 1)

                        @blk.vector
                        def _(e, mm0=mm0, ev0=ev0, ac0=ac0):
                            for ic in range(16):
                                for half in range(2):
                                    e.wait_ge(m_ac.h, ac0 + 2 * ic + half + 1)
                                    e.tensor_tensor(
                                        h_sb[:, ic, half * 512:(half + 1) * 512],
                                        sg[:, half, :],
                                        u_sb[:, ic, half * 512:(half + 1) * 512],
                                        ALU.mult).then_inc(m_ev.h, 1)
                m_mm.v += 32
                m_ev.v += 32
                wl_sem[0].v += 16 * 8
                wl_sem[1].v += 16 * 8
                if out_mode == "g":
                    m_ac.v += 32
            mm0, ev0, st0, wd0 = m_mm.v, m_ev.v, c_st.v, wd_l.v - 16
            with nc.Block() as blk:
                @blk.tensor
                def _(e, ei=ei, mm0=mm0, ev0=ev0, wd0=wd0):
                    e.wait_ge(wd_l.h, wd0 + 16)
                    for s8 in range(8):
                        for half in range(2):
                            if 2 * s8 + half >= 2:
                                e.wait_ge(m_ev.h, ev0 + 2 * s8 + half - 1)
                            for ic in range(16):
                                mi = e.matmul(
                                    eps_[:, half, :],
                                    h_sb[:, ic, s8 * 128:(s8 + 1) * 128],
                                    wd_sb[:, ei % 2, ic,
                                          half * 512:(half + 1) * 512],
                                    start=(ic == 0), stop=(ic == 15))
                            mi.then_inc(m_mm.h, 1)

                @blk.vector
                def _(e, ei=ei, mm0=mm0, ev0=ev0, st0=st0):
                    for s8 in range(8):
                        for half in range(2):
                            e.wait_ge(m_mm.h, mm0 + 2 * s8 + half + 1)
                            if s8 >= 1 and half == 0:
                                e.wait_ge(c_st.h, st0 + 16 * s8)
                            e.tensor_scalar(cstage[:, half, :], eps_[:, half, :],
                                            gat[:, s8:s8 + 1], None,
                                            ALU.mult).then_inc(m_ev.h, 1)

                @blk.sync
                def _(e, ei=ei, ev0=ev0, st0=st0):
                    for s8 in range(8):
                        e.wait_ge(m_ev.h, ev0 + 2 * s8 + 2)
                        e.dma_start(
                            out=contrib[ei].ap()[s8 * 128:(s8 + 1) * 128, :],
                            in_=cstage[:].rearrange("p a b -> p (a b)")
                            ).then_inc(c_st.h, 16)
                    e.wait_ge(c_st.h, st0 + 16 * 8)
            m_mm.v += 16
            m_ev.v += 16
            c_st.v += 16 * 8

    if stage == "expert":
        dsem = kb.sem()
        with nc.Block() as blk:
            @blk.gpsimd
            def _(e):
                n0 = dsem.v
                for t in range(8):
                    e.dma_start(out=dbg[:, t, 0:D],
                                in_=contrib[0].ap()[t * 128:(t + 1) * 128, :]
                                ).then_inc(dsem.h, 16)
                e.wait_ge(dsem.h, n0 + 128)
        dsem.v += 128
        return

    with kb.scope() as sc:
        acc = sc.sb([128, 8, D], F32, "acc")
        slot_sems = [kb.sem(f"cmb{s}") for s in range(8)]
        rounds = []
        for r in range(4):
            for s in range(8):
                rounds.append((s, r * 8 + s))
        with nc.Block() as blk:
            @blk.gpsimd
            def _(e):
                for r in range(4):
                    for step in range(10):
                        for s in range(8):
                            t = r * 8 + s
                            sem = slot_sems[s]
                            base = 10 * 16 * r
                            if step == 0:
                                if r > 0:
                                    e.wait_ge(sem.h, base)
                                e.dma_start(out=acc[:, s, :],
                                            in_=x1[t * 128:(t + 1) * 128, :]
                                            ).then_inc(sem.h, 16)
                            elif step <= 8:
                                ei = step - 1
                                e.wait_ge(sem.h, base + 16 * step)
                                e.indirect_dma_start(
                                    out=acc[:, s, :], out_offset=None,
                                    in_=contrib[ei].ap(),
                                    in_offset=IndirectOffsetOnAxis(
                                        gidx_u[:, ei, t:t + 1], 0),
                                    compute_op=ALU.add,
                                ).then_inc(sem.h, 16)
                            else:
                                e.wait_ge(sem.h, base + 16 * 9)
                                e.dma_start(out=x2[t * 128:(t + 1) * 128, :],
                                            in_=acc[:, s, :]).then_inc(sem.h, 16)
                for s in range(8):
                    e.wait_ge(slot_sems[s].h, 10 * 16 * 4)
                    slot_sems[s].v = 10 * 16 * 4


def build_l0_ffn(nc, stage="full"):
    kb = KB(nc)
    x1 = nc.dram_tensor("x1", [T, D], F32, kind="ExternalInput").ap()
    win = {
        "gate": nc.dram_tensor("gate", [128, KC, 8], F32, kind="ExternalInput").ap(),
        "wg": nc.dram_tensor("wg", [8, 128, KC, 2048], F32, kind="ExternalInput"),
        "wu": nc.dram_tensor("wu", [8, 128, KC, 2048], F32, kind="ExternalInput"),
        "wd": nc.dram_tensor("wd", [8, 128, 16, 1024], F32, kind="ExternalInput"),
    }
    x2 = nc.dram_tensor("x2", [T, D], F32, kind="ExternalOutput").ap()
    dbg = (nc.dram_tensor("dbg", [128, 16, 1040], F32,
                          kind="ExternalOutput").ap()
           if stage != "full" else None)
    emit_l0_ffn(kb, x1, win, x2, stage=stage, dbg=dbg)
    kb.ctx.close()
    return nc


def prep_l0f(ins, core, x1=None):
    return {
        "x1": np.ascontiguousarray(x1) if x1 is not None else None,
        "gate": np.ascontiguousarray(
            ins["L0_ffn_gate_w"].reshape(KC, 128, 8).transpose(1, 0, 2)),
        "wg": np.ascontiguousarray(
            ins["L0_ffn_Wg"].reshape(8, KC, 128, 2048).transpose(0, 2, 1, 3)),
        "wu": np.ascontiguousarray(
            ins["L0_ffn_Wu"].reshape(8, KC, 128, 2048).transpose(0, 2, 1, 3)),
        "wd": np.ascontiguousarray(
            ins["L0_ffn_Wd"].reshape(8, 16, 128, 1024).transpose(0, 2, 1, 3)),
    }




def emit_l1_attn(kb, x2, win, x3, stage="full", dbg=None):
    nc = kb.nc
    CH = 128
    NCH = T // CH
    c = build_consts(kb, ["ident_bf", "ident_f", "ones_row_f", "ones_col_f"])

    wnames = ("wpv", "wk1", "wkv", "wA", "wB", "wo1")
    wsb = {nm: kb.sb([128, KC, D], BF16, nm + "_sb") for nm in wnames}
    wsb["wb1"] = kb.sb([128, KC, 512], BF16, "wb1_sb")
    wsb["wg2"] = kb.sb([128, KC, 1], BF16, "wg2_sb")
    wsb["wb2"] = kb.sb([128, 4, 2], BF16, "wb2_sb")
    bsb = {nm: kb.sb([128, KC], F32, nm + "_sb")
           for nm in ("bpv", "bk1", "bkv", "bg1", "lng", "lnb", "bo1")}
    bsb["bb1"] = kb.sb([128, 4], F32, "bb1_sb")
    bsb["bb2"] = kb.sb([1, 2], F32, "bb2_sb")
    bsb["bg2"] = kb.sb([1, 1], F32, "bg2_sb")
    wl = kb.sem("awl")
    with nc.Block() as blk:
        @blk.gpsimd
        def _(g):
            n = 0
            for nm in wnames + ("wb1", "wg2", "wb2"):
                g.dma_start(out=wsb[nm][:], in_=win[nm]).then_inc(wl.h, 16)
                n += 16
            for nm in ("bpv", "bk1", "bkv", "bg1", "lng", "lnb", "bo1",
                       "bb1", "bb2", "bg2"):
                g.dma_start(out=bsb[nm][:], in_=win[nm]).then_inc(wl.h, 16)
                n += 16
            g.wait_ge(wl.h, wl.v + n)
    wl.v += 16 * 19

    amc = kb.sb([128, KC, CH], F32, "amc")
    ams = kb.sb([128, KC, CH], F32, "ams")
    bmc = kb.sb([128, KC, CH], F32, "bmc")
    bms = kb.sb([128, KC, CH], F32, "bms")
    carr1 = kb.sb([1, 4], F32, "carr1")
    rows = kb.sb([1, 16, CH], F32, "rows")
    x2c = kb.sb([128, D], F32, "x2c")
    ln2tm = kb.sb([128, D], BF16, "ln2tm")
    ln2T = kb.sb([128, KC, CH], BF16, "ln2T")
    pvb = kb.sb([128, KC, CH], BF16, "pvb")
    kprb = kb.sb([128, KC, CH], F32, "kprb")
    kvb = kb.sb([128, KC, CH], BF16, "kvb")
    kcb = kb.sb([128, KC, CH + 1], BF16, "kcb")
    ksb_ = kb.sb([128, KC, CH + 1], BF16, "ksb")
    p1b = kb.sb([128, KC, CH], F32, "p1b")
    p2b = kb.sb([128, KC, CH + 1], F32, "p2b")
    ghb = kb.sb([128, KC, CH], BF16, "ghb")
    pcb = kb.sb([128, KC, CH], BF16, "pcb")
    psb2 = kb.sb([128, KC, CH], BF16, "psb2")
    pptm = kb.sb([128, D], F32, "pptm")
    ppT = kb.sb([128, KC, CH], F32, "ppT")
    f32s = kb.sb([128, KC, CH], F32, "f32s")
    f32t = kb.sb([128, KC, CH], F32, "f32t")
    iscr = kb.sb([128, KC, CH], I32, "iscr_a")
    scrA = kb.sb([128, KC, CH], BF16, "scrA")
    scrB = kb.sb([128, KC, CH], BF16, "scrB")
    blend = kb.sb([128, KC, CH], F32, "blend")
    bhb = kb.sb([128, 4, CH], F32, "bhb")
    bhbf = kb.sb([128, 4, CH], BF16, "bhbf")
    lnbf = kb.sb([128, KC, CH], BF16, "lnbf")
    vgb = kb.sb([128, CH], F32, "vgb")
    w0b = kb.sb([128, CH], F32, "w0b")
    w1b = kb.sb([128, CH], F32, "w1b")
    meanb = kb.sb([128, CH], F32, "meanb")
    rstdb = kb.sb([128, CH], F32, "rstdb")
    ostg = kb.sb([128, CH], F32, "ostg_a")
    x3t = kb.sb([128, D], F32, "x3t")
    mmps = kb.ps([128, 2, 512], F32, name="mmps_a")
    rwps = kb.ps([128, 4, 512], F32, name="rwps_a")

    with nc.Block() as blk:
        @blk.vector
        def _(e):
            for b in (amc, ams, bmc, bms):
                e.memset(b[:, :, CH - 1:CH], 0.0)
            e.memset(rows[:, 1, CH - 1:CH], 0.0)
            e.memset(kcb[:, :, 0:1], 0.0)
            e.memset(ksb_[:, :, 0:1], 0.0)
            e.memset(p2b[:, :, 0:1], 0.0)
            i1 = e.tensor_tensor(carr1[:, 1:2], bsb["bb2"][:, 0:1],
                                 bsb["bb2"][:, 1:2], ALU.subtract)
            kb.edge("v", i1, e)

    a_ld = kb.sem("a_ld")
    a_mm = kb.sem("a_mm")
    a_ev = kb.sem("a_ev")
    a_ac = kb.sem("a_ac")
    a_dv = kb.sem("a_dv")
    a_st = kb.sem("a_st")

    def bc8(buf):
        a = buf[:]
        return bass.AP(a.tensor, a.offset, [a.ap[0], [0, KC], a.ap[1]])

    lnst = kb.sb([128, 8], F32, "lnst_a")
    for ci in range(NCH):
        par = ci % 2
        t = ci
        ld0, ac0, dv0 = a_ld.v, a_ac.v, a_dv.v
        with nc.Block() as blk:
            @blk.sync
            def _(e, t=t, ld0=ld0):
                e.dma_start(out=x2c[:],
                            in_=x2[t * 128:(t + 1) * 128, :]).then_inc(a_ld.h, 16)
                e.dma_start(out=pptm[:],
                            in_=win["pp"][t * 128:(t + 1) * 128, :]
                            ).then_inc(a_ld.h, 16)

            @blk.scalar
            def _(e, ld0=ld0, ac0=ac0, dv0=dv0):
                e.wait_ge(a_ld.h, ld0 + 32)
                x2f = x2c[:].rearrange("p (a b) -> p a b", a=KC)
                e.activation(f32t[:], x2f, AF.Square,
                             accum_out=lnst[:, 0:1]).then_inc(a_ac.h, 1)
                e.wait_ge(a_dv.h, dv0 + 1)
                e.activation(lnst[:, 5:6], lnst[:, 4:5],
                             AF.Sqrt).then_inc(a_ac.h, 1)

            @blk.vector
            def _(e, ld0=ld0, ac0=ac0, dv0=dv0):
                e.wait_ge(a_ld.h, ld0 + 32)
                i1 = e.reduce_sum(lnst[:, 1:2], x2c[:], axis=AXX)
                kb.edge("v", i1, e)
                e.wait_ge(a_ac.h, ac0 + 1)
                i1 = e.tensor_scalar(lnst[:, 2:3], lnst[:, 1:2], 1.0 / D,
                                     None, ALU.mult)
                kb.edge("v", i1, e)
                i1 = e.tensor_tensor(lnst[:, 3:4], lnst[:, 2:3], lnst[:, 2:3],
                                     ALU.mult)
                kb.edge("v", i1, e)
                i1 = e.scalar_tensor_tensor(lnst[:, 4:5], lnst[:, 0:1], 1.0 / D,
                                            lnst[:, 3:4], ALU.mult,
                                            ALU.subtract)
                kb.edge("v", i1, e)
                i1 = e.tensor_scalar(lnst[:, 4:5], lnst[:, 4:5], 1e-5, None,
                                     ALU.add)
                i1.then_inc(a_dv.h, 1)
                e.wait_ge(a_ac.h, ac0 + 2)
                i1 = e.reciprocal(lnst[:, 6:7], lnst[:, 5:6])
                kb.edge("v", i1, e)
                i1 = e.tensor_tensor(lnst[:, 7:8], lnst[:, 2:3], lnst[:, 6:7],
                                     ALU.mult)
                kb.edge("v", i1, e)
                i1 = e.tensor_scalar(lnst[:, 7:8], lnst[:, 7:8], -1.0, None,
                                     ALU.mult)
                kb.edge("v", i1, e)
                e.tensor_scalar(ln2tm[:], x2c[:], lnst[:, 6:7], lnst[:, 7:8],
                                ALU.mult, ALU.add)
        a_ld.v += 32
        a_ac.v += 2
        a_dv.v += 1
        pairs = [(ln2tm[:, k * 128:(k + 1) * 128], ln2T[:, k, :])
                 for k in range(KC)]
        emit_transposes(kb, pairs, c["ident_bf"], grp=1)
        pairsf = [(pptm[:, k * 128:(k + 1) * 128], ppT[:, k, :])
                  for k in range(KC)]
        emit_transposes(kb, pairsf, c["ident_f"], grp=1)

        mm0, ev0 = a_mm.v, a_ev.v
        specs = [("wpv", "bpv", pvb, F32, 0), ("wk1", "bk1", kprb, F32, 0),
                 ("wkv", "bkv", kvb, BF16, 0), ("wA", None, p1b, F32, 0),
                 ("wB", None, p2b, F32, 1), ("wb1", None, bhb, F32, 0)]
        with nc.Block() as blk:
            @blk.tensor
            def _(e, mm0=mm0, ev0=ev0):
                gi = 0
                for (wn, bn, outb, dt_, coff) in specs:
                    nm_ = 4 if wn == "wb1" else KC
                    for m in range(nm_):
                        if gi >= 2:
                            e.wait_ge(a_ev.h, ev0 + gi - 1)
                        for k in range(KC):
                            mi = e.matmul(mmps[:, gi % 2, 0:CH],
                                          wsb[wn][:, k, m * 128:(m + 1) * 128],
                                          ln2T[:, k, :], start=(k == 0),
                                          stop=(k == KC - 1))
                        mi.then_inc(a_mm.h, 1)
                        gi += 1

            @blk.vector
            def _(e, ci=ci, mm0=mm0, ev0=ev0):
                if ci > 0:
                    i1 = e.tensor_copy(p2b[:, :, 0:1], p2b[:, :, CH:CH + 1])
                    kb.edge("v", i1, e)
                gi = 0
                for (wn, bn, outb, dt_, coff) in specs:
                    nm_ = 4 if wn == "wb1" else KC
                    for m in range(nm_):
                        e.wait_ge(a_mm.h, mm0 + gi + 1)
                        dst = (outb[:, m, coff:coff + CH])
                        if bn is None:
                            e.tensor_copy(dst, mmps[:, gi % 2, 0:CH]
                                          ).then_inc(a_ev.h, 1)
                        else:
                            e.tensor_scalar(dst, mmps[:, gi % 2, 0:CH],
                                            bsb[bn][:, m:m + 1], None,
                                            ALU.add).then_inc(a_ev.h, 1)
                        gi += 1
        a_mm.v += 44
        a_ev.v += 44

        ac0, dv0, mm0 = a_ac.v, a_dv.v, a_mm.v
        with nc.Block() as blk:
            @blk.vector
            def _(e, ci=ci, par=par, ac0=ac0, dv0=dv0, mm0=mm0):
                i1 = e.tensor_scalar(f32s[:], ppT[:], INV2PI, None, ALU.mult)
                kb.edge("v", i1, e)
                i1 = e.tensor_copy(iscr[:], f32s[:])
                kb.edge("v", i1, e)
                i1 = e.tensor_copy(f32s[:], iscr[:])
                kb.edge("v", i1, e)
                i1 = e.scalar_tensor_tensor(f32t[:], f32s[:], -2.0 * PI,
                                            ppT[:], ALU.mult, ALU.add)
                kb.edge("v", i1, e)
                i1 = e.tensor_scalar(f32s[:], f32t[:], PI, None, ALU.is_gt)
                kb.edge("v", i1, e)
                i1 = e.scalar_tensor_tensor(f32t[:], f32s[:], -2.0 * PI,
                                            f32t[:], ALU.mult, ALU.add)
                kb.edge("v", i1, e)
                i1 = e.tensor_scalar(f32s[:], f32t[:], -PI, None, ALU.is_lt)
                kb.edge("v", i1, e)
                i1 = e.scalar_tensor_tensor(f32t[:], f32s[:], 2.0 * PI,
                                            f32t[:], ALU.mult, ALU.add)
                kb.edge("v", i1, e)
                i1 = e.tensor_scalar(f32s[:], f32t[:], PI / 2, None, ALU.add)
                kb.edge("v", i1, e)
                i1 = e.tensor_scalar(blend[:], f32s[:], PI, None, ALU.is_gt)
                kb.edge("v", i1, e)
                i1 = e.scalar_tensor_tensor(f32s[:], blend[:], -2.0 * PI,
                                            f32s[:], ALU.mult, ALU.add)
                i1.then_inc(a_dv.h, 1)
                e.wait_ge(a_ac.h, ac0 + 3)
                if ci > 0:
                    i1 = e.tensor_copy(kcb[:, :, 0:1], kcb[:, :, CH:CH + 1])
                    kb.edge("v", i1, e)
                    i1 = e.tensor_copy(ksb_[:, :, 0:1], ksb_[:, :, CH:CH + 1])
                    kb.edge("v", i1, e)
                i1 = e.tensor_scalar(f32t[:], scrA[:], PI, None, ALU.mult)
                kb.edge("v", i1, e)
                i1 = e.tensor_scalar(f32s[:], f32t[:], PI / 2, None, ALU.add)
                kb.edge("v", i1, e)
                i1 = e.tensor_scalar(blend[:], f32s[:], PI, None, ALU.is_gt)
                kb.edge("v", i1, e)
                i1 = e.scalar_tensor_tensor(f32s[:], blend[:], -2.0 * PI,
                                            f32s[:], ALU.mult, ALU.add)
                i1.then_inc(a_dv.h, 1)
                e.wait_ge(a_ac.h, ac0 + 4)
                i1 = e.tensor_tensor(f32t[:], p1b[:], p2b[:, :, 0:CH], ALU.add)
                i1.then_inc(a_dv.h, 1)
                e.wait_ge(a_ac.h, ac0 + 7)
                e.wait_ge(a_mm.h, mm0 + 11)
                i1 = e.tensor_tensor(rows[:, 7, :], rwps[0:1, 2, 0:CH],
                                     rows[:, 6, :], ALU.subtract)
                i1.then_inc(a_dv.h, 1)
                e.wait_ge(a_ac.h, ac0 + 9)
                i1 = e.tensor_tensor_scan(rows[:, 1, :], rows[:, 0, :],
                                          rows[:, 0, :],
                                          rows[:, 1, CH - 1:CH],
                                          ALU.add, ALU.bypass)
                kb.edge("v", i1, e)
                i1 = e.tensor_scalar(rows[:, 5, :], rows[:, 1, :], 1.0,
                                     None, ALU.max)
                i1.then_inc(a_dv.h, 1)
                e.wait_ge(a_ac.h, ac0 + 10)
                i1 = e.reciprocal(rows[:, 14, :], rows[:, 15, :])
                kb.edge("v", i1, e)
                i1 = e.tensor_scalar(rows[:, 5, :], rows[:, 14, :], 1.0 / 32.0,
                                     None, ALU.mult)
                kb.edge("v", i1, e)
                i1 = e.tensor_scalar(rows[:, 4, :], rows[:, 3, :], -1.0, 1.0,
                                     ALU.mult, ALU.add)
                kb.edge("v", i1, e)
                i1 = e.tensor_scalar(rows[:, 8, :], rows[:, 3, :], 1.0 / 32.0,
                                     None, ALU.mult)
                kb.edge("v", i1, e)
                i1 = e.tensor_tensor(rows[:, 9, :], rows[:, 4, :],
                                     rows[:, 5, :], ALU.mult)
                i1.then_inc(a_dv.h, 1)
                e.wait_ge(a_mm.h, mm0 + 14)
                i1 = e.tensor_copy(vgb[:], rwps[:, 0, 0:CH])
                kb.edge("v", i1, e)
                i1 = e.tensor_copy(w0b[:], rwps[:, 1, 0:CH])
                kb.edge("v", i1, e)
                i1 = e.tensor_copy(w1b[:], rwps[:, 2, 0:CH])
                i1.then_inc(a_dv.h, 1)

            @blk.scalar
            def _(e, par=par, ac0=ac0, dv0=dv0, mm0=mm0):
                i1 = e.activation(scrA[:], kprb[:], AF.Tanh)
                i1.then_inc(a_ac.h, 1)
                e.wait_ge(a_dv.h, dv0 + 1)
                i1 = e.activation(psb2[:], f32t[:], AF.Sin)
                kb.edge("s", i1, e)
                i1 = e.activation(pcb[:], f32s[:], AF.Sin)
                i1.then_inc(a_ac.h, 2)
                e.wait_ge(a_dv.h, dv0 + 2)
                i1 = e.activation(ksb_[:, :, 1:CH + 1], f32t[:], AF.Sin)
                kb.edge("s", i1, e)
                i1 = e.activation(kcb[:, :, 1:CH + 1], f32s[:], AF.Sin)
                i1.then_inc(a_ac.h, 1)
                e.wait_ge(a_dv.h, dv0 + 3)
                for kc_ in range(KC):
                    i1 = e.activation(ghb[:, kc_, :], f32t[:, kc_, :], AF.Gelu,
                                      bias=bsb["bg1"][:, kc_:kc_ + 1])
                i1.then_inc(a_ac.h, 1)
                for kc_ in range(4):
                    i1 = e.activation(bhbf[:, kc_, :], bhb[:, kc_, :], AF.Gelu,
                                      bias=bsb["bb1"][:, kc_:kc_ + 1])
                i1.then_inc(a_ac.h, 1)
                e.wait_ge(a_mm.h, mm0 + 10)
                i1 = e.copy(rows[:, 6, :], rwps[0:1, 3, 0:CH])
                i1.then_inc(a_ac.h, 1)
                e.wait_ge(a_dv.h, dv0 + 4)
                i1 = e.activation(rows[:, 3, :], rows[:, 7, :], AF.Sigmoid,
                                  bias=carr1[:, 1:2])
                i1.then_inc(a_ac.h, 1)
                e.wait_ge(a_mm.h, mm0 + 9)
                i1 = e.activation(rows[:, 0, :], rwps[0:1, 1, 0:CH], AF.Sigmoid,
                                  bias=bsb["bg2"][:, 0:1])
                i1.then_inc(a_ac.h, 1)
                e.wait_ge(a_dv.h, dv0 + 5)
                i1 = e.activation(rows[:, 15, :], rows[:, 5, :], AF.Sqrt)
                i1.then_inc(a_ac.h, 1)

            @blk.tensor
            def _(e, ac0=ac0, dv0=dv0, mm0=mm0, ev0=a_ev.v):
                e.wait_ge(a_ac.h, ac0 + 5)
                for k in range(KC):
                    mi = e.matmul(rwps[0:1, 1, 0:CH], wsb["wg2"][:, k, :],
                                  ghb[:, k, :], start=(k == 0),
                                  stop=(k == KC - 1))
                mi.then_inc(a_mm.h, 9)
                e.wait_ge(a_ac.h, ac0 + 6)
                for k in range(4):
                    mi = e.matmul(rwps[0:1, 3, 0:CH], wsb["wb2"][:, k, 1:2],
                                  bhbf[:, k, :], start=(k == 0), stop=(k == 3))
                mi.then_inc(a_mm.h, 1)
                for k in range(4):
                    mi = e.matmul(rwps[0:1, 2, 0:CH], wsb["wb2"][:, k, 0:1],
                                  bhbf[:, k, :], start=(k == 0), stop=(k == 3))
                mi.then_inc(a_mm.h, 1)
                e.wait_ge(a_ac.h, ac0 + 8)
                mi = e.matmul(rwps[:, 0, 0:CH], c["ones_row_f"][:],
                              rows[:, 0, :], start=True, stop=True)
                mi.then_inc(a_mm.h, 1)
                e.wait_ge(a_dv.h, dv0 + 6)
                mi = e.matmul(rwps[:, 1, 0:CH], c["ones_row_f"][:],
                              rows[:, 8, :], start=True, stop=True)
                mi.then_inc(a_mm.h, 1)
                mi = e.matmul(rwps[:, 2, 0:CH], c["ones_row_f"][:],
                              rows[:, 9, :], start=True, stop=True)
                mi.then_inc(a_mm.h, 1)
        a_ac.v = ac0 + 10
        a_dv.v = dv0 + 7
        a_mm.v = mm0 + 14

        dv0 = a_dv.v
        with nc.Block() as blk:
            @blk.vector
            def _(e, par=par, dv0=dv0):
                def scans(dstbuf, srcbuf):
                    for k in range(KC):
                        i1 = e.tensor_tensor_scan(
                            dstbuf[:, k, :], srcbuf[:, k, :],
                            srcbuf[:, k, :],
                            dstbuf[:, k, CH - 1:CH],
                            ALU.add, ALU.bypass)
                    kb.edge("v", i1, e)
                i1 = e.tensor_tensor(scrA[:], pcb[:], pvb[:], ALU.mult)
                kb.edge("v", i1, e)
                scans(amc, scrA)
                i1 = e.tensor_tensor(scrA[:], psb2[:], pvb[:], ALU.mult)
                kb.edge("v", i1, e)
                scans(ams, scrA)
                i1 = e.tensor_tensor(scrA[:], kcb[:, :, 0:CH], kvb[:], ALU.mult)
                kb.edge("v", i1, e)
                i1 = e.tensor_tensor(scrB[:], scrA[:], bc8(vgb), ALU.mult)
                kb.edge("v", i1, e)
                scans(bmc, scrB)
                i1 = e.tensor_tensor(scrA[:], ksb_[:, :, 0:CH], kvb[:], ALU.mult)
                kb.edge("v", i1, e)
                i1 = e.tensor_tensor(scrB[:], scrA[:], bc8(vgb), ALU.mult)
                kb.edge("v", i1, e)
                scans(bms, scrB)
                i1 = e.tensor_tensor(f32s[:], pcb[:], amc[:], ALU.mult)
                kb.edge("v", i1, e)
                i1 = e.tensor_tensor(f32t[:], psb2[:], ams[:], ALU.mult)
                kb.edge("v", i1, e)
                i1 = e.tensor_tensor(f32s[:], f32s[:], f32t[:], ALU.add)
                kb.edge("v", i1, e)
                i1 = e.tensor_tensor(blend[:], f32s[:], bc8(w0b), ALU.mult)
                kb.edge("v", i1, e)
                i1 = e.tensor_tensor(f32s[:], kcb[:, :, 1:CH + 1], bmc[:],
                                     ALU.mult)
                kb.edge("v", i1, e)
                i1 = e.tensor_tensor(f32t[:], ksb_[:, :, 1:CH + 1],
                                     bms[:], ALU.mult)
                kb.edge("v", i1, e)
                i1 = e.tensor_tensor(f32s[:], f32s[:], f32t[:], ALU.add)
                kb.edge("v", i1, e)
                i1 = e.tensor_tensor(f32t[:], f32s[:], bc8(w1b), ALU.mult)
                kb.edge("v", i1, e)
                i1 = e.tensor_tensor(blend[:], blend[:], f32t[:], ALU.add)
                i1.then_inc(a_dv.h, 1)
        a_dv.v += 1

        ac0, dv0, mm0 = a_ac.v, a_dv.v, a_mm.v
        with nc.Block() as blk:
            @blk.scalar
            def _(e, ac0=ac0, dv0=dv0):
                i1 = e.activation(f32t[:], blend[:], AF.Square)
                i1.then_inc(a_ac.h, 1)
                e.wait_ge(a_dv.h, dv0 + 1)
                i1 = e.activation(rows[:, 12, :], rows[:, 11, :], AF.Sqrt)
                i1.then_inc(a_ac.h, 1)

            @blk.tensor
            def _(e, ac0=ac0, dv0=dv0, mm0=mm0):
                for k in range(KC):
                    mi = e.matmul(rwps[0:1, 0, 0:CH], c["ones_col_f"][:],
                                  blend[:, k, :], start=(k == 0),
                                  stop=(k == KC - 1))
                mi.then_inc(a_mm.h, 1)
                e.wait_ge(a_ac.h, ac0 + 1)
                for k in range(KC):
                    mi = e.matmul(rwps[0:1, 1, 0:CH], c["ones_col_f"][:],
                                  f32t[:, k, :], start=(k == 0),
                                  stop=(k == KC - 1))
                mi.then_inc(a_mm.h, 1)
                e.wait_ge(a_dv.h, dv0 + 2)
                mi = e.matmul(rwps[:, 2, 0:CH], c["ones_row_f"][:],
                              rows[:, 10, :], start=True, stop=True)
                mi.then_inc(a_mm.h, 1)
                mi = e.matmul(rwps[:, 3, 0:CH], c["ones_row_f"][:],
                              rows[:, 13, :], start=True, stop=True)
                mi.then_inc(a_mm.h, 1)

            @blk.vector
            def _(e, ac0=ac0, dv0=dv0, mm0=mm0):
                e.wait_ge(a_mm.h, mm0 + 2)
                i1 = e.tensor_scalar(rows[:, 10, :], rwps[0:1, 0, 0:CH],
                                     1.0 / D, None, ALU.mult)
                kb.edge("v", i1, e)
                i1 = e.tensor_tensor(rows[:, 14, :], rows[:, 10, :],
                                     rows[:, 10, :], ALU.mult)
                kb.edge("v", i1, e)
                i1 = e.scalar_tensor_tensor(rows[:, 11, :], rwps[0:1, 1, 0:CH],
                                            1.0 / D, rows[:, 14, :],
                                            ALU.mult, ALU.subtract)
                kb.edge("v", i1, e)
                i1 = e.tensor_scalar(rows[:, 11, :], rows[:, 11, :], 1e-5,
                                     None, ALU.add)
                i1.then_inc(a_dv.h, 1)
                e.wait_ge(a_ac.h, ac0 + 2)
                i1 = e.reciprocal(rows[:, 13, :], rows[:, 12, :])
                kb.edge("v", i1, e)
                nc.vector.engine_nop().then_inc(a_dv.h, 1)
                e.wait_ge(a_mm.h, mm0 + 4)
                i1 = e.tensor_copy(meanb[:], rwps[:, 2, 0:CH])
                kb.edge("v", i1, e)
                i1 = e.tensor_copy(rstdb[:], rwps[:, 3, 0:CH])
                kb.edge("v", i1, e)
                i1 = e.tensor_tensor(f32s[:], blend[:], bc8(meanb), ALU.subtract)
                kb.edge("v", i1, e)
                i1 = e.tensor_tensor(f32s[:], f32s[:], bc8(rstdb), ALU.mult)
                kb.edge("v", i1, e)
                for k in range(KC):
                    i1 = e.tensor_scalar(lnbf[:, k, :], f32s[:, k, :],
                                         bsb["lng"][:, k:k + 1],
                                         bsb["lnb"][:, k:k + 1],
                                         ALU.mult, ALU.add)
                i1.then_inc(a_dv.h, 1)
        a_ac.v = ac0 + 2
        a_dv.v = dv0 + 3
        a_mm.v = mm0 + 4

        mm0, ev0, dv0, st0 = a_mm.v, a_ev.v, a_dv.v, a_st.v
        with nc.Block() as blk:
            @blk.tensor
            def _(e, mm0=mm0, ev0=ev0, dv0=dv0):
                for m in range(KC):
                    if m >= 1:
                        e.wait_ge(a_ev.h, ev0 + m)
                    for k in range(KC):
                        mi = e.matmul(mmps[:, m % 2, 0:CH],
                                      wsb["wo1"][:, k, m * 128:(m + 1) * 128],
                                      lnbf[:, k, :], start=(k == 0),
                                      stop=(k == KC - 1))
                    mi.then_inc(a_mm.h, 1)
                    e.wait_ge(a_ev.h, ev0 + m + 1)
                    e.matmul(rwps[:, m % 2, 0:CH], ostg[:],
                             c["ident_f"][:], start=True,
                             stop=True).then_inc(a_mm.h, 1)

            @blk.vector
            def _(e, mm0=mm0, ev0=ev0, dv0=dv0):
                for m in range(KC):
                    e.wait_ge(a_mm.h, mm0 + 2 * m + 1)
                    e.tensor_scalar(ostg[:], mmps[:, m % 2, 0:CH],
                                    bsb["bo1"][:, m:m + 1], None,
                                    ALU.add).then_inc(a_ev.h, 1)
                    e.wait_ge(a_mm.h, mm0 + 2 * m + 2)
                    i1 = e.tensor_tensor(f32s[:, 0, :], rwps[:, m % 2, 0:CH],
                                         ln2tm[:, m * 128:(m + 1) * 128],
                                         ALU.add)
                    kb.edge("v", i1, e)
                    i1 = e.tensor_tensor(x3t[:, m * 128:(m + 1) * 128],
                                         f32s[:, 0, :],
                                         x2c[:, m * 128:(m + 1) * 128],
                                         ALU.add)
                    kb.edge("v", i1, e)
                nc.vector.engine_nop().then_inc(a_dv.h, 1)

            @blk.sync
            def _(e, t=t, dv0=dv0, st0=st0):
                e.wait_ge(a_dv.h, dv0 + 1)
                e.dma_start(out=x3[t * 128:(t + 1) * 128, :],
                            in_=x3t[:]).then_inc(a_st.h, 16)
                e.wait_ge(a_st.h, st0 + 16)
        a_mm.v += 16
        a_ev.v += KC
        a_dv.v += 1
        a_st.v += 16

        if stage == "chunk0" and ci == 0:
            return




def emit_l1_attn_v2(kb, x2, win, x3, stage="full", dbg=None):
    nc = kb.nc
    c = build_consts(kb, ["ident_bf", "ident_f", "ones_row_bf", "ones_row_f",
                          "ones_col_f"])
    dmt = lambda nm: nc.dram_tensor(kb.name(nm), [128, KC, T], BF16)
    ln2_d = nc.dram_tensor(kb.name("ln2d"), [T, D], BF16)
    pv_d, kv_d, kc_d, ks_d = dmt("pvd"), dmt("kvd"), dmt("kcd"), dmt("ksd")
    pc_d, ps_d = dmt("pcd"), dmt("psd")
    rows_d = nc.dram_tensor(kb.name("rowsd"), [4, T], BF16)

    bsb = {nm: kb.sb([128, KC], F32, nm + "_sb")
           for nm in ("bpv", "bk1", "bkv", "bg1", "lng", "lnb", "bo1")}
    bsb["bb1"] = kb.sb([128, 4], F32, "bb1_sb")
    bsb["bb2"] = kb.sb([1, 2], F32, "bb2_sb")
    bsb["bg2"] = kb.sb([1, 1], F32, "bg2_sb")
    bb2d = kb.sb([1, 1], F32, "bb2d")

    wl = kb.sem("awl")
    a_ld = kb.sem("a_ld")
    b_ld = kb.sem("b_ld")
    a_mm = kb.sem("a_mm")
    a_ev = kb.sem("a_ev")
    a_ac = kb.sem("a_ac")
    a_dv = kb.sem("a_dv")
    a_st = kb.sem("a_st")

    def bc8(buf):
        a = buf[:]
        return bass.AP(a.tensor, a.offset, [a.ap[0], [0, KC], a.ap[1]])

    with kb.scope() as scA:
        ln2T = scA.sb([128, KC, T], BF16, "ln2T")
        wbuf = scA.sb([128, KC, D], BF16, "wbufA")
        wbufB = scA.sb([128, KC, D], BF16, "wbufB")
        wgsmall = scA.sb([128, KC, 1], BF16, "wg2b")
        wbsmall = scA.sb([128, 4, 2], BF16, "wb2b")

        with nc.Block() as blk:
            @blk.gpsimd
            def _(g):
                for nm in ("bpv", "bk1", "bkv", "bg1", "lng", "lnb", "bo1",
                           "bb1", "bb2", "bg2"):
                    g.dma_start(out=bsb[nm][:], in_=win[nm]).then_inc(wl.h, 16)
                g.dma_start(out=wgsmall[:], in_=win["wg2"]).then_inc(wl.h, 16)
                g.dma_start(out=wbsmall[:], in_=win["wb2"]).then_inc(wl.h, 16)
                g.dma_start(out=wbuf[:], in_=win["wpv"]).then_inc(wl.h, 16)
                g.dma_start(out=wbufB[:], in_=win["wkv"]).then_inc(wl.h, 16)
        wl.v += 16 * 14

        with kb.scope() as sc0:
            lnstage = sc0.sb([128, 8, D], BF16, "lnst2")
            for g0 in range(0, NT, 8):
                emit_ln(kb, x2, lambda t: lnstage[:, t % 8, :], nt=8,
                        group=4, tile_off=g0)
                pairs = [(lnstage[:, i, k * 128:(k + 1) * 128],
                          ln2T[:, k, (g0 + i) * 128:(g0 + i + 1) * 128])
                         for i in range(8) for k in range(KC)]
                emit_transposes(kb, pairs, c["ident_bf"], grp=4)
                st0 = a_st.v
                with nc.Block() as blk:
                    @blk.sync
                    def _(e, g0=g0, st0=st0):
                        for i in range(8):
                            t = g0 + i
                            e.dma_start(out=ln2_d.ap()[t * 128:(t + 1) * 128, :],
                                        in_=lnstage[:, i, :]
                                        ).then_inc(a_st.h, 16)
                        e.wait_ge(a_st.h, st0 + 128)
                a_st.v += 128

        stg = scA.sb([128, 4, 512], F32, "stgA")
        stgb = scA.sb([128, 2, 4, 512], BF16, "stgbA")
        f32u = scA.sb([128, 4, 512], F32, "f32uA")
        f32v = scA.sb([128, 4, 512], F32, "f32vA")
        ob1 = scA.sb([128, 2, 4, 512], BF16, "ob1A")
        ob2 = scA.sb([128, 2, 4, 512], BF16, "ob2A")
        ghb = scA.sb([128, 4, 512], BF16, "ghbA")
        vgacc = scA.sb([1, KC, 512], F32, "vgaccA")
        l0acc = scA.sb([1, KC, 512], BF16, "l0accA")
        l1acc = scA.sb([1, KC, 512], BF16, "l1accA")
        pps = scA.ps([128, 2, 512], F32, name="ppsA")
        vgps = scA.ps([1, 2, 512], F32, name="vgpsA")

        def mm_group(e, wsb, m, qc, slot):
            for k in range(KC):
                mi = e.matmul(pps[:, slot, :],
                              wsb[:, k, m * 128:(m + 1) * 128],
                              ln2T[:, k, qc * 512:(qc + 1) * 512],
                              start=(k == 0), stop=(k == KC - 1))
            return mi

        def pass_simple(wsb, bias, dst_d, wlneed):
            mm0, ev0, st0 = a_mm.v, a_ev.v, a_st.v
            with nc.Block() as blk:
                @blk.tensor
                def _(e, mm0=mm0, ev0=ev0):
                    e.wait_ge(wl.h, wlneed)
                    g = 0
                    for m in range(KC):
                        for qc in range(KC):
                            if g >= 2:
                                e.wait_ge(a_ev.h, ev0 + g - 1)
                            mm_group(e, wsb, m, qc, g % 2).then_inc(a_mm.h, 1)
                            g += 1

                @blk.vector
                def _(e, mm0=mm0, ev0=ev0, st0=st0):
                    g = 0
                    for m in range(KC):
                        for h in range(2):
                            H = m * 2 + h
                            if H >= 2:
                                e.wait_ge(a_st.h, st0 + 16 * (H - 1))
                            for q in range(4):
                                e.wait_ge(a_mm.h, mm0 + g + 1)
                                e.tensor_scalar(stgb[:, H % 2, q, :],
                                                pps[:, g % 2, :],
                                                bsb[bias][:, m:m + 1], None,
                                                ALU.add).then_inc(a_ev.h, 1)
                                g += 1

                @blk.sync
                def _(e, mm0=mm0, ev0=ev0, st0=st0):
                    for m in range(KC):
                        for h in range(2):
                            H = m * 2 + h
                            e.wait_ge(a_ev.h, ev0 + 4 * (H + 1))
                            e.dma_start(
                                out=dst_d.ap()[:, m, h * 2048:(h + 1) * 2048],
                                in_=stgb[:, H % 2, :, :].rearrange(
                                    "p a b -> p (a b)")).then_inc(a_st.h, 16)
                    e.wait_ge(a_st.h, st0 + 16 * 16)
            a_mm.v += 64
            a_ev.v += 64
            a_st.v += 256

        def pass_key(wlneed, thb):
            mm0, ev0, ac0, dv0, st0 = a_mm.v, a_ev.v, a_ac.v, a_dv.v, a_st.v
            with nc.Block() as blk:
                @blk.tensor
                def _(e, mm0=mm0, ev0=ev0):
                    e.wait_ge(wl.h, wlneed)
                    g = 0
                    for m in range(KC):
                        for qc in range(KC):
                            if g >= 2:
                                e.wait_ge(a_ev.h, ev0 + g - 1)
                            mm_group(e, wbuf, m, qc, g % 2).then_inc(a_mm.h, 1)
                            g += 1

                @blk.vector
                def _(e, mm0=mm0, ac0=ac0, dv0=dv0):
                    g = 0
                    for H in range(16):
                        m = H // 2
                        if H >= 1:
                            e.wait_ge(a_ac.h, ac0 + 3 * (H - 1) + 1)
                        for q in range(4):
                            e.wait_ge(a_mm.h, mm0 + g + 1)
                            e.tensor_scalar(stg[:, q, :], pps[:, g % 2, :],
                                            bsb["bk1"][:, m:m + 1], None,
                                            ALU.add).then_inc(a_ev.h, 1)
                            g += 1
                        e.wait_ge(a_ac.h, ac0 + 3 * H + 1)
                        i1 = e.tensor_scalar(f32u[:], thb[:], PI, None,
                                             ALU.mult)
                        kb.edge("v", i1, e)
                        i1 = e.tensor_scalar(f32v[:], thb[:], PI, PI / 2,
                                             ALU.mult, ALU.add)
                        kb.edge("v", i1, e)
                        i1 = e.tensor_scalar(ghb[:], f32v[:], PI, None,
                                             ALU.is_gt)
                        kb.edge("v", i1, e)
                        i1 = e.scalar_tensor_tensor(f32v[:], ghb[:], -2.0 * PI,
                                                    f32v[:], ALU.mult, ALU.add)
                        i1.then_inc(a_dv.h, 1)

                @blk.scalar
                def _(e, ev0=ev0, ac0=ac0, dv0=dv0, st0=st0):
                    for H in range(16):
                        e.wait_ge(a_ev.h, ev0 + 4 * (H + 1))
                        e.activation(thb[:], stg[:], AF.Tanh).then_inc(a_ac.h, 1)
                        e.wait_ge(a_dv.h, dv0 + H + 1)
                        if H >= 2:
                            e.wait_ge(a_st.h, st0 + 32 * (H - 1))
                        e.activation(ob1[:, H % 2, :, :], f32u[:],
                                     AF.Sin).then_inc(a_ac.h, 1)
                        e.activation(ob2[:, H % 2, :, :], f32v[:],
                                     AF.Sin).then_inc(a_ac.h, 1)

                @blk.sync
                def _(e, ac0=ac0, st0=st0):
                    for H in range(16):
                        m, h = divmod(H, 2)
                        e.wait_ge(a_ac.h, ac0 + 3 * (H + 1))
                        e.dma_start(out=ks_d.ap()[:, m, h * 2048:(h + 1) * 2048],
                                    in_=ob1[:, H % 2, :, :].rearrange(
                                        "p a b -> p (a b)")).then_inc(a_st.h, 16)
                        e.dma_start(out=kc_d.ap()[:, m, h * 2048:(h + 1) * 2048],
                                    in_=ob2[:, H % 2, :, :].rearrange(
                                        "p a b -> p (a b)")).then_inc(a_st.h, 16)
                    e.wait_ge(a_st.h, st0 + 512)
            a_mm.v += 64
            a_ev.v += 64
            a_ac.v += 48
            a_dv.v += 16
            a_st.v += 512

        def pass_gh(wlneed, p2stage):
            mm0, ev0, ac0, dv0 = a_mm.v, a_ev.v, a_ac.v, a_dv.v
            with nc.Block() as blk:
                @blk.tensor
                def _(e, mm0=mm0, ev0=ev0, ac0=ac0, dv0=dv0):
                    e.wait_ge(wl.h, wlneed)
                    for m in range(KC):
                        for r in range(16):
                            wsb = wbufB if r < 8 else wbuf
                            qc = r % 8
                            gg = m * 16 + r
                            if gg >= 2:
                                e.wait_ge(a_ev.h, ev0 + gg - 1)
                            mm_group(e, wsb, m, qc, gg % 2
                                     ).then_inc(a_mm.h, 1)
                        for qc in range(KC):
                            h = qc // 4
                            e.wait_ge(a_ac.h, ac0 + 2 * m + h + 1)
                            nvg = m * 8 + qc
                            if nvg >= 2:
                                e.wait_ge(a_dv.h, dv0 + nvg - 1)
                            e.matmul(vgps[0:1, qc % 2, :], wgsmall[:, m, :],
                                     ghb[:, qc % 4, :], start=True, stop=True
                                     ).then_inc(a_mm.h, 1)

                @blk.vector
                def _(e, mm0=mm0, ev0=ev0, ac0=ac0, dv0=dv0):
                    e.memset(p2stage[:, 0:1], 0.0)
                    for m in range(KC):
                        for qc in range(KC):
                            e.wait_ge(a_mm.h, mm0 + m * 24 + qc + 1)
                            i1 = e.tensor_copy(
                                p2stage[:, 1 + qc * 512:1 + (qc + 1) * 512],
                                pps[:, (m * 16 + qc) % 2, :])
                            if qc == KC - 1:
                                kb.edge("v", i1, e)
                                nc.vector.engine_nop().then_inc(a_ev.h, 1)
                            else:
                                i1.then_inc(a_ev.h, 1)
                        for qc in range(KC):
                            h = qc // 4
                            mh = 2 * m + h
                            if qc % 4 == 0 and mh >= 1:
                                e.wait_ge(a_ac.h, ac0 + mh)
                            e.wait_ge(a_mm.h, mm0 + m * 24 + 8 + qc + 1)
                            e.tensor_tensor(
                                stg[:, qc % 4, :],
                                pps[:, (m * 16 + 8 + qc) % 2, :],
                                p2stage[:, qc * 512:qc * 512 + 512],
                                ALU.add).then_inc(a_ev.h, 1)
                        for qc in range(KC):
                            e.wait_ge(a_mm.h, mm0 + m * 24 + 16 + qc + 1)
                            if m == 0:
                                i1 = e.tensor_copy(vgacc[0:1, qc, :],
                                                   vgps[0:1, qc % 2, :])
                            else:
                                i1 = e.tensor_tensor(vgacc[0:1, qc, :],
                                                     vgacc[0:1, qc, :],
                                                     vgps[0:1, qc % 2, :],
                                                     ALU.add)
                            kb.edge("v", i1, e)
                            nc.vector.engine_nop().then_inc(a_dv.h, 1)

                @blk.scalar
                def _(e, mm0=mm0, ev0=ev0, ac0=ac0):
                    for m in range(KC):
                        for h in range(2):
                            if not (m == 0 and h == 0):
                                need = (mm0 + m * 24 + 20 if h == 1
                                        else mm0 + m * 24)
                                e.wait_ge(a_mm.h, need)
                            e.wait_ge(a_ev.h, ev0 + m * 16 + 8 + 4 * (h + 1))
                            e.activation(ghb[:], stg[:], AF.Gelu,
                                         bias=bsb["bg1"][:, m:m + 1]
                                         ).then_inc(a_ac.h, 1)
            a_mm.v += 24 * KC
            a_ev.v += 16 * KC
            a_ac.v += 2 * KC
            a_dv.v += 8 * KC

        def pass_bh(wlneed):
            mm0, ev0, ac0, dv0 = a_mm.v, a_ev.v, a_ac.v, a_dv.v
            with nc.Block() as blk:
                @blk.tensor
                def _(e, mm0=mm0, ev0=ev0, ac0=ac0, dv0=dv0):
                    e.wait_ge(wl.h, wlneed)
                    for m in range(4):
                        for qc in range(KC):
                            gg = m * 8 + qc
                            if gg >= 2:
                                e.wait_ge(a_ev.h, ev0 + gg - 1)
                            mm_group(e, wbuf, m, qc, gg % 2).then_inc(a_mm.h, 1)
                        for qc in range(KC):
                            h = qc // 4
                            e.wait_ge(a_ac.h, ac0 + 2 * m + h + 1)
                            nq = m * 8 + qc
                            if nq >= 1:
                                e.wait_ge(a_dv.h, dv0 + nq)
                            e.matmul(vgps[0:1, 0, :], wbsmall[:, m, 0:1],
                                     ghb[:, qc % 4, :], start=True, stop=True
                                     ).then_inc(a_mm.h, 1)
                            e.matmul(vgps[0:1, 1, :], wbsmall[:, m, 1:2],
                                     ghb[:, qc % 4, :], start=True, stop=True
                                     ).then_inc(a_mm.h, 1)

                @blk.vector
                def _(e, mm0=mm0, ev0=ev0, dv0=dv0):
                    for m in range(4):
                        for qc in range(KC):
                            e.wait_ge(a_mm.h, mm0 + m * 24 + qc + 1)
                            e.tensor_copy(stg[:, qc % 4, :],
                                          pps[:, (m * 8 + qc) % 2, :]
                                          ).then_inc(a_ev.h, 1)
                        for qc in range(KC):
                            e.wait_ge(a_mm.h, mm0 + m * 24 + 8 + 2 * (qc + 1))
                            if m == 0:
                                i1 = e.tensor_copy(l0acc[0:1, qc, :],
                                                   vgps[0:1, 0, :])
                                kb.edge("v", i1, e)
                                i1 = e.tensor_copy(l1acc[0:1, qc, :],
                                                   vgps[0:1, 1, :])
                            else:
                                i1 = e.tensor_tensor(l0acc[0:1, qc, :],
                                                     l0acc[0:1, qc, :],
                                                     vgps[0:1, 0, :], ALU.add)
                                kb.edge("v", i1, e)
                                i1 = e.tensor_tensor(l1acc[0:1, qc, :],
                                                     l1acc[0:1, qc, :],
                                                     vgps[0:1, 1, :], ALU.add)
                            kb.edge("v", i1, e)
                            nc.vector.engine_nop().then_inc(a_dv.h, 1)

                @blk.scalar
                def _(e, mm0=mm0, ev0=ev0):
                    for m in range(4):
                        for h in range(2):
                            if not (m == 0 and h == 0):
                                need = (mm0 + m * 24 + 8 + 8 if h == 1
                                        else mm0 + m * 24)
                                e.wait_ge(a_mm.h, need)
                            e.wait_ge(a_ev.h, ev0 + m * 8 + 4 * (h + 1))
                            e.activation(ghb[:], stg[:], AF.Gelu,
                                         bias=bsb["bb1"][:, m:m + 1]
                                         ).then_inc(a_ac.h, 1)
            a_mm.v += 24 * 4
            a_ev.v += 8 * 4
            a_ac.v += 2 * 4
            a_dv.v += 8 * 4

        def pass_rows(pptm):
            ac0, dv0, st0 = a_ac.v, a_dv.v, a_st.v
            R0 = pptm[0:1, :, :].rearrange("p a b -> p (a b)")
            B1 = stgb[0:1, :, :, :].rearrange("p a b c -> p (a b c)")
            B2 = ob1[0:1, :, :, :].rearrange("p a b c -> p (a b c)")
            RV = vgacc[0:1, :, :].rearrange("p a b -> p (a b)")
            L0 = l0acc[0:1, :, :].rearrange("p a b -> p (a b)")
            L1 = l1acc[0:1, :, :].rearrange("p a b -> p (a b)")
            with nc.Block() as blk:
                @blk.vector
                def _(e, ac0=ac0, dv0=dv0):
                    i1 = e.tensor_tensor(bb2d[:], bsb["bb2"][:, 0:1],
                                         bsb["bb2"][:, 1:2], ALU.subtract)
                    kb.edge("v", i1, e)
                    i1 = e.tensor_tensor(R0, L0, L1, ALU.subtract)
                    i1.then_inc(a_dv.h, 1)
                    e.wait_ge(a_ac.h, ac0 + 2)
                    i1 = e.tensor_tensor_scan(RV, L1, L1, 0.0,
                                              ALU.add, ALU.bypass)
                    kb.edge("v", i1, e)
                    i1 = e.tensor_scalar(RV, RV, 1.0, None, ALU.max)
                    i1.then_inc(a_dv.h, 1)
                    e.wait_ge(a_ac.h, ac0 + 3)
                    i1 = e.reciprocal(RV, R0)
                    kb.edge("v", i1, e)
                    i1 = e.tensor_scalar(RV, RV, 1.0 / 32.0, None, ALU.mult)
                    kb.edge("v", i1, e)
                    i1 = e.tensor_scalar(B1, L0, 1.0 / 32.0, None, ALU.mult)
                    i1.then_inc(a_dv.h, 1)
                    i1 = e.tensor_scalar(R0, L0, -1.0, 1.0, ALU.mult, ALU.add)
                    kb.edge("v", i1, e)
                    i1 = e.tensor_tensor(B2, R0, RV, ALU.mult)
                    i1.then_inc(a_dv.h, 1)

                @blk.scalar
                def _(e, ac0=ac0, dv0=dv0):
                    e.wait_ge(a_dv.h, dv0 + 1)
                    e.activation(L0, R0, AF.Sigmoid,
                                 bias=bb2d[:, 0:1]).then_inc(a_ac.h, 1)
                    e.activation(L1, RV, AF.Sigmoid,
                                 bias=bsb["bg2"][:, 0:1]).then_inc(a_ac.h, 1)
                    e.wait_ge(a_dv.h, dv0 + 2)
                    e.activation(R0, RV, AF.Sqrt).then_inc(a_ac.h, 1)

                @blk.sync
                def _(e, ac0=ac0, dv0=dv0, st0=st0):
                    e.wait_ge(a_ac.h, ac0 + 2)
                    e.dma_start(out=rows_d.ap()[0:1, :],
                                in_=L1).then_inc(a_st.h, 16)
                    e.wait_ge(a_dv.h, dv0 + 3)
                    e.dma_start(out=rows_d.ap()[1:2, :],
                                in_=B1).then_inc(a_st.h, 16)
                    e.wait_ge(a_dv.h, dv0 + 4)
                    e.dma_start(out=rows_d.ap()[2:3, :],
                                in_=B2).then_inc(a_st.h, 16)
                    e.wait_ge(a_st.h, st0 + 48)
            a_ac.v += 3
            a_dv.v += 4
            a_st.v += 48

        def pass_pp(pptm, iscr):
            ld0 = a_ld.v
            with nc.Block() as blk:
                @blk.gpsimd
                def _(g):
                    g.dma_start(out=pptm[:],
                                in_=win["pp"][0:512, :]
                                .rearrange("(a p) d -> p a d", p=128)
                                ).then_inc(a_ld.h, 16)
            a_ld.v += 16
            for qc in range(KC):
                with nc.Block() as blk:
                    @blk.sync
                    def _(e, qc=qc):
                        e.wait_ge(a_ld.h, 16 * (ld0 // 16 + qc + 1))
                for kh in range(2):
                    P = qc * 2 + kh
                    pairs = [(pptm[:, i, (kh * 4 + k2) * 128:
                                   (kh * 4 + k2 + 1) * 128],
                              stg[:, k2, i * 128:(i + 1) * 128])
                             for i in range(4) for k2 in range(4)]
                    emit_transposes(kb, pairs, c["ident_f"], grp=2)
                    dv0, ac0, st0 = a_dv.v, a_ac.v, a_st.v
                    with nc.Block() as blk:
                        @blk.gpsimd
                        def _(g, qc=qc, kh=kh):
                            if kh == 1 and qc < KC - 1:
                                g.dma_start(
                                    out=pptm[:],
                                    in_=win["pp"][(qc + 1) * 512:(qc + 2) * 512, :]
                                    .rearrange("(a p) d -> p a d", p=128)
                                    ).then_inc(a_ld.h, 16)

                        @blk.vector
                        def _(e, dv0=dv0):
                            i1 = e.tensor_scalar(f32u[:], stg[:], INV2PI,
                                                 None, ALU.mult)
                            kb.edge("v", i1, e)
                            i1 = e.tensor_copy(iscr[:], f32u[:])
                            kb.edge("v", i1, e)
                            i1 = e.tensor_copy(f32u[:], iscr[:])
                            kb.edge("v", i1, e)
                            i1 = e.scalar_tensor_tensor(stg[:], f32u[:],
                                                        -2.0 * PI, stg[:],
                                                        ALU.mult, ALU.add)
                            kb.edge("v", i1, e)
                            i1 = e.tensor_scalar(ghb[:], stg[:], PI, None,
                                                 ALU.is_gt)
                            kb.edge("v", i1, e)
                            i1 = e.scalar_tensor_tensor(stg[:], ghb[:],
                                                        -2.0 * PI, stg[:],
                                                        ALU.mult, ALU.add)
                            kb.edge("v", i1, e)
                            i1 = e.tensor_scalar(ghb[:], stg[:], -PI, None,
                                                 ALU.is_lt)
                            kb.edge("v", i1, e)
                            i1 = e.scalar_tensor_tensor(stg[:], ghb[:],
                                                        2.0 * PI, stg[:],
                                                        ALU.mult, ALU.add)
                            i1.then_inc(a_dv.h, 1)
                            i1 = e.tensor_scalar(f32v[:], stg[:], PI / 2,
                                                 None, ALU.add)
                            kb.edge("v", i1, e)
                            i1 = e.tensor_scalar(ghb[:], f32v[:], PI, None,
                                                 ALU.is_gt)
                            kb.edge("v", i1, e)
                            i1 = e.scalar_tensor_tensor(f32v[:], ghb[:],
                                                        -2.0 * PI, f32v[:],
                                                        ALU.mult, ALU.add)
                            i1.then_inc(a_dv.h, 1)

                        @blk.scalar
                        def _(e, P=P, dv0=dv0, st0=st0):
                            e.wait_ge(a_dv.h, dv0 + 1)
                            if P >= 2:
                                e.wait_ge(a_st.h, st0 - 32)
                            e.activation(ob1[:, P % 2, :, :], stg[:],
                                         AF.Sin).then_inc(a_ac.h, 1)
                            e.wait_ge(a_dv.h, dv0 + 2)
                            e.activation(ob2[:, P % 2, :, :], f32v[:],
                                         AF.Sin).then_inc(a_ac.h, 1)

                        @blk.sync
                        def _(e, qc=qc, kh=kh, P=P, ac0=ac0):
                            e.wait_ge(a_ac.h, ac0 + 1)
                            e.dma_start(
                                out=ps_d.ap()[:, kh * 4:(kh + 1) * 4,
                                              qc * 512:(qc + 1) * 512],
                                in_=ob1[:, P % 2, :, :]).then_inc(a_st.h, 16)
                            e.wait_ge(a_ac.h, ac0 + 2)
                            e.dma_start(
                                out=pc_d.ap()[:, kh * 4:(kh + 1) * 4,
                                              qc * 512:(qc + 1) * 512],
                                in_=ob2[:, P % 2, :, :]).then_inc(a_st.h, 16)
                    a_dv.v += 2
                    a_ac.v += 2
                    a_st.v += 32
            a_ld.v += 16 * (KC - 1)
            st_all = a_st.v
            with nc.Block() as blk:
                @blk.sync
                def _(e):
                    e.wait_ge(a_st.h, st_all)

        pass_simple(wbuf, "bpv", pv_d, 16 * 14)
        with nc.Block() as blk:
            @blk.gpsimd
            def _(g):
                g.dma_start(out=wbuf[:], in_=win["wk1"]).then_inc(wl.h, 16)
        wl.v += 16
        pass_simple(wbufB, "bkv", kv_d, 16 * 14)
        with nc.Block() as blk:
            @blk.gpsimd
            def _(g):
                g.dma_start(out=wbufB[:], in_=win["wB"]).then_inc(wl.h, 16)
        wl.v += 16
        with kb.scope() as sck:
            pass_key(16 * 15, sck.sb([128, 4, 512], BF16, "thbA"))
        with nc.Block() as blk:
            @blk.gpsimd
            def _(g):
                g.dma_start(out=wbuf[:], in_=win["wA"]).then_inc(wl.h, 16)
        wl.v += 16
        with kb.scope() as scg:
            pass_gh(16 * 17, scg.sb([128, T + 1], BF16, "p2stA"))
        with nc.Block() as blk:
            @blk.gpsimd
            def _(g):
                g.dma_start(out=wbuf[:, :, 0:512],
                            in_=win["wb1"]).then_inc(wl.h, 16)
        wl.v += 16
        pass_bh(16 * 18)
        with kb.scope() as scp:
            pptm_ = scp.sb([128, 4, D], F32, "pptmA")
            iscr_ = scp.sb([128, 4, 512], I32, "iscrA")
            pass_rows(pptm_)
            pass_pp(pptm_, iscr_)

    CB = 512
    with kb.scope() as scB:
        wo_sb = scB.sb([128, KC, D], BF16, "wo2")
        pcb = scB.sb([128, KC, CB], BF16, "pcb2")
        psb = scB.sb([128, KC, CB], BF16, "psb2b")
        pvb = scB.sb([128, KC, CB], BF16, "pvb2")
        kvb = scB.sb([128, KC, CB], BF16, "kvb2")
        kcb = scB.sb([128, KC, CB + 1], BF16, "kcb2")
        ksb_ = scB.sb([128, KC, CB + 1], BF16, "ksb2")
        scrA = scB.sb([128, KC, CB], BF16, "scrA2")
        scrB = scB.sb([128, KC, CB], BF16, "scrB2")
        amc = scB.sb([128, KC, CB], BF16, "amc2")
        ams = scB.sb([128, KC, CB], BF16, "ams2")
        bmc = scB.sb([128, KC, CB], BF16, "bmc2")
        bms = scB.sb([128, KC, CB], BF16, "bms2")
        carryf = scB.sb([128, 4, KC], F32, "carry2")
        f32s = scB.sb([128, KC, CB], F32, "f32s2")
        f32t = scB.sb([128, KC, CB], F32, "f32t2")
        lnbf = scB.sb([128, KC, CB], BF16, "lnbf2")
        rowsl = scB.sb([1, 3, CB], BF16, "rowsl2")
        rowst = scB.sb([1, 4, CB], F32, "rowst2")
        vgb = scB.sb([128, CB], F32, "vgb2")
        w0b = scB.sb([128, CB], F32, "w0b2")
        w1b = scB.sb([128, CB], F32, "w1b2")
        meanb = scB.sb([128, CB], F32, "meanb2")
        rstdb = scB.sb([128, CB], F32, "rstdb2")
        ostg = scB.sb([128, CB], F32, "ostg2")
        x3t = scB.sb([128, 4, D], F32, "x3t2")
        resid = scB.sb([128, 2, D], F32, "resid2")
        ln2s = scB.sb([128, 2, D], BF16, "ln2s2")
        mmps = scB.ps([128, 2, 512], F32, name="mmpsB")
        tpp = scB.ps([128, 2, 512], F32, name="tppB")
        rwps = scB.ps([128, 4, 512], F32, name="rwpsB")

        def issue_chunk_loads(e, cb):
            cs0 = cb * CB
            def dmload(dst, src_d, off, nel):
                ap = bass.AP(src_d.ap().tensor, off,
                             [[KC * T, 128], [T, KC], [1, nel]])
                e.dma_start(out=dst, in_=ap).then_inc(a_ld.h, 16)
            dmload(pcb[:], pc_d, cs0, CB)
            dmload(psb[:], ps_d, cs0, CB)
            dmload(pvb[:], pv_d, cs0, CB)
            dmload(kvb[:], kv_d, cs0, CB)
            if cb == 0:
                dmload(kcb[:, :, 1:CB + 1], kc_d, 0, CB)
                dmload(ksb_[:, :, 1:CB + 1], ks_d, 0, CB)
            else:
                dmload(kcb[:], kc_d, cs0 - 1, CB + 1)
                dmload(ksb_[:], ks_d, cs0 - 1, CB + 1)
            for r in range(3):
                e.dma_start(out=rowsl[:, r, :],
                            in_=rows_d.ap()[r:r + 1, cs0:cs0 + CB]
                            ).then_inc(a_ld.h, 16)

        ld_base = a_ld.v
        with nc.Block() as blk:
            @blk.gpsimd
            def _(g):
                g.dma_start(out=wo_sb[:], in_=win["wo1"]).then_inc(wl.h, 16)
                issue_chunk_loads(g, 0)

            @blk.vector
            def _(e):
                e.memset(carryf[:], 0.0)
                e.memset(kcb[:, :, 0:1], 0.0)
                e.memset(ksb_[:, :, 0:1], 0.0)
        wl.v += 16
        a_ld.v += 144

        for cb in range(KC):
            mm0, dv0 = a_mm.v, a_dv.v
            ldneed = ld_base + 144 * (cb + 1)
            with nc.Block() as blk:
                @blk.tensor
                def _(e, mm0=mm0):
                    e.wait_ge(a_ld.h, ldneed)
                    for i in range(3):
                        mi = e.matmul(rwps[:, i, :], c["ones_row_bf"][:],
                                      rowsl[:, i, :], start=True, stop=True)
                    mi.then_inc(a_mm.h, 1)

                @blk.vector
                def _(e, cb=cb, mm0=mm0, dv0=dv0):
                    e.wait_ge(a_mm.h, mm0 + 1)
                    i1 = e.tensor_copy(vgb[:], rwps[:, 0, :])
                    kb.edge("v", i1, e)
                    i1 = e.tensor_copy(w0b[:], rwps[:, 1, :])
                    kb.edge("v", i1, e)
                    i1 = e.tensor_copy(w1b[:], rwps[:, 2, :])
                    kb.edge("v", i1, e)

                    def scans(si, dstbuf, srcbuf):
                        for k in range(KC):
                            i1 = e.tensor_tensor_scan(
                                dstbuf[:, k, :], srcbuf[:, k, :],
                                srcbuf[:, k, :], carryf[:, si, k:k + 1],
                                ALU.add, ALU.bypass)
                        kb.edge("v", i1, e)
                        i1 = e.tensor_copy(
                            carryf[:, si, :],
                            dstbuf[:, :, CB - 1:CB].rearrange(
                                "p a b -> p (a b)"))
                        kb.edge("v", i1, e)
                    i1 = e.tensor_tensor(scrA[:], pcb[:], pvb[:], ALU.mult)
                    kb.edge("v", i1, e)
                    scans(0, amc, scrA)
                    i1 = e.tensor_tensor(scrA[:], psb[:], pvb[:], ALU.mult)
                    kb.edge("v", i1, e)
                    scans(1, ams, scrA)
                    i1 = e.tensor_tensor(scrA[:], kcb[:, :, 0:CB], kvb[:],
                                         ALU.mult)
                    kb.edge("v", i1, e)
                    i1 = e.tensor_tensor(scrB[:], scrA[:], bc8(vgb), ALU.mult)
                    kb.edge("v", i1, e)
                    scans(2, bmc, scrB)
                    i1 = e.tensor_tensor(scrA[:], ksb_[:, :, 0:CB], kvb[:],
                                         ALU.mult)
                    kb.edge("v", i1, e)
                    i1 = e.tensor_tensor(scrB[:], scrA[:], bc8(vgb), ALU.mult)
                    kb.edge("v", i1, e)
                    scans(3, bms, scrB)
                    i1 = e.tensor_tensor(f32s[:], pcb[:], amc[:], ALU.mult)
                    kb.edge("v", i1, e)
                    i1 = e.tensor_tensor(f32t[:], psb[:], ams[:], ALU.mult)
                    kb.edge("v", i1, e)
                    i1 = e.tensor_tensor(f32s[:], f32s[:], f32t[:], ALU.add)
                    kb.edge("v", i1, e)
                    i1 = e.tensor_tensor(f32s[:], f32s[:], bc8(w0b), ALU.mult)
                    kb.edge("v", i1, e)
                    i1 = e.tensor_tensor(f32t[:], kcb[:, :, 1:CB + 1], bmc[:],
                                         ALU.mult)
                    kb.edge("v", i1, e)
                    i1 = e.tensor_tensor(scrA[:], ksb_[:, :, 1:CB + 1], bms[:],
                                         ALU.mult)
                    kb.edge("v", i1, e)
                    i1 = e.tensor_tensor(f32t[:], f32t[:], scrA[:], ALU.add)
                    kb.edge("v", i1, e)
                    i1 = e.tensor_tensor(f32t[:], f32t[:], bc8(w1b), ALU.mult)
                    kb.edge("v", i1, e)
                    i1 = e.tensor_tensor(f32s[:], f32s[:], f32t[:], ALU.add)
                    i1.then_inc(a_dv.h, 1)
            a_mm.v += 1
            a_dv.v += 1

            ac0, dv0, mm0 = a_ac.v, a_dv.v, a_mm.v
            with nc.Block() as blk:
                @blk.scalar
                def _(e, ac0=ac0, dv0=dv0):
                    i1 = e.activation(f32t[:], f32s[:], AF.Square)
                    i1.then_inc(a_ac.h, 1)
                    e.wait_ge(a_dv.h, dv0 + 1)
                    i1 = e.activation(rowst[:, 2, :], rowst[:, 1, :], AF.Sqrt)
                    i1.then_inc(a_ac.h, 1)

                @blk.tensor
                def _(e, ac0=ac0, dv0=dv0, mm0=mm0):
                    for k in range(KC):
                        mi = e.matmul(rwps[0:1, 0, :], c["ones_col_f"][:],
                                      f32s[:, k, :], start=(k == 0),
                                      stop=(k == KC - 1))
                    mi.then_inc(a_mm.h, 1)
                    e.wait_ge(a_ac.h, ac0 + 1)
                    for k in range(KC):
                        mi = e.matmul(rwps[0:1, 1, :], c["ones_col_f"][:],
                                      f32t[:, k, :], start=(k == 0),
                                      stop=(k == KC - 1))
                    mi.then_inc(a_mm.h, 1)
                    e.wait_ge(a_dv.h, dv0 + 2)
                    mi = e.matmul(rwps[:, 2, :], c["ones_row_f"][:],
                                  rowst[:, 0, :], start=True, stop=True)
                    mi.then_inc(a_mm.h, 1)
                    mi = e.matmul(rwps[:, 3, :], c["ones_row_f"][:],
                                  rowst[:, 3, :], start=True, stop=True)
                    mi.then_inc(a_mm.h, 1)

                @blk.vector
                def _(e, ac0=ac0, dv0=dv0, mm0=mm0):
                    e.wait_ge(a_mm.h, mm0 + 2)
                    i1 = e.tensor_scalar(rowst[:, 0, :], rwps[0:1, 0, :],
                                         1.0 / D, None, ALU.mult)
                    kb.edge("v", i1, e)
                    i1 = e.tensor_tensor(rowst[:, 1, :], rowst[:, 0, :],
                                         rowst[:, 0, :], ALU.mult)
                    kb.edge("v", i1, e)
                    i1 = e.scalar_tensor_tensor(rowst[:, 1, :],
                                                rwps[0:1, 1, :], 1.0 / D,
                                                rowst[:, 1, :], ALU.mult,
                                                ALU.subtract)
                    kb.edge("v", i1, e)
                    i1 = e.tensor_scalar(rowst[:, 1, :], rowst[:, 1, :], 1e-5,
                                         None, ALU.add)
                    i1.then_inc(a_dv.h, 1)
                    e.wait_ge(a_ac.h, ac0 + 2)
                    i1 = e.reciprocal(rowst[:, 3, :], rowst[:, 2, :])
                    kb.edge("v", i1, e)
                    nc.vector.engine_nop().then_inc(a_dv.h, 1)
                    e.wait_ge(a_mm.h, mm0 + 4)
                    i1 = e.tensor_copy(meanb[:], rwps[:, 2, :])
                    kb.edge("v", i1, e)
                    i1 = e.tensor_copy(rstdb[:], rwps[:, 3, :])
                    kb.edge("v", i1, e)
                    i1 = e.tensor_tensor(f32s[:], f32s[:], bc8(meanb),
                                         ALU.subtract)
                    kb.edge("v", i1, e)
                    i1 = e.tensor_tensor(f32s[:], f32s[:], bc8(rstdb),
                                         ALU.mult)
                    kb.edge("v", i1, e)
                    for k in range(KC):
                        i1 = e.tensor_scalar(lnbf[:, k, :], f32s[:, k, :],
                                             bsb["lng"][:, k:k + 1],
                                             bsb["lnb"][:, k:k + 1],
                                             ALU.mult, ALU.add)
                    i1.then_inc(a_dv.h, 1)
            a_ac.v += 2
            a_dv.v += 3
            a_mm.v += 4

            mm0, ev0, dv0, st0, ld0 = a_mm.v, a_ev.v, a_dv.v, a_st.v, b_ld.v
            with nc.Block() as blk:
                @blk.gpsimd
                def _(g, cb=cb):
                    if cb < KC - 1:
                        issue_chunk_loads(g, cb + 1)

                @blk.sync
                def _(e, cb=cb, dv0=dv0, st0=st0, ld0=ld0):
                    for t4 in range(4):
                        t = cb * 4 + t4
                        if t4 >= 2:
                            e.wait_ge(a_dv.h, dv0 + t4 - 1)
                        e.dma_start(out=resid[:, t4 % 2, :],
                                    in_=x2[t * 128:(t + 1) * 128, :]
                                    ).then_inc(b_ld.h, 16)
                        e.dma_start(out=ln2s[:, t4 % 2, :],
                                    in_=ln2_d.ap()[t * 128:(t + 1) * 128, :]
                                    ).then_inc(b_ld.h, 16)
                    for t4 in range(4):
                        t = cb * 4 + t4
                        e.wait_ge(a_dv.h, dv0 + t4 + 1)
                        e.dma_start(out=x3[t * 128:(t + 1) * 128, :],
                                    in_=x3t[:, t4, :]).then_inc(a_st.h, 16)
                    e.wait_ge(a_st.h, st0 + 64)

                @blk.tensor
                def _(e, cb=cb, mm0=mm0, ev0=ev0, dv0=dv0):
                    if cb == 0:
                        e.wait_ge(wl.h, 16 * 19)
                    for m in range(KC):
                        if m >= 2:
                            e.wait_ge(a_ev.h, ev0 + 5 * (m - 2) + 1)
                        for k in range(KC):
                            mi = e.matmul(mmps[:, m % 2, :],
                                          wo_sb[:, k, m * 128:(m + 1) * 128],
                                          lnbf[:, k, :], start=(k == 0),
                                          stop=(k == KC - 1))
                        mi.then_inc(a_mm.h, 1)
                        e.wait_ge(a_ev.h, ev0 + 5 * m + 1)
                        for t4 in range(4):
                            i = 4 * m + t4
                            if i >= 2:
                                j = i - 2
                                e.wait_ge(a_ev.h, ev0 + 5 * (j // 4)
                                          + (j % 4) + 2)
                            e.matmul(tpp[:, i % 2, 0:128],
                                     ostg[:, t4 * 128:(t4 + 1) * 128],
                                     c["ident_f"][:], start=True,
                                     stop=True).then_inc(a_mm.h, 1)

                @blk.vector
                def _(e, cb=cb, mm0=mm0, ev0=ev0, dv0=dv0, ld0=ld0):
                    for m in range(KC):
                        e.wait_ge(a_mm.h, mm0 + 5 * m + 1)
                        e.tensor_scalar(ostg[:], mmps[:, m % 2, :],
                                        bsb["bo1"][:, m:m + 1], None,
                                        ALU.add).then_inc(a_ev.h, 1)
                        for t4 in range(4):
                            i = 4 * m + t4
                            e.wait_ge(a_mm.h, mm0 + 5 * m + 2 + t4)
                            e.tensor_copy(x3t[:, t4, m * 128:(m + 1) * 128],
                                          tpp[:, i % 2, 0:128]
                                          ).then_inc(a_ev.h, 1)
                    for t4 in range(4):
                        e.wait_ge(b_ld.h, ld0 + 32 * (t4 + 1))
                        if t4 == 0:
                            e.wait_ge(a_ev.h, ev0 + 40)
                        i1 = e.tensor_tensor(x3t[:, t4, :], x3t[:, t4, :],
                                             ln2s[:, t4 % 2, :], ALU.add)
                        kb.edge("v", i1, e)
                        i1 = e.tensor_tensor(x3t[:, t4, :], x3t[:, t4, :],
                                             resid[:, t4 % 2, :], ALU.add)
                        kb.edge("v", i1, e)
                        nc.vector.engine_nop().then_inc(a_dv.h, 1)
            a_mm.v += KC * 5
            a_ev.v += KC + 32
            a_dv.v += 4
            b_ld.v += 32 * 4
            if cb < KC - 1:
                a_ld.v += 144
            a_st.v += 64


def build_l1_attn(nc, stage="full"):
    kb = KB(nc)
    x2 = nc.dram_tensor("x2", [T, D], F32, kind="ExternalInput").ap()
    win = {"pp": nc.dram_tensor("pp", [T, D], F32, kind="ExternalInput").ap()}
    for nm, shp in (("wpv", [128, KC, D]), ("wk1", [128, KC, D]),
                    ("wkv", [128, KC, D]), ("wA", [128, KC, D]),
                    ("wB", [128, KC, D]), ("wo1", [128, KC, D]),
                    ("wb1", [128, KC, 512]), ("wg2", [128, KC, 1]),
                    ("wb2", [128, 4, 2]), ("bpv", [128, KC]),
                    ("bk1", [128, KC]), ("bkv", [128, KC]),
                    ("bg1", [128, KC]), ("lng", [128, KC]),
                    ("lnb", [128, KC]), ("bo1", [128, KC]),
                    ("bb1", [128, 4]), ("bb2", [1, 2]), ("bg2", [1, 1])):
        win[nm] = nc.dram_tensor(nm, shp, F32, kind="ExternalInput").ap()
    x3 = nc.dram_tensor("x3", [T, D], F32, kind="ExternalOutput").ap()
    if stage == "v2":
        emit_l1_attn_v2(kb, x2, win, x3)
    else:
        emit_l1_attn(kb, x2, win, x3, stage=stage)
    kb.ctx.close()
    return nc


def prep_l1a(ins, core, x2=None):
    Wg1 = ins["L1_attn_Wg1"]
    return {
        "x2": np.ascontiguousarray(x2) if x2 is not None else None,
        "pp": np.ascontiguousarray(ins["L1_attn_pos_phases"][:T]),
        "wpv": _dm_w(ins["L1_attn_Wpv"]), "wk1": _dm_w(ins["L1_attn_Wk"]),
        "wkv": _dm_w(ins["L1_attn_Wkv"]), "wA": _dm_w(Wg1[:D]),
        "wB": _dm_w(Wg1[D:]), "wo1": _dm_w(ins["L1_attn_Wo"]),
        "wb1": _dm_w(ins["L1_attn_Wb1"]),
        "wg2": _dm_w(ins["L1_attn_Wg2"]),
        "wb2": np.ascontiguousarray(
            ins["L1_attn_Wb2"].reshape(4, 128, 2).transpose(1, 0, 2)),
        "bpv": _dm_b(ins["L1_attn_bpv"]), "bk1": _dm_b(ins["L1_attn_bk"]),
        "bkv": _dm_b(ins["L1_attn_bkv"]), "bg1": _dm_b(ins["L1_attn_bg1"]),
        "lng": _dm_b(ins["L1_attn_ln_g"]), "lnb": _dm_b(ins["L1_attn_ln_b"]),
        "bo1": _dm_b(ins["L1_attn_bo"]), "bb1": _dm_b(ins["L1_attn_bb1"]),
        "bb2": ins["L1_attn_bb2"].reshape(1, 2),
        "bg2": ins["L1_attn_bg2"].reshape(1, 1),
    }




def emit_l1_ffn(kb, x3, win, xout, n_cores=8, stage="full", dbg=None):
    nc = kb.nc
    NP2 = 2048
    MC = 512
    NMC = T // MC
    c = build_consts(kb, ["ident_bf", "ident_f", "ones_row_bf", "ones_row_f",
                          "ones_col_f", "ones_row_bf", "ones_col_bf"])
    e_dram = nc.dram_tensor(kb.name("edram"), [NP2, T], BF16)
    s_dram = nc.dram_tensor(kb.name("sdram"), [NP2, T], BF16)

    rows_rc = kb.sb([1, NMC, MC], BF16, "rows_rc")
    slots_bf = kb.sb([128, KC, NP2], BF16, "slots_bf")

    f_ld = kb.sem("f_ld")
    f_mm = kb.sem("f_mm2")
    f_ac = kb.sem("f_ac")
    f_dv = kb.sem("f_dv")
    f_st = kb.sem("f_st2")
    f_cc = kb.sem("f_cc")
    wl = kb.sem("lfwl")

    ln3_dram = nc.dram_tensor(kb.name("ln3d"), [T, D], BF16)
    scw = kb.scope()
    scw.__enter__()
    lnch = scw.sb([128, 4, D], BF16, "lnch")
    sacc = scw.sb([128, KC, NP2], F32, "sacc")
    sc1 = kb.scope()
    sc1.__enter__()
    phi_sb = sc1.sb([128, KC, NP2], BF16, "phi_sb")
    lnstage = sc1.sb([128, 8, D], BF16, "lnstage")
    ln3T = sc1.sb([128, KC, MC], BF16, "ln3T")
    e_sb = sc1.sb([128, 2, 512], BF16, "e_sb")
    mmps = sc1.ps([128, 2, 512], F32, name="mmps_f")
    rsps = sc1.ps([128, 2, 512], F32, name="rsps_f")
    with nc.Block() as blk:
        @blk.gpsimd
        def _(g):
            g.dma_start(out=phi_sb[:], in_=win["phi"]).then_inc(wl.h, 16)
            g.wait_ge(wl.h, wl.v + 16)
    wl.v += 16

    for g0 in range(0, NT, 8):
        emit_ln(kb, x3, lambda t: lnstage[:, t % 8, :], nt=8, group=4,
                tile_off=g0)
        st0 = f_st.v
        with nc.Block() as blk:
            @blk.sync
            def _(e, g0=g0, st0=st0):
                for i in range(8):
                    t = g0 + i
                    e.dma_start(out=ln3_dram.ap()[t * 128:(t + 1) * 128, :],
                                in_=lnstage[:, i, :]).then_inc(f_st.h, 16)
                e.wait_ge(f_st.h, st0 + 128)
        f_st.v += 128
    for mc in range(NMC):
        ms = slice(mc * MC, (mc + 1) * MC)
        ld0 = f_ld.v
        with nc.Block() as blk:
            @blk.sync
            def _(e, mc=mc, ld0=ld0):
                e.dma_start(out=lnch[:],
                            in_=ln3_dram.ap()[mc * MC:(mc + 1) * MC, :]
                            .rearrange("(a p) d -> p a d", p=128)
                            ).then_inc(f_ld.h, 16)
                e.wait_ge(f_ld.h, ld0 + 16)
        f_ld.v += 16
        pairs = [(lnch[:, mt, k * 128:(k + 1) * 128],
                  ln3T[:, k, mt * 128:(mt + 1) * 128])
                 for mt in range(4) for k in range(KC)]
        emit_transposes(kb, pairs, c["ident_bf"])
        mm0, ac0, st0, dv0 = f_mm.v, f_ac.v, f_st.v, f_dv.v
        with nc.Block() as blk:
            @blk.tensor
            def _(e, mc=mc, mm0=mm0, ac0=ac0):
                for npc in range(16):
                    if npc >= 2:
                        e.wait_ge(f_ac.h, ac0 + npc - 1)
                    for k in range(KC):
                        mi = e.matmul(mmps[:, npc % 2, :],
                                      phi_sb[:, k, npc * 128:(npc + 1) * 128],
                                      ln3T[:, k, :], start=(k == 0),
                                      stop=(k == KC - 1))
                    mi.then_inc(f_mm.h, 1)
                    e.wait_ge(f_ac.h, ac0 + npc + 1)
                    mi = e.matmul(rsps[0:1, 0, :], c["ones_col_bf"][:],
                                  e_sb[:, npc % 2, :], start=(npc == 0),
                                  stop=(npc == 15))
                    mi.then_inc(f_mm.h, 1)

            @blk.scalar
            def _(e, mc=mc, mm0=mm0, st0=st0):
                for npc in range(16):
                    e.wait_ge(f_mm.h, mm0 + 2 * npc + 1)
                    if npc >= 2:
                        e.wait_ge(f_st.h, st0 + 16 * (npc - 1))
                    e.activation(e_sb[:, npc % 2, :], mmps[:, npc % 2, :],
                                 AF.Exp).then_inc(f_ac.h, 1)

            @blk.sync
            def _(e, mc=mc, mm0=mm0, st0=st0):
                for npc in range(16):
                    e.wait_ge(f_mm.h, mm0 + 2 * npc + 2)
                    e.dma_start(
                        out=e_dram.ap()[npc * 128:(npc + 1) * 128, ms],
                        in_=e_sb[:, npc % 2, :]).then_inc(f_st.h, 16)

            @blk.vector
            def _(e, mc=mc, mm0=mm0, dv0=dv0):
                e.wait_ge(f_mm.h, mm0 + 32)
                with nc.allow_low_precision(reason="bf16 combine denominators"):
                    i1 = e.reciprocal(rows_rc[:, mc, :], rsps[0:1, 0, :])
                kb.edge("v", i1, e)
                nc.vector.engine_nop().then_inc(f_dv.h, 1)
        f_mm.v += 32
        f_ac.v += 16
        f_st.v += 16 * 16
        f_dv.v += 1

    cc0, st0 = f_cc.v, f_st.v
    with nc.Block() as blk:
        @blk.gpsimd
        def _(e, cc0=cc0, st0=st0):
            e.wait_ge(f_st.h, st0)
            e.collective_compute(
                "AllReduce", ALU.add,
                replica_groups=[list(range(n_cores))],
                ins=[e_dram.ap().opt()],
                outs=[s_dram.ap().opt()],
            ).then_inc(f_cc.h)
            e.wait_ge(f_cc.h, cc0 + 1)
    f_cc.v += 1

    sc1.__exit__(None, None, None)
    if stage == "g2":
        scw.__exit__(None, None, None)
        return

    sc2 = kb.scope()
    sc2.__enter__()
    en_sb = sc2.sb([128, 2, 512], BF16, "en_sb")
    sn_sb = sc2.sb([128, 2, 512], BF16, "sn_sb")
    ds1 = sc2.sb([128, 2, 512], F32, "ds1")
    ds2 = sc2.sb([128, 2, 512], F32, "ds2")
    dn_sb = sc2.sb([128, 2, 512], BF16, "dn_sb")
    disp = sc2.sb([128, 4, NP2], BF16, "disp")
    mmps = sc2.ps([128, 2, 512], F32, name="mmps_f2")
    rsps = sc2.ps([128, 2, 512], F32, name="rsps_f2")
    with nc.Block() as blk:
        @blk.vector
        def _(e):
            e.memset(sacc[:], 0.0)
    for mc in range(NMC):
        ms = slice(mc * MC, (mc + 1) * MC)
        ld0, mm0, dv0 = f_ld.v, f_mm.v, f_dv.v
        with nc.Block() as blk:
            @blk.sync
            def _(e, mc=mc, ld0=ld0, dv0=dv0, ac0=f_ac.v):
                e.dma_start(out=lnch[:],
                            in_=ln3_dram.ap()[mc * MC:(mc + 1) * MC, :]
                            .rearrange("(a p) d -> p a d", p=128)
                            ).then_inc(f_ld.h, 16)
                for npc in range(16):
                    if npc >= 2:
                        e.wait_ge(f_ac.h, ac0 + 2 * (npc - 2) + 1)
                    e.dma_start(out=en_sb[:, npc % 2, :],
                                in_=e_dram.ap()[npc * 128:(npc + 1) * 128, ms]
                                ).then_inc(f_ld.h, 16)
                    e.dma_start(out=sn_sb[:, npc % 2, :],
                                in_=s_dram.ap()[npc * 128:(npc + 1) * 128, ms]
                                ).then_inc(f_ld.h, 16)

            @blk.scalar
            def _(e, mc=mc, ld0=ld0, mm0=mm0, dv0=dv0, ac0=f_ac.v):
                for npc in range(16):
                    sl = npc % 2
                    e.wait_ge(f_ld.h, ld0 + 16 + 32 * (npc + 1))
                    i1 = e.activation(ds1[:, sl, :], en_sb[:, sl, :], AF.Ln)
                    kb.edge("s", i1, e)
                    i1 = e.activation(ds2[:, sl, :], sn_sb[:, sl, :], AF.Ln)
                    i1.then_inc(f_ac.h, 1)
                    e.wait_ge(f_dv.h, dv0 + 5 * npc + 1)
                    if npc >= 2:
                        e.wait_ge(f_mm.h, mm0 + 4 * (npc - 1))
                    e.activation(dn_sb[:, sl, :], ds1[:, sl, :],
                                 AF.Exp).then_inc(f_ac.h, 1)

            @blk.vector
            def _(e, mc=mc, mm0=mm0, dv0=dv0, ac0=f_ac.v):
                for it in range(17):
                    if it < 16:
                        sl = it % 2
                        e.wait_ge(f_ac.h, ac0 + 2 * it + 1)
                        i1 = e.tensor_tensor(ds1[:, sl, :], ds1[:, sl, :],
                                             ds2[:, sl, :], ALU.subtract)
                        i1.then_inc(f_dv.h, 1)
                    if it >= 1:
                        g_ = it - 1
                        for mt in range(4):
                            i = 4 * g_ + mt
                            e.wait_ge(f_mm.h, mm0 + i + 1)
                            mi = e.tensor_copy(
                                disp[:, mt, g_ * 128:(g_ + 1) * 128],
                                rsps[:, i % 2, 0:128])
                            mi.then_inc(f_dv.h, 1)

            @blk.tensor
            def _(e, mc=mc, mm0=mm0, dv0=dv0, ac0=f_ac.v):
                def evac_cnt(i):
                    q, r = i // 4, i % 4
                    return dv0 + 5 * q + r + 2
                i = 0
                for npc in range(16):
                    e.wait_ge(f_ac.h, ac0 + 2 * npc + 2)
                    for mt in range(4):
                        if i >= 2:
                            e.wait_ge(f_dv.h, evac_cnt(i - 2))
                        mi = e.matmul(rsps[:, i % 2, 0:128],
                                      dn_sb[:, npc % 2, mt * 128:(mt + 1) * 128],
                                      c["ident_bf"][:], start=True, stop=True)
                        mi.then_inc(f_mm.h, 1)
                        i += 1
        f_ld.v += 16 + 32 * 16
        f_dv.v += 16 + 64
        f_mm.v += 64
        f_ac.v += 32
        mm0, dv0 = f_mm.v, f_dv.v
        with nc.Block() as blk:
            @blk.tensor
            def _(e, mc=mc, mm0=mm0, dv0=dv0):
                for dc in range(KC):
                    for npq in range(4):
                        gi = dc * 4 + npq
                        if gi >= 2:
                            e.wait_ge(f_dv.h, dv0 + gi - 1)
                        for mt in range(4):
                            mi = e.matmul(
                                mmps[:, gi % 2, :],
                                lnch[:, mt, dc * 128:(dc + 1) * 128],
                                disp[:, mt, npq * 512:(npq + 1) * 512],
                                start=(mt == 0), stop=(mt == 3))
                        mi.then_inc(f_mm.h, 1)

            @blk.vector
            def _(e, mc=mc, mm0=mm0, dv0=dv0):
                for dc in range(KC):
                    for npq in range(4):
                        gi = dc * 4 + npq
                        e.wait_ge(f_mm.h, mm0 + gi + 1)
                        i1 = e.tensor_tensor(
                            sacc[:, dc, npq * 512:(npq + 1) * 512],
                            sacc[:, dc, npq * 512:(npq + 1) * 512],
                            mmps[:, gi % 2, :], ALU.add)
                        i1.then_inc(f_dv.h, 1)
        f_mm.v += 32
        f_dv.v += 32

    dv0 = f_dv.v
    with nc.Block() as blk:
        @blk.vector
        def _(e, dv0=dv0):
            for dc in range(KC):
                i1 = e.tensor_copy(slots_bf[:, dc, :], sacc[:, dc, :])
            i1.then_inc(f_dv.h, 1)
    f_dv.v += 1
    sc2.__exit__(None, None, None)
    scw.__exit__(None, None, None)

    if stage == "g3":
        dsem = kb.sem()
        with nc.Block() as blk:
            @blk.gpsimd
            def _(e):
                n0 = dsem.v
                for dc in range(KC):
                    e.dma_start(out=dbg[:, dc, :],
                                in_=slots_bf[:, dc, :]).then_inc(dsem.h, 16)
                e.wait_ge(dsem.h, n0 + 16 * KC)
        dsem.v += 16 * KC
        return

    y_sb = kb.sb([128, 16, D], BF16, "y_sb")
    with kb.scope() as sc:
        w4 = sc.sb([128, 2, 4, D], BF16, "w4")
        brow = sc.sb([1, 2, 4, D], BF16, "brow")
        yps = sc.ps([128, 2, 512], F32, name="yps")
        wsem = (kb.sem("w4a"), kb.sem("w4b"))
        bsem = (kb.sem("b4a"), kb.sem("b4b"))
        y_mm = kb.sem("y_mm")
        y_ev = kb.sem("y_ev")
        for ng in range(16):
            b0, mm0, ev0 = bsem[ng % 2].v, y_mm.v, y_ev.v
            w00 = (wsem[0].v, wsem[1].v)
            with nc.Block() as blk:
                @blk.gpsimd
                def _(e, ng=ng, mm0=mm0, w00=w00, ev0=ev0):
                    bap = bass.AP(win["b"].ap().tensor, ng * 4 * D,
                                  [[4 * D, 1], [D, 4], [1, D]])
                    e.dma_start(out=brow[:, ng % 2, :, :],
                                in_=bap).then_inc(bsem[ng % 2].h, 16)
                    for dc in range(KC):
                        if dc >= 2:
                            e.wait_ge(y_mm.h, mm0 + dc - 1)
                        e.dma_start(
                            out=w4[:, dc % 2, :, :],
                            in_=win["w"].ap()[ng * 4:(ng + 1) * 4, :, dc, :]
                            .rearrange("n p d -> p n d")
                            ).then_inc(wsem[dc % 2].h, 16)

                @blk.tensor
                def _(e, ng=ng, w00=w00, b0=b0, mm0=mm0, ev0=ev0):
                    if ng >= 1:
                        e.wait_ge(y_ev.h, ev0)
                    for dc in range(KC):
                        e.wait_ge(wsem[dc % 2].h,
                                  w00[dc % 2] + 16 * (dc // 2 + 1))
                        for half in range(2):
                            hs = slice(half * 512, (half + 1) * 512)
                            for j in range(4):
                                n = ng * 4 + j
                                mi = e.matmul(
                                    yps[32 * j:32 * j + 32, half, :],
                                    slots_bf[:, dc, n * 32:(n + 1) * 32],
                                    w4[:, dc % 2, j, hs],
                                    start=(dc == 0), stop=False,
                                    tile_position=(0, 32 * j))
                        mi.then_inc(y_mm.h, 1)
                    e.wait_ge(bsem[ng % 2].h, b0 + 16)
                    for half in range(2):
                        hs = slice(half * 512, (half + 1) * 512)
                        for j in range(4):
                            mi = e.matmul(
                                yps[32 * j:32 * j + 32, half, :],
                                c["ones_row_bf"][:, 0:32],
                                brow[0:1, ng % 2, j, hs],
                                start=False, stop=True,
                                tile_position=(0, 32 * j))
                        mi.then_inc(y_mm.h, 1)
                y_mm.v += KC + 2
                wsem[0].v += 16 * 4
                wsem[1].v += 16 * 4
                bsem[ng % 2].v += 16

                @blk.vector
                def _(e, ng=ng, mm0=mm0):
                    for half in range(2):
                        e.wait_ge(y_mm.h, mm0 + KC + half + 1)
                        mi = e.tensor_copy(
                            y_sb[:, ng, half * 512:(half + 1) * 512],
                            yps[:, half, :])
                    mi.then_inc(y_ev.h, 1)
                y_ev.v += 1

    if stage == "g4":
        dsem = kb.sem()
        with nc.Block() as blk:
            @blk.gpsimd
            def _(e):
                n0 = dsem.v
                for q in range(16):
                    e.dma_start(out=dbg[:, q, 0:1024],
                                in_=y_sb[:, q, :]).then_inc(dsem.h, 16)
                e.wait_ge(dsem.h, n0 + 256)
        dsem.v += 256
        return

    with kb.scope() as sc:
        comb = sc.sb([128, 16, 512], BF16, "comb")
        en2 = sc.sb([128, 2, 512], BF16, "en2")
        x3c = sc.sb([128, 4, D], F32, "x3c")
        outt = sc.sb([128, 4, D], F32, "outt")
        ops_ = sc.ps([128, 2, 512], F32, name="ops_g5")
        bcp = sc.ps([128, 2, 512], F32, name="bcp_g5")
        g_ld = kb.sem("g5ld")
        g_ld2 = kb.sem("g5ld2")
        g_mm = kb.sem("g5mm")
        g_dv = kb.sem("g5dv")
        g_st = kb.sem("g5st")
        for mc in range(NMC):
            ms = slice(mc * MC, (mc + 1) * MC)
            ld0, ld20, mm0, dv0, st0 = (g_ld.v, g_ld2.v, g_mm.v, g_dv.v,
                                        g_st.v)
            with nc.Block() as blk:
                @blk.sync
                def _(e, mc=mc, ld20=ld20, dv0=dv0, st0=st0):
                    for mt in range(4):
                        t = mc * 4 + mt
                        e.dma_start(out=x3c[:, mt, :],
                                    in_=x3[t * 128:(t + 1) * 128, :]
                                    ).then_inc(g_ld2.h, 16)
                    e.wait_ge(g_dv.h, dv0 + 16 + 8)
                    for mt in range(4):
                        t = mc * 4 + mt
                        e.dma_start(out=xout[t * 128:(t + 1) * 128, :],
                                    in_=outt[:, mt, :]).then_inc(g_st.h, 16)
                    e.wait_ge(g_st.h, st0 + 64)

                @blk.gpsimd
                def _(e, mc=mc, ld0=ld0, dv0=dv0):
                    for npc in range(16):
                        if npc >= 2:
                            e.wait_ge(g_dv.h, dv0 + npc - 1)
                        e.dma_start(
                            out=en2[:, npc % 2, :],
                            in_=e_dram.ap()[npc * 128:(npc + 1) * 128, ms]
                            ).then_inc(g_ld.h, 16)

                @blk.tensor
                def _(e, mc=mc, ld0=ld0, mm0=mm0, dv0=dv0):
                    for npc in range(16):
                        mi = e.matmul(bcp[:, npc % 2, :], c["ones_row_bf"][:],
                                      rows_rc[:, mc, :], start=True, stop=True)
                        mi.then_inc(g_mm.h, 1)
                        e.wait_ge(g_dv.h, dv0 + npc + 1)
                    for mt in range(4):
                        for half in range(2):
                            hs = slice(half * 512, (half + 1) * 512)
                            gi = 2 * mt + half
                            for npc in range(16):
                                mi = e.matmul(
                                    ops_[:, gi % 2, :],
                                    comb[:, npc, mt * 128:(mt + 1) * 128],
                                    y_sb[:, npc, hs],
                                    start=(npc == 0), stop=(npc == 15))
                            mi.then_inc(g_mm.h, 1)
                            if gi < 7:
                                e.wait_ge(g_dv.h, dv0 + 16 + gi)

                @blk.vector
                def _(e, mc=mc, ld0=ld0, ld20=ld20, mm0=mm0, dv0=dv0):
                    e.wait_ge(g_ld2.h, ld20 + 64)
                    for npc in range(16):
                        e.wait_ge(g_ld.h, ld0 + 16 * (npc + 1))
                        e.wait_ge(g_mm.h, mm0 + npc + 1)
                        i1 = e.tensor_tensor(comb[:, npc, :],
                                             en2[:, npc % 2, :],
                                             bcp[:, npc % 2, :], ALU.mult)
                        i1.then_inc(g_dv.h, 1)
                    for mt in range(4):
                        for half in range(2):
                            hs = slice(half * 512, (half + 1) * 512)
                            gi = 2 * mt + half
                            e.wait_ge(g_mm.h, mm0 + 16 + gi + 1)
                            i1 = e.tensor_tensor(outt[:, mt, hs],
                                                 ops_[:, gi % 2, :],
                                                 x3c[:, mt, hs], ALU.add)
                            i1.then_inc(g_dv.h, 1)
            g_ld.v += 16 * 16
            g_ld2.v += 64
            g_mm.v += 16 + 8
            g_dv.v += 16 + 8
            g_st.v += 64


def build_l1_ffn(nc, n_cores=8, stage="full"):
    kb = KB(nc)
    x3 = nc.dram_tensor("x3", [T, D], F32, kind="ExternalInput").ap()
    win = {
        "phi": nc.dram_tensor("phi", [128, KC, 2048], F32,
                              kind="ExternalInput").ap(),
        "w": nc.dram_tensor("w", [64, 128, KC, D], F32, kind="ExternalInput"),
        "b": nc.dram_tensor("b", [64, D], F32, kind="ExternalInput"),
    }
    xout = nc.dram_tensor("xout", [T, D], F32, kind="ExternalOutput").ap()
    dbg = (nc.dram_tensor("dbg", [128, 16, 2048], F32,
                          kind="ExternalOutput").ap()
           if stage != "full" else None)
    emit_l1_ffn(kb, x3, win, xout, n_cores=n_cores, stage=stage, dbg=dbg)
    kb.ctx.close()
    return nc


def prep_l1f(ins, core, x3=None):
    return {
        "x3": np.ascontiguousarray(x3) if x3 is not None else None,
        "phi": np.ascontiguousarray(
            ins["L1_ffn_phi"].reshape(KC, 128, 2048).transpose(1, 0, 2)),
        "w": np.ascontiguousarray(
            ins["L1_ffn_W"].reshape(64, KC, 128, D).transpose(0, 2, 1, 3)),
        "b": np.ascontiguousarray(ins["L1_ffn_b"]),
    }


def build_fused(nc):
    kb = KB(nc)
    x0 = nc.dram_tensor("x0", [T, D], F32, kind="ExternalInput").ap()
    out = nc.dram_tensor("xout", [T, D], F32, kind="ExternalOutput").ap()
    x1 = nc.dram_tensor("x1i", [T, D], F32).ap()
    x2 = nc.dram_tensor("x2i", [T, D], F32).ap()
    x3 = nc.dram_tensor("x3i", [T, D], F32).ap()

    win1 = {}
    for nm in ("wq", "wk", "wv", "wo"):
        win1[nm] = nc.dram_tensor(nm, [128, KC, D], F32,
                                  kind="ExternalInput").ap()
    for nm in ("bq", "bk", "bv", "bo"):
        win1[nm] = nc.dram_tensor(nm, [128, KC], F32,
                                  kind="ExternalInput").ap()
    win2 = {
        "gate": nc.dram_tensor("gate", [128, KC, 8], F32,
                               kind="ExternalInput").ap(),
        "wg": nc.dram_tensor("wg", [8, 128, KC, 2048], F32,
                             kind="ExternalInput"),
        "wu": nc.dram_tensor("wu", [8, 128, KC, 2048], F32,
                             kind="ExternalInput"),
        "wd": nc.dram_tensor("wd", [8, 128, 16, 1024], F32,
                             kind="ExternalInput"),
    }
    win3 = {"pp": nc.dram_tensor("pp", [T, D], F32, kind="ExternalInput").ap()}
    for nm, shp in (("wpv", [128, KC, D]), ("wk1", [128, KC, D]),
                    ("wkv", [128, KC, D]), ("wA", [128, KC, D]),
                    ("wB", [128, KC, D]), ("wo1", [128, KC, D]),
                    ("wb1", [128, KC, 512]), ("wg2", [128, KC, 1]),
                    ("wb2", [128, 4, 2]), ("bpv", [128, KC]),
                    ("bk1", [128, KC]), ("bkv", [128, KC]),
                    ("bg1", [128, KC]), ("lng", [128, KC]),
                    ("lnb", [128, KC]), ("bo1", [128, KC]),
                    ("bb1", [128, 4]), ("bb2", [1, 2]), ("bg2", [1, 1])):
        win3[nm] = nc.dram_tensor(nm, shp, F32, kind="ExternalInput").ap()
    win4 = {
        "phi": nc.dram_tensor("phi", [128, KC, 2048], F32,
                              kind="ExternalInput").ap(),
        "w": nc.dram_tensor("w", [64, 128, KC, D], F32, kind="ExternalInput"),
        "b": nc.dram_tensor("b", [64, D], F32, kind="ExternalInput"),
    }

    for snm, emitfn, args in (("l0a", emit_l0_attn, (kb, x0, win1, x1)),
                              ("l0f", emit_l0_ffn, (kb, x1, win2, x2)),
                              ("l1a", emit_l1_attn_v2, (kb, x2, win3, x3)),
                              ("l1f", emit_l1_ffn, (kb, x3, win4, out))):
        sc = kb.scope()
        sc.__enter__()
        kb.push_scope(sc)
        with nc.named_scope(snm):
            emitfn(*args)
        kb.pop_scope()
        sc.__exit__(None, None, None)
    kb.ctx.close()
    return nc


def prep_fused(ins, core):
    m = prep_l0a(ins, core)
    m.update(prep_l0f(ins, core))
    m.update(prep_l1a(ins, core))
    m.update(prep_l1f(ins, core))
    for k in ("x1", "x2", "x3"):
        m.pop(k, None)
    return m


_NC_CACHE = {}


def _get_nc(name, builder):
    if name not in _NC_CACHE:
        nc = bass.Bass("TRN2", target_bir_lowering=False)
        builder(nc)
        _NC_CACHE[name] = nc
    return _NC_CACHE[name]


def kernel(**inputs):
    ins = inputs
    cores = list(range(8))
    nc = _get_nc("fused", build_fused)
    maps = [prep_fused(ins, c_) for c_ in cores]
    res = run_bass_kernel_spmd(nc, maps, core_ids=cores)
    return np.stack([res.results[c_]["xout"] for c_ in cores])


def kernel_multi(**inputs):
    ins = inputs
    ncores = 8
    cores = list(range(ncores))

    nc1 = _get_nc("l0a", build_l0_attn)
    m0 = prep_l0a(ins, 0)
    maps = []
    for c_ in cores:
        m = dict(m0)
        m["x0"] = np.ascontiguousarray(ins["L0_attn_x"][c_])
        maps.append(m)
    r1 = run_bass_kernel_spmd(nc1, maps, core_ids=cores)
    x1s = [r1.results[c_]["x1"] for c_ in cores]

    nc2 = _get_nc("l0f", build_l0_ffn)
    m0 = prep_l0f(ins, 0)
    maps = []
    for c_ in cores:
        m = dict(m0)
        m["x1"] = np.ascontiguousarray(x1s[c_])
        maps.append(m)
    r2 = run_bass_kernel_spmd(nc2, maps, core_ids=cores)
    x2s = [r2.results[c_]["x2"] for c_ in cores]

    nc3 = _get_nc("l1a", build_l1_attn)
    m0 = prep_l1a(ins, 0)
    maps = []
    for c_ in cores:
        m = dict(m0)
        m["x2"] = np.ascontiguousarray(x2s[c_])
        maps.append(m)
    r3 = run_bass_kernel_spmd(nc3, maps, core_ids=cores)
    x3s = [r3.results[c_]["x3"] for c_ in cores]

    nc4 = _get_nc("l1f", build_l1_ffn)
    m0 = prep_l1f(ins, 0)
    maps = []
    for c_ in cores:
        m = dict(m0)
        m["x3"] = np.ascontiguousarray(x3s[c_])
        maps.append(m)
    r4 = run_bass_kernel_spmd(nc4, maps, core_ids=cores)
    out = np.stack([r4.results[c_]["xout"] for c_ in cores])
    return out



